# version 05
# baseline (speedup 1.0000x reference)
import numpy as np
import ml_dtypes
from contextlib import ExitStack

import concourse.bass as bass
import concourse.mybir as mybir
from concourse.bass_utils import run_bass_kernel_spmd

F32 = mybir.dt.float32
BF16 = mybir.dt.bfloat16
AF = mybir.ActivationFunctionType
ALU = mybir.AluOpType
AX = mybir.AxisListType
BF = ml_dtypes.bfloat16

T = 4096
D = 1024
KC = 8
SUB = 512
NSUB = T // SUB
NH = 16
HD = 64
NE = 8
NQP = T // SUB
NKT = T // 128
ROPE_THETA = 10000.0

DMA_CH = {"sp": "dsp", "pool": "dgp"}
ENG_SECT = {"pe": "tensor", "act": "scalar", "dve": "vector", "pool": "gpsimd", "sp": "sync"}
CHANNELS = ("pe", "act", "dve", "pool", "sp", "dsp", "dgp")


class Prog:

    def __init__(self, nc, stack):
        self.nc = nc
        self.sems = {ch: stack.enter_context(nc.semaphore(f"s_{ch}")) for ch in CHANNELS}
        self.counts = {ch: 0 for ch in CHANNELS}
        self.known = {e: {ch: 0 for ch in CHANNELS} for e in ENG_SECT}
        self.thunks = {e: [] for e in ENG_SECT}

    def _waits_for(self, eng, reads):
        merged = {}
        for tok in reads:
            if tok is None:
                continue
            ch, v = tok
            if v > merged.get(ch, 0):
                merged[ch] = v
        waits = []
        for ch, v in merged.items():
            if self.known[eng][ch] < v:
                waits.append((self.sems[ch], v))
                self.known[eng][ch] = v
        return waits

    def op(self, eng, fn, reads=(), drain=False):
        waits = self._waits_for(eng, reads)
        self.counts[eng] += 1
        tok = (eng, self.counts[eng])
        sem = self.sems[eng]

        def thunk(e, waits=waits, fn=fn, drain=drain, sem=sem):
            for s, v in waits:
                e.wait_ge(s, v)
            if drain:
                e.drain()
            fn(e).then_inc(sem, 1)

        self.thunks[eng].append(thunk)
        return tok

    def dma(self, eng, fn, reads=()):
        waits = self._waits_for(eng, reads)
        ch = DMA_CH[eng]
        self.counts[ch] += 16
        tok = (ch, self.counts[ch])
        sem = self.sems[ch]

        def thunk(e, waits=waits, fn=fn, sem=sem):
            for s, v in waits:
                e.wait_ge(s, v)
            fn(e).then_inc(sem, 16)

        self.thunks[eng].append(thunk)
        return tok

    def barrier(self):
        toks = [(ch, self.counts[ch]) for ch in CHANNELS]
        for eng in ENG_SECT:
            waits = self._waits_for(eng, toks)
            if waits:
                def thunk(e, waits=waits):
                    for s, v in waits:
                        e.wait_ge(s, v)
                self.thunks[eng].append(thunk)

    def emit(self):
        with self.nc.Block() as blk:
            for eng, sect in ENG_SECT.items():
                getattr(blk, sect)(self._make_runner(eng))

    def _make_runner(self, eng):
        thunks = self.thunks[eng]

        def run(e):
            for t in thunks:
                t(e)
        return run



def _host_consts():
    i = np.arange(HD // 2, dtype=np.float64)
    inv_freq = 1.0 / ROPE_THETA ** (2 * i / HD)
    pos = np.arange(T, dtype=np.float64)
    ang = pos[None, :] * inv_freq[:, None]
    cos32 = np.cos(ang)
    sin32 = np.sin(ang)
    cos64 = np.repeat(cos32, 2, axis=0)
    sin64s = np.empty((HD, T))
    sin64s[0::2] = -sin32
    sin64s[1::2] = sin32
    cos128 = np.concatenate([cos64, cos64], 0).astype(np.float32)
    sin128 = np.concatenate([sin64s, sin64s], 0).astype(np.float32)

    p64 = np.zeros((HD, HD), np.float32)
    for k in range(0, HD, 2):
        p64[k, k + 1] = 1.0
        p64[k + 1, k] = 1.0
    p128 = np.zeros((128, 128), np.float32)
    p128[:64, :64] = p64
    p128[64:, 64:] = p64

    cmask = np.zeros((128, 4 * SUB), np.float32)
    dk = np.arange(128)[:, None]
    qr = np.arange(SUB)[None, :]
    for j in range(4):
        cmask[:, j * SUB:(j + 1) * SUB] = (qr >= 128 * j + dk).astype(np.float32)

    ident = np.eye(128, dtype=np.float32)
    ohsel = np.zeros((NE, NE * 128), np.float32)
    for e in range(NE):
        ohsel[e, e * 128:(e + 1) * 128] = 1.0
    return dict(c_cos=cos128, c_sin=sin128, c_p128=p128, c_cmask=cmask,
                c_ident=ident, c_ohsel=ohsel)


def _prep_weights(inp):
    c = lambda a: np.ascontiguousarray(np.asarray(a, dtype=np.float32))
    w = {}
    w["l0_qkvT"] = c(np.asarray(inp["L0_attn_qkv_w"]).T)
    qb = np.asarray(inp["L0_attn_qkv_b"])
    qkvb = np.zeros((128, 10), np.float32)
    qkvb[:, 0:9] = np.asarray(qb).reshape(9, 128).T
    qkvb[0:64, 9] = qb[1088:1152]
    qkvb[64:128, 8] = 0.0
    w["l0_qkvb"] = c(qkvb)
    w["l0_outT"] = c(np.asarray(inp["L0_attn_out_w"]).T)
    w["l0_outb"] = c(np.asarray(inp["L0_attn_out_b"]).reshape(8, 128).T)
    w["l0_WgT"] = c(np.asarray(inp["L0_ffn_Wg"]).T)
    w["l0_bg"] = c(np.asarray(inp["L0_ffn_bg"]).reshape(8, 1))
    w["l0_w1T"] = c(np.asarray(inp["L0_ffn_w1"]).transpose(0, 2, 1))
    w["l0_w2T"] = c(np.asarray(inp["L0_ffn_w2"]).transpose(0, 2, 1))
    w["l1_Wdq"] = c(inp["L1_attn_W_dq"])
    w["l1_Wuq"] = c(inp["L1_attn_W_uq"])
    w["l1_Wdkv"] = c(inp["L1_attn_W_dkv"])
    w["l1_Wuk"] = c(inp["L1_attn_W_uk"])
    w["l1_Wuv"] = c(inp["L1_attn_W_uv"])
    w["l1_Wo"] = c(inp["L1_attn_W_o"])
    w["l1_router"] = c(inp["L1_ffn_w_router"])
    w["l1_fc1"] = c(inp["L1_ffn_fc1_w"])
    w["l1_fc2"] = c(inp["L1_ffn_fc2_w"])
    w["l1_gate"] = c(inp["L1_ffn_gate_w"])
    w["l1_up"] = c(inp["L1_ffn_up_w"])
    w["l1_down"] = c(inp["L1_ffn_down_w"])
    return w


class Bufs:
    pass


def _sb(stack, nc, name, shape, dtype, side=None):
    kw = {} if side is None else {"side": side}
    return stack.enter_context(nc.sbuf_tensor(name, shape, dtype, **kw))


def _ps(stack, nc, name, shape, dtype=F32):
    return stack.enter_context(nc.psum_tensor(name, shape, dtype))


def _chunk3(dram_ap):
    return dram_ap.rearrange("(kc p) m -> p kc m", p=128)



def emit_consts(p, st, dr):
    nc = p.nc
    C = Bufs()
    C.ident_f = _sb(st, nc, "ident_f", [128, 128], F32, side="right")
    C.ident_b = _sb(st, nc, "ident_b", [128, 128], BF16, side="right")
    C.cmask = _sb(st, nc, "cmask", [128, 4 * SUB], BF16, side="right")
    C.ohsel = _sb(st, nc, "ohsel", [NE, NE * 128], BF16, side="right")
    C.ones_b = _sb(st, nc, "ones_b", [128, 1], BF16, side="right")
    C.ones_fr = _sb(st, nc, "ones_fr", [128, 128], F32, side="right")
    C.ones_br = _sb(st, nc, "ones_br", [128, 128], BF16, side="right")
    C.fconst = _sb(st, nc, "fconst", [1, 2], F32, side="right")
    p.op("dve", lambda e: e.memset(C.fconst[0:1, 0:1], 1e-5))
    p.op("dve", lambda e: e.memset(C.fconst[0:1, 1:2], 1e-30))
    p.dma("sp", lambda e: e.dma_start(out=C.ident_f[:, :], in_=dr["c_ident"][:, :]))
    p.dma("pool", lambda e: e.dma_start(out=C.cmask[:, :], in_=dr["c_cmask"][:, :]))
    p.dma("pool", lambda e: e.dma_start(out=C.ident_b[:, :], in_=dr["c_ident"][:, :]))
    p.dma("pool", lambda e: e.dma_start(out=C.ohsel[:, :], in_=dr["c_ohsel"][:, :]))
    p.op("dve", lambda e: e.memset(C.ones_b[:, :], 1.0))
    p.op("dve", lambda e: e.memset(C.ones_fr[:, :], 1.0))
    p.op("dve", lambda e: e.memset(C.ones_br[:, :], 1.0))
    p.barrier()
    return C


def emit_ln(p, st, C, src_dram, xn, layer):
    nc = p.nc
    with ExitStack() as ph:
        x_f = _sb(ph, nc, f"ln_xf{layer}", [128, KC * SUB], F32)
        x_b = _sb(ph, nc, f"ln_xb{layer}", [128, KC * SUB], BF16)
        xsq = _sb(ph, nc, f"ln_xsq{layer}", [128, KC * SUB], BF16)
        tmp = _sb(ph, nc, f"ln_tmp{layer}", [128, KC * SUB], F32)
        rows = _sb(ph, nc, f"ln_rows{layer}", [1, 6 * SUB], F32)
        p_sum = _ps(ph, nc, f"lnps{layer}", [1, SUB])
        p_sq = _ps(ph, nc, f"lnpq{layer}", [1, SUB])
        p_M = _ps(ph, nc, f"lnpM{layer}", [128, SUB])
        p_R = _ps(ph, nc, f"lnpR{layer}", [128, SUB])

        src3 = src_dram[:, :].rearrange("(kc p) t -> p kc t", p=128)
        for s in range(NSUB):
            tl = p.dma("sp", lambda e, s=s: e.dma_start(
                out=x_f[:, :].rearrange("p (kc t) -> p kc t", kc=KC),
                in_=src3[:, :, s * SUB:(s + 1) * SUB]))
            tlb = p.dma("pool", lambda e, s=s: e.dma_start(
                out=x_b[:, :].rearrange("p (kc t) -> p kc t", kc=KC),
                in_=src3[:, :, s * SUB:(s + 1) * SUB]))
            tsq = []
            for kc in range(KC):
                tsq.append(p.op("act", lambda e, kc=kc: e.activation(
                    xsq[:, kc * SUB:(kc + 1) * SUB], x_b[:, kc * SUB:(kc + 1) * SUB],
                    AF.Square), reads=[tlb]))
            tm = tv = None
            for kc in range(KC):
                tm = p.op("pe", lambda e, kc=kc: e.matmul(
                    p_sum[:, :], C.ones_b[:, :], x_b[:, kc * SUB:(kc + 1) * SUB],
                    start=(kc == 0), stop=(kc == KC - 1)), reads=[tlb])
            for kc in range(KC):
                tv = p.op("pe", lambda e, kc=kc: e.matmul(
                    p_sq[:, :], C.ones_b[:, :], xsq[:, kc * SUB:(kc + 1) * SUB],
                    start=(kc == 0), stop=(kc == KC - 1)), reads=[tsq[kc]])
            R = lambda j: rows[0:1, j * SUB:(j + 1) * SUB]
            t1 = p.op("dve", lambda e: e.tensor_scalar_mul(R(0), p_sum[:, :], 1.0 / D),
                      reads=[tm])
            t2 = p.op("dve", lambda e: e.tensor_scalar_mul(R(1), p_sq[:, :], 1.0 / D),
                      reads=[tv], drain=True)
            t3 = p.op("dve", lambda e: e.tensor_mul(R(2), R(0), R(0)),
                      reads=[t1], drain=True)
            t4 = p.op("dve", lambda e: e.tensor_sub(R(3), R(1), R(2)),
                      reads=[t2, t3], drain=True)
            t5 = p.op("act", lambda e: e.activation(R(4), R(3), AF.Ln, bias=C.fconst[0:1, 0:1]),
                      reads=[t4])
            t6 = p.op("act", lambda e: e.activation(R(5), R(4), AF.Exp, scale=-0.5),
                      reads=[t5], drain=True)
            tM = p.op("pe", lambda e: e.matmul(p_M[:, :], C.ones_fr[0:1, :], R(0),
                                               start=True, stop=True), reads=[t1, t3])
            tR = p.op("pe", lambda e: e.matmul(p_R[:, :], C.ones_fr[0:1, :], R(5),
                                               start=True, stop=True), reads=[t6])
            tsub = []
            for kc in range(KC):
                tsub.append(p.op("dve", lambda e, kc=kc: e.tensor_sub(
                    tmp[:, kc * SUB:(kc + 1) * SUB], x_f[:, kc * SUB:(kc + 1) * SUB],
                    p_M[:, :]), reads=[tl, tM]))
            for kc in range(KC):
                p.op("dve", lambda e, kc=kc, s=s: e.tensor_mul(
                    xn[:, kc * T + s * SUB: kc * T + (s + 1) * SUB],
                    tmp[:, kc * SUB:(kc + 1) * SUB], p_R[:, :]), reads=[tsub[kc], tR])
            p.barrier()


def emit_routing(p, st, C, xn, c_all, router_dram, bias_dram, layer):
    nc = p.nc
    with ExitStack() as ph:
        wrt = _sb(ph, nc, f"rt_w{layer}", [128, KC * NE], BF16)
        lg = _sb(ph, nc, f"rt_lg{layer}", [NE, SUB], F32)
        bias_s = _sb(ph, nc, f"rt_b{layer}", [NE, 1], F32)
        ltok = _sb(ph, nc, f"rt_lt{layer}", [128, NE], F32)
        sc = _sb(ph, nc, f"rt_sc{layer}", [128, 8], F32)
        msk = _sb(ph, nc, f"rt_msk{layer}", [128, NE], F32)
        pex = _sb(ph, nc, f"rt_pex{layer}", [128, NE], F32)
        ctok = _sb(ph, nc, f"rt_ct{layer}", [128, NE], F32)
        p_lg = _ps(ph, nc, f"rt_plg{layer}", [NE, SUB])
        p_t = _ps(ph, nc, f"rt_pt{layer}", [128, NE])
        p_ct = _ps(ph, nc, f"rt_pct{layer}", [NE, 128])

        tw = p.dma("pool", lambda e: e.dma_start(
            out=wrt[:, :].rearrange("p (kc e) -> p kc e", kc=KC),
            in_=_chunk3(router_dram[:, :])))
        if bias_dram is not None:
            tb = p.dma("sp", lambda e: e.dma_start(out=bias_s[:, :], in_=bias_dram[:, :]))
        else:
            tb = p.op("dve", lambda e: e.memset(bias_s[:, :], 0.0))

        t0_prev = None
        for s in range(NSUB):
            tmm = None
            for kc in range(KC):
                tmm = p.op("pe", lambda e, kc=kc, s=s: e.matmul(
                    p_lg[:, :], wrt[:, kc * NE:(kc + 1) * NE],
                    xn[:, kc * T + s * SUB: kc * T + (s + 1) * SUB],
                    start=(kc == 0), stop=(kc == KC - 1)), reads=[tw])
            te = p.op("dve", lambda e: e.tensor_scalar(lg[:, :], p_lg[:, :],
                                                       bias_s[0:NE, 0:1], None, ALU.add),
                      reads=[tmm, tb])
            for q in range(SUB // 128):
                tt = p.op("pe", lambda e, q=q: e.transpose(
                    p_t[:, :], lg[:, q * 128:(q + 1) * 128], C.ident_f[0:NE, 0:NE]),
                    reads=[te, t0_prev])
                t0 = p.op("dve", lambda e: e.tensor_copy(ltok[:, :], p_t[:, :]), reads=[tt])
                t0_prev = t0
                t1 = p.op("dve", lambda e: e.reduce_max(sc[:, 0:1], ltok[:, :], axis=AX.X,
                                                        negate=True), reads=[t0], drain=True)
                t2 = p.op("dve", lambda e: e.tensor_scalar(msk[:, :], ltok[:, :], sc[:, 0:1],
                                                           None, ALU.add), reads=[t1],
                          drain=True)
                t3 = p.op("dve", lambda e: e.tensor_scalar(msk[:, :], msk[:, :], 0.0, None,
                                                           ALU.is_ge), reads=[t2], drain=True)
                t4 = p.op("dve", lambda e: e.scalar_tensor_tensor(
                    msk[:, :], msk[:, :], -1e9, ltok[:, :], ALU.mult, ALU.add),
                    reads=[t3], drain=True)
                t5 = p.op("dve", lambda e: e.reduce_max(sc[:, 1:2], msk[:, :], axis=AX.X),
                          reads=[t4], drain=True)
                t6 = p.op("act", lambda e: e.activation(pex[:, :], ltok[:, :], AF.Exp,
                                                        bias=sc[:, 0:1]), reads=[t1])
                t7 = p.op("dve", lambda e: e.tensor_scalar(ctok[:, :], ltok[:, :], sc[:, 1:2],
                                                           None, ALU.is_ge),
                          reads=[t5], drain=True)
                t8 = p.op("dve", lambda e: e.tensor_mul(ctok[:, :], ctok[:, :], pex[:, :]),
                          reads=[t6, t7], drain=True)
                t9 = p.op("dve", lambda e: e.reduce_sum(sc[:, 2:3], ctok[:, :], axis=AX.X),
                          reads=[t8], drain=True)
                ta = p.op("dve", lambda e: e.reciprocal(sc[:, 3:4], sc[:, 2:3]),
                          reads=[t9], drain=True)
                tb2 = p.op("dve", lambda e: e.tensor_scalar(ctok[:, :], ctok[:, :],
                                                            sc[:, 3:4], None, ALU.mult),
                           reads=[ta], drain=True)
                tc = p.op("pe", lambda e: e.transpose(p_ct[:, :], ctok[:, :],
                                                      C.ident_f[:, :]), reads=[tb2])
                p.op("dve", lambda e, s=s, q=q: e.tensor_copy(
                    c_all[:, s * SUB + q * 128: s * SUB + (q + 1) * 128], p_ct[:, :]),
                    reads=[tc], drain=True)
            p.barrier()


def emit_moe(p, st, C, xn, c_all, fsc, layer, experts):
    nc = p.nc
    with ExitStack() as ph:
        a_buf = _sb(ph, nc, f"moe_a{layer}", [128, KC * T], BF16)
        w1s = [_sb(ph, nc, f"moe_w1{layer}_{i}", [128, 2 * KC * 128], BF16) for i in range(2)]
        w2s = [_sb(ph, nc, f"moe_w2{layer}_{i}", [128, KC * 128], BF16) for i in range(2)]
        silu_t = [_sb(ph, nc, f"moe_st{layer}_{i}", [128, SUB], BF16) for i in range(2)]
        c_sb = _sb(ph, nc, f"moe_cb{layer}", [128, T], BF16)
        o_t = [_sb(ph, nc, f"moe_ot{layer}_{i}", [128, SUB], F32) for i in range(3)]
        p_h = [_ps(ph, nc, f"moe_ph{layer}_{i}", [128, SUB]) for i in range(4)]
        p_o = [_ps(ph, nc, f"moe_po{layer}_{i}", [128, SUB]) for i in range(2)]
        p_c = _ps(ph, nc, f"moe_pc{layer}", [128, SUB])

        fsc3 = fsc[:, :].rearrange("(mc p) t -> p mc t", p=128)
        region_tok = {}
        w1_last = [None, None]
        w2_last = [None, None]
        ph_last = [None] * 4
        po_last = [None] * 2
        ot_last = [None] * 3
        cs_last = None
        si_last = [None, None]
        pc_last = None

        for ei, ex in enumerate(experts):
            for ocp in range(KC):
                wb = w1s[ocp % 2]
                twa = p.dma("pool", lambda e, ex=ex, ocp=ocp, wb=wb: e.dma_start(
                    out=wb[:, 0:KC * 128].rearrange("p (kc m) -> p kc m", kc=KC),
                    in_=ex["w1a"](ocp)), reads=[w1_last[ocp % 2]])
                twb = p.dma("pool", lambda e, ex=ex, ocp=ocp, wb=wb: e.dma_start(
                    out=wb[:, KC * 128:2 * KC * 128].rearrange("p (kc m) -> p kc m", kc=KC),
                    in_=ex["w1b"](ocp)), reads=[w1_last[ocp % 2]])
                last_a = None
                for s in range(NSUB):
                    ia, ib = (2 * s) % 4, (2 * s + 1) % 4
                    pa, pb = p_h[ia], p_h[ib]
                    tma = tmb = None
                    for kc in range(KC):
                        tma = p.op("pe", lambda e, kc=kc, s=s, wb=wb, pa=pa: e.matmul(
                            pa[:, :], wb[:, kc * 128:(kc + 1) * 128],
                            xn[:, kc * T + s * SUB: kc * T + (s + 1) * SUB],
                            start=(kc == 0), stop=(kc == KC - 1)),
                            reads=[twa, ph_last[ia]])
                    for kc in range(KC):
                        tmb = p.op("pe", lambda e, kc=kc, s=s, wb=wb, pb=pb: e.matmul(
                            pb[:, :], wb[:, (KC + kc) * 128:(KC + kc + 1) * 128],
                            xn[:, kc * T + s * SUB: kc * T + (s + 1) * SUB],
                            start=(kc == 0), stop=(kc == KC - 1)),
                            reads=[twb, ph_last[ib]])
                    stl = silu_t[s % 2]
                    ts = p.op("act", lambda e, stl=stl, pa=pa: e.activation(
                        stl[:, :], pa[:, :], AF.Silu), reads=[tma, si_last[s % 2]])
                    ta = p.op("dve", lambda e, ocp=ocp, s=s, stl=stl, pb=pb: e.tensor_mul(
                        a_buf[:, ocp * T + s * SUB: ocp * T + (s + 1) * SUB],
                        stl[:, :], pb[:, :]), reads=[ts, tmb])
                    si_last[s % 2] = ta
                    ph_last[ia] = ts
                    ph_last[ib] = ta
                    last_a = ta
                w1_last[ocp % 2] = last_a
            if ex["use_c"]:
                for s in range(NSUB):
                    tcm = p.op("pe", lambda e, ex=ex, s=s: e.matmul(
                        p_c[:, :], C.ohsel[:, ex["e_idx"] * 128:(ex["e_idx"] + 1) * 128],
                        c_all[:, s * SUB:(s + 1) * SUB], start=True, stop=True),
                        reads=[pc_last, ex["c_tok"]])
                    tce = p.op("dve", lambda e, s=s: e.tensor_copy(
                        c_sb[:, s * SUB:(s + 1) * SUB], p_c[:, :]),
                        reads=[tcm, cs_last])
                    pc_last = tce
                cs_tok = pc_last
            else:
                cs_tok = None
            for mc in range(KC):
                tw2 = p.dma("pool", lambda e, ex=ex, mc=mc: e.dma_start(
                    out=w2s[mc % 2][:, :].rearrange("p (kc m) -> p kc m", kc=KC),
                    in_=ex["w2"](mc)), reads=[w2_last[mc % 2]])
                last_ev = None
                for s in range(NSUB):
                    po = p_o[s % 2]
                    tmo = None
                    for kc in range(KC):
                        tmo = p.op("pe", lambda e, kc=kc, s=s, mc=mc, po=po: e.matmul(
                            po[:, :], w2s[mc % 2][:, kc * 128:(kc + 1) * 128],
                            a_buf[:, kc * T + s * SUB: kc * T + (s + 1) * SUB],
                            start=(kc == 0), stop=(kc == KC - 1)),
                            reads=[tw2, po_last[s % 2]])
                    i3 = (mc * NSUB + s) % 3
                    ot = o_t[i3]
                    if ex["use_c"]:
                        tev = p.op("dve", lambda e, ot=ot, po=po, s=s: e.tensor_mul(
                            ot[:, :], po[:, :], c_sb[:, s * SUB:(s + 1) * SUB]),
                            reads=[tmo, cs_tok, ot_last[i3]])
                    else:
                        tev = p.op("dve", lambda e, ot=ot, po=po: e.tensor_copy(
                            ot[:, :], po[:, :]), reads=[tmo, ot_last[i3]])
                    po_last[s % 2] = tev
                    last_ev = tev
                    accum = ALU.add if ei > 0 else ALU.bypass
                    td = p.dma("pool", lambda e, mc=mc, s=s, ot=ot, accum=accum: e.dma_start(
                        out=fsc3[:, mc, s * SUB:(s + 1) * SUB], in_=ot[:, :],
                        accum_op=accum), reads=[tev, region_tok.get((mc, s))])
                    region_tok[(mc, s)] = td
                    ot_last[i3] = td
                w2_last[mc % 2] = last_ev
            if ex["use_c"]:
                cs_last = w2_last[0]
        p.barrier()


def emit_lowrank_proj(p, st, C, xn, w_dram, out_fm, ncol, tag):
    nc = p.nc
    nmc = ncol // 128
    with ExitStack() as ph:
        wt = _sb(ph, nc, f"lr_w_{tag}", [128, KC * ncol], BF16)
        p_x = [_ps(ph, nc, f"lr_p_{tag}{i}", [128, SUB]) for i in range(2)]
        tw = p.dma("pool", lambda e: e.dma_start(
            out=wt[:, :].rearrange("p (kc m) -> p kc m", kc=KC),
            in_=_chunk3(w_dram[:, :])))
        last = [None, None]
        for s in range(NSUB):
            for mc in range(nmc):
                i = (s * nmc + mc) % 2
                px = p_x[i]
                tm = None
                for kc in range(KC):
                    tm = p.op("pe", lambda e, kc=kc, mc=mc, s=s, px=px: e.matmul(
                        px[:, :], wt[:, (kc * nmc + mc) * 128:(kc * nmc + mc + 1) * 128],
                        xn[:, kc * T + s * SUB: kc * T + (s + 1) * SUB],
                        start=(kc == 0), stop=(kc == KC - 1)),
                        reads=[tw, last[i]])
                te = p.op("dve", lambda e, mc=mc, s=s, px=px: e.tensor_copy(
                    out_fm[:, mc * T + s * SUB: mc * T + (s + 1) * SUB], px[:, :]),
                    reads=[tm])
                last[i] = te
        p.barrier()


def _norm_update(p, C, src_ap, nparts, p_n, nrow, nmax, prev_toks, reads, pn_gate):
    gates = pn_gate if isinstance(pn_gate, list) else [pn_gate]
    tn = p.op("pe", lambda e: e.matmul(p_n[:, :], C.ones_b[0:nparts, :], src_ap,
                                       start=True, stop=True),
              reads=list(reads) + gates)
    t2 = p.op("dve", lambda e: e.tensor_copy(nrow[:, :], p_n[:, :]), reads=[tn])
    t3 = p.op("dve", lambda e: e.reduce_max(nmax[0:1, 1:2], nrow[:, :], axis=AX.X),
              reads=[t2], drain=True)
    t4 = p.op("dve", lambda e: e.tensor_max(nmax[0:1, 0:1], nmax[0:1, 0:1],
                                            nmax[0:1, 1:2]),
              reads=[t3] + list(prev_toks), drain=True)
    return tn, t2, t4


def emit_kv_l0(p, st, C, lw, xn, kdup, vtm, nmax_k, layer):
    nc = p.nc
    with ExitStack() as ph:
        wkt = _sb(ph, nc, "kv0_wk", [128, KC * HD], BF16)
        wvt = _sb(ph, nc, "kv0_wv", [128, KC * HD], BF16)
        kraw = _sb(ph, nc, "kv0_kraw", [128, SUB], F32)
        t1b = _sb(ph, nc, "kv0_t1", [128, SUB], F32)
        t2b = _sb(ph, nc, "kv0_t2", [128, SUB], F32)
        v_b = _sb(ph, nc, "kv0_vb", [128, SUB], BF16)
        ksq = _sb(ph, nc, "kv0_ksq", [128, SUB], BF16)
        nrow = _sb(ph, nc, "kv0_nrow", [1, SUB], F32)
        p_k = _ps(ph, nc, "kv0_pk", [128, SUB])
        p_v = _ps(ph, nc, "kv0_pv", [128, SUB])
        p_sw = _ps(ph, nc, "kv0_psw", [128, SUB])
        p_t = _ps(ph, nc, "kv0_pt", [128, 128], BF16)
        p_n = _ps(ph, nc, "kv0_pn", [1, SUB])

        qkvT3 = _chunk3(lw["qkvT"][:, :])
        tk = p.dma("pool", lambda e: e.dma_start(
            out=wkt[:, :].rearrange("p (kc m) -> p kc m", kc=KC),
            in_=qkvT3[:, :, 1024:1088]))
        tv = p.dma("pool", lambda e: e.dma_start(
            out=wvt[:, :].rearrange("p (kc m) -> p kc m", kc=KC),
            in_=qkvT3[:, :, 1088:1152]))
        tmx = p.op("dve", lambda e: e.memset(nmax_k[:, :], 0.0))
        vtm_tok = p.op("dve", lambda e: e.memset(vtm[:, :], 1.0))
        kraw_tok = []
        t1_tok = t2_tok = krot_tok = None
        v_tok = []
        ksq_tok = None
        pn_gate = None
        for s in range(NSUB):
            tmk = tmv = None
            for kc in range(KC):
                tmk = p.op("pe", lambda e, kc=kc, s=s: e.matmul(
                    p_k[0:HD, :], wkt[:, kc * HD:(kc + 1) * HD],
                    xn[:, kc * T + s * SUB: kc * T + (s + 1) * SUB],
                    start=(kc == 0), stop=(kc == KC - 1)), reads=[tk] + kraw_tok)
            for kc in range(KC):
                tmv = p.op("pe", lambda e, kc=kc, s=s: e.matmul(
                    p_v[0:HD, :], wvt[:, kc * HD:(kc + 1) * HD],
                    xn[:, kc * T + s * SUB: kc * T + (s + 1) * SUB],
                    start=(kc == 0), stop=(kc == KC - 1)), reads=[tv] + v_tok)
            tke = p.op("dve", lambda e: e.tensor_scalar(
                kraw[0:HD, :], p_k[0:HD, :], lw["qkvb_s"][0:HD, 8:9], None, ALU.add),
                reads=[tmk, lw["qkvb_tok"]])
            tsw = p.op("pe", lambda e: e.matmul(
                p_sw[0:HD, :], lw["p128f"][0:HD, 0:HD], kraw[0:HD, :],
                start=True, stop=True), reads=[tke, t2_tok])
            tt1 = p.op("dve", lambda e, s=s: e.tensor_mul(
                t1b[0:HD, :], kraw[0:HD, :], lw["cos"][0:HD, s * SUB:(s + 1) * SUB]),
                reads=[tke, krot_tok], drain=True)
            tt2 = p.op("dve", lambda e, s=s: e.tensor_mul(
                t2b[0:HD, :], p_sw[0:HD, :], lw["sin"][0:HD, s * SUB:(s + 1) * SUB]),
                reads=[tsw, krot_tok], drain=True)
            tkr = p.op("dve", lambda e, s=s: e.tensor_add(
                kdup[0:HD, s * SUB:(s + 1) * SUB], t1b[0:HD, :], t2b[0:HD, :]),
                reads=[tt1, tt2], drain=True)
            kraw_tok = [tsw, tt1]
            t2_tok = tt2
            krot_tok = tkr
            tq = p.op("pool", lambda e, s=s: e.tensor_mul(
                ksq[0:HD, :], kdup[0:HD, s * SUB:(s + 1) * SUB],
                kdup[0:HD, s * SUB:(s + 1) * SUB]), reads=[tkr, ksq_tok])
            tn, pn_gate, tmx = _norm_update(p, C, ksq[0:HD, :], HD, p_n, nrow, nmax_k,
                                            [tmx], [tq], pn_gate)
            ksq_tok = tn
            tve = p.op("dve", lambda e: e.tensor_scalar(
                v_b[0:HD, :], p_v[0:HD, :], lw["qkvb_s"][0:HD, 9:10], None, ALU.add),
                reads=[tmv, lw["qkvb_tok"]])
            v_tok = [tve]
            for q in range(SUB // 128):
                kt = s * (SUB // 128) + q
                ttr = p.op("pe", lambda e, q=q: e.transpose(
                    p_t[:, 0:HD], v_b[0:HD, q * 128:(q + 1) * 128], C.ident_b[0:HD, 0:HD]),
                    reads=[tve, vtm_tok])
                tcp = p.op("dve", lambda e, kt=kt: e.tensor_copy(
                    vtm[:, kt * (HD + 1): kt * (HD + 1) + HD], p_t[:, 0:HD]),
                    reads=[ttr], drain=True)
                vtm_tok = tcp
                v_tok.append(ttr)
        p.dma("pool", lambda e: e.dma_start(out=kdup[HD:128, :], in_=kdup[0:HD, :]),
              reads=[krot_tok])
        p.barrier()


def emit_kv_l1(p, st, C, lw, ckv, kfm, vtm, nmax_k, layer):
    nc = p.nc
    with ExitStack() as ph:
        wuk = _sb(ph, nc, "kv1_wuk", [128, 2 * D], BF16)
        wuv = _sb(ph, nc, "kv1_wuv", [128, 2 * D], BF16)
        v_b = _sb(ph, nc, "kv1_vb", [128, SUB], BF16)
        ksq = _sb(ph, nc, "kv1_ksq", [128, SUB], BF16)
        nrow = _sb(ph, nc, "kv1_nrow", [1, SUB], F32)
        p_k = [_ps(ph, nc, f"kv1_pk{i}", [128, SUB]) for i in range(2)]
        p_v = _ps(ph, nc, "kv1_pv", [128, SUB])
        p_t = _ps(ph, nc, "kv1_pt", [128, 128], BF16)
        p_n = _ps(ph, nc, "kv1_pn", [1, SUB])

        tk = p.dma("pool", lambda e: e.dma_start(
            out=wuk[:, :].rearrange("p (kc m) -> p kc m", kc=2),
            in_=lw["Wuk"][:, :].rearrange("(kc p) m -> p kc m", p=128)))
        tv = p.dma("pool", lambda e: e.dma_start(
            out=wuv[:, :].rearrange("p (kc m) -> p kc m", kc=2),
            in_=lw["Wuv"][:, :].rearrange("(kc p) m -> p kc m", p=128)))
        tmx = p.op("dve", lambda e: e.memset(nmax_k[:, :], 0.0))
        vtm_tok = p.op("dve", lambda e: e.memset(vtm[:, :], 1.0))
        klast = [None, None]
        ksq_tok = None
        pn_gate = None
        v_tok = []
        for s in range(NSUB):
            for mc in range(KC):
                i = mc % 2
                pk = p_k[i]
                tm = None
                for kc in range(2):
                    tm = p.op("pe", lambda e, kc=kc, mc=mc, s=s, pk=pk: e.matmul(
                        pk[:, :], wuk[:, (kc * KC + mc) * 128:(kc * KC + mc + 1) * 128],
                        ckv[:, kc * T + s * SUB: kc * T + (s + 1) * SUB],
                        start=(kc == 0), stop=(kc == 1)), reads=[tk, klast[i]])
                te = p.op("dve", lambda e, mc=mc, s=s, pk=pk: e.tensor_copy(
                    kfm[:, mc * T + s * SUB: mc * T + (s + 1) * SUB], pk[:, :]), reads=[tm])
                klast[i] = te
                tq = p.op("pool", lambda e, mc=mc, s=s: e.tensor_mul(
                    ksq[:, :], kfm[:, mc * T + s * SUB: mc * T + (s + 1) * SUB],
                    kfm[:, mc * T + s * SUB: mc * T + (s + 1) * SUB]),
                    reads=[te, ksq_tok])
                tn, pn_gate, tmx = _norm_update(p, C, ksq[:, :], 128, p_n, nrow, nmax_k,
                                                [tmx], [tq], pn_gate)
                ksq_tok = tn
            for h in range(NH):
                tm = None
                for kc in range(2):
                    tm = p.op("pe", lambda e, kc=kc, h=h, s=s: e.matmul(
                        p_v[0:HD, :],
                        wuv[:, kc * KC * 128 + h * HD: kc * KC * 128 + (h + 1) * HD],
                        ckv[:, kc * T + s * SUB: kc * T + (s + 1) * SUB],
                        start=(kc == 0), stop=(kc == 1)), reads=[tv] + v_tok)
                tve = p.op("dve", lambda e: e.tensor_copy(v_b[0:HD, :], p_v[0:HD, :]),
                           reads=[tm])
                v_tok = [tve]
                for q in range(SUB // 128):
                    kt = s * (SUB // 128) + q
                    slot = (kt * NH + h) * (HD + 1)
                    ttr = p.op("pe", lambda e, q=q: e.transpose(
                        p_t[:, 0:HD], v_b[0:HD, q * 128:(q + 1) * 128],
                        C.ident_b[0:HD, 0:HD]), reads=[tve, vtm_tok])
                    tcp = p.op("dve", lambda e, slot=slot: e.tensor_copy(
                        vtm[:, slot: slot + HD], p_t[:, 0:HD]), reads=[ttr], drain=True)
                    vtm_tok = tcp
                    v_tok.append(ttr)
        p.barrier()

def emit_scores(p, st, C, layer, cfg):
    nc = p.nc
    with ExitStack() as ph:
        SB = Bufs()
        q_pan = _sb(ph, nc, f"sc_q{layer}", [128, KC * SUB], BF16)
        o_pan = _sb(ph, nc, f"sc_o{layer}", [128, KC * SUB], BF16)
        o_tmp = _sb(ph, nc, f"sc_ot{layer}", [128, SUB], BF16)
        pt_b = [_sb(ph, nc, f"sc_pt{layer}_{i}", [128, 2 * SUB], BF16) for i in range(2)]
        rbc = _sb(ph, nc, f"sc_rbc{layer}", [128, SUB], BF16)
        recb = _sb(ph, nc, f"sc_rec{layer}", [128, SUB], F32)
        recb2 = _sb(ph, nc, f"sc_rc2{layer}", [128, SUB], BF16)
        qsq = _sb(ph, nc, f"sc_qsq{layer}", [128, SUB], BF16)
        nrow = _sb(ph, nc, f"sc_nrow{layer}", [1, SUB], F32)
        nmax_q = _sb(ph, nc, f"sc_nq{layer}", [128, 4], F32)
        scs = _sb(ph, nc, f"sc_scs{layer}", [1, 8], F32)
        mkneg = _sb(ph, nc, f"sc_mk{layer}", [128, 1], F32)
        wo_t = [_sb(ph, nc, f"sc_wo{layer}_{i}", [128, KC * 128], BF16) for i in range(2)]
        if cfg["outb"] is not None:
            y1_t = [_sb(ph, nc, f"sc_y1{layer}_{i}", [128, SUB], F32) for i in range(2)]
        xf_t = [_sb(ph, nc, f"sc_xf{layer}_{i}", [128, SUB], F32) for i in range(2)]
        ff_t = [_sb(ph, nc, f"sc_ff{layer}_{i}", [128, SUB], F32) for i in range(2)]
        s1_t = [_sb(ph, nc, f"sc_s1{layer}_{i}", [128, SUB], F32) for i in range(2)]
        if cfg.get("rope_scratch"):
            SB.qraw = _sb(ph, nc, f"sc_qraw{layer}", [128, SUB], F32)
            SB.t1b = _sb(ph, nc, f"sc_t1{layer}", [128, SUB], F32)
            SB.t2b = _sb(ph, nc, f"sc_t2{layer}", [128, SUB], F32)
            SB.wq = [_sb(ph, nc, f"sc_wq{layer}_{i}", [128, KC * 128], BF16)
                     for i in range(2)]
        p_s = [_ps(ph, nc, f"sc_ps{layer}_{i}", [128, 2 * SUB]) for i in range(2)]
        p_oa = [_ps(ph, nc, f"sc_poa{layer}_{i}", [128, SUB]) for i in range(2)]
        p_misc = _ps(ph, nc, f"sc_pm{layer}", [128, SUB])
        p_y = _ps(ph, nc, f"sc_py{layer}", [128, SUB])
        p_n = p_y[0:1, 0:SUB]
        SB.p_s = p_s
        SB.ps_last = [None, None]

        fsc3 = cfg["fsc"][:, :].rearrange("(mc p) t -> p mc t", p=128)
        xin3 = cfg["resid_in"][:, :].rearrange("(mc p) t -> p mc t", p=128)
        xout3 = cfg["resid_out"][:, :].rearrange("(mc p) t -> p mc t", p=128)

        pt_last = [None, None]
        py_last = None
        wo_last = [None, None]
        xf_last = [None, None]
        ff_last = [None, None]
        s1_last = [None, None]
        y1_last = [None, None]
        oa_last = [None, None]
        pr_gate = None
        rc2_gate = None
        scr_gate = None
        otmp_last = None
        q_consumed = []
        o_consumed = []
        store_toks = []
        qsq_tok = None
        pn_gate = None
        kv_tok = cfg["kv_tok"]
        pending_epi = []
        tmax = p.op("dve", lambda e: e.memset(nmax_q[:, :], 0.0))
        norm_jobs = []

        def emit_norm_chunk(mc, qtok):
            nonlocal pn_gate, tmax, qsq_tok
            tq = p.op("dve", lambda e, mc=mc: e.tensor_mul(
                qsq[:, :], q_pan[:, mc * SUB:(mc + 1) * SUB],
                q_pan[:, mc * SUB:(mc + 1) * SUB]), reads=[qtok, qsq_tok])
            tn, pn_gate, tmax = _norm_update(p, C, qsq[:, :], 128, p_n, nrow, nmax_q,
                                             [tmax], [tq], [pn_gate, py_last])
            qsq_tok = tn
            return tq

        for qp in range(NQP):
            qtoks = cfg["build_q"](p, qp, q_pan, SB, q_consumed)
            q_consumed = []
            norm_jobs = [(mc, qtoks[mc]) for mc in range(KC)]
            if qp == 0:
                for mc, qtok in norm_jobs:
                    q_consumed.append(emit_norm_chunk(mc, qtok))
                norm_jobs = []
            tb1 = p.op("dve", lambda e: e.tensor_mul(scs[0:1, 0:1], nmax_q[0:1, 0:1],
                                                     cfg["nmax_k"][0:1, 0:1]),
                       reads=[tmax, kv_tok], drain=True)
            tb2a = p.op("act", lambda e: e.activation(scs[0:1, 1:2], scs[0:1, 0:1],
                                                      AF.Ln, bias=C.fconst[0:1, 1:2]),
                        reads=[tb1])
            tb2 = p.op("act", lambda e: e.activation(scs[0:1, 4:5], scs[0:1, 1:2],
                                                     AF.Exp, scale=0.5),
                       reads=[tb2a], drain=True)
            tb3 = p.op("dve", lambda e: e.tensor_scalar_mul(scs[0:1, 2:3], scs[0:1, 4:5],
                                                            -0.125), reads=[tb2], drain=True)
            tb4 = p.op("pe", lambda e: e.matmul(p_misc[:, 0:1], C.ones_fr[0:1, :],
                                                scs[0:1, 2:3], start=True, stop=True),
                       reads=[tb3, pn_gate])
            tbound = p.op("dve", lambda e: e.tensor_copy(mkneg[:, 0:1], p_misc[:, 0:1]),
                          reads=[tb4], drain=True)
            pr_gate = tbound
            nkt = 4 * qp + 4
            npair = nkt // 2
            o_gate = list(o_consumed)
            o_consumed = []
            odd_dma_toks = []
            for h in range(NH):
                hb = 64 * (h & 1)
                hc = h >> 1
                hoa = h % 2
                poa = p_oa[hoa]
                rhs_q = q_pan[hb:hb + 64, hc * SUB:(hc + 1) * SUB]
                tS_last = None

                def emit_S(pr):
                    kt0, kt1 = 2 * pr, 2 * pr + 1
                    i2 = pr % 2
                    ps = p_s[i2]
                    gate = [kv_tok, qtoks[hc], SB.ps_last[i2]]
                    tS0 = p.op("pe", lambda e, h=h, kt0=kt0, ps=ps, rhs_q=rhs_q: e.matmul(
                        ps[:, 0:SUB], cfg["k_lhsT"](h, kt0), rhs_q,
                        start=True, stop=True), reads=gate)
                    tS1 = p.op("pe", lambda e, h=h, kt1=kt1, ps=ps, rhs_q=rhs_q: e.matmul(
                        ps[:, SUB:2 * SUB], cfg["k_lhsT"](h, kt1), rhs_q,
                        start=True, stop=True), reads=gate)
                    return tS0, tS1

                def emit_exp(pr, tS0, tS1):
                    i2 = pr % 2
                    ps = p_s[i2]
                    i3 = pr % 2
                    texp = p.op("act", lambda e, ps=ps, i3=i3: e.activation(
                        pt_b[i3][:, :], ps[:, :], AF.Exp, bias=mkneg[:, 0:1], scale=0.125),
                        reads=[tS0, tS1, tbound, pt_last[i3]])
                    SB.ps_last[i2] = texp
                    tm0 = tm1 = texp
                    for half, kt in ((0, 2 * pr), (1, 2 * pr + 1)):
                        if kt >= nkt - 4:
                            j = kt - (nkt - 4)
                            tmx = p.op("pool", lambda e, i3=i3, j=j, half=half: e.tensor_mul(
                                pt_b[i3][:, half * SUB:(half + 1) * SUB],
                                pt_b[i3][:, half * SUB:(half + 1) * SUB],
                                C.cmask[:, j * SUB:(j + 1) * SUB]), reads=[texp])
                            if half == 0:
                                tm0 = tmx
                            else:
                                tm1 = tmx
                    return tm0, tm1

                def emit_PV(pr, tm0, tm1):
                    i3 = pr % 2
                    tPV0 = p.op("pe", lambda e, h=h, kt0=2 * pr, i3=i3, pr=pr, poa=poa: e.matmul(
                        poa[0:HD + 1, :], cfg["vtm_slice"](h, kt0), pt_b[i3][:, 0:SUB],
                        start=(pr == 0), stop=False),
                        reads=[tm0, tm1, kv_tok] + ([oa_last[hoa]] if pr == 0 else []))
                    tPV1 = p.op("pe", lambda e, h=h, kt1=2 * pr + 1, i3=i3, pr=pr, npair=npair, poa=poa: e.matmul(
                        poa[0:HD + 1, :], cfg["vtm_slice"](h, kt1), pt_b[i3][:, SUB:2 * SUB],
                        start=False, stop=(pr == npair - 1)),
                        reads=[])
                    pt_last[i3] = tPV1
                    return tPV1

                sx = emit_S(0)
                if pending_epi:
                    pending_epi.pop(0)()
                ex_prev = emit_exp(0, *sx)
                tS_last = sx[1]
                for pr in range(1, npair):
                    sx = emit_S(pr)
                    tS_last = sx[1]
                    emit_PV(pr - 1, *ex_prev)
                    ex_prev = emit_exp(pr, *sx)
                emit_PV(npair - 1, *ex_prev)
                q_consumed.append(tS_last)
                def make_epilogue(h=h, hc=hc, hoa=hoa, poa=poa,
                                  tPVl=pt_last[(npair - 1) % 2]):
                    def run():
                        nonlocal scr_gate, rc2_gate, pr_gate, otmp_last
                        tln = p.op("act", lambda e: e.activation(
                            recb[HD:HD + 1, :], poa[HD:HD + 1, :], AF.Ln),
                            reads=[tPVl, scr_gate], drain=True)
                        scr_gate = tln
                        trc2 = p.op("act", lambda e: e.activation(
                            recb2[HD:HD + 1, :], recb[HD:HD + 1, :], AF.Exp, scale=-1.0),
                            reads=[tln, rc2_gate], drain=True)
                        tbc = p.op("pe", lambda e: e.matmul(
                            p_misc[:, :], C.ones_br[HD:HD + 1, 0:128],
                            recb2[HD:HD + 1, :], start=True, stop=True),
                            reads=[trc2, pr_gate])
                        rc2_gate = tbc
                        trbc = p.op("dve", lambda e: e.tensor_copy(rbc[:, :], p_misc[:, :]),
                                    reads=[tbc])
                        pr_gate = trbc
                        if h % 2 == 0:
                            dest = o_pan[0:HD, hc * SUB:(hc + 1) * SUB]
                            tnrm = p.op("dve", lambda e: e.tensor_mul(
                                dest, poa[0:HD, :], rbc[0:HD, :]),
                                reads=[trbc] + o_gate, drain=True)
                        else:
                            tnrm = p.op("dve", lambda e: e.tensor_mul(
                                o_tmp[0:HD, :], poa[0:HD, :], rbc[0:HD, :]),
                                reads=[trbc, otmp_last], drain=True)
                            todd = p.dma("pool", lambda e: e.dma_start(
                                out=o_pan[HD:128, hc * SUB:(hc + 1) * SUB],
                                in_=o_tmp[0:HD, :]), reads=[tnrm] + o_gate)
                            otmp_last = todd
                            odd_dma_toks.append(todd)
                        oa_last[hoa] = tnrm
                    return run
                pending_epi.append(make_epilogue())
                if norm_jobs and h < KC:
                    mc_, qtok_ = norm_jobs[h]
                    q_consumed.append(emit_norm_chunk(mc_, qtok_))
                    if h == KC - 1:
                        norm_jobs = []
            while pending_epi:
                pending_epi.pop(0)()
            opan_toks = [oa_last[0], oa_last[1]] + odd_dma_toks[-1:]
            for mc in range(KC):
                i = mc % 2
                tw = p.dma("pool", lambda e, mc=mc, i=i: e.dma_start(
                    out=wo_t[i][:, :].rearrange("p (kc m) -> p kc m", kc=KC),
                    in_=cfg["wo"](mc)), reads=[wo_last[i]])
                tx = p.dma("sp", lambda e, mc=mc, qp=qp, i=i: e.dma_start(
                    out=xf_t[i][:, :], in_=xin3[:, mc, qp * SUB:(qp + 1) * SUB]),
                    reads=[xf_last[i]])
                tf = p.dma("sp", lambda e, mc=mc, qp=qp, i=i: e.dma_start(
                    out=ff_t[i][:, :], in_=fsc3[:, mc, qp * SUB:(qp + 1) * SUB]),
                    reads=[ff_last[i], cfg["fsc_tok"]])
                ts1 = p.op("dve", lambda e, i=i: e.tensor_add(
                    s1_t[i][:, :], xf_t[i][:, :], ff_t[i][:, :]),
                    reads=[tx, tf, s1_last[i]])
                xf_last[i] = ts1
                ff_last[i] = ts1
                tmy = None
                for kc in range(KC):
                    tmy = p.op("pe", lambda e, kc=kc, mc=mc, i=i: e.matmul(
                        p_y[:, :], wo_t[i][:, kc * 128:(kc + 1) * 128],
                        o_pan[:, kc * SUB:(kc + 1) * SUB],
                        start=(kc == 0), stop=(kc == KC - 1)),
                        reads=[tw, py_last, pn_gate] + opan_toks)
                o_consumed.append(tmy)
                wo_last[i] = tmy
                if cfg["outb"] is not None:
                    bbuf, btok = cfg["outb"]
                    ty1 = p.op("dve", lambda e, mc=mc, i=i, bbuf=bbuf: e.tensor_scalar(
                        y1_t[i][:, :], p_y[:, :], bbuf[0:128, mc:mc + 1], None, ALU.add),
                        reads=[tmy, btok, y1_last[i]])
                    py_last = ty1
                    txo = p.op("dve", lambda e, i=i: e.tensor_add(
                        s1_t[i][:, :], y1_t[i][:, :], s1_t[i][:, :]),
                        reads=[ty1, ts1], drain=True)
                    y1_last[i] = txo
                else:
                    txo = p.op("dve", lambda e, i=i: e.tensor_add(
                        s1_t[i][:, :], p_y[:, :], s1_t[i][:, :]),
                        reads=[tmy, ts1], drain=True)
                    py_last = txo
                tst = p.dma("sp", lambda e, mc=mc, qp=qp, i=i: e.dma_start(
                    out=xout3[:, mc, qp * SUB:(qp + 1) * SUB], in_=s1_t[i][:, :]),
                    reads=[txo])
                s1_last[i] = tst
                store_toks.append(tst)
        p.barrier()
        return store_toks


class _Stop(Exception):
    pass


def build_program(taps=(), upto="full"):
    nc = bass.Bass("TRN2", target_bir_lowering=False)
    dr = {}
    dr["xT"] = nc.dram_tensor("xT", [D, T], F32, kind="ExternalInput").ap()
    for n, sh in [("c_cos", [128, T]), ("c_sin", [128, T]), ("c_p128", [128, 128]),
                  ("c_cmask", [128, 4 * SUB]), ("c_ident", [128, 128]),
                  ("c_ohsel", [NE, NE * 128])]:
        dr[n] = nc.dram_tensor(n, sh, F32, kind="ExternalInput").ap()
    wshapes = dict(
        l0_qkvT=[D, 1152], l0_qkvb=[128, 10], l0_outT=[D, D], l0_outb=[128, 8],
        l0_WgT=[D, NE], l0_bg=[NE, 1], l0_w1T=[NE, D, 2 * D], l0_w2T=[NE, D, D],
        l1_Wdq=[D, 256], l1_Wuq=[256, D], l1_Wdkv=[D, 256], l1_Wuk=[256, D],
        l1_Wuv=[256, D], l1_Wo=[D, D], l1_router=[D, NE], l1_fc1=[NE, D, 2 * D],
        l1_fc2=[NE, D, D], l1_gate=[D, 2 * D], l1_up=[D, 2 * D], l1_down=[2 * D, D])
    for n, sh in wshapes.items():
        dr[n] = nc.dram_tensor(n, sh, F32, kind="ExternalInput").ap()
    resid = nc.dram_tensor("resid", [D, T], F32,
                           kind="ExternalOutput" if "resid" in taps else "Internal").ap()
    fsc = nc.dram_tensor("fsc", [D, T], F32,
                         kind="ExternalOutput" if "fsc" in taps else "Internal").ap()
    outT = nc.dram_tensor("outT", [D, T], F32, kind="ExternalOutput").ap()
    tap_dr = {}
    for tn in taps:
        if tn in ("resid", "fsc"):
            continue
        shape = {"xn": [128, KC * T], "call": [NE, T], "kdup": [128, T],
                 "kfm": [128, KC * T], "vtm0": [128, NKT * (HD + 1)],
                 "vtm1": [128, NKT * NH * (HD + 1)], "cq": [128, 2 * T],
                 "ckv": [128, 2 * T]}[tn.rstrip("01") if tn[-1] in "01" and not tn.startswith("vtm") else tn]
        tap_dr[tn] = nc.dram_tensor(f"tap_{tn}", shape, F32, kind="ExternalOutput").ap()

    try:
        nc = _assemble(nc, dr, resid, fsc, outT, tap_dr, taps, upto)
    except _Stop:
        pass
    return nc


def _assemble(nc, dr, resid, fsc, outT, tap_dr, taps, upto):
    with ExitStack() as st:
        p = Prog(nc, st)
        C = emit_consts(p, st, dr)

        def ck(name):
            if upto == name:
                p.barrier()
                p.emit()
                raise _Stop

        def tap(name, buf, cols):
            if name in tap_dr:
                with ExitStack() as th:
                    tb = _sb(th, nc, f"tapb_{name}", [128, 2048], F32)
                    np_ = buf.shape[0]
                    for c0 in range(0, cols, 2048):
                        w = min(2048, cols - c0)
                        tt = p.op("dve", lambda e, c0=c0, w=w, np_=np_: e.tensor_copy(
                            tb[0:np_, 0:w], buf[0:np_, c0:c0 + w]))
                        p.dma("sp", lambda e, c0=c0, w=w, np_=np_: e.dma_start(
                            out=tap_dr[name][0:np_, c0:c0 + w],
                            in_=tb[0:np_, 0:w]), reads=[tt])
                        p.barrier()

        if True:
            with ExitStack() as l0:
                qkvb_s = _sb(l0, nc, "qkvb0", [128, 10], F32)
                outb_s = _sb(l0, nc, "outb0", [128, 8], F32)
                xn = _sb(l0, nc, "xn0", [128, KC * T], BF16)
                tqb = p.dma("sp", lambda e: e.dma_start(out=qkvb_s[:, :], in_=dr["l0_qkvb"][:, :]))
                tob = p.dma("sp", lambda e: e.dma_start(out=outb_s[:, :], in_=dr["l0_outb"][:, :]))
                p.barrier()

                emit_ln(p, st, C, dr["xT"], xn, 0)
                tap("xn0", xn, KC * T)
                ck("ln0")

                with ExitStack() as lf:
                    c_all = _sb(lf, nc, "call0", [NE, T], BF16)
                    emit_routing(p, st, C, xn, c_all, dr["l0_WgT"], dr["l0_bg"], 0)
                    tap("call0", c_all, T)
                    ck("rt0")
                    w1T = dr["l0_w1T"]
                    w2T = dr["l0_w2T"]
                    experts = []
                    for e_ in range(NE):
                        experts.append(dict(
                            w1a=lambda ocp, e_=e_: _chunk3(w1T[e_])[:, :, ocp * 128:(ocp + 1) * 128],
                            w1b=lambda ocp, e_=e_: _chunk3(w1T[e_])[:, :, D + ocp * 128: D + (ocp + 1) * 128],
                            w2=lambda mc, e_=e_: _chunk3(w2T[e_])[:, :, mc * 128:(mc + 1) * 128],
                            use_c=True, e_idx=e_, c_tok=None))
                    emit_moe(p, st, C, xn, c_all, fsc, 0, experts)
                tap("fsc", None, 0)
                ck("ffn0")

                cosb = _sb(l0, nc, "cos0", [128, T], F32)
                sinb = _sb(l0, nc, "sin0", [128, T], F32)
                p128f = _sb(l0, nc, "p128f", [128, 128], F32)
                p.dma("sp", lambda e: e.dma_start(out=cosb[:, :], in_=dr["c_cos"][:, :]))
                p.dma("sp", lambda e: e.dma_start(out=sinb[:, :], in_=dr["c_sin"][:, :]))
                p.dma("sp", lambda e: e.dma_start(out=p128f[:, :], in_=dr["c_p128"][:, :]))
                p.barrier()
                kdup = _sb(l0, nc, "kdup0", [128, T], BF16, side="right")
                vtm0 = _sb(l0, nc, "vtm0", [128, NKT * (HD + 1)], BF16, side="right")
                nmax_k = _sb(l0, nc, "nmk0", [128, 4], F32, side="right")
                lw0 = dict(qkvT=dr["l0_qkvT"], qkvb_s=qkvb_s, qkvb_tok=tqb,
                           p128f=p128f, cos=cosb, sin=sinb)
                emit_kv_l0(p, st, C, lw0, xn, kdup, vtm0, nmax_k, 0)
                tap("kdup", kdup, T)
                ck("kv0")

                qkvT3 = _chunk3(dr["l0_qkvT"][:, :])

                def build_q0(p, qp, q_pan, SB, q_consumed, wq_last=[None, None],
                             qraw_last=[], t1_last=[], t2_last=[]):
                    toks = []
                    for mc in range(KC):
                        i = mc % 2
                        tw = p.dma("pool", lambda e, mc=mc, i=i: e.dma_start(
                            out=SB.wq[i][:, :].rearrange("p (kc m) -> p kc m", kc=KC),
                            in_=qkvT3[:, :, mc * 128:(mc + 1) * 128]), reads=[wq_last[i]])
                        tm = None
                        for kc in range(KC):
                            tm = p.op("pe", lambda e, kc=kc, qp=qp, i=i: e.matmul(
                                SB.p_s[0][:, 0:SUB], SB.wq[i][:, kc * 128:(kc + 1) * 128],
                                xn[:, kc * T + qp * SUB: kc * T + (qp + 1) * SUB],
                                start=(kc == 0), stop=(kc == KC - 1)),
                                reads=[tw, SB.ps_last[0]])
                        wq_last[i] = tm
                        tev = p.op("dve", lambda e, mc=mc: e.tensor_scalar(
                            SB.qraw[:, :], SB.p_s[0][:, 0:SUB],
                            qkvb_s[0:128, mc:mc + 1], None, ALU.add),
                            reads=[tm, tqb] + qraw_last)
                        SB.ps_last[0] = tev
                        tsw = p.op("pe", lambda e: e.matmul(
                            SB.p_s[1][:, 0:SUB], p128f[:, :], SB.qraw[:, :],
                            start=True, stop=True), reads=[tev, SB.ps_last[1]])
                        tt1 = p.op("dve", lambda e, qp=qp: e.tensor_mul(
                            SB.t1b[:, :], SB.qraw[:, :],
                            cosb[:, qp * SUB:(qp + 1) * SUB]),
                            reads=[tev] + t1_last, drain=True)
                        tt2 = p.op("dve", lambda e, qp=qp: e.tensor_mul(
                            SB.t2b[:, :], SB.p_s[1][:, 0:SUB],
                            sinb[:, qp * SUB:(qp + 1) * SUB]),
                            reads=[tsw] + t2_last, drain=True)
                        SB.ps_last[1] = tt2
                        tq = p.op("dve", lambda e, mc=mc: e.tensor_add(
                            q_pan[:, mc * SUB:(mc + 1) * SUB], SB.t1b[:, :], SB.t2b[:, :]),
                            reads=[tt1, tt2] + q_consumed, drain=True)
                        qraw_last[:] = [tsw, tt1]
                        t1_last[:] = [tq]
                        t2_last[:] = [tq]
                        toks.append(tq)
                    return toks

                cfg0 = dict(
                    build_q=build_q0,
                    k_lhsT=lambda h, kt: kdup[64 * (h & 1):64 * (h & 1) + 64,
                                              kt * 128:(kt + 1) * 128],
                    vtm_slice=lambda h, kt: vtm0[:, kt * (HD + 1): (kt + 1) * (HD + 1)],
                    nmax_k=nmax_k, kv_tok=("dgp", p.counts["dgp"]),
                    wo=lambda mc: _chunk3(dr["l0_outT"][:, :])[:, :, mc * 128:(mc + 1) * 128],
                    outb=(outb_s, tob), resid_in=dr["xT"], resid_out=resid, fsc=fsc,
                    fsc_tok=("dgp", p.counts["dgp"]), rope_scratch=True)
                emit_scores(p, st, C, 0, cfg0)
                tap("resid", None, 0)
                ck("l0")

            with ExitStack() as l1:
                cq = _sb(l1, nc, "cq1", [128, 2 * T], BF16)
                with ExitStack() as ly:
                    ckv = _sb(ly, nc, "ckv1", [128, 2 * T], BF16)
                    with ExitStack() as lx:
                        xn1 = _sb(lx, nc, "xn1", [128, KC * T], BF16)
                        emit_ln(p, st, C, resid, xn1, 1)
                        tap("xn1", xn1, KC * T)
                        ck("ln1")
                        emit_lowrank_proj(p, st, C, xn1, dr["l1_Wdq"], cq, 256, "cq")
                        emit_lowrank_proj(p, st, C, xn1, dr["l1_Wdkv"], ckv, 256, "ckv")
                        tap("cq", cq, 2 * T)
                        tap("ckv", ckv, 2 * T)
                        ck("proj1")

                        with ExitStack() as lf:
                            c_all = _sb(lf, nc, "call1", [NE, T], BF16)
                            emit_routing(p, st, C, xn1, c_all, dr["l1_router"], None, 1)
                            tap("call1", c_all, T)
                            ck("rt1")
                            fc1 = dr["l1_fc1"]
                            fc2 = dr["l1_fc2"]
                            experts = []
                            for e_ in range(NE):
                                experts.append(dict(
                                    w1a=lambda ocp, e_=e_: _chunk3(fc1[e_])[:, :, ocp * 128:(ocp + 1) * 128],
                                    w1b=lambda ocp, e_=e_: _chunk3(fc1[e_])[:, :, D + ocp * 128: D + (ocp + 1) * 128],
                                    w2=lambda mc, e_=e_: _chunk3(fc2[e_])[:, :, mc * 128:(mc + 1) * 128],
                                    use_c=True, e_idx=e_, c_tok=None))
                            for sh in range(2):
                                experts.append(dict(
                                    w1a=lambda ocp, sh=sh: _chunk3(dr["l1_gate"][:, :])[:, :, sh * D + ocp * 128: sh * D + (ocp + 1) * 128],
                                    w1b=lambda ocp, sh=sh: _chunk3(dr["l1_up"][:, :])[:, :, sh * D + ocp * 128: sh * D + (ocp + 1) * 128],
                                    w2=lambda mc, sh=sh: dr["l1_down"][:, :].rearrange(
                                        "(kc p) m -> p kc m", p=128)[:, sh * KC:(sh + 1) * KC,
                                                                     mc * 128:(mc + 1) * 128],
                                    use_c=False, e_idx=0, c_tok=None))
                            emit_moe(p, st, C, xn1, c_all, fsc, 1, experts)
                    ck("ffn1")
                    kfm = _sb(l1, nc, "kfm1", [128, KC * T], BF16, side="right")
                    vtm1 = _sb(l1, nc, "vtm1", [128, NKT * NH * (HD + 1)], BF16, side="right")
                    nmax_k1 = _sb(l1, nc, "nmk1", [128, 4], F32, side="right")
                    lw1 = dict(Wuk=dr["l1_Wuk"], Wuv=dr["l1_Wuv"])
                    emit_kv_l1(p, st, C, lw1, ckv, kfm, vtm1, nmax_k1, 1)
                    tap("kfm", kfm, KC * T)
                    ck("kv1")

                wuq = _sb(l1, nc, "wuq1", [128, 2 * D], BF16, side="right")
                twq = p.dma("pool", lambda e: e.dma_start(
                    out=wuq[:, :].rearrange("p (kc m) -> p kc m", kc=2),
                    in_=dr["l1_Wuq"][:, :].rearrange("(kc p) m -> p kc m", p=128)))

                def build_q1(p, qp, q_pan, SB, q_consumed):
                    toks = []
                    for mc in range(KC):
                        tm = None
                        for kc in range(2):
                            tm = p.op("pe", lambda e, kc=kc, mc=mc, qp=qp: e.matmul(
                                SB.p_s[0][:, 0:SUB],
                                wuq[:, (kc * KC + mc) * 128:(kc * KC + mc + 1) * 128],
                                cq[:, kc * T + qp * SUB: kc * T + (qp + 1) * SUB],
                                start=(kc == 0), stop=(kc == 1)),
                                reads=[twq, SB.ps_last[0]])
                        tq = p.op("dve", lambda e, mc=mc: e.tensor_copy(
                            q_pan[:, mc * SUB:(mc + 1) * SUB], SB.p_s[0][:, 0:SUB]),
                            reads=[tm] + q_consumed)
                        SB.ps_last[0] = tq
                        toks.append(tq)
                    return toks

                cfg1 = dict(
                    build_q=build_q1,
                    k_lhsT=lambda h, kt: kfm[64 * (h & 1):64 * (h & 1) + 64,
                                             (h >> 1) * T + kt * 128:(h >> 1) * T + (kt + 1) * 128],
                    vtm_slice=lambda h, kt: vtm1[:, (kt * NH + h) * (HD + 1):
                                                 (kt * NH + h) * (HD + 1) + HD + 1],
                    nmax_k=nmax_k1, kv_tok=("dve", p.counts["dve"]),
                    wo=lambda mc: _chunk3(dr["l1_Wo"][:, :])[:, :, mc * 128:(mc + 1) * 128],
                    outb=None, resid_in=resid, resid_out=outT, fsc=fsc,
                    fsc_tok=("dgp", p.counts["dgp"]))
                emit_scores(p, st, C, 1, cfg1)

        p.barrier()
        p.emit()
    return nc


_CACHE = {}


def kernel(**inputs):
    consts = _host_consts()
    w = _prep_weights(inputs)
    x = np.asarray(inputs["L0_attn_x"], dtype=np.float32)
    key = "full"
    if key not in _CACHE:
        _CACHE[key] = build_program()
    nc = _CACHE[key]
    in_maps = []
    for b in range(8):
        m = dict(consts)
        m.update(w)
        m["xT"] = np.ascontiguousarray(x[b].T)
        in_maps.append(m)
    res = run_bass_kernel_spmd(nc, in_maps, core_ids=list(range(8)))
    out = np.empty((8, T, D), np.float32)
    for b in range(8):
        out[b] = res.results[b]["outT"].T
    return out


# version 06
# speedup vs baseline: 1.0202x; 1.0202x over previous
import numpy as np
import ml_dtypes
from contextlib import ExitStack

import concourse.bass as bass
import concourse.mybir as mybir
from concourse.bass_utils import run_bass_kernel_spmd

F32 = mybir.dt.float32
BF16 = mybir.dt.bfloat16
AF = mybir.ActivationFunctionType
ALU = mybir.AluOpType
AX = mybir.AxisListType
BF = ml_dtypes.bfloat16

T = 4096
D = 1024
KC = 8
SUB = 512
NSUB = T // SUB
NH = 16
HD = 64
NE = 8
NQP = T // SUB
NKT = T // 128
ROPE_THETA = 10000.0

DMA_CH = {"sp": "dsp", "pool": "dgp"}
ENG_SECT = {"pe": "tensor", "act": "scalar", "dve": "vector", "pool": "gpsimd", "sp": "sync"}
CHANNELS = ("pe", "act", "dve", "pool", "sp", "dsp", "dgp")


class Prog:

    def __init__(self, nc, stack):
        self.nc = nc
        self.sems = {ch: stack.enter_context(nc.semaphore(f"s_{ch}")) for ch in CHANNELS}
        self.counts = {ch: 0 for ch in CHANNELS}
        self.known = {e: {ch: 0 for ch in CHANNELS} for e in ENG_SECT}
        self.thunks = {e: [] for e in ENG_SECT}

    def _waits_for(self, eng, reads):
        merged = {}
        for tok in reads:
            if tok is None:
                continue
            ch, v = tok
            if v > merged.get(ch, 0):
                merged[ch] = v
        waits = []
        for ch, v in merged.items():
            if self.known[eng][ch] < v:
                waits.append((self.sems[ch], v))
                self.known[eng][ch] = v
        return waits

    def op(self, eng, fn, reads=(), drain=False):
        waits = self._waits_for(eng, reads)
        self.counts[eng] += 1
        tok = (eng, self.counts[eng])
        sem = self.sems[eng]

        def thunk(e, waits=waits, fn=fn, drain=drain, sem=sem):
            for s, v in waits:
                e.wait_ge(s, v)
            if drain:
                e.drain()
            fn(e).then_inc(sem, 1)

        self.thunks[eng].append(thunk)
        return tok

    def dma(self, eng, fn, reads=()):
        waits = self._waits_for(eng, reads)
        ch = DMA_CH[eng]
        self.counts[ch] += 16
        tok = (ch, self.counts[ch])
        sem = self.sems[ch]

        def thunk(e, waits=waits, fn=fn, sem=sem):
            for s, v in waits:
                e.wait_ge(s, v)
            fn(e).then_inc(sem, 16)

        self.thunks[eng].append(thunk)
        return tok

    def barrier(self):
        toks = [(ch, self.counts[ch]) for ch in CHANNELS]
        for eng in ENG_SECT:
            waits = self._waits_for(eng, toks)
            if waits:
                def thunk(e, waits=waits):
                    for s, v in waits:
                        e.wait_ge(s, v)
                self.thunks[eng].append(thunk)

    def emit(self):
        with self.nc.Block() as blk:
            for eng, sect in ENG_SECT.items():
                getattr(blk, sect)(self._make_runner(eng))

    def _make_runner(self, eng):
        thunks = self.thunks[eng]

        def run(e):
            for t in thunks:
                t(e)
        return run



def _host_consts():
    i = np.arange(HD // 2, dtype=np.float64)
    inv_freq = 1.0 / ROPE_THETA ** (2 * i / HD)
    pos = np.arange(T, dtype=np.float64)
    ang = pos[None, :] * inv_freq[:, None]
    cos32 = np.cos(ang)
    sin32 = np.sin(ang)
    cos64 = np.repeat(cos32, 2, axis=0)
    sin64s = np.empty((HD, T))
    sin64s[0::2] = -sin32
    sin64s[1::2] = sin32
    cos128 = np.concatenate([cos64, cos64], 0).astype(np.float32)
    sin128 = np.concatenate([sin64s, sin64s], 0).astype(np.float32)

    p64 = np.zeros((HD, HD), np.float32)
    for k in range(0, HD, 2):
        p64[k, k + 1] = 1.0
        p64[k + 1, k] = 1.0
    p128 = np.zeros((128, 128), np.float32)
    p128[:64, :64] = p64
    p128[64:, 64:] = p64

    cmask = np.zeros((128, 4 * SUB), np.float32)
    dk = np.arange(128)[:, None]
    qr = np.arange(SUB)[None, :]
    for j in range(4):
        cmask[:, j * SUB:(j + 1) * SUB] = (qr >= 128 * j + dk).astype(np.float32)

    ident = np.eye(128, dtype=np.float32)
    ohsel = np.zeros((NE, NE * 128), np.float32)
    for e in range(NE):
        ohsel[e, e * 128:(e + 1) * 128] = 1.0
    return dict(c_cos=cos128, c_sin=sin128, c_p128=p128, c_cmask=cmask,
                c_ident=ident, c_ohsel=ohsel)


def _prep_weights(inp):
    c = lambda a: np.ascontiguousarray(np.asarray(a, dtype=np.float32))
    w = {}
    w["l0_qkvT"] = c(np.asarray(inp["L0_attn_qkv_w"]).T)
    qb = np.asarray(inp["L0_attn_qkv_b"])
    qkvb = np.zeros((128, 10), np.float32)
    qkvb[:, 0:9] = np.asarray(qb).reshape(9, 128).T
    qkvb[0:64, 9] = qb[1088:1152]
    qkvb[64:128, 8] = 0.0
    w["l0_qkvb"] = c(qkvb)
    w["l0_outT"] = c(np.asarray(inp["L0_attn_out_w"]).T)
    w["l0_outb"] = c(np.asarray(inp["L0_attn_out_b"]).reshape(8, 128).T)
    w["l0_WgT"] = c(np.asarray(inp["L0_ffn_Wg"]).T)
    w["l0_bg"] = c(np.asarray(inp["L0_ffn_bg"]).reshape(8, 1))
    w["l0_w1T"] = c(np.asarray(inp["L0_ffn_w1"]).transpose(0, 2, 1))
    w["l0_w2T"] = c(np.asarray(inp["L0_ffn_w2"]).transpose(0, 2, 1))
    w["l1_Wdq"] = c(inp["L1_attn_W_dq"])
    w["l1_Wuq"] = c(inp["L1_attn_W_uq"])
    w["l1_Wdkv"] = c(inp["L1_attn_W_dkv"])
    w["l1_Wuk"] = c(inp["L1_attn_W_uk"])
    w["l1_Wuv"] = c(inp["L1_attn_W_uv"])
    w["l1_Wo"] = c(inp["L1_attn_W_o"])
    w["l1_router"] = c(inp["L1_ffn_w_router"])
    w["l1_fc1"] = c(inp["L1_ffn_fc1_w"])
    w["l1_fc2"] = c(inp["L1_ffn_fc2_w"])
    w["l1_gate"] = c(inp["L1_ffn_gate_w"])
    w["l1_up"] = c(inp["L1_ffn_up_w"])
    w["l1_down"] = c(inp["L1_ffn_down_w"])
    return w


class Bufs:
    pass


def _sb(stack, nc, name, shape, dtype, side=None):
    kw = {} if side is None else {"side": side}
    return stack.enter_context(nc.sbuf_tensor(name, shape, dtype, **kw))


def _ps(stack, nc, name, shape, dtype=F32):
    return stack.enter_context(nc.psum_tensor(name, shape, dtype))


def _chunk3(dram_ap):
    return dram_ap.rearrange("(kc p) m -> p kc m", p=128)



def emit_consts(p, st, dr):
    nc = p.nc
    C = Bufs()
    C.ident_f = _sb(st, nc, "ident_f", [128, 128], F32, side="right")
    C.ident_b = _sb(st, nc, "ident_b", [128, 128], BF16, side="right")
    C.cmask = _sb(st, nc, "cmask", [128, 4 * SUB], BF16, side="right")
    C.ohsel = _sb(st, nc, "ohsel", [NE, NE * 128], BF16, side="right")
    C.ones_b = _sb(st, nc, "ones_b", [128, 1], BF16, side="right")
    C.ones_fr = _sb(st, nc, "ones_fr", [128, 128], F32, side="right")
    C.ones_br = _sb(st, nc, "ones_br", [128, 128], BF16, side="right")
    C.fconst = _sb(st, nc, "fconst", [1, 2], F32, side="right")
    p.op("dve", lambda e: e.memset(C.fconst[0:1, 0:1], 1e-5))
    p.op("dve", lambda e: e.memset(C.fconst[0:1, 1:2], 1e-30))
    p.dma("sp", lambda e: e.dma_start(out=C.ident_f[:, :], in_=dr["c_ident"][:, :]))
    p.dma("pool", lambda e: e.dma_start(out=C.cmask[:, :], in_=dr["c_cmask"][:, :]))
    p.dma("pool", lambda e: e.dma_start(out=C.ident_b[:, :], in_=dr["c_ident"][:, :]))
    p.dma("pool", lambda e: e.dma_start(out=C.ohsel[:, :], in_=dr["c_ohsel"][:, :]))
    p.op("dve", lambda e: e.memset(C.ones_b[:, :], 1.0))
    p.op("dve", lambda e: e.memset(C.ones_fr[:, :], 1.0))
    p.op("dve", lambda e: e.memset(C.ones_br[:, :], 1.0))
    p.barrier()
    return C

def emit_ln(p, st, C, src_dram, xn, layer):
    nc = p.nc
    with ExitStack() as ph:
        x_f = [_sb(ph, nc, f"ln_xf{layer}_{i}", [128, KC * SUB], F32) for i in range(2)]
        x_b = [_sb(ph, nc, f"ln_xb{layer}_{i}", [128, KC * SUB], BF16) for i in range(2)]
        xsq = _sb(ph, nc, f"ln_xsq{layer}", [128, KC * SUB], BF16)
        tmp = _sb(ph, nc, f"ln_tmp{layer}", [128, KC * SUB], F32)
        rows = [_sb(ph, nc, f"ln_rows{layer}_{i}", [1, 6 * SUB], F32) for i in range(2)]
        p_sum = [_ps(ph, nc, f"lnps{layer}_{i}", [1, SUB]) for i in range(2)]
        p_sq = [_ps(ph, nc, f"lnpq{layer}_{i}", [1, SUB]) for i in range(2)]
        p_M = [_ps(ph, nc, f"lnpM{layer}_{i}", [128, SUB]) for i in range(2)]
        p_R = [_ps(ph, nc, f"lnpR{layer}_{i}", [128, SUB]) for i in range(2)]

        src3 = src_dram[:, :].rearrange("(kc p) t -> p kc t", p=128)
        xf_gate = [None, None]
        xb_gate = [[], []]
        xsq_gate = None
        tmp_gate = None
        rows_gate = [None, None]
        psum_gate = [None, None]
        psq_gate = [None, None]
        pM_gate = [None, None]
        pR_gate = [None, None]
        for s in range(NSUB):
            i = s % 2
            tl = p.dma("sp", lambda e, s=s, i=i: e.dma_start(
                out=x_f[i][:, :].rearrange("p (kc t) -> p kc t", kc=KC),
                in_=src3[:, :, s * SUB:(s + 1) * SUB]), reads=[xf_gate[i]])
            tlb = p.dma("pool", lambda e, s=s, i=i: e.dma_start(
                out=x_b[i][:, :].rearrange("p (kc t) -> p kc t", kc=KC),
                in_=src3[:, :, s * SUB:(s + 1) * SUB]), reads=xb_gate[i])
            tsq = []
            for kc in range(KC):
                tsq.append(p.op("act", lambda e, kc=kc, i=i: e.activation(
                    xsq[:, kc * SUB:(kc + 1) * SUB], x_b[i][:, kc * SUB:(kc + 1) * SUB],
                    AF.Square), reads=[tlb, xsq_gate]))
            tm = tv = None
            for kc in range(KC):
                tm = p.op("pe", lambda e, kc=kc, i=i: e.matmul(
                    p_sum[i][:, :], C.ones_b[:, :], x_b[i][:, kc * SUB:(kc + 1) * SUB],
                    start=(kc == 0), stop=(kc == KC - 1)), reads=[tlb, psum_gate[i]])
            for kc in range(KC):
                tv = p.op("pe", lambda e, kc=kc, i=i: e.matmul(
                    p_sq[i][:, :], C.ones_b[:, :], xsq[:, kc * SUB:(kc + 1) * SUB],
                    start=(kc == 0), stop=(kc == KC - 1)), reads=[tsq[kc], psq_gate[i]])
            xsq_gate = tv
            R = lambda j, i=i: rows[i][0:1, j * SUB:(j + 1) * SUB]
            t1 = p.op("dve", lambda e, R=R, i=i: e.tensor_scalar_mul(R(0), p_sum[i][:, :], 1.0 / D),
                      reads=[tm, rows_gate[i]])
            t2 = p.op("dve", lambda e, R=R, i=i: e.tensor_scalar_mul(R(1), p_sq[i][:, :], 1.0 / D),
                      reads=[tv], drain=True)
            psum_gate[i] = t1
            psq_gate[i] = t2
            t3 = p.op("dve", lambda e, R=R: e.tensor_mul(R(2), R(0), R(0)),
                      reads=[t1], drain=True)
            t4 = p.op("dve", lambda e, R=R: e.tensor_sub(R(3), R(1), R(2)),
                      reads=[t2, t3], drain=True)
            t5 = p.op("act", lambda e, R=R: e.activation(R(4), R(3), AF.Ln,
                                                         bias=C.fconst[0:1, 0:1]),
                      reads=[t4])
            t6 = p.op("act", lambda e, R=R: e.activation(R(5), R(4), AF.Exp, scale=-0.5),
                      reads=[t5], drain=True)
            tM = p.op("pe", lambda e, R=R, i=i: e.matmul(p_M[i][:, :], C.ones_fr[0:1, :],
                                                         R(0), start=True, stop=True),
                      reads=[t1, t3, pM_gate[i]])
            tR = p.op("pe", lambda e, R=R, i=i: e.matmul(p_R[i][:, :], C.ones_fr[0:1, :],
                                                         R(5), start=True, stop=True),
                      reads=[t6, pR_gate[i]])
            rows_gate[i] = tR
            tsub = []
            for kc in range(KC):
                tsub.append(p.op("dve", lambda e, kc=kc, i=i: e.tensor_sub(
                    tmp[:, kc * SUB:(kc + 1) * SUB], x_f[i][:, kc * SUB:(kc + 1) * SUB],
                    p_M[i][:, :]), reads=[tl, tM, tmp_gate]))
            tout = []
            for kc in range(KC):
                tout.append(p.op("dve", lambda e, kc=kc, s=s, i=i: e.tensor_mul(
                    xn[:, kc * T + s * SUB: kc * T + (s + 1) * SUB],
                    tmp[:, kc * SUB:(kc + 1) * SUB], p_R[i][:, :]), reads=[tsub[kc], tR]))
            tmp_gate = tout[-1]
            pM_gate[i] = tout[-1]
            pR_gate[i] = tout[-1]
            xf_gate[i] = tout[-1]
            xb_gate[i] = [tm, tsq[-1]]
        p.barrier()


def emit_routing(p, st, C, xn, c_all, router_dram, bias_dram, layer):
    nc = p.nc
    NQ = SUB // 128
    with ExitStack() as ph:
        wrt = _sb(ph, nc, f"rt_w{layer}", [128, KC * NE], BF16)
        lg = _sb(ph, nc, f"rt_lg{layer}", [NE, SUB], F32)
        bias_s = _sb(ph, nc, f"rt_b{layer}", [NE, 1], F32)
        ltok = [_sb(ph, nc, f"rt_lt{layer}_{q}", [128, NE], F32) for q in range(NQ)]
        sc = [_sb(ph, nc, f"rt_sc{layer}_{q}", [128, 8], F32) for q in range(NQ)]
        msk = [_sb(ph, nc, f"rt_msk{layer}_{q}", [128, NE], F32) for q in range(NQ)]
        pex = [_sb(ph, nc, f"rt_pex{layer}_{q}", [128, NE], F32) for q in range(NQ)]
        ctok = [_sb(ph, nc, f"rt_ct{layer}_{q}", [128, NE], F32) for q in range(NQ)]
        p_lg = [_ps(ph, nc, f"rt_plg{layer}_{i}", [NE, SUB]) for i in range(2)]
        p_t = [_ps(ph, nc, f"rt_pt{layer}_{i}", [128, NE]) for i in range(2)]
        p_ct = [_ps(ph, nc, f"rt_pct{layer}_{i}", [NE, 128]) for i in range(2)]

        tw = p.dma("pool", lambda e: e.dma_start(
            out=wrt[:, :].rearrange("p (kc e) -> p kc e", kc=KC),
            in_=_chunk3(router_dram[:, :])))
        if bias_dram is not None:
            tb = p.dma("sp", lambda e: e.dma_start(out=bias_s[:, :], in_=bias_dram[:, :]))
        else:
            tb = p.op("dve", lambda e: e.memset(bias_s[:, :], 0.0))

        plg_gate = [None, None]
        pt_gate = [None, None]
        pct_gate = [None, None]
        lg_gate = None
        for s in range(NSUB):
            il = s % 2
            tmm = None
            for kc in range(KC):
                tmm = p.op("pe", lambda e, kc=kc, s=s, il=il: e.matmul(
                    p_lg[il][:, :], wrt[:, kc * NE:(kc + 1) * NE],
                    xn[:, kc * T + s * SUB: kc * T + (s + 1) * SUB],
                    start=(kc == 0), stop=(kc == KC - 1)), reads=[tw, plg_gate[il]])
            te = p.op("dve", lambda e, il=il: e.tensor_scalar(lg[:, :], p_lg[il][:, :],
                                                              bias_s[0:NE, 0:1], None,
                                                              ALU.add),
                      reads=[tmm, tb, lg_gate])
            plg_gate[il] = te
            t0s = []
            for q in range(NQ):
                it = q % 2
                tt = p.op("pe", lambda e, q=q, it=it: e.transpose(
                    p_t[it][:, :], lg[:, q * 128:(q + 1) * 128], C.ident_f[0:NE, 0:NE]),
                    reads=[te, pt_gate[it]])
                t0 = p.op("dve", lambda e, q=q, it=it: e.tensor_copy(ltok[q][:, :],
                                                                     p_t[it][:, :]),
                          reads=[tt])
                pt_gate[it] = t0
                t0s.append(t0)
            lg_gate = t0s[-1]
            tA = [p.op("dve", lambda e, q=q: e.reduce_max(sc[q][:, 0:1], ltok[q][:, :],
                                                          axis=AX.X, negate=True),
                       reads=[t0s[q]]) for q in range(NQ)]
            tB = [p.op("dve", lambda e, q=q: e.tensor_scalar(msk[q][:, :], ltok[q][:, :],
                                                             sc[q][:, 0:1], None, ALU.add),
                       reads=[tA[q]]) for q in range(NQ)]
            tC = [p.op("dve", lambda e, q=q: e.tensor_scalar(msk[q][:, :], msk[q][:, :],
                                                             0.0, None, ALU.is_ge),
                       reads=[tB[q]]) for q in range(NQ)]
            tD = [p.op("dve", lambda e, q=q: e.scalar_tensor_tensor(
                msk[q][:, :], msk[q][:, :], -1e9, ltok[q][:, :], ALU.mult, ALU.add),
                reads=[tC[q]]) for q in range(NQ)]
            tE = [p.op("dve", lambda e, q=q: e.reduce_max(sc[q][:, 1:2], msk[q][:, :],
                                                          axis=AX.X), reads=[tD[q]])
                  for q in range(NQ)]
            tF = [p.op("act", lambda e, q=q: e.activation(pex[q][:, :], ltok[q][:, :],
                                                          AF.Exp, bias=sc[q][:, 0:1]),
                       reads=[tA[q]]) for q in range(NQ)]
            tG = [p.op("dve", lambda e, q=q: e.tensor_scalar(ctok[q][:, :], ltok[q][:, :],
                                                             sc[q][:, 1:2], None, ALU.is_ge),
                       reads=[tE[q]]) for q in range(NQ)]
            tH = [p.op("dve", lambda e, q=q: e.tensor_mul(ctok[q][:, :], ctok[q][:, :],
                                                          pex[q][:, :]),
                       reads=[tF[q], tG[q]]) for q in range(NQ)]
            tI = [p.op("dve", lambda e, q=q: e.reduce_sum(sc[q][:, 2:3], ctok[q][:, :],
                                                          axis=AX.X), reads=[tH[q]])
                  for q in range(NQ)]
            tJ = [p.op("dve", lambda e, q=q: e.reciprocal(sc[q][:, 3:4], sc[q][:, 2:3]),
                       reads=[tI[q]]) for q in range(NQ)]
            tK = [p.op("dve", lambda e, q=q: e.tensor_scalar(ctok[q][:, :], ctok[q][:, :],
                                                             sc[q][:, 3:4], None, ALU.mult),
                       reads=[tJ[q]]) for q in range(NQ)]
            for q in range(NQ):
                it = q % 2
                tc = p.op("pe", lambda e, q=q, it=it: e.transpose(
                    p_ct[it][:, :], ctok[q][:, :], C.ident_f[:, :]),
                    reads=[tK[q], pct_gate[it]])
                tcp = p.op("dve", lambda e, s=s, q=q, it=it: e.tensor_copy(
                    c_all[:, s * SUB + q * 128: s * SUB + (q + 1) * 128], p_ct[it][:, :]),
                    reads=[tc])
                pct_gate[it] = tcp
        p.barrier()


def emit_moe(p, st, C, xn, c_all, fsc, layer, experts):
    nc = p.nc
    with ExitStack() as ph:
        a_buf = _sb(ph, nc, f"moe_a{layer}", [128, KC * T], BF16)
        w1s = [_sb(ph, nc, f"moe_w1{layer}_{i}", [128, 2 * KC * 128], BF16) for i in range(2)]
        w2s = [_sb(ph, nc, f"moe_w2{layer}_{i}", [128, KC * 128], BF16) for i in range(2)]
        silu_t = [_sb(ph, nc, f"moe_st{layer}_{i}", [128, SUB], BF16) for i in range(2)]
        c_sb = _sb(ph, nc, f"moe_cb{layer}", [128, T], BF16)
        o_t = [_sb(ph, nc, f"moe_ot{layer}_{i}", [128, SUB], F32) for i in range(3)]
        p_h = [_ps(ph, nc, f"moe_ph{layer}_{i}", [128, SUB]) for i in range(4)]
        p_o = [_ps(ph, nc, f"moe_po{layer}_{i}", [128, SUB]) for i in range(2)]
        p_c = _ps(ph, nc, f"moe_pc{layer}", [128, SUB])

        fsc3 = fsc[:, :].rearrange("(mc p) t -> p mc t", p=128)
        region_tok = {}
        w1_last = [None, None]
        w2_last = [None, None]
        ph_last = [None] * 4
        po_last = [None] * 2
        ot_last = [None] * 3
        cs_last = None
        si_last = [None, None]
        pc_last = None

        for ei, ex in enumerate(experts):
            for ocp in range(KC):
                wb = w1s[ocp % 2]
                twa = p.dma("pool", lambda e, ex=ex, ocp=ocp, wb=wb: e.dma_start(
                    out=wb[:, 0:KC * 128].rearrange("p (kc m) -> p kc m", kc=KC),
                    in_=ex["w1a"](ocp)), reads=[w1_last[ocp % 2]])
                twb = p.dma("pool", lambda e, ex=ex, ocp=ocp, wb=wb: e.dma_start(
                    out=wb[:, KC * 128:2 * KC * 128].rearrange("p (kc m) -> p kc m", kc=KC),
                    in_=ex["w1b"](ocp)), reads=[w1_last[ocp % 2]])
                last_a = None
                for s in range(NSUB):
                    ia, ib = (2 * s) % 4, (2 * s + 1) % 4
                    pa, pb = p_h[ia], p_h[ib]
                    tma = tmb = None
                    for kc in range(KC):
                        tma = p.op("pe", lambda e, kc=kc, s=s, wb=wb, pa=pa: e.matmul(
                            pa[:, :], wb[:, kc * 128:(kc + 1) * 128],
                            xn[:, kc * T + s * SUB: kc * T + (s + 1) * SUB],
                            start=(kc == 0), stop=(kc == KC - 1)),
                            reads=[twa, ph_last[ia]])
                    for kc in range(KC):
                        tmb = p.op("pe", lambda e, kc=kc, s=s, wb=wb, pb=pb: e.matmul(
                            pb[:, :], wb[:, (KC + kc) * 128:(KC + kc + 1) * 128],
                            xn[:, kc * T + s * SUB: kc * T + (s + 1) * SUB],
                            start=(kc == 0), stop=(kc == KC - 1)),
                            reads=[twb, ph_last[ib]])
                    stl = silu_t[s % 2]
                    ts = p.op("act", lambda e, stl=stl, pa=pa: e.activation(
                        stl[:, :], pa[:, :], AF.Silu), reads=[tma, si_last[s % 2]])
                    ta = p.op("dve", lambda e, ocp=ocp, s=s, stl=stl, pb=pb: e.tensor_mul(
                        a_buf[:, ocp * T + s * SUB: ocp * T + (s + 1) * SUB],
                        stl[:, :], pb[:, :]), reads=[ts, tmb])
                    si_last[s % 2] = ta
                    ph_last[ia] = ts
                    ph_last[ib] = ta
                    last_a = ta
                w1_last[ocp % 2] = last_a
            if ex["use_c"]:
                for s in range(NSUB):
                    tcm = p.op("pe", lambda e, ex=ex, s=s: e.matmul(
                        p_c[:, :], C.ohsel[:, ex["e_idx"] * 128:(ex["e_idx"] + 1) * 128],
                        c_all[:, s * SUB:(s + 1) * SUB], start=True, stop=True),
                        reads=[pc_last, ex["c_tok"]])
                    tce = p.op("dve", lambda e, s=s: e.tensor_copy(
                        c_sb[:, s * SUB:(s + 1) * SUB], p_c[:, :]),
                        reads=[tcm, cs_last])
                    pc_last = tce
                cs_tok = pc_last
            else:
                cs_tok = None
            for mc in range(KC):
                tw2 = p.dma("pool", lambda e, ex=ex, mc=mc: e.dma_start(
                    out=w2s[mc % 2][:, :].rearrange("p (kc m) -> p kc m", kc=KC),
                    in_=ex["w2"](mc)), reads=[w2_last[mc % 2]])
                last_ev = None
                for s in range(NSUB):
                    po = p_o[s % 2]
                    tmo = None
                    for kc in range(KC):
                        tmo = p.op("pe", lambda e, kc=kc, s=s, mc=mc, po=po: e.matmul(
                            po[:, :], w2s[mc % 2][:, kc * 128:(kc + 1) * 128],
                            a_buf[:, kc * T + s * SUB: kc * T + (s + 1) * SUB],
                            start=(kc == 0), stop=(kc == KC - 1)),
                            reads=[tw2, po_last[s % 2]])
                    i3 = (mc * NSUB + s) % 3
                    ot = o_t[i3]
                    if ex["use_c"]:
                        tev = p.op("dve", lambda e, ot=ot, po=po, s=s: e.tensor_mul(
                            ot[:, :], po[:, :], c_sb[:, s * SUB:(s + 1) * SUB]),
                            reads=[tmo, cs_tok, ot_last[i3]])
                    else:
                        tev = p.op("dve", lambda e, ot=ot, po=po: e.tensor_copy(
                            ot[:, :], po[:, :]), reads=[tmo, ot_last[i3]])
                    po_last[s % 2] = tev
                    last_ev = tev
                    accum = ALU.add if ei > 0 else ALU.bypass
                    td = p.dma("pool", lambda e, mc=mc, s=s, ot=ot, accum=accum: e.dma_start(
                        out=fsc3[:, mc, s * SUB:(s + 1) * SUB], in_=ot[:, :],
                        accum_op=accum), reads=[tev, region_tok.get((mc, s))])
                    region_tok[(mc, s)] = td
                    ot_last[i3] = td
                w2_last[mc % 2] = last_ev
            if ex["use_c"]:
                cs_last = w2_last[0]
        p.barrier()


def emit_lowrank_proj(p, st, C, xn, w_dram, out_fm, ncol, tag):
    nc = p.nc
    nmc = ncol // 128
    with ExitStack() as ph:
        wt = _sb(ph, nc, f"lr_w_{tag}", [128, KC * ncol], BF16)
        p_x = [_ps(ph, nc, f"lr_p_{tag}{i}", [128, SUB]) for i in range(2)]
        tw = p.dma("pool", lambda e: e.dma_start(
            out=wt[:, :].rearrange("p (kc m) -> p kc m", kc=KC),
            in_=_chunk3(w_dram[:, :])))
        last = [None, None]
        for s in range(NSUB):
            for mc in range(nmc):
                i = (s * nmc + mc) % 2
                px = p_x[i]
                tm = None
                for kc in range(KC):
                    tm = p.op("pe", lambda e, kc=kc, mc=mc, s=s, px=px: e.matmul(
                        px[:, :], wt[:, (kc * nmc + mc) * 128:(kc * nmc + mc + 1) * 128],
                        xn[:, kc * T + s * SUB: kc * T + (s + 1) * SUB],
                        start=(kc == 0), stop=(kc == KC - 1)),
                        reads=[tw, last[i]])
                te = p.op("dve", lambda e, mc=mc, s=s, px=px: e.tensor_copy(
                    out_fm[:, mc * T + s * SUB: mc * T + (s + 1) * SUB], px[:, :]),
                    reads=[tm])
                last[i] = te
        p.barrier()


def _norm_update(p, C, src_ap, nparts, p_n, nrow, nmax, prev_toks, reads, pn_gate):
    gates = pn_gate if isinstance(pn_gate, list) else [pn_gate]
    tn = p.op("pe", lambda e: e.matmul(p_n[:, :], C.ones_b[0:nparts, :], src_ap,
                                       start=True, stop=True),
              reads=list(reads) + gates)
    t2 = p.op("dve", lambda e: e.tensor_copy(nrow[:, :], p_n[:, :]), reads=[tn])
    t3 = p.op("dve", lambda e: e.reduce_max(nmax[0:1, 1:2], nrow[:, :], axis=AX.X),
              reads=[t2], drain=True)
    t4 = p.op("dve", lambda e: e.tensor_max(nmax[0:1, 0:1], nmax[0:1, 0:1],
                                            nmax[0:1, 1:2]),
              reads=[t3] + list(prev_toks), drain=True)
    return tn, t2, t4


def emit_kv_l0(p, st, C, lw, xn, kdup, vtm, nmax_k, layer):
    nc = p.nc
    with ExitStack() as ph:
        wkt = _sb(ph, nc, "kv0_wk", [128, KC * HD], BF16)
        wvt = _sb(ph, nc, "kv0_wv", [128, KC * HD], BF16)
        kraw = _sb(ph, nc, "kv0_kraw", [128, SUB], F32)
        t1b = _sb(ph, nc, "kv0_t1", [128, SUB], F32)
        t2b = _sb(ph, nc, "kv0_t2", [128, SUB], F32)
        v_b = _sb(ph, nc, "kv0_vb", [128, SUB], BF16)
        ksq = _sb(ph, nc, "kv0_ksq", [128, SUB], BF16)
        nrow = _sb(ph, nc, "kv0_nrow", [1, SUB], F32)
        p_k = _ps(ph, nc, "kv0_pk", [128, SUB])
        p_v = _ps(ph, nc, "kv0_pv", [128, SUB])
        p_sw = _ps(ph, nc, "kv0_psw", [128, SUB])
        p_t = _ps(ph, nc, "kv0_pt", [128, 128], BF16)
        p_n = _ps(ph, nc, "kv0_pn", [1, SUB])

        qkvT3 = _chunk3(lw["qkvT"][:, :])
        tk = p.dma("pool", lambda e: e.dma_start(
            out=wkt[:, :].rearrange("p (kc m) -> p kc m", kc=KC),
            in_=qkvT3[:, :, 1024:1088]))
        tv = p.dma("pool", lambda e: e.dma_start(
            out=wvt[:, :].rearrange("p (kc m) -> p kc m", kc=KC),
            in_=qkvT3[:, :, 1088:1152]))
        tmx = p.op("dve", lambda e: e.memset(nmax_k[:, :], 0.0))
        vtm_tok = p.op("dve", lambda e: e.memset(vtm[:, :], 1.0))
        kraw_tok = []
        t1_tok = t2_tok = krot_tok = None
        v_tok = []
        ksq_tok = None
        pn_gate = None
        for s in range(NSUB):
            tmk = tmv = None
            for kc in range(KC):
                tmk = p.op("pe", lambda e, kc=kc, s=s: e.matmul(
                    p_k[0:HD, :], wkt[:, kc * HD:(kc + 1) * HD],
                    xn[:, kc * T + s * SUB: kc * T + (s + 1) * SUB],
                    start=(kc == 0), stop=(kc == KC - 1)), reads=[tk] + kraw_tok)
            for kc in range(KC):
                tmv = p.op("pe", lambda e, kc=kc, s=s: e.matmul(
                    p_v[0:HD, :], wvt[:, kc * HD:(kc + 1) * HD],
                    xn[:, kc * T + s * SUB: kc * T + (s + 1) * SUB],
                    start=(kc == 0), stop=(kc == KC - 1)), reads=[tv] + v_tok)
            tke = p.op("dve", lambda e: e.tensor_scalar(
                kraw[0:HD, :], p_k[0:HD, :], lw["qkvb_s"][0:HD, 8:9], None, ALU.add),
                reads=[tmk, lw["qkvb_tok"]])
            tsw = p.op("pe", lambda e: e.matmul(
                p_sw[0:HD, :], lw["p128f"][0:HD, 0:HD], kraw[0:HD, :],
                start=True, stop=True), reads=[tke, t2_tok])
            tt1 = p.op("dve", lambda e, s=s: e.tensor_mul(
                t1b[0:HD, :], kraw[0:HD, :], lw["cos"][0:HD, s * SUB:(s + 1) * SUB]),
                reads=[tke, krot_tok], drain=True)
            tt2 = p.op("dve", lambda e, s=s: e.tensor_mul(
                t2b[0:HD, :], p_sw[0:HD, :], lw["sin"][0:HD, s * SUB:(s + 1) * SUB]),
                reads=[tsw, krot_tok], drain=True)
            tkr = p.op("dve", lambda e, s=s: e.tensor_add(
                kdup[0:HD, s * SUB:(s + 1) * SUB], t1b[0:HD, :], t2b[0:HD, :]),
                reads=[tt1, tt2], drain=True)
            kraw_tok = [tsw, tt1]
            t2_tok = tt2
            krot_tok = tkr
            tq = p.op("pool", lambda e, s=s: e.tensor_mul(
                ksq[0:HD, :], kdup[0:HD, s * SUB:(s + 1) * SUB],
                kdup[0:HD, s * SUB:(s + 1) * SUB]), reads=[tkr, ksq_tok])
            tn, pn_gate, tmx = _norm_update(p, C, ksq[0:HD, :], HD, p_n, nrow, nmax_k,
                                            [tmx], [tq], pn_gate)
            ksq_tok = tn
            tve = p.op("dve", lambda e: e.tensor_scalar(
                v_b[0:HD, :], p_v[0:HD, :], lw["qkvb_s"][0:HD, 9:10], None, ALU.add),
                reads=[tmv, lw["qkvb_tok"]])
            v_tok = [tve]
            for q in range(SUB // 128):
                kt = s * (SUB // 128) + q
                ttr = p.op("pe", lambda e, q=q: e.transpose(
                    p_t[:, 0:HD], v_b[0:HD, q * 128:(q + 1) * 128], C.ident_b[0:HD, 0:HD]),
                    reads=[tve, vtm_tok])
                tcp = p.op("dve", lambda e, kt=kt: e.tensor_copy(
                    vtm[:, kt * (HD + 1): kt * (HD + 1) + HD], p_t[:, 0:HD]),
                    reads=[ttr], drain=True)
                vtm_tok = tcp
                v_tok.append(ttr)
        p.dma("pool", lambda e: e.dma_start(out=kdup[HD:128, :], in_=kdup[0:HD, :]),
              reads=[krot_tok])
        p.barrier()


def emit_kv_l1(p, st, C, lw, ckv, kfm, vtm, nmax_k, layer):
    nc = p.nc
    with ExitStack() as ph:
        wuk = _sb(ph, nc, "kv1_wuk", [128, 2 * D], BF16)
        wuv = _sb(ph, nc, "kv1_wuv", [128, 2 * D], BF16)
        v_b = _sb(ph, nc, "kv1_vb", [128, SUB], BF16)
        ksq = _sb(ph, nc, "kv1_ksq", [128, SUB], BF16)
        nrow = _sb(ph, nc, "kv1_nrow", [1, SUB], F32)
        p_k = [_ps(ph, nc, f"kv1_pk{i}", [128, SUB]) for i in range(2)]
        p_v = _ps(ph, nc, "kv1_pv", [128, SUB])
        p_t = _ps(ph, nc, "kv1_pt", [128, 128], BF16)
        p_n = _ps(ph, nc, "kv1_pn", [1, SUB])

        tk = p.dma("pool", lambda e: e.dma_start(
            out=wuk[:, :].rearrange("p (kc m) -> p kc m", kc=2),
            in_=lw["Wuk"][:, :].rearrange("(kc p) m -> p kc m", p=128)))
        tv = p.dma("pool", lambda e: e.dma_start(
            out=wuv[:, :].rearrange("p (kc m) -> p kc m", kc=2),
            in_=lw["Wuv"][:, :].rearrange("(kc p) m -> p kc m", p=128)))
        tmx = p.op("dve", lambda e: e.memset(nmax_k[:, :], 0.0))
        vtm_tok = p.op("dve", lambda e: e.memset(vtm[:, :], 1.0))
        klast = [None, None]
        ksq_tok = None
        pn_gate = None
        v_tok = []
        for s in range(NSUB):
            for mc in range(KC):
                i = mc % 2
                pk = p_k[i]
                tm = None
                for kc in range(2):
                    tm = p.op("pe", lambda e, kc=kc, mc=mc, s=s, pk=pk: e.matmul(
                        pk[:, :], wuk[:, (kc * KC + mc) * 128:(kc * KC + mc + 1) * 128],
                        ckv[:, kc * T + s * SUB: kc * T + (s + 1) * SUB],
                        start=(kc == 0), stop=(kc == 1)), reads=[tk, klast[i]])
                te = p.op("dve", lambda e, mc=mc, s=s, pk=pk: e.tensor_copy(
                    kfm[:, mc * T + s * SUB: mc * T + (s + 1) * SUB], pk[:, :]), reads=[tm])
                klast[i] = te
                tq = p.op("pool", lambda e, mc=mc, s=s: e.tensor_mul(
                    ksq[:, :], kfm[:, mc * T + s * SUB: mc * T + (s + 1) * SUB],
                    kfm[:, mc * T + s * SUB: mc * T + (s + 1) * SUB]),
                    reads=[te, ksq_tok])
                tn, pn_gate, tmx = _norm_update(p, C, ksq[:, :], 128, p_n, nrow, nmax_k,
                                                [tmx], [tq], pn_gate)
                ksq_tok = tn
            for h in range(NH):
                tm = None
                for kc in range(2):
                    tm = p.op("pe", lambda e, kc=kc, h=h, s=s: e.matmul(
                        p_v[0:HD, :],
                        wuv[:, kc * KC * 128 + h * HD: kc * KC * 128 + (h + 1) * HD],
                        ckv[:, kc * T + s * SUB: kc * T + (s + 1) * SUB],
                        start=(kc == 0), stop=(kc == 1)), reads=[tv] + v_tok)
                tve = p.op("dve", lambda e: e.tensor_copy(v_b[0:HD, :], p_v[0:HD, :]),
                           reads=[tm])
                v_tok = [tve]
                for q in range(SUB // 128):
                    kt = s * (SUB // 128) + q
                    slot = (kt * NH + h) * (HD + 1)
                    ttr = p.op("pe", lambda e, q=q: e.transpose(
                        p_t[:, 0:HD], v_b[0:HD, q * 128:(q + 1) * 128],
                        C.ident_b[0:HD, 0:HD]), reads=[tve, vtm_tok])
                    tcp = p.op("dve", lambda e, slot=slot: e.tensor_copy(
                        vtm[:, slot: slot + HD], p_t[:, 0:HD]), reads=[ttr], drain=True)
                    vtm_tok = tcp
                    v_tok.append(ttr)
        p.barrier()

def emit_scores(p, st, C, layer, cfg):
    nc = p.nc
    with ExitStack() as ph:
        SB = Bufs()
        q_pan = _sb(ph, nc, f"sc_q{layer}", [128, KC * SUB], BF16)
        o_pan = _sb(ph, nc, f"sc_o{layer}", [128, KC * SUB], BF16)
        o_tmp = _sb(ph, nc, f"sc_ot{layer}", [128, SUB], BF16)
        pt_b = [_sb(ph, nc, f"sc_pt{layer}_{i}", [128, 2 * SUB], BF16) for i in range(2)]
        rbc = _sb(ph, nc, f"sc_rbc{layer}", [128, SUB], BF16)
        recb = _sb(ph, nc, f"sc_rec{layer}", [128, SUB], F32)
        recb2 = _sb(ph, nc, f"sc_rc2{layer}", [128, SUB], BF16)
        qsq = _sb(ph, nc, f"sc_qsq{layer}", [128, SUB], BF16)
        nrow = _sb(ph, nc, f"sc_nrow{layer}", [1, SUB], F32)
        nmax_q = _sb(ph, nc, f"sc_nq{layer}", [128, 4], F32)
        scs = _sb(ph, nc, f"sc_scs{layer}", [1, 8], F32)
        mkneg = _sb(ph, nc, f"sc_mk{layer}", [128, 1], F32)
        wo_t = [_sb(ph, nc, f"sc_wo{layer}_{i}", [128, KC * 128], BF16) for i in range(2)]
        if cfg["outb"] is not None:
            y1_t = [_sb(ph, nc, f"sc_y1{layer}_{i}", [128, SUB], F32) for i in range(2)]
        xf_t = [_sb(ph, nc, f"sc_xf{layer}_{i}", [128, SUB], F32) for i in range(2)]
        ff_t = [_sb(ph, nc, f"sc_ff{layer}_{i}", [128, SUB], F32) for i in range(2)]
        s1_t = [_sb(ph, nc, f"sc_s1{layer}_{i}", [128, SUB], F32) for i in range(2)]
        if cfg.get("rope_scratch"):
            SB.qraw = _sb(ph, nc, f"sc_qraw{layer}", [128, SUB], F32)
            SB.t1b = _sb(ph, nc, f"sc_t1{layer}", [128, SUB], F32)
            SB.t2b = _sb(ph, nc, f"sc_t2{layer}", [128, SUB], F32)
            SB.wq = [_sb(ph, nc, f"sc_wq{layer}_{i}", [128, KC * 128], BF16)
                     for i in range(2)]
        p_s = [_ps(ph, nc, f"sc_ps{layer}_{i}", [128, 2 * SUB]) for i in range(2)]
        p_oa = [_ps(ph, nc, f"sc_poa{layer}_{i}", [128, SUB]) for i in range(2)]
        p_misc = _ps(ph, nc, f"sc_pm{layer}", [128, SUB])
        p_y = _ps(ph, nc, f"sc_py{layer}", [128, SUB])
        p_n = p_y[0:1, 0:SUB]
        SB.p_s = p_s
        SB.ps_last = [None, None]

        fsc3 = cfg["fsc"][:, :].rearrange("(mc p) t -> p mc t", p=128)
        xin3 = cfg["resid_in"][:, :].rearrange("(mc p) t -> p mc t", p=128)
        xout3 = cfg["resid_out"][:, :].rearrange("(mc p) t -> p mc t", p=128)

        pt_last = [None, None]
        py_last = None
        wo_last = [None, None]
        xf_last = [None, None]
        ff_last = [None, None]
        s1_last = [None, None]
        y1_last = [None, None]
        oa_last = [None, None]
        pr_gate = None
        rc2_gate = None
        scr_gate = None
        otmp_last = None
        q_consumed = []
        o_consumed = []
        store_toks = []
        qsq_tok = None
        pn_gate = None
        kv_tok = cfg["kv_tok"]
        pending_epi = []
        tmax = p.op("dve", lambda e: e.memset(nmax_q[:, :], 0.0))
        norm_jobs = []

        def emit_norm_chunk(mc, qtok):
            nonlocal pn_gate, tmax, qsq_tok
            tq = p.op("dve", lambda e, mc=mc: e.tensor_mul(
                qsq[:, :], q_pan[:, mc * SUB:(mc + 1) * SUB],
                q_pan[:, mc * SUB:(mc + 1) * SUB]), reads=[qtok, qsq_tok])
            tn, pn_gate, tmax = _norm_update(p, C, qsq[:, :], 128, p_n, nrow, nmax_q,
                                             [tmax], [tq], [pn_gate, py_last])
            qsq_tok = tn
            return tq

        for qp in range(NQP):
            qtoks = cfg["build_q"](p, qp, q_pan, SB, q_consumed)
            q_consumed = []
            norm_jobs = [(mc, qtoks[mc]) for mc in range(KC)]
            if qp == 0:
                for mc, qtok in norm_jobs:
                    q_consumed.append(emit_norm_chunk(mc, qtok))
                norm_jobs = []
            tb1 = p.op("dve", lambda e: e.tensor_mul(scs[0:1, 0:1], nmax_q[0:1, 0:1],
                                                     cfg["nmax_k"][0:1, 0:1]),
                       reads=[tmax, kv_tok], drain=True)
            tb2a = p.op("act", lambda e: e.activation(scs[0:1, 1:2], scs[0:1, 0:1],
                                                      AF.Ln, bias=C.fconst[0:1, 1:2]),
                        reads=[tb1])
            tb2 = p.op("act", lambda e: e.activation(scs[0:1, 4:5], scs[0:1, 1:2],
                                                     AF.Exp, scale=0.5),
                       reads=[tb2a], drain=True)
            tb3 = p.op("dve", lambda e: e.tensor_scalar_mul(scs[0:1, 2:3], scs[0:1, 4:5],
                                                            -0.125), reads=[tb2], drain=True)
            tb4 = p.op("pe", lambda e: e.matmul(p_misc[:, 0:1], C.ones_fr[0:1, :],
                                                scs[0:1, 2:3], start=True, stop=True),
                       reads=[tb3, pn_gate])
            tbound = p.op("dve", lambda e: e.tensor_copy(mkneg[:, 0:1], p_misc[:, 0:1]),
                          reads=[tb4], drain=True)
            pr_gate = tbound
            nkt = 4 * qp + 4
            npair = nkt // 2
            o_gate = list(o_consumed)
            o_consumed = []
            odd_dma_toks = []
            for h in range(NH):
                hb = 64 * (h & 1)
                hc = h >> 1
                hoa = h % 2
                poa = p_oa[hoa]
                rhs_q = q_pan[hb:hb + 64, hc * SUB:(hc + 1) * SUB]
                tS_last = None

                def emit_S(pr):
                    kt0, kt1 = 2 * pr, 2 * pr + 1
                    i2 = pr % 2
                    ps = p_s[i2]
                    gate = [kv_tok, qtoks[hc], SB.ps_last[i2]]
                    tS0 = p.op("pe", lambda e, h=h, kt0=kt0, ps=ps, rhs_q=rhs_q: e.matmul(
                        ps[:, 0:SUB], cfg["k_lhsT"](h, kt0), rhs_q,
                        start=True, stop=True), reads=gate)
                    tS1 = p.op("pe", lambda e, h=h, kt1=kt1, ps=ps, rhs_q=rhs_q: e.matmul(
                        ps[:, SUB:2 * SUB], cfg["k_lhsT"](h, kt1), rhs_q,
                        start=True, stop=True), reads=gate)
                    return tS0, tS1

                def emit_exp(pr, tS0, tS1):
                    i2 = pr % 2
                    ps = p_s[i2]
                    i3 = pr % 2
                    texp = p.op("act", lambda e, ps=ps, i3=i3: e.activation(
                        pt_b[i3][:, :], ps[:, :], AF.Exp, bias=mkneg[:, 0:1], scale=0.125),
                        reads=[tS0, tS1, tbound, pt_last[i3]])
                    SB.ps_last[i2] = texp
                    tm0 = tm1 = texp
                    for half, kt in ((0, 2 * pr), (1, 2 * pr + 1)):
                        if kt >= nkt - 4:
                            j = kt - (nkt - 4)
                            tmx = p.op("pool", lambda e, i3=i3, j=j, half=half: e.tensor_mul(
                                pt_b[i3][:, half * SUB:(half + 1) * SUB],
                                pt_b[i3][:, half * SUB:(half + 1) * SUB],
                                C.cmask[:, j * SUB:(j + 1) * SUB]), reads=[texp])
                            if half == 0:
                                tm0 = tmx
                            else:
                                tm1 = tmx
                    return tm0, tm1

                def emit_PV(pr, tm0, tm1):
                    i3 = pr % 2
                    tPV0 = p.op("pe", lambda e, h=h, kt0=2 * pr, i3=i3, pr=pr, poa=poa: e.matmul(
                        poa[0:HD + 1, :], cfg["vtm_slice"](h, kt0), pt_b[i3][:, 0:SUB],
                        start=(pr == 0), stop=False),
                        reads=[tm0, tm1, kv_tok] + ([oa_last[hoa]] if pr == 0 else []))
                    tPV1 = p.op("pe", lambda e, h=h, kt1=2 * pr + 1, i3=i3, pr=pr, npair=npair, poa=poa: e.matmul(
                        poa[0:HD + 1, :], cfg["vtm_slice"](h, kt1), pt_b[i3][:, SUB:2 * SUB],
                        start=False, stop=(pr == npair - 1)),
                        reads=[])
                    pt_last[i3] = tPV1
                    return tPV1

                sx = emit_S(0)
                if pending_epi:
                    pending_epi.pop(0)()
                ex_prev = emit_exp(0, *sx)
                tS_last = sx[1]
                for pr in range(1, npair):
                    sx = emit_S(pr)
                    tS_last = sx[1]
                    emit_PV(pr - 1, *ex_prev)
                    ex_prev = emit_exp(pr, *sx)
                emit_PV(npair - 1, *ex_prev)
                q_consumed.append(tS_last)
                def make_epilogue(h=h, hc=hc, hoa=hoa, poa=poa,
                                  tPVl=pt_last[(npair - 1) % 2]):
                    def run():
                        nonlocal scr_gate, rc2_gate, pr_gate, otmp_last
                        tln = p.op("act", lambda e: e.activation(
                            recb[HD:HD + 1, :], poa[HD:HD + 1, :], AF.Ln),
                            reads=[tPVl, scr_gate], drain=True)
                        scr_gate = tln
                        trc2 = p.op("act", lambda e: e.activation(
                            recb2[HD:HD + 1, :], recb[HD:HD + 1, :], AF.Exp, scale=-1.0),
                            reads=[tln, rc2_gate], drain=True)
                        tbc = p.op("pe", lambda e: e.matmul(
                            p_misc[:, :], C.ones_br[HD:HD + 1, 0:128],
                            recb2[HD:HD + 1, :], start=True, stop=True),
                            reads=[trc2, pr_gate])
                        rc2_gate = tbc
                        trbc = p.op("dve", lambda e: e.tensor_copy(rbc[:, :], p_misc[:, :]),
                                    reads=[tbc])
                        pr_gate = trbc
                        if h % 2 == 0:
                            dest = o_pan[0:HD, hc * SUB:(hc + 1) * SUB]
                            tnrm = p.op("dve", lambda e: e.tensor_mul(
                                dest, poa[0:HD, :], rbc[0:HD, :]),
                                reads=[trbc] + o_gate, drain=True)
                        else:
                            tnrm = p.op("dve", lambda e: e.tensor_mul(
                                o_tmp[0:HD, :], poa[0:HD, :], rbc[0:HD, :]),
                                reads=[trbc, otmp_last], drain=True)
                            todd = p.dma("pool", lambda e: e.dma_start(
                                out=o_pan[HD:128, hc * SUB:(hc + 1) * SUB],
                                in_=o_tmp[0:HD, :]), reads=[tnrm] + o_gate)
                            otmp_last = todd
                            odd_dma_toks.append(todd)
                        oa_last[hoa] = tnrm
                    return run
                pending_epi.append(make_epilogue())
                if norm_jobs and h < KC:
                    mc_, qtok_ = norm_jobs[h]
                    q_consumed.append(emit_norm_chunk(mc_, qtok_))
                    if h == KC - 1:
                        norm_jobs = []
            while pending_epi:
                pending_epi.pop(0)()
            opan_toks = [oa_last[0], oa_last[1]] + odd_dma_toks[-1:]
            for mc in range(KC):
                i = mc % 2
                tw = p.dma("pool", lambda e, mc=mc, i=i: e.dma_start(
                    out=wo_t[i][:, :].rearrange("p (kc m) -> p kc m", kc=KC),
                    in_=cfg["wo"](mc)), reads=[wo_last[i]])
                tx = p.dma("sp", lambda e, mc=mc, qp=qp, i=i: e.dma_start(
                    out=xf_t[i][:, :], in_=xin3[:, mc, qp * SUB:(qp + 1) * SUB]),
                    reads=[xf_last[i]])
                tf = p.dma("sp", lambda e, mc=mc, qp=qp, i=i: e.dma_start(
                    out=ff_t[i][:, :], in_=fsc3[:, mc, qp * SUB:(qp + 1) * SUB]),
                    reads=[ff_last[i], cfg["fsc_tok"]])
                ts1 = p.op("dve", lambda e, i=i: e.tensor_add(
                    s1_t[i][:, :], xf_t[i][:, :], ff_t[i][:, :]),
                    reads=[tx, tf, s1_last[i]])
                xf_last[i] = ts1
                ff_last[i] = ts1
                tmy = None
                for kc in range(KC):
                    tmy = p.op("pe", lambda e, kc=kc, mc=mc, i=i: e.matmul(
                        p_y[:, :], wo_t[i][:, kc * 128:(kc + 1) * 128],
                        o_pan[:, kc * SUB:(kc + 1) * SUB],
                        start=(kc == 0), stop=(kc == KC - 1)),
                        reads=[tw, py_last, pn_gate] + opan_toks)
                o_consumed.append(tmy)
                wo_last[i] = tmy
                if cfg["outb"] is not None:
                    bbuf, btok = cfg["outb"]
                    ty1 = p.op("dve", lambda e, mc=mc, i=i, bbuf=bbuf: e.tensor_scalar(
                        y1_t[i][:, :], p_y[:, :], bbuf[0:128, mc:mc + 1], None, ALU.add),
                        reads=[tmy, btok, y1_last[i]])
                    py_last = ty1
                    txo = p.op("dve", lambda e, i=i: e.tensor_add(
                        s1_t[i][:, :], y1_t[i][:, :], s1_t[i][:, :]),
                        reads=[ty1, ts1], drain=True)
                    y1_last[i] = txo
                else:
                    txo = p.op("dve", lambda e, i=i: e.tensor_add(
                        s1_t[i][:, :], p_y[:, :], s1_t[i][:, :]),
                        reads=[tmy, ts1], drain=True)
                    py_last = txo
                tst = p.dma("sp", lambda e, mc=mc, qp=qp, i=i: e.dma_start(
                    out=xout3[:, mc, qp * SUB:(qp + 1) * SUB], in_=s1_t[i][:, :]),
                    reads=[txo])
                s1_last[i] = tst
                store_toks.append(tst)
        p.barrier()
        return store_toks


class _Stop(Exception):
    pass


def build_program(taps=(), upto="full"):
    nc = bass.Bass("TRN2", target_bir_lowering=False)
    dr = {}
    dr["xT"] = nc.dram_tensor("xT", [D, T], F32, kind="ExternalInput").ap()
    for n, sh in [("c_cos", [128, T]), ("c_sin", [128, T]), ("c_p128", [128, 128]),
                  ("c_cmask", [128, 4 * SUB]), ("c_ident", [128, 128]),
                  ("c_ohsel", [NE, NE * 128])]:
        dr[n] = nc.dram_tensor(n, sh, F32, kind="ExternalInput").ap()
    wshapes = dict(
        l0_qkvT=[D, 1152], l0_qkvb=[128, 10], l0_outT=[D, D], l0_outb=[128, 8],
        l0_WgT=[D, NE], l0_bg=[NE, 1], l0_w1T=[NE, D, 2 * D], l0_w2T=[NE, D, D],
        l1_Wdq=[D, 256], l1_Wuq=[256, D], l1_Wdkv=[D, 256], l1_Wuk=[256, D],
        l1_Wuv=[256, D], l1_Wo=[D, D], l1_router=[D, NE], l1_fc1=[NE, D, 2 * D],
        l1_fc2=[NE, D, D], l1_gate=[D, 2 * D], l1_up=[D, 2 * D], l1_down=[2 * D, D])
    for n, sh in wshapes.items():
        dr[n] = nc.dram_tensor(n, sh, F32, kind="ExternalInput").ap()
    resid = nc.dram_tensor("resid", [D, T], F32,
                           kind="ExternalOutput" if "resid" in taps else "Internal").ap()
    fsc = nc.dram_tensor("fsc", [D, T], F32,
                         kind="ExternalOutput" if "fsc" in taps else "Internal").ap()
    outT = nc.dram_tensor("outT", [D, T], F32, kind="ExternalOutput").ap()
    tap_dr = {}
    for tn in taps:
        if tn in ("resid", "fsc"):
            continue
        shape = {"xn": [128, KC * T], "call": [NE, T], "kdup": [128, T],
                 "kfm": [128, KC * T], "vtm0": [128, NKT * (HD + 1)],
                 "vtm1": [128, NKT * NH * (HD + 1)], "cq": [128, 2 * T],
                 "ckv": [128, 2 * T]}[tn.rstrip("01") if tn[-1] in "01" and not tn.startswith("vtm") else tn]
        tap_dr[tn] = nc.dram_tensor(f"tap_{tn}", shape, F32, kind="ExternalOutput").ap()

    try:
        nc = _assemble(nc, dr, resid, fsc, outT, tap_dr, taps, upto)
    except _Stop:
        pass
    return nc


def _assemble(nc, dr, resid, fsc, outT, tap_dr, taps, upto):
    with ExitStack() as st:
        p = Prog(nc, st)
        C = emit_consts(p, st, dr)

        def ck(name):
            if upto == name:
                p.barrier()
                p.emit()
                raise _Stop

        def tap(name, buf, cols):
            if name in tap_dr:
                with ExitStack() as th:
                    tb = _sb(th, nc, f"tapb_{name}", [128, 2048], F32)
                    np_ = buf.shape[0]
                    for c0 in range(0, cols, 2048):
                        w = min(2048, cols - c0)
                        tt = p.op("dve", lambda e, c0=c0, w=w, np_=np_: e.tensor_copy(
                            tb[0:np_, 0:w], buf[0:np_, c0:c0 + w]))
                        p.dma("sp", lambda e, c0=c0, w=w, np_=np_: e.dma_start(
                            out=tap_dr[name][0:np_, c0:c0 + w],
                            in_=tb[0:np_, 0:w]), reads=[tt])
                        p.barrier()

        if True:
            with ExitStack() as l0:
                qkvb_s = _sb(l0, nc, "qkvb0", [128, 10], F32)
                outb_s = _sb(l0, nc, "outb0", [128, 8], F32)
                xn = _sb(l0, nc, "xn0", [128, KC * T], BF16)
                tqb = p.dma("sp", lambda e: e.dma_start(out=qkvb_s[:, :], in_=dr["l0_qkvb"][:, :]))
                tob = p.dma("sp", lambda e: e.dma_start(out=outb_s[:, :], in_=dr["l0_outb"][:, :]))
                p.barrier()

                emit_ln(p, st, C, dr["xT"], xn, 0)
                tap("xn0", xn, KC * T)
                ck("ln0")

                with ExitStack() as lf:
                    c_all = _sb(lf, nc, "call0", [NE, T], BF16)
                    emit_routing(p, st, C, xn, c_all, dr["l0_WgT"], dr["l0_bg"], 0)
                    tap("call0", c_all, T)
                    ck("rt0")
                    w1T = dr["l0_w1T"]
                    w2T = dr["l0_w2T"]
                    experts = []
                    for e_ in range(NE):
                        experts.append(dict(
                            w1a=lambda ocp, e_=e_: _chunk3(w1T[e_])[:, :, ocp * 128:(ocp + 1) * 128],
                            w1b=lambda ocp, e_=e_: _chunk3(w1T[e_])[:, :, D + ocp * 128: D + (ocp + 1) * 128],
                            w2=lambda mc, e_=e_: _chunk3(w2T[e_])[:, :, mc * 128:(mc + 1) * 128],
                            use_c=True, e_idx=e_, c_tok=None))
                    emit_moe(p, st, C, xn, c_all, fsc, 0, experts)
                tap("fsc", None, 0)
                ck("ffn0")

                cosb = _sb(l0, nc, "cos0", [128, T], F32)
                sinb = _sb(l0, nc, "sin0", [128, T], F32)
                p128f = _sb(l0, nc, "p128f", [128, 128], F32)
                p.dma("sp", lambda e: e.dma_start(out=cosb[:, :], in_=dr["c_cos"][:, :]))
                p.dma("sp", lambda e: e.dma_start(out=sinb[:, :], in_=dr["c_sin"][:, :]))
                p.dma("sp", lambda e: e.dma_start(out=p128f[:, :], in_=dr["c_p128"][:, :]))
                p.barrier()
                kdup = _sb(l0, nc, "kdup0", [128, T], BF16, side="right")
                vtm0 = _sb(l0, nc, "vtm0", [128, NKT * (HD + 1)], BF16, side="right")
                nmax_k = _sb(l0, nc, "nmk0", [128, 4], F32, side="right")
                lw0 = dict(qkvT=dr["l0_qkvT"], qkvb_s=qkvb_s, qkvb_tok=tqb,
                           p128f=p128f, cos=cosb, sin=sinb)
                emit_kv_l0(p, st, C, lw0, xn, kdup, vtm0, nmax_k, 0)
                tap("kdup", kdup, T)
                ck("kv0")

                qkvT3 = _chunk3(dr["l0_qkvT"][:, :])

                def build_q0(p, qp, q_pan, SB, q_consumed, wq_last=[None, None],
                             qraw_last=[], t1_last=[], t2_last=[]):
                    toks = []
                    for mc in range(KC):
                        i = mc % 2
                        tw = p.dma("pool", lambda e, mc=mc, i=i: e.dma_start(
                            out=SB.wq[i][:, :].rearrange("p (kc m) -> p kc m", kc=KC),
                            in_=qkvT3[:, :, mc * 128:(mc + 1) * 128]), reads=[wq_last[i]])
                        tm = None
                        for kc in range(KC):
                            tm = p.op("pe", lambda e, kc=kc, qp=qp, i=i: e.matmul(
                                SB.p_s[0][:, 0:SUB], SB.wq[i][:, kc * 128:(kc + 1) * 128],
                                xn[:, kc * T + qp * SUB: kc * T + (qp + 1) * SUB],
                                start=(kc == 0), stop=(kc == KC - 1)),
                                reads=[tw, SB.ps_last[0]])
                        wq_last[i] = tm
                        tev = p.op("dve", lambda e, mc=mc: e.tensor_scalar(
                            SB.qraw[:, :], SB.p_s[0][:, 0:SUB],
                            qkvb_s[0:128, mc:mc + 1], None, ALU.add),
                            reads=[tm, tqb] + qraw_last)
                        SB.ps_last[0] = tev
                        tsw = p.op("pe", lambda e: e.matmul(
                            SB.p_s[1][:, 0:SUB], p128f[:, :], SB.qraw[:, :],
                            start=True, stop=True), reads=[tev, SB.ps_last[1]])
                        tt1 = p.op("dve", lambda e, qp=qp: e.tensor_mul(
                            SB.t1b[:, :], SB.qraw[:, :],
                            cosb[:, qp * SUB:(qp + 1) * SUB]),
                            reads=[tev] + t1_last, drain=True)
                        tt2 = p.op("dve", lambda e, qp=qp: e.tensor_mul(
                            SB.t2b[:, :], SB.p_s[1][:, 0:SUB],
                            sinb[:, qp * SUB:(qp + 1) * SUB]),
                            reads=[tsw] + t2_last, drain=True)
                        SB.ps_last[1] = tt2
                        tq = p.op("dve", lambda e, mc=mc: e.tensor_add(
                            q_pan[:, mc * SUB:(mc + 1) * SUB], SB.t1b[:, :], SB.t2b[:, :]),
                            reads=[tt1, tt2] + q_consumed, drain=True)
                        qraw_last[:] = [tsw, tt1]
                        t1_last[:] = [tq]
                        t2_last[:] = [tq]
                        toks.append(tq)
                    return toks

                cfg0 = dict(
                    build_q=build_q0,
                    k_lhsT=lambda h, kt: kdup[64 * (h & 1):64 * (h & 1) + 64,
                                              kt * 128:(kt + 1) * 128],
                    vtm_slice=lambda h, kt: vtm0[:, kt * (HD + 1): (kt + 1) * (HD + 1)],
                    nmax_k=nmax_k, kv_tok=("dgp", p.counts["dgp"]),
                    wo=lambda mc: _chunk3(dr["l0_outT"][:, :])[:, :, mc * 128:(mc + 1) * 128],
                    outb=(outb_s, tob), resid_in=dr["xT"], resid_out=resid, fsc=fsc,
                    fsc_tok=("dgp", p.counts["dgp"]), rope_scratch=True)
                emit_scores(p, st, C, 0, cfg0)
                tap("resid", None, 0)
                ck("l0")

            with ExitStack() as l1:
                cq = _sb(l1, nc, "cq1", [128, 2 * T], BF16)
                with ExitStack() as ly:
                    ckv = _sb(ly, nc, "ckv1", [128, 2 * T], BF16)
                    with ExitStack() as lx:
                        xn1 = _sb(lx, nc, "xn1", [128, KC * T], BF16)
                        emit_ln(p, st, C, resid, xn1, 1)
                        tap("xn1", xn1, KC * T)
                        ck("ln1")
                        emit_lowrank_proj(p, st, C, xn1, dr["l1_Wdq"], cq, 256, "cq")
                        emit_lowrank_proj(p, st, C, xn1, dr["l1_Wdkv"], ckv, 256, "ckv")
                        tap("cq", cq, 2 * T)
                        tap("ckv", ckv, 2 * T)
                        ck("proj1")

                        with ExitStack() as lf:
                            c_all = _sb(lf, nc, "call1", [NE, T], BF16)
                            emit_routing(p, st, C, xn1, c_all, dr["l1_router"], None, 1)
                            tap("call1", c_all, T)
                            ck("rt1")
                            fc1 = dr["l1_fc1"]
                            fc2 = dr["l1_fc2"]
                            experts = []
                            for e_ in range(NE):
                                experts.append(dict(
                                    w1a=lambda ocp, e_=e_: _chunk3(fc1[e_])[:, :, ocp * 128:(ocp + 1) * 128],
                                    w1b=lambda ocp, e_=e_: _chunk3(fc1[e_])[:, :, D + ocp * 128: D + (ocp + 1) * 128],
                                    w2=lambda mc, e_=e_: _chunk3(fc2[e_])[:, :, mc * 128:(mc + 1) * 128],
                                    use_c=True, e_idx=e_, c_tok=None))
                            for sh in range(2):
                                experts.append(dict(
                                    w1a=lambda ocp, sh=sh: _chunk3(dr["l1_gate"][:, :])[:, :, sh * D + ocp * 128: sh * D + (ocp + 1) * 128],
                                    w1b=lambda ocp, sh=sh: _chunk3(dr["l1_up"][:, :])[:, :, sh * D + ocp * 128: sh * D + (ocp + 1) * 128],
                                    w2=lambda mc, sh=sh: dr["l1_down"][:, :].rearrange(
                                        "(kc p) m -> p kc m", p=128)[:, sh * KC:(sh + 1) * KC,
                                                                     mc * 128:(mc + 1) * 128],
                                    use_c=False, e_idx=0, c_tok=None))
                            emit_moe(p, st, C, xn1, c_all, fsc, 1, experts)
                    ck("ffn1")
                    kfm = _sb(l1, nc, "kfm1", [128, KC * T], BF16, side="right")
                    vtm1 = _sb(l1, nc, "vtm1", [128, NKT * NH * (HD + 1)], BF16, side="right")
                    nmax_k1 = _sb(l1, nc, "nmk1", [128, 4], F32, side="right")
                    lw1 = dict(Wuk=dr["l1_Wuk"], Wuv=dr["l1_Wuv"])
                    emit_kv_l1(p, st, C, lw1, ckv, kfm, vtm1, nmax_k1, 1)
                    tap("kfm", kfm, KC * T)
                    ck("kv1")

                wuq = _sb(l1, nc, "wuq1", [128, 2 * D], BF16, side="right")
                twq = p.dma("pool", lambda e: e.dma_start(
                    out=wuq[:, :].rearrange("p (kc m) -> p kc m", kc=2),
                    in_=dr["l1_Wuq"][:, :].rearrange("(kc p) m -> p kc m", p=128)))

                def build_q1(p, qp, q_pan, SB, q_consumed):
                    toks = []
                    for mc in range(KC):
                        tm = None
                        for kc in range(2):
                            tm = p.op("pe", lambda e, kc=kc, mc=mc, qp=qp: e.matmul(
                                SB.p_s[0][:, 0:SUB],
                                wuq[:, (kc * KC + mc) * 128:(kc * KC + mc + 1) * 128],
                                cq[:, kc * T + qp * SUB: kc * T + (qp + 1) * SUB],
                                start=(kc == 0), stop=(kc == 1)),
                                reads=[twq, SB.ps_last[0]])
                        tq = p.op("dve", lambda e, mc=mc: e.tensor_copy(
                            q_pan[:, mc * SUB:(mc + 1) * SUB], SB.p_s[0][:, 0:SUB]),
                            reads=[tm] + q_consumed)
                        SB.ps_last[0] = tq
                        toks.append(tq)
                    return toks

                cfg1 = dict(
                    build_q=build_q1,
                    k_lhsT=lambda h, kt: kfm[64 * (h & 1):64 * (h & 1) + 64,
                                             (h >> 1) * T + kt * 128:(h >> 1) * T + (kt + 1) * 128],
                    vtm_slice=lambda h, kt: vtm1[:, (kt * NH + h) * (HD + 1):
                                                 (kt * NH + h) * (HD + 1) + HD + 1],
                    nmax_k=nmax_k1, kv_tok=("dve", p.counts["dve"]),
                    wo=lambda mc: _chunk3(dr["l1_Wo"][:, :])[:, :, mc * 128:(mc + 1) * 128],
                    outb=None, resid_in=resid, resid_out=outT, fsc=fsc,
                    fsc_tok=("dgp", p.counts["dgp"]))
                emit_scores(p, st, C, 1, cfg1)

        p.barrier()
        p.emit()
    return nc


_CACHE = {}


def kernel(**inputs):
    consts = _host_consts()
    w = _prep_weights(inputs)
    x = np.asarray(inputs["L0_attn_x"], dtype=np.float32)
    key = "full"
    if key not in _CACHE:
        _CACHE[key] = build_program()
    nc = _CACHE[key]
    in_maps = []
    for b in range(8):
        m = dict(consts)
        m.update(w)
        m["xT"] = np.ascontiguousarray(x[b].T)
        in_maps.append(m)
    res = run_bass_kernel_spmd(nc, in_maps, core_ids=list(range(8)))
    out = np.empty((8, T, D), np.float32)
    for b in range(8):
        out[b] = res.results[b]["outT"].T
    return out


# version 08
# speedup vs baseline: 1.1030x; 1.0812x over previous
import numpy as np
import ml_dtypes
from contextlib import ExitStack

import concourse.bass as bass
import concourse.mybir as mybir
from concourse.bass_utils import run_bass_kernel_spmd

F32 = mybir.dt.float32
BF16 = mybir.dt.bfloat16
AF = mybir.ActivationFunctionType
ALU = mybir.AluOpType
AX = mybir.AxisListType
BF = ml_dtypes.bfloat16

T = 4096
D = 1024
KC = 8
SUB = 512
NSUB = T // SUB
NH = 16
HD = 64
NE = 8
NQP = T // SUB
NKT = T // 128
ROPE_THETA = 10000.0

DMA_CH = {"sp": "dsp", "pool": "dgp"}
ENG_SECT = {"pe": "tensor", "act": "scalar", "dve": "vector", "pool": "gpsimd", "sp": "sync"}
CHANNELS = ("pe", "act", "dve", "pool", "sp", "dsp", "dgp")


class Prog:

    def __init__(self, nc, stack):
        self.nc = nc
        self.sems = {ch: stack.enter_context(nc.semaphore(f"s_{ch}")) for ch in CHANNELS}
        self.counts = {ch: 0 for ch in CHANNELS}
        self.known = {e: {ch: 0 for ch in CHANNELS} for e in ENG_SECT}
        self.thunks = {e: [] for e in ENG_SECT}

    def _waits_for(self, eng, reads):
        merged = {}
        for tok in reads:
            if tok is None:
                continue
            ch, v = tok
            if v > merged.get(ch, 0):
                merged[ch] = v
        waits = []
        for ch, v in merged.items():
            if self.known[eng][ch] < v:
                waits.append((self.sems[ch], v))
                self.known[eng][ch] = v
        return waits

    def op(self, eng, fn, reads=(), drain=False):
        waits = self._waits_for(eng, reads)
        self.counts[eng] += 1
        tok = (eng, self.counts[eng])
        sem = self.sems[eng]

        def thunk(e, waits=waits, fn=fn, drain=drain, sem=sem):
            for s, v in waits:
                e.wait_ge(s, v)
            if drain:
                e.drain()
            fn(e).then_inc(sem, 1)

        self.thunks[eng].append(thunk)
        return tok

    def dma(self, eng, fn, reads=()):
        waits = self._waits_for(eng, reads)
        ch = DMA_CH[eng]
        self.counts[ch] += 16
        tok = (ch, self.counts[ch])
        sem = self.sems[ch]

        def thunk(e, waits=waits, fn=fn, sem=sem):
            for s, v in waits:
                e.wait_ge(s, v)
            fn(e).then_inc(sem, 16)

        self.thunks[eng].append(thunk)
        return tok

    def barrier(self):
        toks = [(ch, self.counts[ch]) for ch in CHANNELS]
        for eng in ENG_SECT:
            waits = self._waits_for(eng, toks)
            if waits:
                def thunk(e, waits=waits):
                    for s, v in waits:
                        e.wait_ge(s, v)
                self.thunks[eng].append(thunk)

    def emit(self):
        with self.nc.Block() as blk:
            for eng, sect in ENG_SECT.items():
                getattr(blk, sect)(self._make_runner(eng))

    def _make_runner(self, eng):
        thunks = self.thunks[eng]

        def run(e):
            for t in thunks:
                t(e)
        return run



def _host_consts():
    i = np.arange(HD // 2, dtype=np.float64)
    inv_freq = 1.0 / ROPE_THETA ** (2 * i / HD)
    pos = np.arange(T, dtype=np.float64)
    ang = pos[None, :] * inv_freq[:, None]
    cos32 = np.cos(ang)
    sin32 = np.sin(ang)
    cos64 = np.repeat(cos32, 2, axis=0)
    sin64s = np.empty((HD, T))
    sin64s[0::2] = -sin32
    sin64s[1::2] = sin32
    cos128 = np.concatenate([cos64, cos64], 0).astype(np.float32)
    sin128 = np.concatenate([sin64s, sin64s], 0).astype(np.float32)

    p64 = np.zeros((HD, HD), np.float32)
    for k in range(0, HD, 2):
        p64[k, k + 1] = 1.0
        p64[k + 1, k] = 1.0
    p128 = np.zeros((128, 128), np.float32)
    p128[:64, :64] = p64
    p128[64:, 64:] = p64

    cmask = np.zeros((128, 4 * SUB), np.float32)
    dk = np.arange(128)[:, None]
    qr = np.arange(SUB)[None, :]
    for j in range(4):
        cmask[:, j * SUB:(j + 1) * SUB] = (qr >= 128 * j + dk).astype(np.float32)

    ident = np.eye(128, dtype=np.float32)
    ohsel = np.zeros((NE, NE * 128), np.float32)
    for e in range(NE):
        ohsel[e, e * 128:(e + 1) * 128] = 1.0
    return dict(c_cos=cos128, c_sin=sin128, c_p128=p128, c_cmask=cmask,
                c_ident=ident, c_ohsel=ohsel)


def _prep_weights(inp):
    c = lambda a: np.ascontiguousarray(np.asarray(a, dtype=np.float32))
    w = {}
    w["l0_qkvT"] = c(np.asarray(inp["L0_attn_qkv_w"]).T)
    qb = np.asarray(inp["L0_attn_qkv_b"])
    qkvb = np.zeros((128, 10), np.float32)
    qkvb[:, 0:9] = np.asarray(qb).reshape(9, 128).T
    qkvb[0:64, 9] = qb[1088:1152]
    qkvb[64:128, 8] = 0.0
    w["l0_qkvb"] = c(qkvb)
    w["l0_outT"] = c(np.asarray(inp["L0_attn_out_w"]).T)
    w["l0_outb"] = c(np.asarray(inp["L0_attn_out_b"]).reshape(8, 128).T)
    w["l0_WgT"] = c(np.asarray(inp["L0_ffn_Wg"]).T)
    w["l0_bg"] = c(np.asarray(inp["L0_ffn_bg"]).reshape(8, 1))
    w["l0_w1T"] = c(np.asarray(inp["L0_ffn_w1"]).transpose(0, 2, 1))
    w["l0_w2T"] = c(np.asarray(inp["L0_ffn_w2"]).transpose(0, 2, 1))
    w["l1_Wdq"] = c(inp["L1_attn_W_dq"])
    w["l1_Wuq"] = c(inp["L1_attn_W_uq"])
    w["l1_Wdkv"] = c(inp["L1_attn_W_dkv"])
    w["l1_Wuk"] = c(inp["L1_attn_W_uk"])
    w["l1_Wuv"] = c(inp["L1_attn_W_uv"])
    w["l1_Wo"] = c(inp["L1_attn_W_o"])
    w["l1_router"] = c(inp["L1_ffn_w_router"])
    w["l1_fc1"] = c(inp["L1_ffn_fc1_w"])
    w["l1_fc2"] = c(inp["L1_ffn_fc2_w"])
    w["l1_gate"] = c(inp["L1_ffn_gate_w"])
    w["l1_up"] = c(inp["L1_ffn_up_w"])
    w["l1_down"] = c(inp["L1_ffn_down_w"])
    return w


class Bufs:
    pass


def _sb(stack, nc, name, shape, dtype, side=None):
    kw = {} if side is None else {"side": side}
    return stack.enter_context(nc.sbuf_tensor(name, shape, dtype, **kw))


def _ps(stack, nc, name, shape, dtype=F32):
    return stack.enter_context(nc.psum_tensor(name, shape, dtype))


def _chunk3(dram_ap):
    return dram_ap.rearrange("(kc p) m -> p kc m", p=128)



def emit_consts(p, st, dr):
    nc = p.nc
    C = Bufs()
    C.ident_f = _sb(st, nc, "ident_f", [128, 128], F32, side="right")
    C.ident_b = _sb(st, nc, "ident_b", [128, 128], BF16, side="right")
    C.cmask = _sb(st, nc, "cmask", [128, 4 * SUB], BF16, side="right")
    C.ohsel = _sb(st, nc, "ohsel", [NE, NE * 128], BF16, side="right")
    C.ones_b = _sb(st, nc, "ones_b", [128, 1], BF16, side="right")
    C.ones_fr = _sb(st, nc, "ones_fr", [128, 128], F32, side="right")
    C.ones_br = _sb(st, nc, "ones_br", [128, 128], BF16, side="right")
    C.fconst = _sb(st, nc, "fconst", [1, 2], F32, side="right")
    p.op("dve", lambda e: e.memset(C.fconst[0:1, 0:1], 1e-5))
    p.op("dve", lambda e: e.memset(C.fconst[0:1, 1:2], 1e-30))
    p.dma("sp", lambda e: e.dma_start(out=C.ident_f[:, :], in_=dr["c_ident"][:, :]))
    p.dma("pool", lambda e: e.dma_start(out=C.cmask[:, :], in_=dr["c_cmask"][:, :]))
    p.dma("pool", lambda e: e.dma_start(out=C.ident_b[:, :], in_=dr["c_ident"][:, :]))
    p.dma("pool", lambda e: e.dma_start(out=C.ohsel[:, :], in_=dr["c_ohsel"][:, :]))
    p.op("dve", lambda e: e.memset(C.ones_b[:, :], 1.0))
    p.op("dve", lambda e: e.memset(C.ones_fr[:, :], 1.0))
    p.op("dve", lambda e: e.memset(C.ones_br[:, :], 1.0))
    p.barrier()
    return C

def emit_ln(p, st, C, src_dram, xn, layer):
    nc = p.nc
    with ExitStack() as ph:
        x_f = [_sb(ph, nc, f"ln_xf{layer}_{i}", [128, KC * SUB], F32) for i in range(2)]
        x_b = [_sb(ph, nc, f"ln_xb{layer}_{i}", [128, KC * SUB], BF16) for i in range(2)]
        xsq = _sb(ph, nc, f"ln_xsq{layer}", [128, KC * SUB], BF16)
        tmp = _sb(ph, nc, f"ln_tmp{layer}", [128, KC * SUB], F32)
        rows = [_sb(ph, nc, f"ln_rows{layer}_{i}", [1, 6 * SUB], F32) for i in range(2)]
        p_sum = [_ps(ph, nc, f"lnps{layer}_{i}", [1, SUB]) for i in range(2)]
        p_sq = [_ps(ph, nc, f"lnpq{layer}_{i}", [1, SUB]) for i in range(2)]
        p_M = [_ps(ph, nc, f"lnpM{layer}_{i}", [128, SUB]) for i in range(2)]
        p_R = [_ps(ph, nc, f"lnpR{layer}_{i}", [128, SUB]) for i in range(2)]

        src3 = src_dram[:, :].rearrange("(kc p) t -> p kc t", p=128)
        xf_gate = [None, None]
        xb_gate = [[], []]
        xsq_gate = None
        tmp_gate = None
        rows_gate = [None, None]
        psum_gate = [None, None]
        psq_gate = [None, None]
        pM_gate = [None, None]
        pR_gate = [None, None]
        for s in range(NSUB):
            i = s % 2
            tl = p.dma("sp", lambda e, s=s, i=i: e.dma_start(
                out=x_f[i][:, :].rearrange("p (kc t) -> p kc t", kc=KC),
                in_=src3[:, :, s * SUB:(s + 1) * SUB]), reads=[xf_gate[i]])
            tlb = p.dma("pool", lambda e, s=s, i=i: e.dma_start(
                out=x_b[i][:, :].rearrange("p (kc t) -> p kc t", kc=KC),
                in_=src3[:, :, s * SUB:(s + 1) * SUB]), reads=xb_gate[i])
            tsq = []
            for kc in range(KC):
                tsq.append(p.op("act", lambda e, kc=kc, i=i: e.activation(
                    xsq[:, kc * SUB:(kc + 1) * SUB], x_b[i][:, kc * SUB:(kc + 1) * SUB],
                    AF.Square), reads=[tlb, xsq_gate]))
            tm = tv = None
            for kc in range(KC):
                tm = p.op("pe", lambda e, kc=kc, i=i: e.matmul(
                    p_sum[i][:, :], C.ones_b[:, :], x_b[i][:, kc * SUB:(kc + 1) * SUB],
                    start=(kc == 0), stop=(kc == KC - 1)), reads=[tlb, psum_gate[i]])
            for kc in range(KC):
                tv = p.op("pe", lambda e, kc=kc, i=i: e.matmul(
                    p_sq[i][:, :], C.ones_b[:, :], xsq[:, kc * SUB:(kc + 1) * SUB],
                    start=(kc == 0), stop=(kc == KC - 1)), reads=[tsq[kc], psq_gate[i]])
            xsq_gate = tv
            R = lambda j, i=i: rows[i][0:1, j * SUB:(j + 1) * SUB]
            t1 = p.op("dve", lambda e, R=R, i=i: e.tensor_scalar_mul(R(0), p_sum[i][:, :], 1.0 / D),
                      reads=[tm, rows_gate[i]])
            t2 = p.op("dve", lambda e, R=R, i=i: e.tensor_scalar_mul(R(1), p_sq[i][:, :], 1.0 / D),
                      reads=[tv], drain=True)
            psum_gate[i] = t1
            psq_gate[i] = t2
            t3 = p.op("dve", lambda e, R=R: e.tensor_mul(R(2), R(0), R(0)),
                      reads=[t1], drain=True)
            t4 = p.op("dve", lambda e, R=R: e.tensor_sub(R(3), R(1), R(2)),
                      reads=[t2, t3], drain=True)
            t5 = p.op("act", lambda e, R=R: e.activation(R(4), R(3), AF.Ln,
                                                         bias=C.fconst[0:1, 0:1]),
                      reads=[t4])
            t6 = p.op("act", lambda e, R=R: e.activation(R(5), R(4), AF.Exp, scale=-0.5),
                      reads=[t5], drain=True)
            tM = p.op("pe", lambda e, R=R, i=i: e.matmul(p_M[i][:, :], C.ones_fr[0:1, :],
                                                         R(0), start=True, stop=True),
                      reads=[t1, t3, pM_gate[i]])
            tR = p.op("pe", lambda e, R=R, i=i: e.matmul(p_R[i][:, :], C.ones_fr[0:1, :],
                                                         R(5), start=True, stop=True),
                      reads=[t6, pR_gate[i]])
            rows_gate[i] = tR
            tsub = []
            for kc in range(KC):
                tsub.append(p.op("dve", lambda e, kc=kc, i=i: e.tensor_sub(
                    tmp[:, kc * SUB:(kc + 1) * SUB], x_f[i][:, kc * SUB:(kc + 1) * SUB],
                    p_M[i][:, :]), reads=[tl, tM, tmp_gate]))
            tout = []
            for kc in range(KC):
                tout.append(p.op("dve", lambda e, kc=kc, s=s, i=i: e.tensor_mul(
                    xn[:, kc * T + s * SUB: kc * T + (s + 1) * SUB],
                    tmp[:, kc * SUB:(kc + 1) * SUB], p_R[i][:, :]), reads=[tsub[kc], tR]))
            tmp_gate = tout[-1]
            pM_gate[i] = tout[-1]
            pR_gate[i] = tout[-1]
            xf_gate[i] = tout[-1]
            xb_gate[i] = [tm, tsq[-1]]
        p.barrier()


def emit_routing(p, st, C, xn, c_all, router_dram, bias_dram, layer):
    nc = p.nc
    NQ = SUB // 128
    with ExitStack() as ph:
        wrt = _sb(ph, nc, f"rt_w{layer}", [128, KC * NE], BF16)
        lg = _sb(ph, nc, f"rt_lg{layer}", [NE, SUB], F32)
        bias_s = _sb(ph, nc, f"rt_b{layer}", [NE, 1], F32)
        ltok = [_sb(ph, nc, f"rt_lt{layer}_{q}", [128, NE], F32) for q in range(NQ)]
        sc = [_sb(ph, nc, f"rt_sc{layer}_{q}", [128, 8], F32) for q in range(NQ)]
        msk = [_sb(ph, nc, f"rt_msk{layer}_{q}", [128, NE], F32) for q in range(NQ)]
        pex = [_sb(ph, nc, f"rt_pex{layer}_{q}", [128, NE], F32) for q in range(NQ)]
        ctok = [_sb(ph, nc, f"rt_ct{layer}_{q}", [128, NE], F32) for q in range(NQ)]
        p_lg = [_ps(ph, nc, f"rt_plg{layer}_{i}", [NE, SUB]) for i in range(2)]
        p_t = [_ps(ph, nc, f"rt_pt{layer}_{i}", [128, NE]) for i in range(2)]
        p_ct = [_ps(ph, nc, f"rt_pct{layer}_{i}", [NE, 128]) for i in range(2)]

        tw = p.dma("pool", lambda e: e.dma_start(
            out=wrt[:, :].rearrange("p (kc e) -> p kc e", kc=KC),
            in_=_chunk3(router_dram[:, :])))
        if bias_dram is not None:
            tb = p.dma("sp", lambda e: e.dma_start(out=bias_s[:, :], in_=bias_dram[:, :]))
        else:
            tb = p.op("dve", lambda e: e.memset(bias_s[:, :], 0.0))

        plg_gate = [None, None]
        pt_gate = [None, None]
        pct_gate = [None, None]
        lg_gate = None
        for s in range(NSUB):
            il = s % 2
            tmm = None
            for kc in range(KC):
                tmm = p.op("pe", lambda e, kc=kc, s=s, il=il: e.matmul(
                    p_lg[il][:, :], wrt[:, kc * NE:(kc + 1) * NE],
                    xn[:, kc * T + s * SUB: kc * T + (s + 1) * SUB],
                    start=(kc == 0), stop=(kc == KC - 1)), reads=[tw, plg_gate[il]])
            te = p.op("dve", lambda e, il=il: e.tensor_scalar(lg[:, :], p_lg[il][:, :],
                                                              bias_s[0:NE, 0:1], None,
                                                              ALU.add),
                      reads=[tmm, tb, lg_gate])
            plg_gate[il] = te
            t0s = []
            for q in range(NQ):
                it = q % 2
                tt = p.op("pe", lambda e, q=q, it=it: e.transpose(
                    p_t[it][:, :], lg[:, q * 128:(q + 1) * 128], C.ident_f[0:NE, 0:NE]),
                    reads=[te, pt_gate[it]])
                t0 = p.op("dve", lambda e, q=q, it=it: e.tensor_copy(ltok[q][:, :],
                                                                     p_t[it][:, :]),
                          reads=[tt])
                pt_gate[it] = t0
                t0s.append(t0)
            lg_gate = t0s[-1]
            tA = [p.op("dve", lambda e, q=q: e.reduce_max(sc[q][:, 0:1], ltok[q][:, :],
                                                          axis=AX.X, negate=True),
                       reads=[t0s[q]]) for q in range(NQ)]
            tB = [p.op("dve", lambda e, q=q: e.tensor_scalar(msk[q][:, :], ltok[q][:, :],
                                                             sc[q][:, 0:1], None, ALU.add),
                       reads=[tA[q]]) for q in range(NQ)]
            tC = [p.op("dve", lambda e, q=q: e.tensor_scalar(msk[q][:, :], msk[q][:, :],
                                                             0.0, None, ALU.is_ge),
                       reads=[tB[q]]) for q in range(NQ)]
            tD = [p.op("dve", lambda e, q=q: e.scalar_tensor_tensor(
                msk[q][:, :], msk[q][:, :], -1e9, ltok[q][:, :], ALU.mult, ALU.add),
                reads=[tC[q]]) for q in range(NQ)]
            tE = [p.op("dve", lambda e, q=q: e.reduce_max(sc[q][:, 1:2], msk[q][:, :],
                                                          axis=AX.X), reads=[tD[q]])
                  for q in range(NQ)]
            tF = [p.op("act", lambda e, q=q: e.activation(pex[q][:, :], ltok[q][:, :],
                                                          AF.Exp, bias=sc[q][:, 0:1]),
                       reads=[tA[q]]) for q in range(NQ)]
            tG = [p.op("dve", lambda e, q=q: e.tensor_scalar(ctok[q][:, :], ltok[q][:, :],
                                                             sc[q][:, 1:2], None, ALU.is_ge),
                       reads=[tE[q]]) for q in range(NQ)]
            tH = [p.op("dve", lambda e, q=q: e.tensor_mul(ctok[q][:, :], ctok[q][:, :],
                                                          pex[q][:, :]),
                       reads=[tF[q], tG[q]]) for q in range(NQ)]
            tI = [p.op("dve", lambda e, q=q: e.reduce_sum(sc[q][:, 2:3], ctok[q][:, :],
                                                          axis=AX.X), reads=[tH[q]])
                  for q in range(NQ)]
            tJ = [p.op("dve", lambda e, q=q: e.reciprocal(sc[q][:, 3:4], sc[q][:, 2:3]),
                       reads=[tI[q]]) for q in range(NQ)]
            tK = [p.op("dve", lambda e, q=q: e.tensor_scalar(ctok[q][:, :], ctok[q][:, :],
                                                             sc[q][:, 3:4], None, ALU.mult),
                       reads=[tJ[q]]) for q in range(NQ)]
            for q in range(NQ):
                it = q % 2
                tc = p.op("pe", lambda e, q=q, it=it: e.transpose(
                    p_ct[it][:, :], ctok[q][:, :], C.ident_f[:, :]),
                    reads=[tK[q], pct_gate[it]])
                tcp = p.op("dve", lambda e, s=s, q=q, it=it: e.tensor_copy(
                    c_all[:, s * SUB + q * 128: s * SUB + (q + 1) * 128], p_ct[it][:, :]),
                    reads=[tc])
                pct_gate[it] = tcp
        p.barrier()


def emit_moe(p, st, C, xn, c_all, fsc, layer, experts):
    nc = p.nc
    with ExitStack() as ph:
        a_buf = _sb(ph, nc, f"moe_a{layer}", [128, KC * T], BF16)
        w1s = [_sb(ph, nc, f"moe_w1{layer}_{i}", [128, 2 * KC * 128], BF16) for i in range(2)]
        w2s = [_sb(ph, nc, f"moe_w2{layer}_{i}", [128, KC * 128], BF16) for i in range(2)]
        silu_t = [_sb(ph, nc, f"moe_st{layer}_{i}", [128, SUB], BF16) for i in range(2)]
        c_sb = _sb(ph, nc, f"moe_cb{layer}", [128, T], BF16)
        o_t = [_sb(ph, nc, f"moe_ot{layer}_{i}", [128, 2 * SUB], F32) for i in range(2)]
        p_h = [_ps(ph, nc, f"moe_ph{layer}_{i}", [128, SUB]) for i in range(4)]
        p_o = [_ps(ph, nc, f"moe_po{layer}_{i}", [128, SUB]) for i in range(2)]
        p_c = _ps(ph, nc, f"moe_pc{layer}", [128, SUB])

        fsc3 = fsc[:, :].rearrange("(mc p) t -> p mc t", p=128)
        region_tok = {}
        w1_last = [None, None]
        w2_last = [None, None]
        ph_last = [None] * 4
        po_last = [None] * 2
        ot_last = [None] * 3
        cs_last = None
        si_last = [None, None]
        pc_last = None

        for ei, ex in enumerate(experts):
            for ocp in range(KC):
                wb = w1s[ocp % 2]
                twa = p.dma("pool", lambda e, ex=ex, ocp=ocp, wb=wb: e.dma_start(
                    out=wb[:, 0:KC * 128].rearrange("p (kc m) -> p kc m", kc=KC),
                    in_=ex["w1a"](ocp)), reads=[w1_last[ocp % 2]])
                twb = p.dma("pool", lambda e, ex=ex, ocp=ocp, wb=wb: e.dma_start(
                    out=wb[:, KC * 128:2 * KC * 128].rearrange("p (kc m) -> p kc m", kc=KC),
                    in_=ex["w1b"](ocp)), reads=[w1_last[ocp % 2]])
                last_a = None
                for s in range(NSUB):
                    ia, ib = (2 * s) % 4, (2 * s + 1) % 4
                    pa, pb = p_h[ia], p_h[ib]
                    tma = tmb = None
                    for kc in range(KC):
                        tma = p.op("pe", lambda e, kc=kc, s=s, wb=wb, pa=pa: e.matmul(
                            pa[:, :], wb[:, kc * 128:(kc + 1) * 128],
                            xn[:, kc * T + s * SUB: kc * T + (s + 1) * SUB],
                            start=(kc == 0), stop=(kc == KC - 1)),
                            reads=[twa, ph_last[ia]])
                    for kc in range(KC):
                        tmb = p.op("pe", lambda e, kc=kc, s=s, wb=wb, pb=pb: e.matmul(
                            pb[:, :], wb[:, (KC + kc) * 128:(KC + kc + 1) * 128],
                            xn[:, kc * T + s * SUB: kc * T + (s + 1) * SUB],
                            start=(kc == 0), stop=(kc == KC - 1)),
                            reads=[twb, ph_last[ib]])
                    stl = silu_t[s % 2]
                    ts = p.op("act", lambda e, stl=stl, pa=pa: e.activation(
                        stl[:, :], pa[:, :], AF.Silu), reads=[tma, si_last[s % 2]])
                    ta = p.op("dve", lambda e, ocp=ocp, s=s, stl=stl, pb=pb: e.tensor_mul(
                        a_buf[:, ocp * T + s * SUB: ocp * T + (s + 1) * SUB],
                        stl[:, :], pb[:, :]), reads=[ts, tmb])
                    si_last[s % 2] = ta
                    ph_last[ia] = ts
                    ph_last[ib] = ta
                    last_a = ta
                w1_last[ocp % 2] = last_a
            if ex["use_c"]:
                for s in range(NSUB):
                    tcm = p.op("pe", lambda e, ex=ex, s=s: e.matmul(
                        p_c[:, :], C.ohsel[:, ex["e_idx"] * 128:(ex["e_idx"] + 1) * 128],
                        c_all[:, s * SUB:(s + 1) * SUB], start=True, stop=True),
                        reads=[pc_last, ex["c_tok"]])
                    tce = p.op("dve", lambda e, s=s: e.tensor_copy(
                        c_sb[:, s * SUB:(s + 1) * SUB], p_c[:, :]),
                        reads=[tcm, cs_last])
                    pc_last = tce
                cs_tok = pc_last
            else:
                cs_tok = None
            for mc in range(KC):
                tw2 = p.dma("pool", lambda e, ex=ex, mc=mc: e.dma_start(
                    out=w2s[mc % 2][:, :].rearrange("p (kc m) -> p kc m", kc=KC),
                    in_=ex["w2"](mc)), reads=[w2_last[mc % 2]])
                last_ev = None
                for s in range(NSUB):
                    po = p_o[s % 2]
                    tmo = None
                    for kc in range(KC):
                        tmo = p.op("pe", lambda e, kc=kc, s=s, mc=mc, po=po: e.matmul(
                            po[:, :], w2s[mc % 2][:, kc * 128:(kc + 1) * 128],
                            a_buf[:, kc * T + s * SUB: kc * T + (s + 1) * SUB],
                            start=(kc == 0), stop=(kc == KC - 1)),
                            reads=[tw2, po_last[s % 2]])
                    i3 = (s // 2) % 2
                    half = s % 2
                    ot = o_t[i3]
                    if ex["use_c"]:
                        tev = p.op("dve", lambda e, ot=ot, po=po, s=s, half=half: e.tensor_mul(
                            ot[:, half * SUB:(half + 1) * SUB], po[:, :],
                            c_sb[:, s * SUB:(s + 1) * SUB]),
                            reads=[tmo, cs_tok, ot_last[i3]])
                    else:
                        tev = p.op("dve", lambda e, ot=ot, po=po, half=half: e.tensor_copy(
                            ot[:, half * SUB:(half + 1) * SUB], po[:, :]),
                            reads=[tmo, ot_last[i3]])
                    po_last[s % 2] = tev
                    last_ev = tev
                    if half == 1:
                        accum = ALU.add if ei > 0 else ALU.bypass
                        td = p.dma("pool", lambda e, mc=mc, s=s, ot=ot, accum=accum: e.dma_start(
                            out=fsc3[:, mc, (s - 1) * SUB:(s + 1) * SUB], in_=ot[:, :],
                            accum_op=accum),
                            reads=[tev, po_last[0], region_tok.get((mc, s // 2))])
                        region_tok[(mc, s // 2)] = td
                        ot_last[i3] = td
                w2_last[mc % 2] = last_ev
            if ex["use_c"]:
                cs_last = w2_last[0]
        p.barrier()


def emit_lowrank_proj(p, st, C, xn, w_dram, out_fm, ncol, tag):
    nc = p.nc
    nmc = ncol // 128
    with ExitStack() as ph:
        wt = _sb(ph, nc, f"lr_w_{tag}", [128, KC * ncol], BF16)
        p_x = [_ps(ph, nc, f"lr_p_{tag}{i}", [128, SUB]) for i in range(2)]
        tw = p.dma("pool", lambda e: e.dma_start(
            out=wt[:, :].rearrange("p (kc m) -> p kc m", kc=KC),
            in_=_chunk3(w_dram[:, :])))
        last = [None, None]
        for s in range(NSUB):
            for mc in range(nmc):
                i = (s * nmc + mc) % 2
                px = p_x[i]
                tm = None
                for kc in range(KC):
                    tm = p.op("pe", lambda e, kc=kc, mc=mc, s=s, px=px: e.matmul(
                        px[:, :], wt[:, (kc * nmc + mc) * 128:(kc * nmc + mc + 1) * 128],
                        xn[:, kc * T + s * SUB: kc * T + (s + 1) * SUB],
                        start=(kc == 0), stop=(kc == KC - 1)),
                        reads=[tw, last[i]])
                te = p.op("dve", lambda e, mc=mc, s=s, px=px: e.tensor_copy(
                    out_fm[:, mc * T + s * SUB: mc * T + (s + 1) * SUB], px[:, :]),
                    reads=[tm])
                last[i] = te
        p.barrier()


def _norm_update(p, C, src_ap, nparts, p_n, nrow, nmax, prev_toks, reads, pn_gate):
    gates = pn_gate if isinstance(pn_gate, list) else [pn_gate]
    tn = p.op("pe", lambda e: e.matmul(p_n[:, :], C.ones_b[0:nparts, :], src_ap,
                                       start=True, stop=True),
              reads=list(reads) + gates)
    t2 = p.op("dve", lambda e: e.tensor_copy(nrow[:, :], p_n[:, :]), reads=[tn])
    t3 = p.op("dve", lambda e: e.reduce_max(nmax[0:1, 1:2], nrow[:, :], axis=AX.X),
              reads=[t2], drain=True)
    t4 = p.op("dve", lambda e: e.tensor_max(nmax[0:1, 0:1], nmax[0:1, 0:1],
                                            nmax[0:1, 1:2]),
              reads=[t3] + list(prev_toks), drain=True)
    return tn, t2, t4


def emit_kv_l0(p, st, C, lw, xn, kdup, vtm, nmax_k, layer):
    nc = p.nc
    with ExitStack() as ph:
        wkt = _sb(ph, nc, "kv0_wk", [128, KC * HD], BF16)
        wvt = _sb(ph, nc, "kv0_wv", [128, KC * HD], BF16)
        kraw = _sb(ph, nc, "kv0_kraw", [128, SUB], F32)
        t1b = _sb(ph, nc, "kv0_t1", [128, SUB], F32)
        t2b = _sb(ph, nc, "kv0_t2", [128, SUB], F32)
        v_b = _sb(ph, nc, "kv0_vb", [128, SUB], BF16)
        ksq = _sb(ph, nc, "kv0_ksq", [128, SUB], BF16)
        nrow = _sb(ph, nc, "kv0_nrow", [1, SUB], F32)
        p_k = _ps(ph, nc, "kv0_pk", [128, SUB])
        p_v = _ps(ph, nc, "kv0_pv", [128, SUB])
        p_sw = _ps(ph, nc, "kv0_psw", [128, SUB])
        p_t = [_ps(ph, nc, f"kv0_pt{i}", [128, 128], BF16) for i in range(2)]
        p_n = _ps(ph, nc, "kv0_pn", [1, SUB])

        qkvT3 = _chunk3(lw["qkvT"][:, :])
        tk = p.dma("pool", lambda e: e.dma_start(
            out=wkt[:, :].rearrange("p (kc m) -> p kc m", kc=KC),
            in_=qkvT3[:, :, 1024:1088]))
        tv = p.dma("pool", lambda e: e.dma_start(
            out=wvt[:, :].rearrange("p (kc m) -> p kc m", kc=KC),
            in_=qkvT3[:, :, 1088:1152]))
        tmx = p.op("dve", lambda e: e.memset(nmax_k[:, :], 0.0))
        vtm_tok = p.op("dve", lambda e: e.memset(vtm[:, :], 1.0))
        pt_gate = [None, None]
        kraw_tok = []
        t1_tok = t2_tok = krot_tok = None
        v_tok = []
        ksq_tok = None
        pn_gate = None
        for s in range(NSUB):
            tmk = tmv = None
            for kc in range(KC):
                tmk = p.op("pe", lambda e, kc=kc, s=s: e.matmul(
                    p_k[0:HD, :], wkt[:, kc * HD:(kc + 1) * HD],
                    xn[:, kc * T + s * SUB: kc * T + (s + 1) * SUB],
                    start=(kc == 0), stop=(kc == KC - 1)), reads=[tk] + kraw_tok)
            for kc in range(KC):
                tmv = p.op("pe", lambda e, kc=kc, s=s: e.matmul(
                    p_v[0:HD, :], wvt[:, kc * HD:(kc + 1) * HD],
                    xn[:, kc * T + s * SUB: kc * T + (s + 1) * SUB],
                    start=(kc == 0), stop=(kc == KC - 1)), reads=[tv] + v_tok)
            tke = p.op("dve", lambda e: e.tensor_scalar(
                kraw[0:HD, :], p_k[0:HD, :], lw["qkvb_s"][0:HD, 8:9], None, ALU.add),
                reads=[tmk, lw["qkvb_tok"]])
            tsw = p.op("pe", lambda e: e.matmul(
                p_sw[0:HD, :], lw["p128f"][0:HD, 0:HD], kraw[0:HD, :],
                start=True, stop=True), reads=[tke, t2_tok])
            tt1 = p.op("dve", lambda e, s=s: e.tensor_mul(
                t1b[0:HD, :], kraw[0:HD, :], lw["cos"][0:HD, s * SUB:(s + 1) * SUB]),
                reads=[tke, krot_tok], drain=True)
            tt2 = p.op("dve", lambda e, s=s: e.tensor_mul(
                t2b[0:HD, :], p_sw[0:HD, :], lw["sin"][0:HD, s * SUB:(s + 1) * SUB]),
                reads=[tsw, krot_tok], drain=True)
            tkr = p.op("dve", lambda e, s=s: e.tensor_add(
                kdup[0:HD, s * SUB:(s + 1) * SUB], t1b[0:HD, :], t2b[0:HD, :]),
                reads=[tt1, tt2], drain=True)
            kraw_tok = [tsw, tt1]
            t2_tok = tt2
            krot_tok = tkr
            tq = p.op("pool", lambda e, s=s: e.tensor_mul(
                ksq[0:HD, :], kdup[0:HD, s * SUB:(s + 1) * SUB],
                kdup[0:HD, s * SUB:(s + 1) * SUB]), reads=[tkr, ksq_tok])
            tn, pn_gate, tmx = _norm_update(p, C, ksq[0:HD, :], HD, p_n, nrow, nmax_k,
                                            [tmx], [tq], pn_gate)
            ksq_tok = tn
            tve = p.op("dve", lambda e: e.tensor_scalar(
                v_b[0:HD, :], p_v[0:HD, :], lw["qkvb_s"][0:HD, 9:10], None, ALU.add),
                reads=[tmv, lw["qkvb_tok"]])
            v_tok = [tve]
            for q in range(SUB // 128):
                kt = s * (SUB // 128) + q
                it = q % 2
                pt = p_t[it]
                ttr = p.op("pe", lambda e, q=q, pt=pt: e.transpose(
                    pt[:, 0:HD], v_b[0:HD, q * 128:(q + 1) * 128], C.ident_b[0:HD, 0:HD]),
                    reads=[tve, pt_gate[it]])
                tcp = p.op("dve", lambda e, kt=kt, pt=pt: e.tensor_copy(
                    vtm[:, kt * (HD + 1): kt * (HD + 1) + HD], pt[:, 0:HD]),
                    reads=[ttr, vtm_tok])
                pt_gate[it] = tcp
                v_tok.append(ttr)
        p.dma("pool", lambda e: e.dma_start(out=kdup[HD:128, :], in_=kdup[0:HD, :]),
              reads=[krot_tok])
        p.barrier()


def emit_kv_l1(p, st, C, lw, ckv, kfm, vtm, nmax_k, layer):
    nc = p.nc
    with ExitStack() as ph:
        wuk = _sb(ph, nc, "kv1_wuk", [128, 2 * D], BF16)
        wuv = _sb(ph, nc, "kv1_wuv", [128, 2 * D], BF16)
        v_bs = [_sb(ph, nc, f"kv1_vb{i}", [128, SUB], BF16) for i in range(2)]
        ksq = _sb(ph, nc, "kv1_ksq", [128, SUB], BF16)
        nrow = _sb(ph, nc, "kv1_nrow", [1, SUB], F32)
        p_k = [_ps(ph, nc, f"kv1_pk{i}", [128, SUB]) for i in range(2)]
        p_v = _ps(ph, nc, "kv1_pv", [128, SUB])
        p_t = [_ps(ph, nc, f"kv1_pt{i}", [128, 128], BF16) for i in range(2)]
        p_n = _ps(ph, nc, "kv1_pn", [1, SUB])

        tk = p.dma("pool", lambda e: e.dma_start(
            out=wuk[:, :].rearrange("p (kc m) -> p kc m", kc=2),
            in_=lw["Wuk"][:, :].rearrange("(kc p) m -> p kc m", p=128)))
        tv = p.dma("pool", lambda e: e.dma_start(
            out=wuv[:, :].rearrange("p (kc m) -> p kc m", kc=2),
            in_=lw["Wuv"][:, :].rearrange("(kc p) m -> p kc m", p=128)))
        tmx = p.op("dve", lambda e: e.memset(nmax_k[:, :], 0.0))
        vtm_tok = p.op("dve", lambda e: e.memset(vtm[:, :], 1.0))
        pt_gate = [None, None]
        vb_gate = [[], []]
        klast = [None, None]
        ksq_tok = None
        pn_gate = None
        v_tok = []
        for s in range(NSUB):
            for mc in range(KC):
                i = mc % 2
                pk = p_k[i]
                tm = None
                for kc in range(2):
                    tm = p.op("pe", lambda e, kc=kc, mc=mc, s=s, pk=pk: e.matmul(
                        pk[:, :], wuk[:, (kc * KC + mc) * 128:(kc * KC + mc + 1) * 128],
                        ckv[:, kc * T + s * SUB: kc * T + (s + 1) * SUB],
                        start=(kc == 0), stop=(kc == 1)), reads=[tk, klast[i]])
                te = p.op("dve", lambda e, mc=mc, s=s, pk=pk: e.tensor_copy(
                    kfm[:, mc * T + s * SUB: mc * T + (s + 1) * SUB], pk[:, :]), reads=[tm])
                klast[i] = te
                tq = p.op("pool", lambda e, mc=mc, s=s: e.tensor_mul(
                    ksq[:, :], kfm[:, mc * T + s * SUB: mc * T + (s + 1) * SUB],
                    kfm[:, mc * T + s * SUB: mc * T + (s + 1) * SUB]),
                    reads=[te, ksq_tok])
                tn, pn_gate, tmx = _norm_update(p, C, ksq[:, :], 128, p_n, nrow, nmax_k,
                                                [tmx], [tq], pn_gate)
                ksq_tok = tn
            for h in range(NH):
                v_b = v_bs[h % 2]
                tm = None
                for kc in range(2):
                    tm = p.op("pe", lambda e, kc=kc, h=h, s=s: e.matmul(
                        p_v[0:HD, :],
                        wuv[:, kc * KC * 128 + h * HD: kc * KC * 128 + (h + 1) * HD],
                        ckv[:, kc * T + s * SUB: kc * T + (s + 1) * SUB],
                        start=(kc == 0), stop=(kc == 1)), reads=[tv] + v_tok)
                tve = p.op("dve", lambda e, v_b=v_b: e.tensor_copy(v_b[0:HD, :],
                                                                   p_v[0:HD, :]),
                           reads=[tm] + vb_gate[h % 2])
                v_tok = [tve]
                vb_gate[h % 2] = []
                for q in range(SUB // 128):
                    kt = s * (SUB // 128) + q
                    slot = (kt * NH + h) * (HD + 1)
                    it = q % 2
                    pt = p_t[it]
                    ttr = p.op("pe", lambda e, q=q, pt=pt, v_b=v_b: e.transpose(
                        pt[:, 0:HD], v_b[0:HD, q * 128:(q + 1) * 128],
                        C.ident_b[0:HD, 0:HD]), reads=[tve, pt_gate[it]])
                    vb_gate[h % 2].append(ttr)
                    tcp = p.op("dve", lambda e, slot=slot, pt=pt: e.tensor_copy(
                        vtm[:, slot: slot + HD], pt[:, 0:HD]), reads=[ttr, vtm_tok])
                    pt_gate[it] = tcp
                    v_tok.append(ttr)
        p.barrier()

def emit_scores(p, st, C, layer, cfg):
    nc = p.nc
    with ExitStack() as ph:
        SB = Bufs()
        q_pan = _sb(ph, nc, f"sc_q{layer}", [128, KC * SUB], BF16)
        o_pan = _sb(ph, nc, f"sc_o{layer}", [128, KC * SUB], BF16)
        o_tmp = _sb(ph, nc, f"sc_ot{layer}", [128, SUB], BF16)
        pt_b = [_sb(ph, nc, f"sc_pt{layer}_{i}", [128, 2 * SUB], BF16) for i in range(2)]
        rbc = _sb(ph, nc, f"sc_rbc{layer}", [128, SUB], BF16)
        recb = _sb(ph, nc, f"sc_rec{layer}", [128, SUB], F32)
        recb2 = _sb(ph, nc, f"sc_rc2{layer}", [128, SUB], BF16)
        qsq = _sb(ph, nc, f"sc_qsq{layer}", [128, SUB], BF16)
        nrow = _sb(ph, nc, f"sc_nrow{layer}", [1, SUB], F32)
        nmax_q = _sb(ph, nc, f"sc_nq{layer}", [128, 4], F32)
        scs = _sb(ph, nc, f"sc_scs{layer}", [1, 8], F32)
        mkneg = _sb(ph, nc, f"sc_mk{layer}", [128, 1], F32)
        wo_t = [_sb(ph, nc, f"sc_wo{layer}_{i}", [128, KC * 128], BF16) for i in range(2)]
        if cfg["outb"] is not None:
            y1_t = [_sb(ph, nc, f"sc_y1{layer}_{i}", [128, SUB], F32) for i in range(2)]
        xf_t = [_sb(ph, nc, f"sc_xf{layer}_{i}", [128, SUB], F32) for i in range(2)]
        ff_t = [_sb(ph, nc, f"sc_ff{layer}_{i}", [128, SUB], F32) for i in range(2)]
        s1_t = [_sb(ph, nc, f"sc_s1{layer}_{i}", [128, SUB], F32) for i in range(2)]
        if cfg.get("rope_scratch"):
            SB.qraw = _sb(ph, nc, f"sc_qraw{layer}", [128, SUB], F32)
            SB.t1b = _sb(ph, nc, f"sc_t1{layer}", [128, SUB], F32)
            SB.t2b = _sb(ph, nc, f"sc_t2{layer}", [128, SUB], F32)
            SB.wq = [_sb(ph, nc, f"sc_wq{layer}_{i}", [128, KC * 128], BF16)
                     for i in range(2)]
        p_s = [_ps(ph, nc, f"sc_ps{layer}_{i}", [128, 2 * SUB]) for i in range(2)]
        p_oa = [_ps(ph, nc, f"sc_poa{layer}_{i}", [128, SUB]) for i in range(2)]
        p_misc = _ps(ph, nc, f"sc_pm{layer}", [128, SUB])
        p_y = _ps(ph, nc, f"sc_py{layer}", [128, SUB])
        p_n = p_y[0:1, 0:SUB]
        SB.p_s = p_s
        SB.ps_last = [None, None]

        fsc3 = cfg["fsc"][:, :].rearrange("(mc p) t -> p mc t", p=128)
        xin3 = cfg["resid_in"][:, :].rearrange("(mc p) t -> p mc t", p=128)
        xout3 = cfg["resid_out"][:, :].rearrange("(mc p) t -> p mc t", p=128)

        pt_last = [None, None]
        py_last = None
        wo_last = [None, None]
        xf_last = [None, None]
        ff_last = [None, None]
        s1_last = [None, None]
        y1_last = [None, None]
        oa_last = [None, None]
        pr_gate = None
        rc2_gate = None
        scr_gate = None
        otmp_last = None
        q_consumed = []
        o_consumed = []
        store_toks = []
        qsq_tok = None
        pn_gate = None
        kv_tok = cfg["kv_tok"]
        pending_epi = []
        tmax = p.op("dve", lambda e: e.memset(nmax_q[:, :], 0.0))
        norm_jobs = []

        def emit_norm_chunk(mc, qtok):
            nonlocal pn_gate, tmax, qsq_tok
            tq = p.op("dve", lambda e, mc=mc: e.tensor_mul(
                qsq[:, :], q_pan[:, mc * SUB:(mc + 1) * SUB],
                q_pan[:, mc * SUB:(mc + 1) * SUB]), reads=[qtok, qsq_tok])
            tn, pn_gate, tmax = _norm_update(p, C, qsq[:, :], 128, p_n, nrow, nmax_q,
                                             [tmax], [tq], [pn_gate, py_last])
            qsq_tok = tn
            return tq

        for qp in range(NQP):
            qtoks = cfg["build_q"](p, qp, q_pan, SB, q_consumed)
            q_consumed = []
            norm_jobs = [(mc, qtoks[mc]) for mc in range(KC)]
            if qp == 0:
                for mc, qtok in norm_jobs:
                    q_consumed.append(emit_norm_chunk(mc, qtok))
                norm_jobs = []
            tb1 = p.op("dve", lambda e: e.tensor_mul(scs[0:1, 0:1], nmax_q[0:1, 0:1],
                                                     cfg["nmax_k"][0:1, 0:1]),
                       reads=[tmax, kv_tok], drain=True)
            tb2a = p.op("act", lambda e: e.activation(scs[0:1, 1:2], scs[0:1, 0:1],
                                                      AF.Ln, bias=C.fconst[0:1, 1:2]),
                        reads=[tb1])
            tb2 = p.op("act", lambda e: e.activation(scs[0:1, 4:5], scs[0:1, 1:2],
                                                     AF.Exp, scale=0.5),
                       reads=[tb2a], drain=True)
            tb3 = p.op("dve", lambda e: e.tensor_scalar_mul(scs[0:1, 2:3], scs[0:1, 4:5],
                                                            -0.125), reads=[tb2], drain=True)
            tb4 = p.op("pe", lambda e: e.matmul(p_misc[:, 0:1], C.ones_fr[0:1, :],
                                                scs[0:1, 2:3], start=True, stop=True),
                       reads=[tb3, pn_gate])
            tbound = p.op("dve", lambda e: e.tensor_copy(mkneg[:, 0:1], p_misc[:, 0:1]),
                          reads=[tb4], drain=True)
            pr_gate = tbound
            nkt = 4 * qp + 4
            o_gate = list(o_consumed)
            o_consumed = []
            odd_dma_toks = []
            for c in range(KC):
                hA, hB = 2 * c, 2 * c + 1
                rhs_qA = q_pan[0:HD, c * SUB:(c + 1) * SUB]
                rhs_qB = q_pan[HD:128, c * SUB:(c + 1) * SUB]
                tS_last = None

                def emit_S(kt):
                    i2 = kt % 2
                    ps = p_s[i2]
                    gate = [kv_tok, qtoks[c], SB.ps_last[i2]]
                    tSA = p.op("pe", lambda e, hA=hA, kt=kt, ps=ps, rhs_qA=rhs_qA: e.matmul(
                        ps[:, 0:SUB], cfg["k_lhsT"](hA, kt), rhs_qA,
                        start=True, stop=True, tile_position=(0, 0)), reads=gate)
                    tSB = p.op("pe", lambda e, hB=hB, kt=kt, ps=ps, rhs_qB=rhs_qB: e.matmul(
                        ps[:, SUB:2 * SUB], cfg["k_lhsT"](hB, kt), rhs_qB,
                        start=True, stop=True, tile_position=(64, 0)), reads=gate)
                    return tSA, tSB

                def emit_exp(kt, tSA, tSB):
                    i2 = kt % 2
                    ps = p_s[i2]
                    texp = p.op("act", lambda e, ps=ps, i2=i2: e.activation(
                        pt_b[i2][:, :], ps[:, :], AF.Exp, bias=mkneg[:, 0:1], scale=0.125),
                        reads=[tSA, tSB, tbound, pt_last[i2]])
                    SB.ps_last[i2] = texp
                    tm0 = tm1 = texp
                    if kt >= nkt - 4:
                        j = kt - (nkt - 4)
                        tm0 = p.op("pool", lambda e, i2=i2, j=j: e.tensor_mul(
                            pt_b[i2][:, 0:SUB], pt_b[i2][:, 0:SUB],
                            C.cmask[:, j * SUB:(j + 1) * SUB]), reads=[texp])
                        tm1 = p.op("pool", lambda e, i2=i2, j=j: e.tensor_mul(
                            pt_b[i2][:, SUB:2 * SUB], pt_b[i2][:, SUB:2 * SUB],
                            C.cmask[:, j * SUB:(j + 1) * SUB]), reads=[texp])
                    return tm0, tm1

                def emit_PV(kt, tm0, tm1):
                    i2 = kt % 2
                    tPVA = p.op("pe", lambda e, hA=hA, kt=kt, i2=i2: e.matmul(
                        p_oa[0][0:HD + 1, :], cfg["vtm_slice"](hA, kt), pt_b[i2][:, 0:SUB],
                        start=(kt == 0), stop=(kt == nkt - 1)),
                        reads=[tm0, tm1, kv_tok] + ([oa_last[0]] if kt == 0 else []))
                    tPVB = p.op("pe", lambda e, hB=hB, kt=kt, i2=i2: e.matmul(
                        p_oa[1][0:HD + 1, :], cfg["vtm_slice"](hB, kt), pt_b[i2][:, SUB:2 * SUB],
                        start=(kt == 0), stop=(kt == nkt - 1)),
                        reads=([oa_last[1]] if kt == 0 else []))
                    pt_last[i2] = tPVB
                    return tPVB

                sx = emit_S(0)
                while pending_epi:
                    pending_epi.pop(0)()
                ex_prev = emit_exp(0, *sx)
                tS_last = sx[1]
                for kt in range(1, nkt):
                    sx = emit_S(kt)
                    tS_last = sx[1]
                    emit_PV(kt - 1, *ex_prev)
                    ex_prev = emit_exp(kt, *sx)
                emit_PV(nkt - 1, *ex_prev)
                q_consumed.append(tS_last)
                def make_epilogue(h, hc, hoa, tPVl=pt_last[(nkt - 1) % 2]):
                    poa = p_oa[hoa]
                    def run():
                        nonlocal scr_gate, rc2_gate, pr_gate, otmp_last
                        tln = p.op("act", lambda e: e.activation(
                            recb[HD:HD + 1, :], poa[HD:HD + 1, :], AF.Ln),
                            reads=[tPVl, scr_gate], drain=True)
                        scr_gate = tln
                        trc2 = p.op("act", lambda e: e.activation(
                            recb2[HD:HD + 1, :], recb[HD:HD + 1, :], AF.Exp, scale=-1.0),
                            reads=[tln, rc2_gate], drain=True)
                        tbc = p.op("pe", lambda e: e.matmul(
                            p_misc[:, :], C.ones_br[HD:HD + 1, 0:128],
                            recb2[HD:HD + 1, :], start=True, stop=True),
                            reads=[trc2, pr_gate])
                        rc2_gate = tbc
                        trbc = p.op("dve", lambda e: e.tensor_copy(rbc[:, :], p_misc[:, :]),
                                    reads=[tbc])
                        pr_gate = trbc
                        if h % 2 == 0:
                            dest = o_pan[0:HD, hc * SUB:(hc + 1) * SUB]
                            tnrm = p.op("dve", lambda e: e.tensor_mul(
                                dest, poa[0:HD, :], rbc[0:HD, :]),
                                reads=[trbc] + o_gate, drain=True)
                        else:
                            tnrm = p.op("dve", lambda e: e.tensor_mul(
                                o_tmp[0:HD, :], poa[0:HD, :], rbc[0:HD, :]),
                                reads=[trbc, otmp_last], drain=True)
                            todd = p.dma("pool", lambda e: e.dma_start(
                                out=o_pan[HD:128, hc * SUB:(hc + 1) * SUB],
                                in_=o_tmp[0:HD, :]), reads=[tnrm] + o_gate)
                            otmp_last = todd
                            odd_dma_toks.append(todd)
                        oa_last[hoa] = tnrm
                    return run
                pending_epi.append(make_epilogue(hA, c, 0))
                pending_epi.append(make_epilogue(hB, c, 1))
                if norm_jobs and c < KC:
                    mc_, qtok_ = norm_jobs[c]
                    q_consumed.append(emit_norm_chunk(mc_, qtok_))
                    if c == KC - 1:
                        norm_jobs = []

            while pending_epi:
                pending_epi.pop(0)()
            opan_toks = [oa_last[0], oa_last[1]] + odd_dma_toks[-1:]
            for mc in range(KC):
                i = mc % 2
                tw = p.dma("pool", lambda e, mc=mc, i=i: e.dma_start(
                    out=wo_t[i][:, :].rearrange("p (kc m) -> p kc m", kc=KC),
                    in_=cfg["wo"](mc)), reads=[wo_last[i]])
                tx = p.dma("sp", lambda e, mc=mc, qp=qp, i=i: e.dma_start(
                    out=xf_t[i][:, :], in_=xin3[:, mc, qp * SUB:(qp + 1) * SUB]),
                    reads=[xf_last[i]])
                tf = p.dma("sp", lambda e, mc=mc, qp=qp, i=i: e.dma_start(
                    out=ff_t[i][:, :], in_=fsc3[:, mc, qp * SUB:(qp + 1) * SUB]),
                    reads=[ff_last[i], cfg["fsc_tok"]])
                ts1 = p.op("dve", lambda e, i=i: e.tensor_add(
                    s1_t[i][:, :], xf_t[i][:, :], ff_t[i][:, :]),
                    reads=[tx, tf, s1_last[i]])
                xf_last[i] = ts1
                ff_last[i] = ts1
                tmy = None
                for kc in range(KC):
                    tmy = p.op("pe", lambda e, kc=kc, mc=mc, i=i: e.matmul(
                        p_y[:, :], wo_t[i][:, kc * 128:(kc + 1) * 128],
                        o_pan[:, kc * SUB:(kc + 1) * SUB],
                        start=(kc == 0), stop=(kc == KC - 1)),
                        reads=[tw, py_last, pn_gate] + opan_toks)
                o_consumed.append(tmy)
                wo_last[i] = tmy
                if cfg["outb"] is not None:
                    bbuf, btok = cfg["outb"]
                    ty1 = p.op("dve", lambda e, mc=mc, i=i, bbuf=bbuf: e.tensor_scalar(
                        y1_t[i][:, :], p_y[:, :], bbuf[0:128, mc:mc + 1], None, ALU.add),
                        reads=[tmy, btok, y1_last[i]])
                    py_last = ty1
                    txo = p.op("dve", lambda e, i=i: e.tensor_add(
                        s1_t[i][:, :], y1_t[i][:, :], s1_t[i][:, :]),
                        reads=[ty1, ts1], drain=True)
                    y1_last[i] = txo
                else:
                    txo = p.op("dve", lambda e, i=i: e.tensor_add(
                        s1_t[i][:, :], p_y[:, :], s1_t[i][:, :]),
                        reads=[tmy, ts1], drain=True)
                    py_last = txo
                tst = p.dma("sp", lambda e, mc=mc, qp=qp, i=i: e.dma_start(
                    out=xout3[:, mc, qp * SUB:(qp + 1) * SUB], in_=s1_t[i][:, :]),
                    reads=[txo])
                s1_last[i] = tst
                store_toks.append(tst)
        p.barrier()
        return store_toks


class _Stop(Exception):
    pass


def build_program(taps=(), upto="full"):
    nc = bass.Bass("TRN2", target_bir_lowering=False)
    dr = {}
    dr["xT"] = nc.dram_tensor("xT", [D, T], F32, kind="ExternalInput").ap()
    for n, sh in [("c_cos", [128, T]), ("c_sin", [128, T]), ("c_p128", [128, 128]),
                  ("c_cmask", [128, 4 * SUB]), ("c_ident", [128, 128]),
                  ("c_ohsel", [NE, NE * 128])]:
        dr[n] = nc.dram_tensor(n, sh, F32, kind="ExternalInput").ap()
    wshapes = dict(
        l0_qkvT=[D, 1152], l0_qkvb=[128, 10], l0_outT=[D, D], l0_outb=[128, 8],
        l0_WgT=[D, NE], l0_bg=[NE, 1], l0_w1T=[NE, D, 2 * D], l0_w2T=[NE, D, D],
        l1_Wdq=[D, 256], l1_Wuq=[256, D], l1_Wdkv=[D, 256], l1_Wuk=[256, D],
        l1_Wuv=[256, D], l1_Wo=[D, D], l1_router=[D, NE], l1_fc1=[NE, D, 2 * D],
        l1_fc2=[NE, D, D], l1_gate=[D, 2 * D], l1_up=[D, 2 * D], l1_down=[2 * D, D])
    for n, sh in wshapes.items():
        dr[n] = nc.dram_tensor(n, sh, F32, kind="ExternalInput").ap()
    resid = nc.dram_tensor("resid", [D, T], F32,
                           kind="ExternalOutput" if "resid" in taps else "Internal").ap()
    fsc = nc.dram_tensor("fsc", [D, T], F32,
                         kind="ExternalOutput" if "fsc" in taps else "Internal").ap()
    outT = nc.dram_tensor("outT", [D, T], F32, kind="ExternalOutput").ap()
    tap_dr = {}
    for tn in taps:
        if tn in ("resid", "fsc"):
            continue
        shape = {"xn": [128, KC * T], "call": [NE, T], "kdup": [128, T],
                 "kfm": [128, KC * T], "vtm0": [128, NKT * (HD + 1)],
                 "vtm1": [128, NKT * NH * (HD + 1)], "cq": [128, 2 * T],
                 "ckv": [128, 2 * T]}[tn.rstrip("01") if tn[-1] in "01" and not tn.startswith("vtm") else tn]
        tap_dr[tn] = nc.dram_tensor(f"tap_{tn}", shape, F32, kind="ExternalOutput").ap()

    try:
        nc = _assemble(nc, dr, resid, fsc, outT, tap_dr, taps, upto)
    except _Stop:
        pass
    return nc


def _assemble(nc, dr, resid, fsc, outT, tap_dr, taps, upto):
    with ExitStack() as st:
        p = Prog(nc, st)
        C = emit_consts(p, st, dr)

        def ck(name):
            if upto == name:
                p.barrier()
                p.emit()
                raise _Stop

        def tap(name, buf, cols):
            if name in tap_dr:
                with ExitStack() as th:
                    tb = _sb(th, nc, f"tapb_{name}", [128, 2048], F32)
                    np_ = buf.shape[0]
                    for c0 in range(0, cols, 2048):
                        w = min(2048, cols - c0)
                        tt = p.op("dve", lambda e, c0=c0, w=w, np_=np_: e.tensor_copy(
                            tb[0:np_, 0:w], buf[0:np_, c0:c0 + w]))
                        p.dma("sp", lambda e, c0=c0, w=w, np_=np_: e.dma_start(
                            out=tap_dr[name][0:np_, c0:c0 + w],
                            in_=tb[0:np_, 0:w]), reads=[tt])
                        p.barrier()

        if True:
            with ExitStack() as l0:
                qkvb_s = _sb(l0, nc, "qkvb0", [128, 10], F32)
                outb_s = _sb(l0, nc, "outb0", [128, 8], F32)
                xn = _sb(l0, nc, "xn0", [128, KC * T], BF16)
                tqb = p.dma("sp", lambda e: e.dma_start(out=qkvb_s[:, :], in_=dr["l0_qkvb"][:, :]))
                tob = p.dma("sp", lambda e: e.dma_start(out=outb_s[:, :], in_=dr["l0_outb"][:, :]))
                p.barrier()

                emit_ln(p, st, C, dr["xT"], xn, 0)
                tap("xn0", xn, KC * T)
                ck("ln0")

                with ExitStack() as lf:
                    c_all = _sb(lf, nc, "call0", [NE, T], BF16)
                    emit_routing(p, st, C, xn, c_all, dr["l0_WgT"], dr["l0_bg"], 0)
                    tap("call0", c_all, T)
                    ck("rt0")
                    w1T = dr["l0_w1T"]
                    w2T = dr["l0_w2T"]
                    experts = []
                    for e_ in range(NE):
                        experts.append(dict(
                            w1a=lambda ocp, e_=e_: _chunk3(w1T[e_])[:, :, ocp * 128:(ocp + 1) * 128],
                            w1b=lambda ocp, e_=e_: _chunk3(w1T[e_])[:, :, D + ocp * 128: D + (ocp + 1) * 128],
                            w2=lambda mc, e_=e_: _chunk3(w2T[e_])[:, :, mc * 128:(mc + 1) * 128],
                            use_c=True, e_idx=e_, c_tok=None))
                    emit_moe(p, st, C, xn, c_all, fsc, 0, experts)
                tap("fsc", None, 0)
                ck("ffn0")

                cosb = _sb(l0, nc, "cos0", [128, T], F32)
                sinb = _sb(l0, nc, "sin0", [128, T], F32)
                p128f = _sb(l0, nc, "p128f", [128, 128], F32)
                p.dma("sp", lambda e: e.dma_start(out=cosb[:, :], in_=dr["c_cos"][:, :]))
                p.dma("sp", lambda e: e.dma_start(out=sinb[:, :], in_=dr["c_sin"][:, :]))
                p.dma("sp", lambda e: e.dma_start(out=p128f[:, :], in_=dr["c_p128"][:, :]))
                p.barrier()
                kdup = _sb(l0, nc, "kdup0", [128, T], BF16, side="right")
                vtm0 = _sb(l0, nc, "vtm0", [128, NKT * (HD + 1)], BF16, side="right")
                nmax_k = _sb(l0, nc, "nmk0", [128, 4], F32, side="right")
                lw0 = dict(qkvT=dr["l0_qkvT"], qkvb_s=qkvb_s, qkvb_tok=tqb,
                           p128f=p128f, cos=cosb, sin=sinb)
                emit_kv_l0(p, st, C, lw0, xn, kdup, vtm0, nmax_k, 0)
                tap("kdup", kdup, T)
                ck("kv0")

                qkvT3 = _chunk3(dr["l0_qkvT"][:, :])

                def build_q0(p, qp, q_pan, SB, q_consumed, wq_last=[None, None],
                             qraw_last=[], t1_last=[], t2_last=[]):
                    toks = []
                    for mc in range(KC):
                        i = mc % 2
                        tw = p.dma("pool", lambda e, mc=mc, i=i: e.dma_start(
                            out=SB.wq[i][:, :].rearrange("p (kc m) -> p kc m", kc=KC),
                            in_=qkvT3[:, :, mc * 128:(mc + 1) * 128]), reads=[wq_last[i]])
                        tm = None
                        for kc in range(KC):
                            tm = p.op("pe", lambda e, kc=kc, qp=qp, i=i: e.matmul(
                                SB.p_s[0][:, 0:SUB], SB.wq[i][:, kc * 128:(kc + 1) * 128],
                                xn[:, kc * T + qp * SUB: kc * T + (qp + 1) * SUB],
                                start=(kc == 0), stop=(kc == KC - 1)),
                                reads=[tw, SB.ps_last[0]])
                        wq_last[i] = tm
                        tev = p.op("dve", lambda e, mc=mc: e.tensor_scalar(
                            SB.qraw[:, :], SB.p_s[0][:, 0:SUB],
                            qkvb_s[0:128, mc:mc + 1], None, ALU.add),
                            reads=[tm, tqb] + qraw_last)
                        SB.ps_last[0] = tev
                        tsw = p.op("pe", lambda e: e.matmul(
                            SB.p_s[1][:, 0:SUB], p128f[:, :], SB.qraw[:, :],
                            start=True, stop=True), reads=[tev, SB.ps_last[1]])
                        tt1 = p.op("dve", lambda e, qp=qp: e.tensor_mul(
                            SB.t1b[:, :], SB.qraw[:, :],
                            cosb[:, qp * SUB:(qp + 1) * SUB]),
                            reads=[tev] + t1_last, drain=True)
                        tt2 = p.op("dve", lambda e, qp=qp: e.tensor_mul(
                            SB.t2b[:, :], SB.p_s[1][:, 0:SUB],
                            sinb[:, qp * SUB:(qp + 1) * SUB]),
                            reads=[tsw] + t2_last, drain=True)
                        SB.ps_last[1] = tt2
                        tq = p.op("dve", lambda e, mc=mc: e.tensor_add(
                            q_pan[:, mc * SUB:(mc + 1) * SUB], SB.t1b[:, :], SB.t2b[:, :]),
                            reads=[tt1, tt2] + q_consumed, drain=True)
                        qraw_last[:] = [tsw, tt1]
                        t1_last[:] = [tq]
                        t2_last[:] = [tq]
                        toks.append(tq)
                    return toks

                cfg0 = dict(
                    build_q=build_q0,
                    k_lhsT=lambda h, kt: kdup[64 * (h & 1):64 * (h & 1) + 64,
                                              kt * 128:(kt + 1) * 128],
                    vtm_slice=lambda h, kt: vtm0[:, kt * (HD + 1): (kt + 1) * (HD + 1)],
                    nmax_k=nmax_k, kv_tok=("dgp", p.counts["dgp"]),
                    wo=lambda mc: _chunk3(dr["l0_outT"][:, :])[:, :, mc * 128:(mc + 1) * 128],
                    outb=(outb_s, tob), resid_in=dr["xT"], resid_out=resid, fsc=fsc,
                    fsc_tok=("dgp", p.counts["dgp"]), rope_scratch=True)
                emit_scores(p, st, C, 0, cfg0)
                tap("resid", None, 0)
                ck("l0")

            with ExitStack() as l1:
                cq = _sb(l1, nc, "cq1", [128, 2 * T], BF16)
                with ExitStack() as ly:
                    ckv = _sb(ly, nc, "ckv1", [128, 2 * T], BF16)
                    with ExitStack() as lx:
                        xn1 = _sb(lx, nc, "xn1", [128, KC * T], BF16)
                        emit_ln(p, st, C, resid, xn1, 1)
                        tap("xn1", xn1, KC * T)
                        ck("ln1")
                        emit_lowrank_proj(p, st, C, xn1, dr["l1_Wdq"], cq, 256, "cq")
                        emit_lowrank_proj(p, st, C, xn1, dr["l1_Wdkv"], ckv, 256, "ckv")
                        tap("cq", cq, 2 * T)
                        tap("ckv", ckv, 2 * T)
                        ck("proj1")

                        with ExitStack() as lf:
                            c_all = _sb(lf, nc, "call1", [NE, T], BF16)
                            emit_routing(p, st, C, xn1, c_all, dr["l1_router"], None, 1)
                            tap("call1", c_all, T)
                            ck("rt1")
                            fc1 = dr["l1_fc1"]
                            fc2 = dr["l1_fc2"]
                            experts = []
                            for e_ in range(NE):
                                experts.append(dict(
                                    w1a=lambda ocp, e_=e_: _chunk3(fc1[e_])[:, :, ocp * 128:(ocp + 1) * 128],
                                    w1b=lambda ocp, e_=e_: _chunk3(fc1[e_])[:, :, D + ocp * 128: D + (ocp + 1) * 128],
                                    w2=lambda mc, e_=e_: _chunk3(fc2[e_])[:, :, mc * 128:(mc + 1) * 128],
                                    use_c=True, e_idx=e_, c_tok=None))
                            for sh in range(2):
                                experts.append(dict(
                                    w1a=lambda ocp, sh=sh: _chunk3(dr["l1_gate"][:, :])[:, :, sh * D + ocp * 128: sh * D + (ocp + 1) * 128],
                                    w1b=lambda ocp, sh=sh: _chunk3(dr["l1_up"][:, :])[:, :, sh * D + ocp * 128: sh * D + (ocp + 1) * 128],
                                    w2=lambda mc, sh=sh: dr["l1_down"][:, :].rearrange(
                                        "(kc p) m -> p kc m", p=128)[:, sh * KC:(sh + 1) * KC,
                                                                     mc * 128:(mc + 1) * 128],
                                    use_c=False, e_idx=0, c_tok=None))
                            emit_moe(p, st, C, xn1, c_all, fsc, 1, experts)
                    ck("ffn1")
                    kfm = _sb(l1, nc, "kfm1", [128, KC * T], BF16, side="right")
                    vtm1 = _sb(l1, nc, "vtm1", [128, NKT * NH * (HD + 1)], BF16, side="right")
                    nmax_k1 = _sb(l1, nc, "nmk1", [128, 4], F32, side="right")
                    lw1 = dict(Wuk=dr["l1_Wuk"], Wuv=dr["l1_Wuv"])
                    emit_kv_l1(p, st, C, lw1, ckv, kfm, vtm1, nmax_k1, 1)
                    tap("kfm", kfm, KC * T)
                    ck("kv1")

                wuq = _sb(l1, nc, "wuq1", [128, 2 * D], BF16, side="right")
                twq = p.dma("pool", lambda e: e.dma_start(
                    out=wuq[:, :].rearrange("p (kc m) -> p kc m", kc=2),
                    in_=dr["l1_Wuq"][:, :].rearrange("(kc p) m -> p kc m", p=128)))

                def build_q1(p, qp, q_pan, SB, q_consumed):
                    toks = []
                    for mc in range(KC):
                        tm = None
                        for kc in range(2):
                            tm = p.op("pe", lambda e, kc=kc, mc=mc, qp=qp: e.matmul(
                                SB.p_s[0][:, 0:SUB],
                                wuq[:, (kc * KC + mc) * 128:(kc * KC + mc + 1) * 128],
                                cq[:, kc * T + qp * SUB: kc * T + (qp + 1) * SUB],
                                start=(kc == 0), stop=(kc == 1)),
                                reads=[twq, SB.ps_last[0]])
                        tq = p.op("dve", lambda e, mc=mc: e.tensor_copy(
                            q_pan[:, mc * SUB:(mc + 1) * SUB], SB.p_s[0][:, 0:SUB]),
                            reads=[tm] + q_consumed)
                        SB.ps_last[0] = tq
                        toks.append(tq)
                    return toks

                cfg1 = dict(
                    build_q=build_q1,
                    k_lhsT=lambda h, kt: kfm[64 * (h & 1):64 * (h & 1) + 64,
                                             (h >> 1) * T + kt * 128:(h >> 1) * T + (kt + 1) * 128],
                    vtm_slice=lambda h, kt: vtm1[:, (kt * NH + h) * (HD + 1):
                                                 (kt * NH + h) * (HD + 1) + HD + 1],
                    nmax_k=nmax_k1, kv_tok=("dve", p.counts["dve"]),
                    wo=lambda mc: _chunk3(dr["l1_Wo"][:, :])[:, :, mc * 128:(mc + 1) * 128],
                    outb=None, resid_in=resid, resid_out=outT, fsc=fsc,
                    fsc_tok=("dgp", p.counts["dgp"]))
                emit_scores(p, st, C, 1, cfg1)

        p.barrier()
        p.emit()
    return nc


_CACHE = {}


def kernel(**inputs):
    consts = _host_consts()
    w = _prep_weights(inputs)
    x = np.asarray(inputs["L0_attn_x"], dtype=np.float32)
    key = "full"
    if key not in _CACHE:
        _CACHE[key] = build_program()
    nc = _CACHE[key]
    in_maps = []
    for b in range(8):
        m = dict(consts)
        m.update(w)
        m["xT"] = np.ascontiguousarray(x[b].T)
        in_maps.append(m)
    res = run_bass_kernel_spmd(nc, in_maps, core_ids=list(range(8)))
    out = np.empty((8, T, D), np.float32)
    for b in range(8):
        out[b] = res.results[b]["outT"].T
    return out


# version 09
# speedup vs baseline: 1.1462x; 1.0392x over previous
import numpy as np
import ml_dtypes
from contextlib import ExitStack

import concourse.bass as bass
import concourse.mybir as mybir
from concourse.bass_utils import run_bass_kernel_spmd

F32 = mybir.dt.float32
BF16 = mybir.dt.bfloat16
AF = mybir.ActivationFunctionType
ALU = mybir.AluOpType
AX = mybir.AxisListType
BF = ml_dtypes.bfloat16

T = 4096
D = 1024
KC = 8
SUB = 512
NSUB = T // SUB
NH = 16
HD = 64
NE = 8
NQP = T // SUB
NKT = T // 128
ROPE_THETA = 10000.0

DMA_CH = {"sp": "dsp", "pool": "dgp"}
ENG_SECT = {"pe": "tensor", "act": "scalar", "dve": "vector", "pool": "gpsimd", "sp": "sync"}
CHANNELS = ("pe", "act", "dve", "pool", "sp", "dsp", "dgp")


class Prog:

    def __init__(self, nc, stack):
        self.nc = nc
        self.sems = {ch: stack.enter_context(nc.semaphore(f"s_{ch}")) for ch in CHANNELS}
        self.counts = {ch: 0 for ch in CHANNELS}
        self.known = {e: {ch: 0 for ch in CHANNELS} for e in ENG_SECT}
        self.thunks = {e: [] for e in ENG_SECT}

    def _waits_for(self, eng, reads):
        merged = {}
        for tok in reads:
            if tok is None:
                continue
            ch, v = tok
            if v > merged.get(ch, 0):
                merged[ch] = v
        waits = []
        for ch, v in merged.items():
            if self.known[eng][ch] < v:
                waits.append((self.sems[ch], v))
                self.known[eng][ch] = v
        return waits

    def op(self, eng, fn, reads=(), drain=False):
        waits = self._waits_for(eng, reads)
        self.counts[eng] += 1
        tok = (eng, self.counts[eng])
        sem = self.sems[eng]

        def thunk(e, waits=waits, fn=fn, drain=drain, sem=sem):
            for s, v in waits:
                e.wait_ge(s, v)
            if drain:
                e.drain()
            fn(e).then_inc(sem, 1)

        self.thunks[eng].append(thunk)
        return tok

    def dma(self, eng, fn, reads=()):
        waits = self._waits_for(eng, reads)
        ch = DMA_CH[eng]
        self.counts[ch] += 16
        tok = (ch, self.counts[ch])
        sem = self.sems[ch]

        def thunk(e, waits=waits, fn=fn, sem=sem):
            for s, v in waits:
                e.wait_ge(s, v)
            fn(e).then_inc(sem, 16)

        self.thunks[eng].append(thunk)
        return tok

    def barrier(self):
        toks = [(ch, self.counts[ch]) for ch in CHANNELS]
        for eng in ENG_SECT:
            waits = self._waits_for(eng, toks)
            if waits:
                def thunk(e, waits=waits):
                    for s, v in waits:
                        e.wait_ge(s, v)
                self.thunks[eng].append(thunk)

    def emit(self):
        with self.nc.Block() as blk:
            for eng, sect in ENG_SECT.items():
                getattr(blk, sect)(self._make_runner(eng))

    def _make_runner(self, eng):
        thunks = self.thunks[eng]

        def run(e):
            for t in thunks:
                t(e)
        return run



def _host_consts():
    i = np.arange(HD // 2, dtype=np.float64)
    inv_freq = 1.0 / ROPE_THETA ** (2 * i / HD)
    pos = np.arange(T, dtype=np.float64)
    ang = pos[None, :] * inv_freq[:, None]
    cos32 = np.cos(ang)
    sin32 = np.sin(ang)
    cos64 = np.repeat(cos32, 2, axis=0)
    sin64s = np.empty((HD, T))
    sin64s[0::2] = -sin32
    sin64s[1::2] = sin32
    cos128 = np.concatenate([cos64, cos64], 0).astype(np.float32)
    sin128 = np.concatenate([sin64s, sin64s], 0).astype(np.float32)

    p64 = np.zeros((HD, HD), np.float32)
    for k in range(0, HD, 2):
        p64[k, k + 1] = 1.0
        p64[k + 1, k] = 1.0
    p128 = np.zeros((128, 128), np.float32)
    p128[:64, :64] = p64
    p128[64:, 64:] = p64

    cmask = np.zeros((128, 4 * SUB), np.float32)
    dk = np.arange(128)[:, None]
    qr = np.arange(SUB)[None, :]
    for j in range(4):
        cmask[:, j * SUB:(j + 1) * SUB] = (qr >= 128 * j + dk).astype(np.float32)

    ident = np.eye(128, dtype=np.float32)
    ohsel = np.zeros((NE, NE * 128), np.float32)
    for e in range(NE):
        ohsel[e, e * 128:(e + 1) * 128] = 1.0
    return dict(c_cos=cos128, c_sin=sin128, c_p128=p128, c_cmask=cmask,
                c_ident=ident, c_ohsel=ohsel)


def _prep_weights(inp):
    c = lambda a: np.ascontiguousarray(np.asarray(a, dtype=np.float32))
    w = {}
    w["l0_qkvT"] = c(np.asarray(inp["L0_attn_qkv_w"]).T)
    qb = np.asarray(inp["L0_attn_qkv_b"])
    qkvb = np.zeros((128, 10), np.float32)
    qkvb[:, 0:9] = np.asarray(qb).reshape(9, 128).T
    qkvb[0:64, 9] = qb[1088:1152]
    qkvb[64:128, 8] = 0.0
    w["l0_qkvb"] = c(qkvb)
    w["l0_outT"] = c(np.asarray(inp["L0_attn_out_w"]).T)
    w["l0_outb"] = c(np.asarray(inp["L0_attn_out_b"]).reshape(8, 128).T)
    w["l0_WgT"] = c(np.asarray(inp["L0_ffn_Wg"]).T)
    w["l0_bg"] = c(np.asarray(inp["L0_ffn_bg"]).reshape(8, 1))
    w["l0_w1T"] = c(np.asarray(inp["L0_ffn_w1"]).transpose(0, 2, 1))
    w["l0_w2T"] = c(np.asarray(inp["L0_ffn_w2"]).transpose(0, 2, 1))
    w["l1_Wdq"] = c(inp["L1_attn_W_dq"])
    w["l1_Wuq"] = c(inp["L1_attn_W_uq"])
    w["l1_Wdkv"] = c(inp["L1_attn_W_dkv"])
    w["l1_Wuk"] = c(inp["L1_attn_W_uk"])
    w["l1_Wuv"] = c(inp["L1_attn_W_uv"])
    w["l1_Wo"] = c(inp["L1_attn_W_o"])
    w["l1_router"] = c(inp["L1_ffn_w_router"])
    w["l1_fc1"] = c(inp["L1_ffn_fc1_w"])
    w["l1_fc2"] = c(inp["L1_ffn_fc2_w"])
    w["l1_gate"] = c(inp["L1_ffn_gate_w"])
    w["l1_up"] = c(inp["L1_ffn_up_w"])
    w["l1_down"] = c(inp["L1_ffn_down_w"])
    return w


class Bufs:
    pass


def _sb(stack, nc, name, shape, dtype, side=None):
    kw = {} if side is None else {"side": side}
    return stack.enter_context(nc.sbuf_tensor(name, shape, dtype, **kw))


def _ps(stack, nc, name, shape, dtype=F32):
    return stack.enter_context(nc.psum_tensor(name, shape, dtype))


def _chunk3(dram_ap):
    return dram_ap.rearrange("(kc p) m -> p kc m", p=128)



def emit_consts(p, st, dr):
    nc = p.nc
    C = Bufs()
    C.ident_f = _sb(st, nc, "ident_f", [128, 128], F32, side="right")
    C.ident_b = _sb(st, nc, "ident_b", [128, 128], BF16, side="right")
    C.cmask = _sb(st, nc, "cmask", [128, 4 * SUB], BF16, side="right")
    C.ohsel = _sb(st, nc, "ohsel", [NE, NE * 128], BF16, side="right")
    C.ones_b = _sb(st, nc, "ones_b", [128, 1], BF16, side="right")
    C.ones_fr = _sb(st, nc, "ones_fr", [128, 128], F32, side="right")
    C.ones_br = _sb(st, nc, "ones_br", [128, 128], BF16, side="right")
    C.fconst = _sb(st, nc, "fconst", [1, 2], F32, side="right")
    p.op("dve", lambda e: e.memset(C.fconst[0:1, 0:1], 1e-5))
    p.op("dve", lambda e: e.memset(C.fconst[0:1, 1:2], 1e-30))
    p.dma("sp", lambda e: e.dma_start(out=C.ident_f[:, :], in_=dr["c_ident"][:, :]))
    p.dma("pool", lambda e: e.dma_start(out=C.cmask[:, :], in_=dr["c_cmask"][:, :]))
    p.dma("pool", lambda e: e.dma_start(out=C.ident_b[:, :], in_=dr["c_ident"][:, :]))
    p.dma("pool", lambda e: e.dma_start(out=C.ohsel[:, :], in_=dr["c_ohsel"][:, :]))
    p.op("dve", lambda e: e.memset(C.ones_b[:, :], 1.0))
    p.op("dve", lambda e: e.memset(C.ones_fr[:, :], 1.0))
    p.op("dve", lambda e: e.memset(C.ones_br[:, :], 1.0))
    p.barrier()
    return C

def emit_ln(p, st, C, src_dram, xn, layer):
    nc = p.nc
    with ExitStack() as ph:
        x_f = [_sb(ph, nc, f"ln_xf{layer}_{i}", [128, KC * SUB], F32) for i in range(2)]
        x_b = [_sb(ph, nc, f"ln_xb{layer}_{i}", [128, KC * SUB], BF16) for i in range(2)]
        xsq = _sb(ph, nc, f"ln_xsq{layer}", [128, KC * SUB], BF16)
        tmp = _sb(ph, nc, f"ln_tmp{layer}", [128, KC * SUB], F32)
        rows = [_sb(ph, nc, f"ln_rows{layer}_{i}", [1, 6 * SUB], F32) for i in range(2)]
        p_sum = [_ps(ph, nc, f"lnps{layer}_{i}", [1, SUB]) for i in range(2)]
        p_sq = [_ps(ph, nc, f"lnpq{layer}_{i}", [1, SUB]) for i in range(2)]
        p_M = [_ps(ph, nc, f"lnpM{layer}_{i}", [128, SUB]) for i in range(2)]
        p_R = [_ps(ph, nc, f"lnpR{layer}_{i}", [128, SUB]) for i in range(2)]

        src3 = src_dram[:, :].rearrange("(kc p) t -> p kc t", p=128)
        xf_gate = [None, None]
        xb_gate = [[], []]
        xsq_gate = None
        tmp_gate = None
        rows_gate = [None, None]
        psum_gate = [None, None]
        psq_gate = [None, None]
        pM_gate = [None, None]
        pR_gate = [None, None]
        for s in range(NSUB):
            i = s % 2
            tl = p.dma("sp", lambda e, s=s, i=i: e.dma_start(
                out=x_f[i][:, :].rearrange("p (kc t) -> p kc t", kc=KC),
                in_=src3[:, :, s * SUB:(s + 1) * SUB]), reads=[xf_gate[i]])
            tlb = p.dma("pool", lambda e, s=s, i=i: e.dma_start(
                out=x_b[i][:, :].rearrange("p (kc t) -> p kc t", kc=KC),
                in_=src3[:, :, s * SUB:(s + 1) * SUB]), reads=xb_gate[i])
            tsq = []
            for kc in range(KC):
                tsq.append(p.op("act", lambda e, kc=kc, i=i: e.activation(
                    xsq[:, kc * SUB:(kc + 1) * SUB], x_b[i][:, kc * SUB:(kc + 1) * SUB],
                    AF.Square), reads=[tlb, xsq_gate]))
            tm = tv = None
            for kc in range(KC):
                tm = p.op("pe", lambda e, kc=kc, i=i: e.matmul(
                    p_sum[i][:, :], C.ones_b[:, :], x_b[i][:, kc * SUB:(kc + 1) * SUB],
                    start=(kc == 0), stop=(kc == KC - 1)), reads=[tlb, psum_gate[i]])
            for kc in range(KC):
                tv = p.op("pe", lambda e, kc=kc, i=i: e.matmul(
                    p_sq[i][:, :], C.ones_b[:, :], xsq[:, kc * SUB:(kc + 1) * SUB],
                    start=(kc == 0), stop=(kc == KC - 1)), reads=[tsq[kc], psq_gate[i]])
            xsq_gate = tv
            R = lambda j, i=i: rows[i][0:1, j * SUB:(j + 1) * SUB]
            t1 = p.op("dve", lambda e, R=R, i=i: e.tensor_scalar_mul(R(0), p_sum[i][:, :], 1.0 / D),
                      reads=[tm, rows_gate[i]])
            t2 = p.op("dve", lambda e, R=R, i=i: e.tensor_scalar_mul(R(1), p_sq[i][:, :], 1.0 / D),
                      reads=[tv], drain=True)
            psum_gate[i] = t1
            psq_gate[i] = t2
            t3 = p.op("dve", lambda e, R=R: e.tensor_mul(R(2), R(0), R(0)),
                      reads=[t1], drain=True)
            t4 = p.op("dve", lambda e, R=R: e.tensor_sub(R(3), R(1), R(2)),
                      reads=[t2, t3], drain=True)
            t5 = p.op("act", lambda e, R=R: e.activation(R(4), R(3), AF.Ln,
                                                         bias=C.fconst[0:1, 0:1]),
                      reads=[t4])
            t6 = p.op("act", lambda e, R=R: e.activation(R(5), R(4), AF.Exp, scale=-0.5),
                      reads=[t5], drain=True)
            tM = p.op("pe", lambda e, R=R, i=i: e.matmul(p_M[i][:, :], C.ones_fr[0:1, :],
                                                         R(0), start=True, stop=True),
                      reads=[t1, t3, pM_gate[i]])
            tR = p.op("pe", lambda e, R=R, i=i: e.matmul(p_R[i][:, :], C.ones_fr[0:1, :],
                                                         R(5), start=True, stop=True),
                      reads=[t6, pR_gate[i]])
            rows_gate[i] = tR
            tsub = []
            for kc in range(KC):
                tsub.append(p.op("dve", lambda e, kc=kc, i=i: e.tensor_sub(
                    tmp[:, kc * SUB:(kc + 1) * SUB], x_f[i][:, kc * SUB:(kc + 1) * SUB],
                    p_M[i][:, :]), reads=[tl, tM, tmp_gate]))
            tout = []
            for kc in range(KC):
                tout.append(p.op("dve", lambda e, kc=kc, s=s, i=i: e.tensor_mul(
                    xn[:, kc * T + s * SUB: kc * T + (s + 1) * SUB],
                    tmp[:, kc * SUB:(kc + 1) * SUB], p_R[i][:, :]), reads=[tsub[kc], tR]))
            tmp_gate = tout[-1]
            pM_gate[i] = tout[-1]
            pR_gate[i] = tout[-1]
            xf_gate[i] = tout[-1]
            xb_gate[i] = [tm, tsq[-1]]
        p.barrier()


def emit_routing(p, st, C, xn, c_all, router_dram, bias_dram, layer):
    nc = p.nc
    NQ = SUB // 128
    with ExitStack() as ph:
        wrt = _sb(ph, nc, f"rt_w{layer}", [128, KC * NE], BF16)
        lg = _sb(ph, nc, f"rt_lg{layer}", [NE, SUB], F32)
        bias_s = _sb(ph, nc, f"rt_b{layer}", [NE, 1], F32)
        ltok = [_sb(ph, nc, f"rt_lt{layer}_{q}", [128, NE], F32) for q in range(NQ)]
        sc = [_sb(ph, nc, f"rt_sc{layer}_{q}", [128, 8], F32) for q in range(NQ)]
        msk = [_sb(ph, nc, f"rt_msk{layer}_{q}", [128, NE], F32) for q in range(NQ)]
        pex = [_sb(ph, nc, f"rt_pex{layer}_{q}", [128, NE], F32) for q in range(NQ)]
        ctok = [_sb(ph, nc, f"rt_ct{layer}_{q}", [128, NE], F32) for q in range(NQ)]
        p_lg = [_ps(ph, nc, f"rt_plg{layer}_{i}", [NE, SUB]) for i in range(2)]
        p_t = [_ps(ph, nc, f"rt_pt{layer}_{i}", [128, NE]) for i in range(2)]
        p_ct = [_ps(ph, nc, f"rt_pct{layer}_{i}", [NE, 128]) for i in range(2)]

        tw = p.dma("pool", lambda e: e.dma_start(
            out=wrt[:, :].rearrange("p (kc e) -> p kc e", kc=KC),
            in_=_chunk3(router_dram[:, :])))
        if bias_dram is not None:
            tb = p.dma("sp", lambda e: e.dma_start(out=bias_s[:, :], in_=bias_dram[:, :]))
        else:
            tb = p.op("dve", lambda e: e.memset(bias_s[:, :], 0.0))

        plg_gate = [None, None]
        pt_gate = [None, None]
        pct_gate = [None, None]
        lg_gate = None
        for s in range(NSUB):
            il = s % 2
            tmm = None
            for kc in range(KC):
                tmm = p.op("pe", lambda e, kc=kc, s=s, il=il: e.matmul(
                    p_lg[il][:, :], wrt[:, kc * NE:(kc + 1) * NE],
                    xn[:, kc * T + s * SUB: kc * T + (s + 1) * SUB],
                    start=(kc == 0), stop=(kc == KC - 1)), reads=[tw, plg_gate[il]])
            te = p.op("dve", lambda e, il=il: e.tensor_scalar(lg[:, :], p_lg[il][:, :],
                                                              bias_s[0:NE, 0:1], None,
                                                              ALU.add),
                      reads=[tmm, tb, lg_gate])
            plg_gate[il] = te
            t0s = []
            for q in range(NQ):
                it = q % 2
                tt = p.op("pe", lambda e, q=q, it=it: e.transpose(
                    p_t[it][:, :], lg[:, q * 128:(q + 1) * 128], C.ident_f[0:NE, 0:NE]),
                    reads=[te, pt_gate[it]])
                t0 = p.op("dve", lambda e, q=q, it=it: e.tensor_copy(ltok[q][:, :],
                                                                     p_t[it][:, :]),
                          reads=[tt])
                pt_gate[it] = t0
                t0s.append(t0)
            lg_gate = t0s[-1]
            tA = [p.op("dve", lambda e, q=q: e.reduce_max(sc[q][:, 0:1], ltok[q][:, :],
                                                          axis=AX.X, negate=True),
                       reads=[t0s[q]]) for q in range(NQ)]
            tB = [p.op("dve", lambda e, q=q: e.tensor_scalar(msk[q][:, :], ltok[q][:, :],
                                                             sc[q][:, 0:1], None, ALU.add),
                       reads=[tA[q]]) for q in range(NQ)]
            tC = [p.op("dve", lambda e, q=q: e.tensor_scalar(msk[q][:, :], msk[q][:, :],
                                                             0.0, None, ALU.is_ge),
                       reads=[tB[q]]) for q in range(NQ)]
            tD = [p.op("dve", lambda e, q=q: e.scalar_tensor_tensor(
                msk[q][:, :], msk[q][:, :], -1e9, ltok[q][:, :], ALU.mult, ALU.add),
                reads=[tC[q]]) for q in range(NQ)]
            tE = [p.op("dve", lambda e, q=q: e.reduce_max(sc[q][:, 1:2], msk[q][:, :],
                                                          axis=AX.X), reads=[tD[q]])
                  for q in range(NQ)]
            tF = [p.op("act", lambda e, q=q: e.activation(pex[q][:, :], ltok[q][:, :],
                                                          AF.Exp, bias=sc[q][:, 0:1]),
                       reads=[tA[q]]) for q in range(NQ)]
            tG = [p.op("dve", lambda e, q=q: e.tensor_scalar(ctok[q][:, :], ltok[q][:, :],
                                                             sc[q][:, 1:2], None, ALU.is_ge),
                       reads=[tE[q]]) for q in range(NQ)]
            tH = [p.op("dve", lambda e, q=q: e.tensor_mul(ctok[q][:, :], ctok[q][:, :],
                                                          pex[q][:, :]),
                       reads=[tF[q], tG[q]]) for q in range(NQ)]
            tI = [p.op("dve", lambda e, q=q: e.reduce_sum(sc[q][:, 2:3], ctok[q][:, :],
                                                          axis=AX.X), reads=[tH[q]])
                  for q in range(NQ)]
            tJ = [p.op("dve", lambda e, q=q: e.reciprocal(sc[q][:, 3:4], sc[q][:, 2:3]),
                       reads=[tI[q]]) for q in range(NQ)]
            tK = [p.op("dve", lambda e, q=q: e.tensor_scalar(ctok[q][:, :], ctok[q][:, :],
                                                             sc[q][:, 3:4], None, ALU.mult),
                       reads=[tJ[q]]) for q in range(NQ)]
            for q in range(NQ):
                it = q % 2
                tc = p.op("pe", lambda e, q=q, it=it: e.transpose(
                    p_ct[it][:, :], ctok[q][:, :], C.ident_f[:, :]),
                    reads=[tK[q], pct_gate[it]])
                tcp = p.op("dve", lambda e, s=s, q=q, it=it: e.tensor_copy(
                    c_all[:, s * SUB + q * 128: s * SUB + (q + 1) * 128], p_ct[it][:, :]),
                    reads=[tc])
                pct_gate[it] = tcp
        p.barrier()


def emit_moe(p, st, C, xn, c_all, fsc, layer, experts):
    nc = p.nc
    with ExitStack() as ph:
        a_buf = _sb(ph, nc, f"moe_a{layer}", [128, KC * T], BF16)
        w1s = [_sb(ph, nc, f"moe_w1{layer}_{i}", [128, 2 * KC * 128], BF16) for i in range(2)]
        w2s = [_sb(ph, nc, f"moe_w2{layer}_{i}", [128, KC * 128], BF16) for i in range(2)]
        silu_t = [_sb(ph, nc, f"moe_st{layer}_{i}", [128, SUB], BF16) for i in range(2)]
        c_sb = _sb(ph, nc, f"moe_cb{layer}", [128, T], BF16)
        o_t = [_sb(ph, nc, f"moe_ot{layer}_{i}", [128, SUB], F32) for i in range(3)]
        p_h = [_ps(ph, nc, f"moe_ph{layer}_{i}", [128, SUB]) for i in range(4)]
        p_o = [_ps(ph, nc, f"moe_po{layer}_{i}", [128, SUB]) for i in range(2)]
        p_c = _ps(ph, nc, f"moe_pc{layer}", [128, SUB])

        fsc3 = fsc[:, :].rearrange("(mc p) t -> p mc t", p=128)
        region_tok = {}
        w1_last = [None, None]
        w2_last = [None, None]
        ph_last = [None] * 4
        po_last = [None] * 2
        ot_last = [None] * 3
        cs_last = None
        si_last = [None, None]
        pc_last = None

        for ei, ex in enumerate(experts):
            for ocp in range(KC):
                wb = w1s[ocp % 2]
                twa = p.dma("pool", lambda e, ex=ex, ocp=ocp, wb=wb: e.dma_start(
                    out=wb[:, 0:KC * 128].rearrange("p (kc m) -> p kc m", kc=KC),
                    in_=ex["w1a"](ocp)), reads=[w1_last[ocp % 2]])
                twb = p.dma("pool", lambda e, ex=ex, ocp=ocp, wb=wb: e.dma_start(
                    out=wb[:, KC * 128:2 * KC * 128].rearrange("p (kc m) -> p kc m", kc=KC),
                    in_=ex["w1b"](ocp)), reads=[w1_last[ocp % 2]])
                last_a = None
                for s in range(NSUB):
                    ia, ib = (2 * s) % 4, (2 * s + 1) % 4
                    pa, pb = p_h[ia], p_h[ib]
                    tma = tmb = None
                    for kc in range(KC):
                        tma = p.op("pe", lambda e, kc=kc, s=s, wb=wb, pa=pa: e.matmul(
                            pa[:, :], wb[:, kc * 128:(kc + 1) * 128],
                            xn[:, kc * T + s * SUB: kc * T + (s + 1) * SUB],
                            start=(kc == 0), stop=(kc == KC - 1)),
                            reads=[twa, ph_last[ia]])
                    for kc in range(KC):
                        tmb = p.op("pe", lambda e, kc=kc, s=s, wb=wb, pb=pb: e.matmul(
                            pb[:, :], wb[:, (KC + kc) * 128:(KC + kc + 1) * 128],
                            xn[:, kc * T + s * SUB: kc * T + (s + 1) * SUB],
                            start=(kc == 0), stop=(kc == KC - 1)),
                            reads=[twb, ph_last[ib]])
                    stl = silu_t[s % 2]
                    ts = p.op("act", lambda e, stl=stl, pa=pa: e.activation(
                        stl[:, :], pa[:, :], AF.Silu), reads=[tma, si_last[s % 2]])
                    ta = p.op("dve", lambda e, ocp=ocp, s=s, stl=stl, pb=pb: e.tensor_mul(
                        a_buf[:, ocp * T + s * SUB: ocp * T + (s + 1) * SUB],
                        stl[:, :], pb[:, :]), reads=[ts, tmb])
                    si_last[s % 2] = ta
                    ph_last[ia] = ts
                    ph_last[ib] = ta
                    last_a = ta
                w1_last[ocp % 2] = last_a
            if ex["use_c"]:
                for s in range(NSUB):
                    tcm = p.op("pe", lambda e, ex=ex, s=s: e.matmul(
                        p_c[:, :], C.ohsel[:, ex["e_idx"] * 128:(ex["e_idx"] + 1) * 128],
                        c_all[:, s * SUB:(s + 1) * SUB], start=True, stop=True),
                        reads=[pc_last, ex["c_tok"]])
                    tce = p.op("dve", lambda e, s=s: e.tensor_copy(
                        c_sb[:, s * SUB:(s + 1) * SUB], p_c[:, :]),
                        reads=[tcm, cs_last])
                    pc_last = tce
                cs_tok = pc_last
            else:
                cs_tok = None
            for mc in range(KC):
                tw2 = p.dma("pool", lambda e, ex=ex, mc=mc: e.dma_start(
                    out=w2s[mc % 2][:, :].rearrange("p (kc m) -> p kc m", kc=KC),
                    in_=ex["w2"](mc)), reads=[w2_last[mc % 2]])
                last_ev = None
                for s in range(NSUB):
                    po = p_o[s % 2]
                    tmo = None
                    for kc in range(KC):
                        tmo = p.op("pe", lambda e, kc=kc, s=s, mc=mc, po=po: e.matmul(
                            po[:, :], w2s[mc % 2][:, kc * 128:(kc + 1) * 128],
                            a_buf[:, kc * T + s * SUB: kc * T + (s + 1) * SUB],
                            start=(kc == 0), stop=(kc == KC - 1)),
                            reads=[tw2, po_last[s % 2]])
                    i3 = (mc * NSUB + s) % 3
                    ot = o_t[i3]
                    if ex["use_c"]:
                        tev = p.op("dve", lambda e, ot=ot, po=po, s=s: e.tensor_mul(
                            ot[:, :], po[:, :], c_sb[:, s * SUB:(s + 1) * SUB]),
                            reads=[tmo, cs_tok, ot_last[i3]])
                    else:
                        tev = p.op("dve", lambda e, ot=ot, po=po: e.tensor_copy(
                            ot[:, :], po[:, :]), reads=[tmo, ot_last[i3]])
                    po_last[s % 2] = tev
                    last_ev = tev
                    accum = ALU.add if ei > 0 else ALU.bypass
                    td = p.dma("pool", lambda e, mc=mc, s=s, ot=ot, accum=accum: e.dma_start(
                        out=fsc3[:, mc, s * SUB:(s + 1) * SUB], in_=ot[:, :],
                        accum_op=accum), reads=[tev, region_tok.get((mc, s))])
                    region_tok[(mc, s)] = td
                    ot_last[i3] = td
                w2_last[mc % 2] = last_ev
            if ex["use_c"]:
                cs_last = w2_last[0]
        p.barrier()


def emit_lowrank_proj(p, st, C, xn, w_dram, out_fm, ncol, tag):
    nc = p.nc
    nmc = ncol // 128
    with ExitStack() as ph:
        wt = _sb(ph, nc, f"lr_w_{tag}", [128, KC * ncol], BF16)
        p_x = [_ps(ph, nc, f"lr_p_{tag}{i}", [128, SUB]) for i in range(2)]
        tw = p.dma("pool", lambda e: e.dma_start(
            out=wt[:, :].rearrange("p (kc m) -> p kc m", kc=KC),
            in_=_chunk3(w_dram[:, :])))
        last = [None, None]
        for s in range(NSUB):
            for mc in range(nmc):
                i = (s * nmc + mc) % 2
                px = p_x[i]
                tm = None
                for kc in range(KC):
                    tm = p.op("pe", lambda e, kc=kc, mc=mc, s=s, px=px: e.matmul(
                        px[:, :], wt[:, (kc * nmc + mc) * 128:(kc * nmc + mc + 1) * 128],
                        xn[:, kc * T + s * SUB: kc * T + (s + 1) * SUB],
                        start=(kc == 0), stop=(kc == KC - 1)),
                        reads=[tw, last[i]])
                te = p.op("dve", lambda e, mc=mc, s=s, px=px: e.tensor_copy(
                    out_fm[:, mc * T + s * SUB: mc * T + (s + 1) * SUB], px[:, :]),
                    reads=[tm])
                last[i] = te
        p.barrier()


def _norm_update(p, C, src_ap, nparts, p_n, nrow, nmax, prev_toks, reads, pn_gate):
    gates = pn_gate if isinstance(pn_gate, list) else [pn_gate]
    tn = p.op("pe", lambda e: e.matmul(p_n[:, :], C.ones_b[0:nparts, :], src_ap,
                                       start=True, stop=True),
              reads=list(reads) + gates)
    t2 = p.op("dve", lambda e: e.tensor_copy(nrow[:, :], p_n[:, :]), reads=[tn])
    t3 = p.op("dve", lambda e: e.reduce_max(nmax[0:1, 1:2], nrow[:, :], axis=AX.X),
              reads=[t2], drain=True)
    t4 = p.op("dve", lambda e: e.tensor_max(nmax[0:1, 0:1], nmax[0:1, 0:1],
                                            nmax[0:1, 1:2]),
              reads=[t3] + list(prev_toks), drain=True)
    return tn, t2, t4


def emit_kv_l0(p, st, C, lw, xn, kdup, vtm, nmax_k, layer):
    nc = p.nc
    with ExitStack() as ph:
        wkt = _sb(ph, nc, "kv0_wk", [128, KC * HD], BF16)
        wvt = _sb(ph, nc, "kv0_wv", [128, KC * HD], BF16)
        kraw = _sb(ph, nc, "kv0_kraw", [128, SUB], F32)
        t1b = _sb(ph, nc, "kv0_t1", [128, SUB], F32)
        t2b = _sb(ph, nc, "kv0_t2", [128, SUB], F32)
        v_b = _sb(ph, nc, "kv0_vb", [128, SUB], BF16)
        ksq = _sb(ph, nc, "kv0_ksq", [128, SUB], BF16)
        nrow = _sb(ph, nc, "kv0_nrow", [1, SUB], F32)
        p_k = _ps(ph, nc, "kv0_pk", [128, SUB])
        p_v = _ps(ph, nc, "kv0_pv", [128, SUB])
        p_sw = _ps(ph, nc, "kv0_psw", [128, SUB])
        p_t = [_ps(ph, nc, f"kv0_pt{i}", [128, 128], BF16) for i in range(2)]
        p_n = _ps(ph, nc, "kv0_pn", [1, SUB])

        qkvT3 = _chunk3(lw["qkvT"][:, :])
        tk = p.dma("pool", lambda e: e.dma_start(
            out=wkt[:, :].rearrange("p (kc m) -> p kc m", kc=KC),
            in_=qkvT3[:, :, 1024:1088]))
        tv = p.dma("pool", lambda e: e.dma_start(
            out=wvt[:, :].rearrange("p (kc m) -> p kc m", kc=KC),
            in_=qkvT3[:, :, 1088:1152]))
        tmx = p.op("dve", lambda e: e.memset(nmax_k[:, :], 0.0))
        vtm_tok = p.op("dve", lambda e: e.memset(vtm[:, :], 1.0))
        pt_gate = [None, None]
        kraw_tok = []
        t1_tok = t2_tok = krot_tok = None
        v_tok = []
        ksq_tok = None
        pn_gate = None
        for s in range(NSUB):
            tmk = tmv = None
            for kc in range(KC):
                tmk = p.op("pe", lambda e, kc=kc, s=s: e.matmul(
                    p_k[0:HD, :], wkt[:, kc * HD:(kc + 1) * HD],
                    xn[:, kc * T + s * SUB: kc * T + (s + 1) * SUB],
                    start=(kc == 0), stop=(kc == KC - 1)), reads=[tk] + kraw_tok)
            for kc in range(KC):
                tmv = p.op("pe", lambda e, kc=kc, s=s: e.matmul(
                    p_v[0:HD, :], wvt[:, kc * HD:(kc + 1) * HD],
                    xn[:, kc * T + s * SUB: kc * T + (s + 1) * SUB],
                    start=(kc == 0), stop=(kc == KC - 1)), reads=[tv] + v_tok)
            tke = p.op("dve", lambda e: e.tensor_scalar(
                kraw[0:HD, :], p_k[0:HD, :], lw["qkvb_s"][0:HD, 8:9], None, ALU.add),
                reads=[tmk, lw["qkvb_tok"]])
            tsw = p.op("pe", lambda e: e.matmul(
                p_sw[0:HD, :], lw["p128f"][0:HD, 0:HD], kraw[0:HD, :],
                start=True, stop=True), reads=[tke, t2_tok])
            tt1 = p.op("dve", lambda e, s=s: e.tensor_mul(
                t1b[0:HD, :], kraw[0:HD, :], lw["cos"][0:HD, s * SUB:(s + 1) * SUB]),
                reads=[tke, krot_tok], drain=True)
            tt2 = p.op("dve", lambda e, s=s: e.tensor_mul(
                t2b[0:HD, :], p_sw[0:HD, :], lw["sin"][0:HD, s * SUB:(s + 1) * SUB]),
                reads=[tsw, krot_tok], drain=True)
            tkr = p.op("dve", lambda e, s=s: e.tensor_add(
                kdup[0:HD, s * SUB:(s + 1) * SUB], t1b[0:HD, :], t2b[0:HD, :]),
                reads=[tt1, tt2], drain=True)
            kraw_tok = [tsw, tt1]
            t2_tok = tt2
            krot_tok = tkr
            tq = p.op("pool", lambda e, s=s: e.tensor_mul(
                ksq[0:HD, :], kdup[0:HD, s * SUB:(s + 1) * SUB],
                kdup[0:HD, s * SUB:(s + 1) * SUB]), reads=[tkr, ksq_tok])
            tn, pn_gate, tmx = _norm_update(p, C, ksq[0:HD, :], HD, p_n, nrow, nmax_k,
                                            [tmx], [tq], pn_gate)
            ksq_tok = tn
            tve = p.op("dve", lambda e: e.tensor_scalar(
                v_b[0:HD, :], p_v[0:HD, :], lw["qkvb_s"][0:HD, 9:10], None, ALU.add),
                reads=[tmv, lw["qkvb_tok"]])
            v_tok = [tve]
            for q in range(SUB // 128):
                kt = s * (SUB // 128) + q
                it = q % 2
                pt = p_t[it]
                ttr = p.op("pe", lambda e, q=q, pt=pt: e.transpose(
                    pt[:, 0:HD], v_b[0:HD, q * 128:(q + 1) * 128], C.ident_b[0:HD, 0:HD]),
                    reads=[tve, pt_gate[it]])
                tcp = p.op("dve", lambda e, kt=kt, pt=pt: e.tensor_copy(
                    vtm[:, kt * (HD + 1): kt * (HD + 1) + HD], pt[:, 0:HD]),
                    reads=[ttr, vtm_tok])
                pt_gate[it] = tcp
                v_tok.append(ttr)
        p.dma("pool", lambda e: e.dma_start(out=kdup[HD:128, :], in_=kdup[0:HD, :]),
              reads=[krot_tok])
        p.barrier()


def emit_kv_l1(p, st, C, lw, ckv, kfm, vtm, nmax_k, layer):
    nc = p.nc
    with ExitStack() as ph:
        wuk = _sb(ph, nc, "kv1_wuk", [128, 2 * D], BF16)
        wuv = _sb(ph, nc, "kv1_wuv", [128, 2 * D], BF16)
        v_bs = [_sb(ph, nc, f"kv1_vb{i}", [128, SUB], BF16) for i in range(2)]
        ksq = _sb(ph, nc, "kv1_ksq", [128, SUB], BF16)
        nrow = _sb(ph, nc, "kv1_nrow", [1, SUB], F32)
        p_k = [_ps(ph, nc, f"kv1_pk{i}", [128, SUB]) for i in range(2)]
        p_v = _ps(ph, nc, "kv1_pv", [128, SUB])
        p_t = [_ps(ph, nc, f"kv1_pt{i}", [128, 128], BF16) for i in range(2)]
        p_n = _ps(ph, nc, "kv1_pn", [1, SUB])

        tk = p.dma("pool", lambda e: e.dma_start(
            out=wuk[:, :].rearrange("p (kc m) -> p kc m", kc=2),
            in_=lw["Wuk"][:, :].rearrange("(kc p) m -> p kc m", p=128)))
        tv = p.dma("pool", lambda e: e.dma_start(
            out=wuv[:, :].rearrange("p (kc m) -> p kc m", kc=2),
            in_=lw["Wuv"][:, :].rearrange("(kc p) m -> p kc m", p=128)))
        tmx = p.op("dve", lambda e: e.memset(nmax_k[:, :], 0.0))
        vtm_tok = p.op("dve", lambda e: e.memset(vtm[:, :], 1.0))
        pt_gate = [None, None]
        vb_gate = [[], []]
        klast = [None, None]
        ksq_tok = None
        pn_gate = None
        v_tok = []
        for s in range(NSUB):
            for mc in range(KC):
                i = mc % 2
                pk = p_k[i]
                tm = None
                for kc in range(2):
                    tm = p.op("pe", lambda e, kc=kc, mc=mc, s=s, pk=pk: e.matmul(
                        pk[:, :], wuk[:, (kc * KC + mc) * 128:(kc * KC + mc + 1) * 128],
                        ckv[:, kc * T + s * SUB: kc * T + (s + 1) * SUB],
                        start=(kc == 0), stop=(kc == 1)), reads=[tk, klast[i]])
                te = p.op("dve", lambda e, mc=mc, s=s, pk=pk: e.tensor_copy(
                    kfm[:, mc * T + s * SUB: mc * T + (s + 1) * SUB], pk[:, :]), reads=[tm])
                klast[i] = te
                tq = p.op("pool", lambda e, mc=mc, s=s: e.tensor_mul(
                    ksq[:, :], kfm[:, mc * T + s * SUB: mc * T + (s + 1) * SUB],
                    kfm[:, mc * T + s * SUB: mc * T + (s + 1) * SUB]),
                    reads=[te, ksq_tok])
                tn, pn_gate, tmx = _norm_update(p, C, ksq[:, :], 128, p_n, nrow, nmax_k,
                                                [tmx], [tq], pn_gate)
                ksq_tok = tn
            for h in range(NH):
                v_b = v_bs[h % 2]
                tm = None
                for kc in range(2):
                    tm = p.op("pe", lambda e, kc=kc, h=h, s=s: e.matmul(
                        p_v[0:HD, :],
                        wuv[:, kc * KC * 128 + h * HD: kc * KC * 128 + (h + 1) * HD],
                        ckv[:, kc * T + s * SUB: kc * T + (s + 1) * SUB],
                        start=(kc == 0), stop=(kc == 1)), reads=[tv] + v_tok)
                tve = p.op("dve", lambda e, v_b=v_b: e.tensor_copy(v_b[0:HD, :],
                                                                   p_v[0:HD, :]),
                           reads=[tm] + vb_gate[h % 2])
                v_tok = [tve]
                vb_gate[h % 2] = []
                for q in range(SUB // 128):
                    kt = s * (SUB // 128) + q
                    slot = (kt * NH + h) * (HD + 1)
                    it = q % 2
                    pt = p_t[it]
                    ttr = p.op("pe", lambda e, q=q, pt=pt, v_b=v_b: e.transpose(
                        pt[:, 0:HD], v_b[0:HD, q * 128:(q + 1) * 128],
                        C.ident_b[0:HD, 0:HD]), reads=[tve, pt_gate[it]])
                    vb_gate[h % 2].append(ttr)
                    tcp = p.op("dve", lambda e, slot=slot, pt=pt: e.tensor_copy(
                        vtm[:, slot: slot + HD], pt[:, 0:HD]), reads=[ttr, vtm_tok])
                    pt_gate[it] = tcp
                    v_tok.append(ttr)
        p.barrier()

def emit_scores(p, st, C, layer, cfg):
    nc = p.nc
    with ExitStack() as ph:
        SB = Bufs()
        q_pan = _sb(ph, nc, f"sc_q{layer}", [128, KC * SUB], BF16)
        o_pan = _sb(ph, nc, f"sc_o{layer}", [128, KC * SUB], BF16)
        o_tmp = _sb(ph, nc, f"sc_ot{layer}", [128, SUB], BF16)
        pt_b = [_sb(ph, nc, f"sc_pt{layer}_{i}", [128, 2 * SUB], BF16) for i in range(2)]
        rbc = _sb(ph, nc, f"sc_rbc{layer}", [128, SUB], BF16)
        recb = _sb(ph, nc, f"sc_rec{layer}", [128, SUB], F32)
        recb2 = _sb(ph, nc, f"sc_rc2{layer}", [128, SUB], BF16)
        qsq = _sb(ph, nc, f"sc_qsq{layer}", [128, SUB], BF16)
        nrow = _sb(ph, nc, f"sc_nrow{layer}", [1, SUB], F32)
        nmax_q = _sb(ph, nc, f"sc_nq{layer}", [128, 4], F32)
        scs = _sb(ph, nc, f"sc_scs{layer}", [1, 8], F32)
        mkneg = _sb(ph, nc, f"sc_mk{layer}", [128, 1], F32)
        wo_t = [_sb(ph, nc, f"sc_wo{layer}_{i}", [128, KC * 128], BF16) for i in range(2)]
        if cfg["outb"] is not None:
            y1_t = [_sb(ph, nc, f"sc_y1{layer}_{i}", [128, SUB], F32) for i in range(2)]
        xf_t = [_sb(ph, nc, f"sc_xf{layer}_{i}", [128, SUB], F32) for i in range(2)]
        ff_t = [_sb(ph, nc, f"sc_ff{layer}_{i}", [128, SUB], F32) for i in range(2)]
        s1_t = [_sb(ph, nc, f"sc_s1{layer}_{i}", [128, SUB], F32) for i in range(2)]
        if cfg.get("rope_scratch"):
            SB.qraw = _sb(ph, nc, f"sc_qraw{layer}", [128, SUB], F32)
            SB.t1b = _sb(ph, nc, f"sc_t1{layer}", [128, SUB], F32)
            SB.t2b = _sb(ph, nc, f"sc_t2{layer}", [128, SUB], F32)
            SB.wq = [_sb(ph, nc, f"sc_wq{layer}_{i}", [128, KC * 128], BF16)
                     for i in range(2)]
        p_s = [_ps(ph, nc, f"sc_ps{layer}_{i}", [128, 2 * SUB]) for i in range(2)]
        p_oa = [_ps(ph, nc, f"sc_poa{layer}_{i}", [128, SUB]) for i in range(2)]
        p_misc = _ps(ph, nc, f"sc_pm{layer}", [128, SUB])
        p_y = _ps(ph, nc, f"sc_py{layer}", [128, SUB])
        p_n = p_y[0:1, 0:SUB]
        SB.p_s = p_s
        SB.ps_last = [None, None]

        fsc3 = cfg["fsc"][:, :].rearrange("(mc p) t -> p mc t", p=128)
        xin3 = cfg["resid_in"][:, :].rearrange("(mc p) t -> p mc t", p=128)
        xout3 = cfg["resid_out"][:, :].rearrange("(mc p) t -> p mc t", p=128)

        pt_last = [None, None]
        py_last = None
        wo_last = [None, None]
        xf_last = [None, None]
        ff_last = [None, None]
        s1_last = [None, None]
        y1_last = [None, None]
        oa_last = [None, None]
        pr_gate = None
        rc2_gate = None
        scr_gate = None
        otmp_last = None
        q_consumed = []
        o_consumed = []
        store_toks = []
        qsq_tok = None
        pn_gate = None
        kv_tok = cfg["kv_tok"]
        pending_epi = []
        tmax = p.op("dve", lambda e: e.memset(nmax_q[:, :], 0.0))
        norm_jobs = []

        def emit_norm_chunk(mc, qtok):
            nonlocal pn_gate, tmax, qsq_tok
            tq = p.op("dve", lambda e, mc=mc: e.tensor_mul(
                qsq[:, :], q_pan[:, mc * SUB:(mc + 1) * SUB],
                q_pan[:, mc * SUB:(mc + 1) * SUB]), reads=[qtok, qsq_tok])
            tn, pn_gate, tmax = _norm_update(p, C, qsq[:, :], 128, p_n, nrow, nmax_q,
                                             [tmax], [tq], [pn_gate, py_last])
            qsq_tok = tn
            return tq

        for qp in range(NQP):
            qtoks = cfg["build_q"](p, qp, q_pan, SB, q_consumed)
            q_consumed = []
            norm_jobs = [(mc, qtoks[mc]) for mc in range(KC)]
            if qp == 0:
                for mc, qtok in norm_jobs:
                    q_consumed.append(emit_norm_chunk(mc, qtok))
                norm_jobs = []
            tb1 = p.op("dve", lambda e: e.tensor_mul(scs[0:1, 0:1], nmax_q[0:1, 0:1],
                                                     cfg["nmax_k"][0:1, 0:1]),
                       reads=[tmax, kv_tok], drain=True)
            tb2a = p.op("act", lambda e: e.activation(scs[0:1, 1:2], scs[0:1, 0:1],
                                                      AF.Ln, bias=C.fconst[0:1, 1:2]),
                        reads=[tb1])
            tb2 = p.op("act", lambda e: e.activation(scs[0:1, 4:5], scs[0:1, 1:2],
                                                     AF.Exp, scale=0.5),
                       reads=[tb2a], drain=True)
            tb3 = p.op("dve", lambda e: e.tensor_scalar_mul(scs[0:1, 2:3], scs[0:1, 4:5],
                                                            -0.125), reads=[tb2], drain=True)
            tb4 = p.op("pe", lambda e: e.matmul(p_misc[:, 0:1], C.ones_fr[0:1, :],
                                                scs[0:1, 2:3], start=True, stop=True),
                       reads=[tb3, pn_gate])
            tbound = p.op("dve", lambda e: e.tensor_copy(mkneg[:, 0:1], p_misc[:, 0:1]),
                          reads=[tb4], drain=True)
            pr_gate = tbound
            nkt = 4 * qp + 4
            o_gate = list(o_consumed)
            o_consumed = []
            odd_dma_toks = []
            for c in range(KC):
                hA, hB = 2 * c, 2 * c + 1
                rhs_qA = q_pan[0:HD, c * SUB:(c + 1) * SUB]
                rhs_qB = q_pan[HD:128, c * SUB:(c + 1) * SUB]
                tS_last = None

                def emit_S(kt):
                    i2 = kt % 2
                    ps = p_s[i2]
                    gate = [kv_tok, qtoks[c], SB.ps_last[i2]]
                    tSA = p.op("pe", lambda e, hA=hA, kt=kt, ps=ps, rhs_qA=rhs_qA: e.matmul(
                        ps[:, 0:SUB], cfg["k_lhsT"](hA, kt), rhs_qA,
                        start=True, stop=True, tile_position=(0, 0)), reads=gate)
                    tSB = p.op("pe", lambda e, hB=hB, kt=kt, ps=ps, rhs_qB=rhs_qB: e.matmul(
                        ps[:, SUB:2 * SUB], cfg["k_lhsT"](hB, kt), rhs_qB,
                        start=True, stop=True, tile_position=(64, 0)), reads=gate)
                    return tSA, tSB

                def emit_exp(kt, tSA, tSB):
                    i2 = kt % 2
                    ps = p_s[i2]
                    texp = p.op("act", lambda e, ps=ps, i2=i2: e.activation(
                        pt_b[i2][:, :], ps[:, :], AF.Exp, bias=mkneg[:, 0:1], scale=0.125),
                        reads=[tSA, tSB, tbound, pt_last[i2]])
                    SB.ps_last[i2] = texp
                    tm0 = tm1 = texp
                    if kt >= nkt - 4:
                        j = kt - (nkt - 4)
                        tm0 = p.op("pool", lambda e, i2=i2, j=j: e.tensor_mul(
                            pt_b[i2][:, 0:SUB], pt_b[i2][:, 0:SUB],
                            C.cmask[:, j * SUB:(j + 1) * SUB]), reads=[texp])
                        tm1 = p.op("pool", lambda e, i2=i2, j=j: e.tensor_mul(
                            pt_b[i2][:, SUB:2 * SUB], pt_b[i2][:, SUB:2 * SUB],
                            C.cmask[:, j * SUB:(j + 1) * SUB]), reads=[texp])
                    return tm0, tm1

                def emit_PV(kt, tm0, tm1):
                    i2 = kt % 2
                    tPVA = p.op("pe", lambda e, hA=hA, kt=kt, i2=i2: e.matmul(
                        p_oa[0][0:HD + 1, :], cfg["vtm_slice"](hA, kt), pt_b[i2][:, 0:SUB],
                        start=(kt == 0), stop=(kt == nkt - 1)),
                        reads=[tm0, tm1, kv_tok] + ([oa_last[0]] if kt == 0 else []))
                    tPVB = p.op("pe", lambda e, hB=hB, kt=kt, i2=i2: e.matmul(
                        p_oa[1][0:HD + 1, :], cfg["vtm_slice"](hB, kt), pt_b[i2][:, SUB:2 * SUB],
                        start=(kt == 0), stop=(kt == nkt - 1)),
                        reads=([oa_last[1]] if kt == 0 else []))
                    pt_last[i2] = tPVB
                    return tPVB

                sx = emit_S(0)
                while pending_epi:
                    pending_epi.pop(0)()
                ex_prev = emit_exp(0, *sx)
                tS_last = sx[1]
                for kt in range(1, nkt):
                    sx = emit_S(kt)
                    tS_last = sx[1]
                    emit_PV(kt - 1, *ex_prev)
                    ex_prev = emit_exp(kt, *sx)
                emit_PV(nkt - 1, *ex_prev)
                q_consumed.append(tS_last)
                def make_epilogue(h, hc, hoa, tPVl=pt_last[(nkt - 1) % 2]):
                    poa = p_oa[hoa]
                    def run():
                        nonlocal scr_gate, rc2_gate, pr_gate, otmp_last
                        tln = p.op("act", lambda e: e.activation(
                            recb[HD:HD + 1, :], poa[HD:HD + 1, :], AF.Ln),
                            reads=[tPVl, scr_gate], drain=True)
                        scr_gate = tln
                        trc2 = p.op("act", lambda e: e.activation(
                            recb2[HD:HD + 1, :], recb[HD:HD + 1, :], AF.Exp, scale=-1.0),
                            reads=[tln, rc2_gate], drain=True)
                        tbc = p.op("pe", lambda e: e.matmul(
                            p_misc[:, :], C.ones_br[HD:HD + 1, 0:128],
                            recb2[HD:HD + 1, :], start=True, stop=True),
                            reads=[trc2, pr_gate])
                        rc2_gate = tbc
                        trbc = p.op("dve", lambda e: e.tensor_copy(rbc[:, :], p_misc[:, :]),
                                    reads=[tbc])
                        pr_gate = trbc
                        if h % 2 == 0:
                            dest = o_pan[0:HD, hc * SUB:(hc + 1) * SUB]
                            tnrm = p.op("dve", lambda e: e.tensor_mul(
                                dest, poa[0:HD, :], rbc[0:HD, :]),
                                reads=[trbc] + o_gate, drain=True)
                        else:
                            tnrm = p.op("dve", lambda e: e.tensor_mul(
                                o_tmp[0:HD, :], poa[0:HD, :], rbc[0:HD, :]),
                                reads=[trbc, otmp_last], drain=True)
                            todd = p.dma("pool", lambda e: e.dma_start(
                                out=o_pan[HD:128, hc * SUB:(hc + 1) * SUB],
                                in_=o_tmp[0:HD, :]), reads=[tnrm] + o_gate)
                            otmp_last = todd
                            odd_dma_toks.append(todd)
                        oa_last[hoa] = tnrm
                    return run
                pending_epi.append(make_epilogue(hA, c, 0))
                pending_epi.append(make_epilogue(hB, c, 1))
                if norm_jobs and c < KC:
                    mc_, qtok_ = norm_jobs[c]
                    q_consumed.append(emit_norm_chunk(mc_, qtok_))
                    if c == KC - 1:
                        norm_jobs = []

            while pending_epi:
                pending_epi.pop(0)()
            opan_toks = [oa_last[0], oa_last[1]] + odd_dma_toks[-1:]
            for mc in range(KC):
                i = mc % 2
                tw = p.dma("pool", lambda e, mc=mc, i=i: e.dma_start(
                    out=wo_t[i][:, :].rearrange("p (kc m) -> p kc m", kc=KC),
                    in_=cfg["wo"](mc)), reads=[wo_last[i]])
                tx = p.dma("sp", lambda e, mc=mc, qp=qp, i=i: e.dma_start(
                    out=xf_t[i][:, :], in_=xin3[:, mc, qp * SUB:(qp + 1) * SUB]),
                    reads=[xf_last[i]])
                tf = p.dma("sp", lambda e, mc=mc, qp=qp, i=i: e.dma_start(
                    out=ff_t[i][:, :], in_=fsc3[:, mc, qp * SUB:(qp + 1) * SUB]),
                    reads=[ff_last[i], cfg["fsc_tok"]])
                ts1 = p.op("dve", lambda e, i=i: e.tensor_add(
                    s1_t[i][:, :], xf_t[i][:, :], ff_t[i][:, :]),
                    reads=[tx, tf, s1_last[i]])
                xf_last[i] = ts1
                ff_last[i] = ts1
                tmy = None
                for kc in range(KC):
                    tmy = p.op("pe", lambda e, kc=kc, mc=mc, i=i: e.matmul(
                        p_y[:, :], wo_t[i][:, kc * 128:(kc + 1) * 128],
                        o_pan[:, kc * SUB:(kc + 1) * SUB],
                        start=(kc == 0), stop=(kc == KC - 1)),
                        reads=[tw, py_last, pn_gate] + opan_toks)
                o_consumed.append(tmy)
                wo_last[i] = tmy
                if cfg["outb"] is not None:
                    bbuf, btok = cfg["outb"]
                    ty1 = p.op("dve", lambda e, mc=mc, i=i, bbuf=bbuf: e.tensor_scalar(
                        y1_t[i][:, :], p_y[:, :], bbuf[0:128, mc:mc + 1], None, ALU.add),
                        reads=[tmy, btok, y1_last[i]])
                    py_last = ty1
                    txo = p.op("dve", lambda e, i=i: e.tensor_add(
                        s1_t[i][:, :], y1_t[i][:, :], s1_t[i][:, :]),
                        reads=[ty1, ts1], drain=True)
                    y1_last[i] = txo
                else:
                    txo = p.op("dve", lambda e, i=i: e.tensor_add(
                        s1_t[i][:, :], p_y[:, :], s1_t[i][:, :]),
                        reads=[tmy, ts1], drain=True)
                    py_last = txo
                tst = p.dma("sp", lambda e, mc=mc, qp=qp, i=i: e.dma_start(
                    out=xout3[:, mc, qp * SUB:(qp + 1) * SUB], in_=s1_t[i][:, :]),
                    reads=[txo])
                s1_last[i] = tst
                store_toks.append(tst)
        p.barrier()
        return store_toks


class _Stop(Exception):
    pass


def build_program(taps=(), upto="full"):
    nc = bass.Bass("TRN2", target_bir_lowering=False)
    dr = {}
    dr["xT"] = nc.dram_tensor("xT", [D, T], F32, kind="ExternalInput").ap()
    for n, sh in [("c_cos", [128, T]), ("c_sin", [128, T]), ("c_p128", [128, 128]),
                  ("c_cmask", [128, 4 * SUB]), ("c_ident", [128, 128]),
                  ("c_ohsel", [NE, NE * 128])]:
        dr[n] = nc.dram_tensor(n, sh, F32, kind="ExternalInput").ap()
    wshapes = dict(
        l0_qkvT=[D, 1152], l0_qkvb=[128, 10], l0_outT=[D, D], l0_outb=[128, 8],
        l0_WgT=[D, NE], l0_bg=[NE, 1], l0_w1T=[NE, D, 2 * D], l0_w2T=[NE, D, D],
        l1_Wdq=[D, 256], l1_Wuq=[256, D], l1_Wdkv=[D, 256], l1_Wuk=[256, D],
        l1_Wuv=[256, D], l1_Wo=[D, D], l1_router=[D, NE], l1_fc1=[NE, D, 2 * D],
        l1_fc2=[NE, D, D], l1_gate=[D, 2 * D], l1_up=[D, 2 * D], l1_down=[2 * D, D])
    for n, sh in wshapes.items():
        dr[n] = nc.dram_tensor(n, sh, F32, kind="ExternalInput").ap()
    resid = nc.dram_tensor("resid", [D, T], F32,
                           kind="ExternalOutput" if "resid" in taps else "Internal").ap()
    fsc = nc.dram_tensor("fsc", [D, T], F32,
                         kind="ExternalOutput" if "fsc" in taps else "Internal").ap()
    outT = nc.dram_tensor("outT", [D, T], F32, kind="ExternalOutput").ap()
    tap_dr = {}
    for tn in taps:
        if tn in ("resid", "fsc"):
            continue
        shape = {"xn": [128, KC * T], "call": [NE, T], "kdup": [128, T],
                 "kfm": [128, KC * T], "vtm0": [128, NKT * (HD + 1)],
                 "vtm1": [128, NKT * NH * (HD + 1)], "cq": [128, 2 * T],
                 "ckv": [128, 2 * T]}[tn.rstrip("01") if tn[-1] in "01" and not tn.startswith("vtm") else tn]
        tap_dr[tn] = nc.dram_tensor(f"tap_{tn}", shape, F32, kind="ExternalOutput").ap()

    try:
        nc = _assemble(nc, dr, resid, fsc, outT, tap_dr, taps, upto)
    except _Stop:
        pass
    return nc


def _assemble(nc, dr, resid, fsc, outT, tap_dr, taps, upto):
    with ExitStack() as st:
        p = Prog(nc, st)
        C = emit_consts(p, st, dr)

        def ck(name):
            if upto == name:
                p.barrier()
                p.emit()
                raise _Stop

        def tap(name, buf, cols):
            if name in tap_dr:
                with ExitStack() as th:
                    tb = _sb(th, nc, f"tapb_{name}", [128, 2048], F32)
                    np_ = buf.shape[0]
                    for c0 in range(0, cols, 2048):
                        w = min(2048, cols - c0)
                        tt = p.op("dve", lambda e, c0=c0, w=w, np_=np_: e.tensor_copy(
                            tb[0:np_, 0:w], buf[0:np_, c0:c0 + w]))
                        p.dma("sp", lambda e, c0=c0, w=w, np_=np_: e.dma_start(
                            out=tap_dr[name][0:np_, c0:c0 + w],
                            in_=tb[0:np_, 0:w]), reads=[tt])
                        p.barrier()

        if True:
            with ExitStack() as l0:
                qkvb_s = _sb(l0, nc, "qkvb0", [128, 10], F32)
                outb_s = _sb(l0, nc, "outb0", [128, 8], F32)
                xn = _sb(l0, nc, "xn0", [128, KC * T], BF16)
                tqb = p.dma("sp", lambda e: e.dma_start(out=qkvb_s[:, :], in_=dr["l0_qkvb"][:, :]))
                tob = p.dma("sp", lambda e: e.dma_start(out=outb_s[:, :], in_=dr["l0_outb"][:, :]))
                p.barrier()

                emit_ln(p, st, C, dr["xT"], xn, 0)
                tap("xn0", xn, KC * T)
                ck("ln0")

                with ExitStack() as lf:
                    c_all = _sb(lf, nc, "call0", [NE, T], BF16)
                    emit_routing(p, st, C, xn, c_all, dr["l0_WgT"], dr["l0_bg"], 0)
                    tap("call0", c_all, T)
                    ck("rt0")
                    w1T = dr["l0_w1T"]
                    w2T = dr["l0_w2T"]
                    experts = []
                    for e_ in range(NE):
                        experts.append(dict(
                            w1a=lambda ocp, e_=e_: _chunk3(w1T[e_])[:, :, ocp * 128:(ocp + 1) * 128],
                            w1b=lambda ocp, e_=e_: _chunk3(w1T[e_])[:, :, D + ocp * 128: D + (ocp + 1) * 128],
                            w2=lambda mc, e_=e_: _chunk3(w2T[e_])[:, :, mc * 128:(mc + 1) * 128],
                            use_c=True, e_idx=e_, c_tok=None))
                    emit_moe(p, st, C, xn, c_all, fsc, 0, experts)
                tap("fsc", None, 0)
                ck("ffn0")

                cosb = _sb(l0, nc, "cos0", [128, T], F32)
                sinb = _sb(l0, nc, "sin0", [128, T], F32)
                p128f = _sb(l0, nc, "p128f", [128, 128], F32)
                p.dma("sp", lambda e: e.dma_start(out=cosb[:, :], in_=dr["c_cos"][:, :]))
                p.dma("sp", lambda e: e.dma_start(out=sinb[:, :], in_=dr["c_sin"][:, :]))
                p.dma("sp", lambda e: e.dma_start(out=p128f[:, :], in_=dr["c_p128"][:, :]))
                p.barrier()
                kdup = _sb(l0, nc, "kdup0", [128, T], BF16, side="right")
                vtm0 = _sb(l0, nc, "vtm0", [128, NKT * (HD + 1)], BF16, side="right")
                nmax_k = _sb(l0, nc, "nmk0", [128, 4], F32, side="right")
                lw0 = dict(qkvT=dr["l0_qkvT"], qkvb_s=qkvb_s, qkvb_tok=tqb,
                           p128f=p128f, cos=cosb, sin=sinb)
                emit_kv_l0(p, st, C, lw0, xn, kdup, vtm0, nmax_k, 0)
                tap("kdup", kdup, T)
                ck("kv0")

                qkvT3 = _chunk3(dr["l0_qkvT"][:, :])

                def build_q0(p, qp, q_pan, SB, q_consumed, wq_last=[None, None],
                             qraw_last=[], t1_last=[], t2_last=[]):
                    toks = []
                    for mc in range(KC):
                        i = mc % 2
                        tw = p.dma("pool", lambda e, mc=mc, i=i: e.dma_start(
                            out=SB.wq[i][:, :].rearrange("p (kc m) -> p kc m", kc=KC),
                            in_=qkvT3[:, :, mc * 128:(mc + 1) * 128]), reads=[wq_last[i]])
                        tm = None
                        for kc in range(KC):
                            tm = p.op("pe", lambda e, kc=kc, qp=qp, i=i: e.matmul(
                                SB.p_s[0][:, 0:SUB], SB.wq[i][:, kc * 128:(kc + 1) * 128],
                                xn[:, kc * T + qp * SUB: kc * T + (qp + 1) * SUB],
                                start=(kc == 0), stop=(kc == KC - 1)),
                                reads=[tw, SB.ps_last[0]])
                        wq_last[i] = tm
                        tev = p.op("dve", lambda e, mc=mc: e.tensor_scalar(
                            SB.qraw[:, :], SB.p_s[0][:, 0:SUB],
                            qkvb_s[0:128, mc:mc + 1], None, ALU.add),
                            reads=[tm, tqb] + qraw_last)
                        SB.ps_last[0] = tev
                        tsw = p.op("pe", lambda e: e.matmul(
                            SB.p_s[1][:, 0:SUB], p128f[:, :], SB.qraw[:, :],
                            start=True, stop=True), reads=[tev, SB.ps_last[1]])
                        tt1 = p.op("dve", lambda e, qp=qp: e.tensor_mul(
                            SB.t1b[:, :], SB.qraw[:, :],
                            cosb[:, qp * SUB:(qp + 1) * SUB]),
                            reads=[tev] + t1_last, drain=True)
                        tt2 = p.op("dve", lambda e, qp=qp: e.tensor_mul(
                            SB.t2b[:, :], SB.p_s[1][:, 0:SUB],
                            sinb[:, qp * SUB:(qp + 1) * SUB]),
                            reads=[tsw] + t2_last, drain=True)
                        SB.ps_last[1] = tt2
                        tq = p.op("dve", lambda e, mc=mc: e.tensor_add(
                            q_pan[:, mc * SUB:(mc + 1) * SUB], SB.t1b[:, :], SB.t2b[:, :]),
                            reads=[tt1, tt2] + q_consumed, drain=True)
                        qraw_last[:] = [tsw, tt1]
                        t1_last[:] = [tq]
                        t2_last[:] = [tq]
                        toks.append(tq)
                    return toks

                cfg0 = dict(
                    build_q=build_q0,
                    k_lhsT=lambda h, kt: kdup[64 * (h & 1):64 * (h & 1) + 64,
                                              kt * 128:(kt + 1) * 128],
                    vtm_slice=lambda h, kt: vtm0[:, kt * (HD + 1): (kt + 1) * (HD + 1)],
                    nmax_k=nmax_k, kv_tok=("dgp", p.counts["dgp"]),
                    wo=lambda mc: _chunk3(dr["l0_outT"][:, :])[:, :, mc * 128:(mc + 1) * 128],
                    outb=(outb_s, tob), resid_in=dr["xT"], resid_out=resid, fsc=fsc,
                    fsc_tok=("dgp", p.counts["dgp"]), rope_scratch=True)
                emit_scores(p, st, C, 0, cfg0)
                tap("resid", None, 0)
                ck("l0")

            with ExitStack() as l1:
                cq = _sb(l1, nc, "cq1", [128, 2 * T], BF16)
                with ExitStack() as ly:
                    ckv = _sb(ly, nc, "ckv1", [128, 2 * T], BF16)
                    with ExitStack() as lx:
                        xn1 = _sb(lx, nc, "xn1", [128, KC * T], BF16)
                        emit_ln(p, st, C, resid, xn1, 1)
                        tap("xn1", xn1, KC * T)
                        ck("ln1")
                        emit_lowrank_proj(p, st, C, xn1, dr["l1_Wdq"], cq, 256, "cq")
                        emit_lowrank_proj(p, st, C, xn1, dr["l1_Wdkv"], ckv, 256, "ckv")
                        tap("cq", cq, 2 * T)
                        tap("ckv", ckv, 2 * T)
                        ck("proj1")

                        with ExitStack() as lf:
                            c_all = _sb(lf, nc, "call1", [NE, T], BF16)
                            emit_routing(p, st, C, xn1, c_all, dr["l1_router"], None, 1)
                            tap("call1", c_all, T)
                            ck("rt1")
                            fc1 = dr["l1_fc1"]
                            fc2 = dr["l1_fc2"]
                            experts = []
                            for e_ in range(NE):
                                experts.append(dict(
                                    w1a=lambda ocp, e_=e_: _chunk3(fc1[e_])[:, :, ocp * 128:(ocp + 1) * 128],
                                    w1b=lambda ocp, e_=e_: _chunk3(fc1[e_])[:, :, D + ocp * 128: D + (ocp + 1) * 128],
                                    w2=lambda mc, e_=e_: _chunk3(fc2[e_])[:, :, mc * 128:(mc + 1) * 128],
                                    use_c=True, e_idx=e_, c_tok=None))
                            for sh in range(2):
                                experts.append(dict(
                                    w1a=lambda ocp, sh=sh: _chunk3(dr["l1_gate"][:, :])[:, :, sh * D + ocp * 128: sh * D + (ocp + 1) * 128],
                                    w1b=lambda ocp, sh=sh: _chunk3(dr["l1_up"][:, :])[:, :, sh * D + ocp * 128: sh * D + (ocp + 1) * 128],
                                    w2=lambda mc, sh=sh: dr["l1_down"][:, :].rearrange(
                                        "(kc p) m -> p kc m", p=128)[:, sh * KC:(sh + 1) * KC,
                                                                     mc * 128:(mc + 1) * 128],
                                    use_c=False, e_idx=0, c_tok=None))
                            emit_moe(p, st, C, xn1, c_all, fsc, 1, experts)
                    ck("ffn1")
                    kfm = _sb(l1, nc, "kfm1", [128, KC * T], BF16, side="right")
                    vtm1 = _sb(l1, nc, "vtm1", [128, NKT * NH * (HD + 1)], BF16, side="right")
                    nmax_k1 = _sb(l1, nc, "nmk1", [128, 4], F32, side="right")
                    lw1 = dict(Wuk=dr["l1_Wuk"], Wuv=dr["l1_Wuv"])
                    emit_kv_l1(p, st, C, lw1, ckv, kfm, vtm1, nmax_k1, 1)
                    tap("kfm", kfm, KC * T)
                    ck("kv1")

                wuq = _sb(l1, nc, "wuq1", [128, 2 * D], BF16, side="right")
                twq = p.dma("pool", lambda e: e.dma_start(
                    out=wuq[:, :].rearrange("p (kc m) -> p kc m", kc=2),
                    in_=dr["l1_Wuq"][:, :].rearrange("(kc p) m -> p kc m", p=128)))

                def build_q1(p, qp, q_pan, SB, q_consumed):
                    toks = []
                    for mc in range(KC):
                        tm = None
                        for kc in range(2):
                            tm = p.op("pe", lambda e, kc=kc, mc=mc, qp=qp: e.matmul(
                                SB.p_s[0][:, 0:SUB],
                                wuq[:, (kc * KC + mc) * 128:(kc * KC + mc + 1) * 128],
                                cq[:, kc * T + qp * SUB: kc * T + (qp + 1) * SUB],
                                start=(kc == 0), stop=(kc == 1)),
                                reads=[twq, SB.ps_last[0]])
                        tq = p.op("dve", lambda e, mc=mc: e.tensor_copy(
                            q_pan[:, mc * SUB:(mc + 1) * SUB], SB.p_s[0][:, 0:SUB]),
                            reads=[tm] + q_consumed)
                        SB.ps_last[0] = tq
                        toks.append(tq)
                    return toks

                cfg1 = dict(
                    build_q=build_q1,
                    k_lhsT=lambda h, kt: kfm[64 * (h & 1):64 * (h & 1) + 64,
                                             (h >> 1) * T + kt * 128:(h >> 1) * T + (kt + 1) * 128],
                    vtm_slice=lambda h, kt: vtm1[:, (kt * NH + h) * (HD + 1):
                                                 (kt * NH + h) * (HD + 1) + HD + 1],
                    nmax_k=nmax_k1, kv_tok=("dve", p.counts["dve"]),
                    wo=lambda mc: _chunk3(dr["l1_Wo"][:, :])[:, :, mc * 128:(mc + 1) * 128],
                    outb=None, resid_in=resid, resid_out=outT, fsc=fsc,
                    fsc_tok=("dgp", p.counts["dgp"]))
                emit_scores(p, st, C, 1, cfg1)

        p.barrier()
        p.emit()
    return nc


_CACHE = {}


def kernel(**inputs):
    consts = _host_consts()
    w = _prep_weights(inputs)
    x = np.asarray(inputs["L0_attn_x"], dtype=np.float32)
    key = "full"
    if key not in _CACHE:
        _CACHE[key] = build_program()
    nc = _CACHE[key]
    in_maps = []
    for b in range(8):
        m = dict(consts)
        m.update(w)
        m["xT"] = np.ascontiguousarray(x[b].T)
        in_maps.append(m)
    res = run_bass_kernel_spmd(nc, in_maps, core_ids=list(range(8)))
    out = np.empty((8, T, D), np.float32)
    for b in range(8):
        out[b] = res.results[b]["outT"].T
    return out


# version 10
# speedup vs baseline: 1.2428x; 1.0843x over previous
import numpy as np
import ml_dtypes
from contextlib import ExitStack

import concourse.bass as bass
import concourse.mybir as mybir
from concourse.bass_utils import run_bass_kernel_spmd

F32 = mybir.dt.float32
BF16 = mybir.dt.bfloat16
AF = mybir.ActivationFunctionType
ALU = mybir.AluOpType
AX = mybir.AxisListType
BF = ml_dtypes.bfloat16

T = 4096
D = 1024
KC = 8
SUB = 512
NSUB = T // SUB
NH = 16
HD = 64
NE = 8
NQP = T // SUB
NKT = T // 128
ROPE_THETA = 10000.0

DMA_CH = {"sp": "dsp", "pool": "dgp"}
ENG_SECT = {"pe": "tensor", "act": "scalar", "dve": "vector", "pool": "gpsimd", "sp": "sync"}
CHANNELS = ("pe", "act", "dve", "pool", "sp", "dsp", "dgp")


class Prog:

    def __init__(self, nc, stack):
        self.nc = nc
        self.sems = {ch: stack.enter_context(nc.semaphore(f"s_{ch}")) for ch in CHANNELS}
        self.counts = {ch: 0 for ch in CHANNELS}
        self.known = {e: {ch: 0 for ch in CHANNELS} for e in ENG_SECT}
        self.thunks = {e: [] for e in ENG_SECT}

    def _waits_for(self, eng, reads):
        merged = {}
        for tok in reads:
            if tok is None:
                continue
            ch, v = tok
            if v > merged.get(ch, 0):
                merged[ch] = v
        waits = []
        for ch, v in merged.items():
            if self.known[eng][ch] < v:
                waits.append((self.sems[ch], v))
                self.known[eng][ch] = v
        return waits

    def op(self, eng, fn, reads=(), drain=False):
        waits = self._waits_for(eng, reads)
        self.counts[eng] += 1
        tok = (eng, self.counts[eng])
        sem = self.sems[eng]

        def thunk(e, waits=waits, fn=fn, drain=drain, sem=sem):
            for s, v in waits:
                e.wait_ge(s, v)
            if drain:
                e.drain()
            fn(e).then_inc(sem, 1)

        self.thunks[eng].append(thunk)
        return tok

    def dma(self, eng, fn, reads=()):
        waits = self._waits_for(eng, reads)
        ch = DMA_CH[eng]
        self.counts[ch] += 16
        tok = (ch, self.counts[ch])
        sem = self.sems[ch]

        def thunk(e, waits=waits, fn=fn, sem=sem):
            for s, v in waits:
                e.wait_ge(s, v)
            fn(e).then_inc(sem, 16)

        self.thunks[eng].append(thunk)
        return tok

    def barrier(self):
        toks = [(ch, self.counts[ch]) for ch in CHANNELS]
        for eng in ENG_SECT:
            waits = self._waits_for(eng, toks)
            if waits:
                def thunk(e, waits=waits):
                    for s, v in waits:
                        e.wait_ge(s, v)
                self.thunks[eng].append(thunk)

    def emit(self):
        with self.nc.Block() as blk:
            for eng, sect in ENG_SECT.items():
                getattr(blk, sect)(self._make_runner(eng))

    def _make_runner(self, eng):
        thunks = self.thunks[eng]

        def run(e):
            for t in thunks:
                t(e)
        return run



def _host_consts():
    i = np.arange(HD // 2, dtype=np.float64)
    inv_freq = 1.0 / ROPE_THETA ** (2 * i / HD)
    pos = np.arange(T, dtype=np.float64)
    ang = pos[None, :] * inv_freq[:, None]
    cos32 = np.cos(ang)
    sin32 = np.sin(ang)
    cos64 = np.repeat(cos32, 2, axis=0)
    sin64s = np.empty((HD, T))
    sin64s[0::2] = -sin32
    sin64s[1::2] = sin32
    cos128 = np.concatenate([cos64, cos64], 0).astype(np.float32)
    sin128 = np.concatenate([sin64s, sin64s], 0).astype(np.float32)

    p64 = np.zeros((HD, HD), np.float32)
    for k in range(0, HD, 2):
        p64[k, k + 1] = 1.0
        p64[k + 1, k] = 1.0
    p128 = np.zeros((128, 128), np.float32)
    p128[:64, :64] = p64
    p128[64:, 64:] = p64

    cmask = np.zeros((128, 4 * SUB), np.float32)
    dk = np.arange(128)[:, None]
    qr = np.arange(SUB)[None, :]
    for j in range(4):
        cmask[:, j * SUB:(j + 1) * SUB] = (qr >= 128 * j + dk).astype(np.float32)

    ident = np.eye(128, dtype=np.float32)
    ohsel = np.zeros((NE, NE * 128), np.float32)
    for e in range(NE):
        ohsel[e, e * 128:(e + 1) * 128] = 1.0
    return dict(c_cos=cos128, c_sin=sin128, c_p128=p128, c_cmask=cmask,
                c_ident=ident, c_ohsel=ohsel)


def _prep_weights(inp):
    c = lambda a: np.ascontiguousarray(np.asarray(a, dtype=np.float32))
    w = {}
    w["l0_qkvT"] = c(np.asarray(inp["L0_attn_qkv_w"]).T)
    qb = np.asarray(inp["L0_attn_qkv_b"])
    qkvb = np.zeros((128, 10), np.float32)
    qkvb[:, 0:9] = np.asarray(qb).reshape(9, 128).T
    qkvb[0:64, 9] = qb[1088:1152]
    qkvb[64:128, 8] = 0.0
    w["l0_qkvb"] = c(qkvb)
    w["l0_outT"] = c(np.asarray(inp["L0_attn_out_w"]).T)
    w["l0_outb"] = c(np.asarray(inp["L0_attn_out_b"]).reshape(8, 128).T)
    w["l0_WgT"] = c(np.asarray(inp["L0_ffn_Wg"]).T)
    w["l0_bg"] = c(np.asarray(inp["L0_ffn_bg"]).reshape(8, 1))
    w["l0_w1T"] = c(np.asarray(inp["L0_ffn_w1"]).transpose(0, 2, 1))
    w["l0_w2T"] = c(np.asarray(inp["L0_ffn_w2"]).transpose(0, 2, 1))
    w["l1_Wdq"] = c(inp["L1_attn_W_dq"])
    w["l1_Wuq"] = c(inp["L1_attn_W_uq"])
    w["l1_Wdkv"] = c(inp["L1_attn_W_dkv"])
    w["l1_Wuk"] = c(inp["L1_attn_W_uk"])
    w["l1_Wuv"] = c(inp["L1_attn_W_uv"])
    w["l1_Wo"] = c(inp["L1_attn_W_o"])
    w["l1_router"] = c(inp["L1_ffn_w_router"])
    w["l1_fc1"] = c(inp["L1_ffn_fc1_w"])
    w["l1_fc2"] = c(inp["L1_ffn_fc2_w"])
    w["l1_gate"] = c(inp["L1_ffn_gate_w"])
    w["l1_up"] = c(inp["L1_ffn_up_w"])
    w["l1_down"] = c(inp["L1_ffn_down_w"])
    return w


class Bufs:
    pass


def _sb(stack, nc, name, shape, dtype, side=None):
    kw = {} if side is None else {"side": side}
    return stack.enter_context(nc.sbuf_tensor(name, shape, dtype, **kw))


def _ps(stack, nc, name, shape, dtype=F32):
    return stack.enter_context(nc.psum_tensor(name, shape, dtype))


def _chunk3(dram_ap):
    return dram_ap.rearrange("(kc p) m -> p kc m", p=128)



def emit_consts(p, st, dr):
    nc = p.nc
    C = Bufs()
    C.ident_f = _sb(st, nc, "ident_f", [128, 128], F32, side="right")
    C.ident_b = _sb(st, nc, "ident_b", [128, 128], BF16, side="right")
    C.cmask = _sb(st, nc, "cmask", [128, 4 * SUB], BF16, side="right")
    C.ohsel = _sb(st, nc, "ohsel", [NE, NE * 128], BF16, side="right")
    C.ones_b = _sb(st, nc, "ones_b", [128, 1], BF16, side="right")
    C.ones_fr = _sb(st, nc, "ones_fr", [128, 128], F32, side="right")
    C.ones_br = _sb(st, nc, "ones_br", [128, 128], BF16, side="right")
    C.fconst = _sb(st, nc, "fconst", [1, 2], F32, side="right")
    p.op("dve", lambda e: e.memset(C.fconst[0:1, 0:1], 1e-5))
    p.op("dve", lambda e: e.memset(C.fconst[0:1, 1:2], 1e-30))
    p.dma("sp", lambda e: e.dma_start(out=C.ident_f[:, :], in_=dr["c_ident"][:, :]))
    p.dma("pool", lambda e: e.dma_start(out=C.cmask[:, :], in_=dr["c_cmask"][:, :]))
    p.dma("pool", lambda e: e.dma_start(out=C.ident_b[:, :], in_=dr["c_ident"][:, :]))
    p.dma("pool", lambda e: e.dma_start(out=C.ohsel[:, :], in_=dr["c_ohsel"][:, :]))
    p.op("dve", lambda e: e.memset(C.ones_b[:, :], 1.0))
    p.op("dve", lambda e: e.memset(C.ones_fr[:, :], 1.0))
    p.op("dve", lambda e: e.memset(C.ones_br[:, :], 1.0))
    p.barrier()
    return C

def emit_ln(p, st, C, src_dram, xn, layer):
    nc = p.nc
    with ExitStack() as ph:
        x_f = [_sb(ph, nc, f"ln_xf{layer}_{i}", [128, KC * SUB], F32) for i in range(2)]
        x_b = [_sb(ph, nc, f"ln_xb{layer}_{i}", [128, KC * SUB], BF16) for i in range(2)]
        xsq = _sb(ph, nc, f"ln_xsq{layer}", [128, KC * SUB], BF16)
        tmp = _sb(ph, nc, f"ln_tmp{layer}", [128, KC * SUB], F32)
        rows = [_sb(ph, nc, f"ln_rows{layer}_{i}", [1, 6 * SUB], F32) for i in range(2)]
        p_sum = [_ps(ph, nc, f"lnps{layer}_{i}", [1, SUB]) for i in range(2)]
        p_sq = [_ps(ph, nc, f"lnpq{layer}_{i}", [1, SUB]) for i in range(2)]
        p_M = [_ps(ph, nc, f"lnpM{layer}_{i}", [128, SUB]) for i in range(2)]
        p_R = [_ps(ph, nc, f"lnpR{layer}_{i}", [128, SUB]) for i in range(2)]

        src3 = src_dram[:, :].rearrange("(kc p) t -> p kc t", p=128)
        xf_gate = [None, None]
        xb_gate = [[], []]
        xsq_gate = None
        tmp_gate = None
        rows_gate = [None, None]
        psum_gate = [None, None]
        psq_gate = [None, None]
        pM_gate = [None, None]
        pR_gate = [None, None]
        for s in range(NSUB):
            i = s % 2
            tl = p.dma("sp", lambda e, s=s, i=i: e.dma_start(
                out=x_f[i][:, :].rearrange("p (kc t) -> p kc t", kc=KC),
                in_=src3[:, :, s * SUB:(s + 1) * SUB]), reads=[xf_gate[i]])
            tlb = p.dma("pool", lambda e, s=s, i=i: e.dma_start(
                out=x_b[i][:, :].rearrange("p (kc t) -> p kc t", kc=KC),
                in_=src3[:, :, s * SUB:(s + 1) * SUB]), reads=xb_gate[i])
            tsq = []
            for kc in range(KC):
                tsq.append(p.op("act", lambda e, kc=kc, i=i: e.activation(
                    xsq[:, kc * SUB:(kc + 1) * SUB], x_b[i][:, kc * SUB:(kc + 1) * SUB],
                    AF.Square), reads=[tlb, xsq_gate]))
            tm = tv = None
            for kc in range(KC):
                tm = p.op("pe", lambda e, kc=kc, i=i: e.matmul(
                    p_sum[i][:, :], C.ones_b[:, :], x_b[i][:, kc * SUB:(kc + 1) * SUB],
                    start=(kc == 0), stop=(kc == KC - 1)), reads=[tlb, psum_gate[i]])
            for kc in range(KC):
                tv = p.op("pe", lambda e, kc=kc, i=i: e.matmul(
                    p_sq[i][:, :], C.ones_b[:, :], xsq[:, kc * SUB:(kc + 1) * SUB],
                    start=(kc == 0), stop=(kc == KC - 1)), reads=[tsq[kc], psq_gate[i]])
            xsq_gate = tv
            R = lambda j, i=i: rows[i][0:1, j * SUB:(j + 1) * SUB]
            t1 = p.op("dve", lambda e, R=R, i=i: e.tensor_scalar_mul(R(0), p_sum[i][:, :], 1.0 / D),
                      reads=[tm, rows_gate[i]])
            t2 = p.op("dve", lambda e, R=R, i=i: e.tensor_scalar_mul(R(1), p_sq[i][:, :], 1.0 / D),
                      reads=[tv], drain=True)
            psum_gate[i] = t1
            psq_gate[i] = t2
            t3 = p.op("dve", lambda e, R=R: e.tensor_mul(R(2), R(0), R(0)),
                      reads=[t1], drain=True)
            t4 = p.op("dve", lambda e, R=R: e.tensor_sub(R(3), R(1), R(2)),
                      reads=[t2, t3], drain=True)
            t5 = p.op("act", lambda e, R=R: e.activation(R(4), R(3), AF.Ln,
                                                         bias=C.fconst[0:1, 0:1]),
                      reads=[t4])
            t6 = p.op("act", lambda e, R=R: e.activation(R(5), R(4), AF.Exp, scale=-0.5),
                      reads=[t5], drain=True)
            tM = p.op("pe", lambda e, R=R, i=i: e.matmul(p_M[i][:, :], C.ones_fr[0:1, :],
                                                         R(0), start=True, stop=True),
                      reads=[t1, t3, pM_gate[i]])
            tR = p.op("pe", lambda e, R=R, i=i: e.matmul(p_R[i][:, :], C.ones_fr[0:1, :],
                                                         R(5), start=True, stop=True),
                      reads=[t6, pR_gate[i]])
            rows_gate[i] = tR
            tsub = []
            for kc in range(KC):
                tsub.append(p.op("dve", lambda e, kc=kc, i=i: e.tensor_sub(
                    tmp[:, kc * SUB:(kc + 1) * SUB], x_f[i][:, kc * SUB:(kc + 1) * SUB],
                    p_M[i][:, :]), reads=[tl, tM, tmp_gate]))
            tout = []
            for kc in range(KC):
                tout.append(p.op("dve", lambda e, kc=kc, s=s, i=i: e.tensor_mul(
                    xn[:, kc * T + s * SUB: kc * T + (s + 1) * SUB],
                    tmp[:, kc * SUB:(kc + 1) * SUB], p_R[i][:, :]), reads=[tsub[kc], tR]))
            tmp_gate = tout[-1]
            pM_gate[i] = tout[-1]
            pR_gate[i] = tout[-1]
            xf_gate[i] = tout[-1]
            xb_gate[i] = [tm, tsq[-1]]
        p.barrier()


def emit_routing(p, st, C, xn, c_all, router_dram, bias_dram, layer):
    nc = p.nc
    NQ = SUB // 128
    with ExitStack() as ph:
        wrt = _sb(ph, nc, f"rt_w{layer}", [128, KC * NE], BF16)
        lg = _sb(ph, nc, f"rt_lg{layer}", [NE, SUB], F32)
        bias_s = _sb(ph, nc, f"rt_b{layer}", [NE, 1], F32)
        ltok = [_sb(ph, nc, f"rt_lt{layer}_{q}", [128, NE], F32) for q in range(NQ)]
        sc = [_sb(ph, nc, f"rt_sc{layer}_{q}", [128, 8], F32) for q in range(NQ)]
        msk = [_sb(ph, nc, f"rt_msk{layer}_{q}", [128, NE], F32) for q in range(NQ)]
        pex = [_sb(ph, nc, f"rt_pex{layer}_{q}", [128, NE], F32) for q in range(NQ)]
        ctok = [_sb(ph, nc, f"rt_ct{layer}_{q}", [128, NE], F32) for q in range(NQ)]
        p_lg = [_ps(ph, nc, f"rt_plg{layer}_{i}", [NE, SUB]) for i in range(2)]
        p_t = [_ps(ph, nc, f"rt_pt{layer}_{i}", [128, NE]) for i in range(2)]
        p_ct = [_ps(ph, nc, f"rt_pct{layer}_{i}", [NE, 128]) for i in range(2)]

        tw = p.dma("pool", lambda e: e.dma_start(
            out=wrt[:, :].rearrange("p (kc e) -> p kc e", kc=KC),
            in_=_chunk3(router_dram[:, :])))
        if bias_dram is not None:
            tb = p.dma("sp", lambda e: e.dma_start(out=bias_s[:, :], in_=bias_dram[:, :]))
        else:
            tb = p.op("dve", lambda e: e.memset(bias_s[:, :], 0.0))

        plg_gate = [None, None]
        pt_gate = [None, None]
        pct_gate = [None, None]
        lg_gate = None
        for s in range(NSUB):
            il = s % 2
            tmm = None
            for kc in range(KC):
                tmm = p.op("pe", lambda e, kc=kc, s=s, il=il: e.matmul(
                    p_lg[il][:, :], wrt[:, kc * NE:(kc + 1) * NE],
                    xn[:, kc * T + s * SUB: kc * T + (s + 1) * SUB],
                    start=(kc == 0), stop=(kc == KC - 1)), reads=[tw, plg_gate[il]])
            te = p.op("dve", lambda e, il=il: e.tensor_scalar(lg[:, :], p_lg[il][:, :],
                                                              bias_s[0:NE, 0:1], None,
                                                              ALU.add),
                      reads=[tmm, tb, lg_gate])
            plg_gate[il] = te
            t0s = []
            for q in range(NQ):
                it = q % 2
                tt = p.op("pe", lambda e, q=q, it=it: e.transpose(
                    p_t[it][:, :], lg[:, q * 128:(q + 1) * 128], C.ident_f[0:NE, 0:NE]),
                    reads=[te, pt_gate[it]])
                t0 = p.op("dve", lambda e, q=q, it=it: e.tensor_copy(ltok[q][:, :],
                                                                     p_t[it][:, :]),
                          reads=[tt])
                pt_gate[it] = t0
                t0s.append(t0)
            lg_gate = t0s[-1]
            tA = [p.op("dve", lambda e, q=q: e.reduce_max(sc[q][:, 0:1], ltok[q][:, :],
                                                          axis=AX.X, negate=True),
                       reads=[t0s[q]]) for q in range(NQ)]
            tB = [p.op("dve", lambda e, q=q: e.tensor_scalar(msk[q][:, :], ltok[q][:, :],
                                                             sc[q][:, 0:1], None, ALU.add),
                       reads=[tA[q]]) for q in range(NQ)]
            tC = [p.op("dve", lambda e, q=q: e.tensor_scalar(msk[q][:, :], msk[q][:, :],
                                                             0.0, None, ALU.is_ge),
                       reads=[tB[q]]) for q in range(NQ)]
            tD = [p.op("dve", lambda e, q=q: e.scalar_tensor_tensor(
                msk[q][:, :], msk[q][:, :], -1e9, ltok[q][:, :], ALU.mult, ALU.add),
                reads=[tC[q]]) for q in range(NQ)]
            tE = [p.op("dve", lambda e, q=q: e.reduce_max(sc[q][:, 1:2], msk[q][:, :],
                                                          axis=AX.X), reads=[tD[q]])
                  for q in range(NQ)]
            tF = [p.op("act", lambda e, q=q: e.activation(pex[q][:, :], ltok[q][:, :],
                                                          AF.Exp, bias=sc[q][:, 0:1]),
                       reads=[tA[q]]) for q in range(NQ)]
            tG = [p.op("dve", lambda e, q=q: e.tensor_scalar(ctok[q][:, :], ltok[q][:, :],
                                                             sc[q][:, 1:2], None, ALU.is_ge),
                       reads=[tE[q]]) for q in range(NQ)]
            tH = [p.op("dve", lambda e, q=q: e.tensor_mul(ctok[q][:, :], ctok[q][:, :],
                                                          pex[q][:, :]),
                       reads=[tF[q], tG[q]]) for q in range(NQ)]
            tI = [p.op("dve", lambda e, q=q: e.reduce_sum(sc[q][:, 2:3], ctok[q][:, :],
                                                          axis=AX.X), reads=[tH[q]])
                  for q in range(NQ)]
            tJ = [p.op("dve", lambda e, q=q: e.reciprocal(sc[q][:, 3:4], sc[q][:, 2:3]),
                       reads=[tI[q]]) for q in range(NQ)]
            tK = [p.op("dve", lambda e, q=q: e.tensor_scalar(ctok[q][:, :], ctok[q][:, :],
                                                             sc[q][:, 3:4], None, ALU.mult),
                       reads=[tJ[q]]) for q in range(NQ)]
            for q in range(NQ):
                it = q % 2
                tc = p.op("pe", lambda e, q=q, it=it: e.transpose(
                    p_ct[it][:, :], ctok[q][:, :], C.ident_f[:, :]),
                    reads=[tK[q], pct_gate[it]])
                tcp = p.op("dve", lambda e, s=s, q=q, it=it: e.tensor_copy(
                    c_all[:, s * SUB + q * 128: s * SUB + (q + 1) * 128], p_ct[it][:, :]),
                    reads=[tc])
                pct_gate[it] = tcp
        p.barrier()


def emit_moe(p, st, C, xn, c_all, fsc, layer, experts):
    nc = p.nc
    with ExitStack() as ph:
        a_buf = _sb(ph, nc, f"moe_a{layer}", [128, KC * T], BF16)
        w1s = [_sb(ph, nc, f"moe_w1{layer}_{i}", [128, 2 * KC * 128], BF16) for i in range(2)]
        w2s = [_sb(ph, nc, f"moe_w2{layer}_{i}", [128, KC * 128], BF16) for i in range(2)]
        silu_t = [_sb(ph, nc, f"moe_st{layer}_{i}", [128, SUB], BF16) for i in range(2)]
        c_sb = _sb(ph, nc, f"moe_cb{layer}", [128, T], BF16)
        o_t = [_sb(ph, nc, f"moe_ot{layer}_{i}", [128, SUB], F32) for i in range(3)]
        p_h = [_ps(ph, nc, f"moe_ph{layer}_{i}", [128, SUB]) for i in range(4)]
        p_o = [_ps(ph, nc, f"moe_po{layer}_{i}", [128, SUB]) for i in range(2)]
        p_c = _ps(ph, nc, f"moe_pc{layer}", [128, SUB])

        fsc3 = fsc[:, :].rearrange("(mc p) t -> p mc t", p=128)
        region_tok = {}
        w1_last = [None, None]
        w2_last = [None, None]
        ph_last = [None] * 4
        po_last = [None] * 2
        ot_last = [None] * 3
        cs_last = None
        si_last = [None, None]
        pc_last = None

        def w1_load(ex, ocp, slot):
            twa = p.dma("pool", lambda e, ex=ex, ocp=ocp, slot=slot: e.dma_start(
                out=w1s[slot][:, 0:KC * 128].rearrange("p (kc m) -> p kc m", kc=KC),
                in_=ex["w1a"](ocp)), reads=[w1_last[slot]])
            twb = p.dma("pool", lambda e, ex=ex, ocp=ocp, slot=slot: e.dma_start(
                out=w1s[slot][:, KC * 128:2 * KC * 128].rearrange("p (kc m) -> p kc m", kc=KC),
                in_=ex["w1b"](ocp)), reads=[w1_last[slot]])
            return twa, twb

        w1_pending = {}
        for ei, ex in enumerate(experts):
            if (ei, 0) not in w1_pending:
                w1_pending[(ei, 0)] = w1_load(ex, 0, 0)
            for ocp in range(KC):
                wb = w1s[ocp % 2]
                twa, twb = w1_pending.pop((ei, ocp))
                if ocp + 1 < KC:
                    w1_pending[(ei, ocp + 1)] = w1_load(ex, ocp + 1, (ocp + 1) % 2)
                elif ei + 1 < len(experts):
                    w1_pending[(ei + 1, 0)] = w1_load(experts[ei + 1], 0, 0)
                last_a = None
                for s in range(NSUB):
                    ia, ib = (2 * s) % 4, (2 * s + 1) % 4
                    pa, pb = p_h[ia], p_h[ib]
                    tma = tmb = None
                    for kc in range(KC):
                        tma = p.op("pe", lambda e, kc=kc, s=s, wb=wb, pa=pa: e.matmul(
                            pa[:, :], wb[:, kc * 128:(kc + 1) * 128],
                            xn[:, kc * T + s * SUB: kc * T + (s + 1) * SUB],
                            start=(kc == 0), stop=(kc == KC - 1)),
                            reads=[twa, ph_last[ia]])
                    for kc in range(KC):
                        tmb = p.op("pe", lambda e, kc=kc, s=s, wb=wb, pb=pb: e.matmul(
                            pb[:, :], wb[:, (KC + kc) * 128:(KC + kc + 1) * 128],
                            xn[:, kc * T + s * SUB: kc * T + (s + 1) * SUB],
                            start=(kc == 0), stop=(kc == KC - 1)),
                            reads=[twb, ph_last[ib]])
                    stl = silu_t[s % 2]
                    ts = p.op("act", lambda e, stl=stl, pa=pa: e.activation(
                        stl[:, :], pa[:, :], AF.Silu), reads=[tma, si_last[s % 2]])
                    ta = p.op("dve", lambda e, ocp=ocp, s=s, stl=stl, pb=pb: e.tensor_mul(
                        a_buf[:, ocp * T + s * SUB: ocp * T + (s + 1) * SUB],
                        stl[:, :], pb[:, :]), reads=[ts, tmb])
                    si_last[s % 2] = ta
                    ph_last[ia] = ts
                    ph_last[ib] = ta
                    last_a = ta
                w1_last[ocp % 2] = last_a
            if ex["use_c"]:
                for s in range(NSUB):
                    tcm = p.op("pe", lambda e, ex=ex, s=s: e.matmul(
                        p_c[:, :], C.ohsel[:, ex["e_idx"] * 128:(ex["e_idx"] + 1) * 128],
                        c_all[:, s * SUB:(s + 1) * SUB], start=True, stop=True),
                        reads=[pc_last, ex["c_tok"]])
                    tce = p.op("dve", lambda e, s=s: e.tensor_copy(
                        c_sb[:, s * SUB:(s + 1) * SUB], p_c[:, :]),
                        reads=[tcm, cs_last])
                    pc_last = tce
                cs_tok = pc_last
            else:
                cs_tok = None
            def w2_load(mc):
                return p.dma("pool", lambda e, ex=ex, mc=mc: e.dma_start(
                    out=w2s[mc % 2][:, :].rearrange("p (kc m) -> p kc m", kc=KC),
                    in_=ex["w2"](mc)), reads=[w2_last[mc % 2]])
            w2_next = w2_load(0)
            for mc in range(KC):
                tw2 = w2_next
                if mc + 1 < KC:
                    w2_next = w2_load(mc + 1)
                last_ev = None
                for s in range(NSUB):
                    po = p_o[s % 2]
                    tmo = None
                    for kc in range(KC):
                        tmo = p.op("pe", lambda e, kc=kc, s=s, mc=mc, po=po: e.matmul(
                            po[:, :], w2s[mc % 2][:, kc * 128:(kc + 1) * 128],
                            a_buf[:, kc * T + s * SUB: kc * T + (s + 1) * SUB],
                            start=(kc == 0), stop=(kc == KC - 1)),
                            reads=[tw2, po_last[s % 2]])
                    i3 = (mc * NSUB + s) % 3
                    ot = o_t[i3]
                    if ex["use_c"]:
                        tev = p.op("dve", lambda e, ot=ot, po=po, s=s: e.tensor_mul(
                            ot[:, :], po[:, :], c_sb[:, s * SUB:(s + 1) * SUB]),
                            reads=[tmo, cs_tok, ot_last[i3]])
                    else:
                        tev = p.op("dve", lambda e, ot=ot, po=po: e.tensor_copy(
                            ot[:, :], po[:, :]), reads=[tmo, ot_last[i3]])
                    po_last[s % 2] = tev
                    last_ev = tev
                    accum = ALU.add if ei > 0 else ALU.bypass
                    td = p.dma("pool", lambda e, mc=mc, s=s, ot=ot, accum=accum: e.dma_start(
                        out=fsc3[:, mc, s * SUB:(s + 1) * SUB], in_=ot[:, :],
                        accum_op=accum), reads=[tev, region_tok.get((mc, s))])
                    region_tok[(mc, s)] = td
                    ot_last[i3] = td
                w2_last[mc % 2] = last_ev
            if ex["use_c"]:
                cs_last = w2_last[0]
        p.barrier()


def emit_lowrank_proj(p, st, C, xn, w_dram, out_fm, ncol, tag):
    nc = p.nc
    nmc = ncol // 128
    with ExitStack() as ph:
        wt = _sb(ph, nc, f"lr_w_{tag}", [128, KC * ncol], BF16)
        p_x = [_ps(ph, nc, f"lr_p_{tag}{i}", [128, SUB]) for i in range(2)]
        tw = p.dma("pool", lambda e: e.dma_start(
            out=wt[:, :].rearrange("p (kc m) -> p kc m", kc=KC),
            in_=_chunk3(w_dram[:, :])))
        last = [None, None]
        for s in range(NSUB):
            for mc in range(nmc):
                i = (s * nmc + mc) % 2
                px = p_x[i]
                tm = None
                for kc in range(KC):
                    tm = p.op("pe", lambda e, kc=kc, mc=mc, s=s, px=px: e.matmul(
                        px[:, :], wt[:, (kc * nmc + mc) * 128:(kc * nmc + mc + 1) * 128],
                        xn[:, kc * T + s * SUB: kc * T + (s + 1) * SUB],
                        start=(kc == 0), stop=(kc == KC - 1)),
                        reads=[tw, last[i]])
                te = p.op("dve", lambda e, mc=mc, s=s, px=px: e.tensor_copy(
                    out_fm[:, mc * T + s * SUB: mc * T + (s + 1) * SUB], px[:, :]),
                    reads=[tm])
                last[i] = te
        p.barrier()


def _norm_update(p, C, src_ap, nparts, p_n, nrow, nmax, prev_toks, reads, pn_gate):
    gates = pn_gate if isinstance(pn_gate, list) else [pn_gate]
    tn = p.op("pe", lambda e: e.matmul(p_n[:, :], C.ones_b[0:nparts, :], src_ap,
                                       start=True, stop=True),
              reads=list(reads) + gates)
    t2 = p.op("dve", lambda e: e.tensor_copy(nrow[:, :], p_n[:, :]), reads=[tn])
    t3 = p.op("dve", lambda e: e.reduce_max(nmax[0:1, 1:2], nrow[:, :], axis=AX.X),
              reads=[t2], drain=True)
    t4 = p.op("dve", lambda e: e.tensor_max(nmax[0:1, 0:1], nmax[0:1, 0:1],
                                            nmax[0:1, 1:2]),
              reads=[t3] + list(prev_toks), drain=True)
    return tn, t2, t4


def emit_kv_l0(p, st, C, lw, xn, kdup, vtm, nmax_k, layer):
    nc = p.nc
    with ExitStack() as ph:
        wkt = _sb(ph, nc, "kv0_wk", [128, KC * HD], BF16)
        wvt = _sb(ph, nc, "kv0_wv", [128, KC * HD], BF16)
        kraw = _sb(ph, nc, "kv0_kraw", [128, SUB], F32)
        t1b = _sb(ph, nc, "kv0_t1", [128, SUB], F32)
        t2b = _sb(ph, nc, "kv0_t2", [128, SUB], F32)
        v_b = _sb(ph, nc, "kv0_vb", [128, SUB], BF16)
        ksq = _sb(ph, nc, "kv0_ksq", [128, SUB], BF16)
        nrow = _sb(ph, nc, "kv0_nrow", [1, SUB], F32)
        p_k = _ps(ph, nc, "kv0_pk", [128, SUB])
        p_v = _ps(ph, nc, "kv0_pv", [128, SUB])
        p_sw = _ps(ph, nc, "kv0_psw", [128, SUB])
        p_t = [_ps(ph, nc, f"kv0_pt{i}", [128, 128], BF16) for i in range(2)]
        p_n = _ps(ph, nc, "kv0_pn", [1, SUB])

        qkvT3 = _chunk3(lw["qkvT"][:, :])
        tk = p.dma("pool", lambda e: e.dma_start(
            out=wkt[:, :].rearrange("p (kc m) -> p kc m", kc=KC),
            in_=qkvT3[:, :, 1024:1088]))
        tv = p.dma("pool", lambda e: e.dma_start(
            out=wvt[:, :].rearrange("p (kc m) -> p kc m", kc=KC),
            in_=qkvT3[:, :, 1088:1152]))
        tmx = p.op("dve", lambda e: e.memset(nmax_k[:, :], 0.0))
        vtm_tok = p.op("dve", lambda e: e.memset(vtm[:, :], 1.0))
        pt_gate = [None, None]
        kraw_tok = []
        t1_tok = t2_tok = krot_tok = None
        v_tok = []
        ksq_tok = None
        pn_gate = None
        for s in range(NSUB):
            tmk = tmv = None
            for kc in range(KC):
                tmk = p.op("pe", lambda e, kc=kc, s=s: e.matmul(
                    p_k[0:HD, :], wkt[:, kc * HD:(kc + 1) * HD],
                    xn[:, kc * T + s * SUB: kc * T + (s + 1) * SUB],
                    start=(kc == 0), stop=(kc == KC - 1)), reads=[tk] + kraw_tok)
            for kc in range(KC):
                tmv = p.op("pe", lambda e, kc=kc, s=s: e.matmul(
                    p_v[0:HD, :], wvt[:, kc * HD:(kc + 1) * HD],
                    xn[:, kc * T + s * SUB: kc * T + (s + 1) * SUB],
                    start=(kc == 0), stop=(kc == KC - 1)), reads=[tv] + v_tok)
            tke = p.op("dve", lambda e: e.tensor_scalar(
                kraw[0:HD, :], p_k[0:HD, :], lw["qkvb_s"][0:HD, 8:9], None, ALU.add),
                reads=[tmk, lw["qkvb_tok"]])
            tsw = p.op("pe", lambda e: e.matmul(
                p_sw[0:HD, :], lw["p128f"][0:HD, 0:HD], kraw[0:HD, :],
                start=True, stop=True), reads=[tke, t2_tok])
            tt1 = p.op("dve", lambda e, s=s: e.tensor_mul(
                t1b[0:HD, :], kraw[0:HD, :], lw["cos"][0:HD, s * SUB:(s + 1) * SUB]),
                reads=[tke, krot_tok], drain=True)
            tt2 = p.op("dve", lambda e, s=s: e.tensor_mul(
                t2b[0:HD, :], p_sw[0:HD, :], lw["sin"][0:HD, s * SUB:(s + 1) * SUB]),
                reads=[tsw, krot_tok], drain=True)
            tkr = p.op("dve", lambda e, s=s: e.tensor_add(
                kdup[0:HD, s * SUB:(s + 1) * SUB], t1b[0:HD, :], t2b[0:HD, :]),
                reads=[tt1, tt2], drain=True)
            kraw_tok = [tsw, tt1]
            t2_tok = tt2
            krot_tok = tkr
            tq = p.op("pool", lambda e, s=s: e.tensor_mul(
                ksq[0:HD, :], kdup[0:HD, s * SUB:(s + 1) * SUB],
                kdup[0:HD, s * SUB:(s + 1) * SUB]), reads=[tkr, ksq_tok])
            tn, pn_gate, tmx = _norm_update(p, C, ksq[0:HD, :], HD, p_n, nrow, nmax_k,
                                            [tmx], [tq], pn_gate)
            ksq_tok = tn
            tve = p.op("dve", lambda e: e.tensor_scalar(
                v_b[0:HD, :], p_v[0:HD, :], lw["qkvb_s"][0:HD, 9:10], None, ALU.add),
                reads=[tmv, lw["qkvb_tok"]])
            v_tok = [tve]
            for q in range(SUB // 128):
                kt = s * (SUB // 128) + q
                it = q % 2
                pt = p_t[it]
                ttr = p.op("pe", lambda e, q=q, pt=pt: e.transpose(
                    pt[:, 0:HD], v_b[0:HD, q * 128:(q + 1) * 128], C.ident_b[0:HD, 0:HD]),
                    reads=[tve, pt_gate[it]])
                tcp = p.op("dve", lambda e, kt=kt, pt=pt: e.tensor_copy(
                    vtm[:, kt * (HD + 1): kt * (HD + 1) + HD], pt[:, 0:HD]),
                    reads=[ttr, vtm_tok])
                pt_gate[it] = tcp
                v_tok.append(ttr)
        p.dma("pool", lambda e: e.dma_start(out=kdup[HD:128, :], in_=kdup[0:HD, :]),
              reads=[krot_tok])
        p.barrier()


def emit_kv_l1(p, st, C, lw, ckv, kfm, vtm, nmax_k, layer):
    nc = p.nc
    with ExitStack() as ph:
        wuk = _sb(ph, nc, "kv1_wuk", [128, 2 * D], BF16)
        wuv = _sb(ph, nc, "kv1_wuv", [128, 2 * D], BF16)
        v_bs = [_sb(ph, nc, f"kv1_vb{i}", [128, SUB], BF16) for i in range(2)]
        ksq = _sb(ph, nc, "kv1_ksq", [128, SUB], BF16)
        nrow = _sb(ph, nc, "kv1_nrow", [1, SUB], F32)
        p_k = [_ps(ph, nc, f"kv1_pk{i}", [128, SUB]) for i in range(2)]
        p_v = _ps(ph, nc, "kv1_pv", [128, SUB])
        p_t = [_ps(ph, nc, f"kv1_pt{i}", [128, 128], BF16) for i in range(2)]
        p_n = _ps(ph, nc, "kv1_pn", [1, SUB])

        tk = p.dma("pool", lambda e: e.dma_start(
            out=wuk[:, :].rearrange("p (kc m) -> p kc m", kc=2),
            in_=lw["Wuk"][:, :].rearrange("(kc p) m -> p kc m", p=128)))
        tv = p.dma("pool", lambda e: e.dma_start(
            out=wuv[:, :].rearrange("p (kc m) -> p kc m", kc=2),
            in_=lw["Wuv"][:, :].rearrange("(kc p) m -> p kc m", p=128)))
        tmx = p.op("dve", lambda e: e.memset(nmax_k[:, :], 0.0))
        vtm_tok = p.op("dve", lambda e: e.memset(vtm[:, :], 1.0))
        pt_gate = [None, None]
        vb_gate = [[], []]
        klast = [None, None]
        ksq_tok = None
        pn_gate = None
        v_tok = []
        for s in range(NSUB):
            for mc in range(KC):
                i = mc % 2
                pk = p_k[i]
                tm = None
                for kc in range(2):
                    tm = p.op("pe", lambda e, kc=kc, mc=mc, s=s, pk=pk: e.matmul(
                        pk[:, :], wuk[:, (kc * KC + mc) * 128:(kc * KC + mc + 1) * 128],
                        ckv[:, kc * T + s * SUB: kc * T + (s + 1) * SUB],
                        start=(kc == 0), stop=(kc == 1)), reads=[tk, klast[i]])
                te = p.op("dve", lambda e, mc=mc, s=s, pk=pk: e.tensor_copy(
                    kfm[:, mc * T + s * SUB: mc * T + (s + 1) * SUB], pk[:, :]), reads=[tm])
                klast[i] = te
                tq = p.op("pool", lambda e, mc=mc, s=s: e.tensor_mul(
                    ksq[:, :], kfm[:, mc * T + s * SUB: mc * T + (s + 1) * SUB],
                    kfm[:, mc * T + s * SUB: mc * T + (s + 1) * SUB]),
                    reads=[te, ksq_tok])
                tn, pn_gate, tmx = _norm_update(p, C, ksq[:, :], 128, p_n, nrow, nmax_k,
                                                [tmx], [tq], pn_gate)
                ksq_tok = tn
            for h in range(NH):
                v_b = v_bs[h % 2]
                tm = None
                for kc in range(2):
                    tm = p.op("pe", lambda e, kc=kc, h=h, s=s: e.matmul(
                        p_v[0:HD, :],
                        wuv[:, kc * KC * 128 + h * HD: kc * KC * 128 + (h + 1) * HD],
                        ckv[:, kc * T + s * SUB: kc * T + (s + 1) * SUB],
                        start=(kc == 0), stop=(kc == 1)), reads=[tv] + v_tok)
                tve = p.op("dve", lambda e, v_b=v_b: e.tensor_copy(v_b[0:HD, :],
                                                                   p_v[0:HD, :]),
                           reads=[tm] + vb_gate[h % 2])
                v_tok = [tve]
                vb_gate[h % 2] = []
                for q in range(SUB // 128):
                    kt = s * (SUB // 128) + q
                    slot = (kt * NH + h) * (HD + 1)
                    it = q % 2
                    pt = p_t[it]
                    ttr = p.op("pe", lambda e, q=q, pt=pt, v_b=v_b: e.transpose(
                        pt[:, 0:HD], v_b[0:HD, q * 128:(q + 1) * 128],
                        C.ident_b[0:HD, 0:HD]), reads=[tve, pt_gate[it]])
                    vb_gate[h % 2].append(ttr)
                    tcp = p.op("dve", lambda e, slot=slot, pt=pt: e.tensor_copy(
                        vtm[:, slot: slot + HD], pt[:, 0:HD]), reads=[ttr, vtm_tok])
                    pt_gate[it] = tcp
                    v_tok.append(ttr)
        p.barrier()

def emit_scores(p, st, C, layer, cfg):
    nc = p.nc
    with ExitStack() as ph:
        SB = Bufs()
        q_pan = _sb(ph, nc, f"sc_q{layer}", [128, KC * SUB], BF16)
        o_pan = _sb(ph, nc, f"sc_o{layer}", [128, KC * SUB], BF16)
        o_tmp = _sb(ph, nc, f"sc_ot{layer}", [128, SUB], BF16)
        pt_b = [_sb(ph, nc, f"sc_pt{layer}_{i}", [128, 2 * SUB], BF16) for i in range(2)]
        rbc = _sb(ph, nc, f"sc_rbc{layer}", [128, SUB], BF16)
        recb = _sb(ph, nc, f"sc_rec{layer}", [128, SUB], F32)
        recb2 = _sb(ph, nc, f"sc_rc2{layer}", [128, SUB], BF16)
        qsq = _sb(ph, nc, f"sc_qsq{layer}", [128, SUB], BF16)
        nrow = _sb(ph, nc, f"sc_nrow{layer}", [1, SUB], F32)
        nmax_q = _sb(ph, nc, f"sc_nq{layer}", [128, 4], F32)
        scs = _sb(ph, nc, f"sc_scs{layer}", [1, 8], F32)
        mkneg = _sb(ph, nc, f"sc_mk{layer}", [128, 1], F32)
        wo_t = [_sb(ph, nc, f"sc_wo{layer}_{i}", [128, KC * 128], BF16) for i in range(2)]
        if cfg["outb"] is not None:
            y1_t = [_sb(ph, nc, f"sc_y1{layer}_{i}", [128, SUB], F32) for i in range(2)]
        xf_t = [_sb(ph, nc, f"sc_xf{layer}_{i}", [128, SUB], F32) for i in range(2)]
        ff_t = [_sb(ph, nc, f"sc_ff{layer}_{i}", [128, SUB], F32) for i in range(2)]
        s1_t = [_sb(ph, nc, f"sc_s1{layer}_{i}", [128, SUB], F32) for i in range(2)]
        if cfg.get("rope_scratch"):
            SB.qraw = _sb(ph, nc, f"sc_qraw{layer}", [128, SUB], F32)
            SB.t1b = _sb(ph, nc, f"sc_t1{layer}", [128, SUB], F32)
            SB.t2b = _sb(ph, nc, f"sc_t2{layer}", [128, SUB], F32)
            SB.wq = [_sb(ph, nc, f"sc_wq{layer}_{i}", [128, KC * 128], BF16)
                     for i in range(2)]
        p_s = [_ps(ph, nc, f"sc_ps{layer}_{i}", [128, 2 * SUB]) for i in range(2)]
        p_oa = [_ps(ph, nc, f"sc_poa{layer}_{i}", [128, SUB]) for i in range(2)]
        p_misc = _ps(ph, nc, f"sc_pm{layer}", [128, SUB])
        p_y = _ps(ph, nc, f"sc_py{layer}", [128, SUB])
        p_n = p_y[0:1, 0:SUB]
        SB.p_s = p_s
        SB.ps_last = [None, None]

        fsc3 = cfg["fsc"][:, :].rearrange("(mc p) t -> p mc t", p=128)
        xin3 = cfg["resid_in"][:, :].rearrange("(mc p) t -> p mc t", p=128)
        xout3 = cfg["resid_out"][:, :].rearrange("(mc p) t -> p mc t", p=128)

        pt_last = [None, None]
        py_last = None
        wo_last = [None, None]
        xf_last = [None, None]
        ff_last = [None, None]
        s1_last = [None, None]
        y1_last = [None, None]
        oa_last = [None, None]
        pr_gate = None
        rc2_gate = None
        scr_gate = None
        otmp_last = None
        q_consumed = []
        o_consumed = []
        store_toks = []
        qsq_tok = None
        pn_gate = None
        kv_tok = cfg["kv_tok"]
        wo_pend = [None]
        pending_epi = []
        tmax = p.op("dve", lambda e: e.memset(nmax_q[:, :], 0.0))
        norm_jobs = []

        def emit_norm_chunk(mc, qtok):
            nonlocal pn_gate, tmax, qsq_tok
            tq = p.op("dve", lambda e, mc=mc: e.tensor_mul(
                qsq[:, :], q_pan[:, mc * SUB:(mc + 1) * SUB],
                q_pan[:, mc * SUB:(mc + 1) * SUB]), reads=[qtok, qsq_tok])
            tn, pn_gate, tmax = _norm_update(p, C, qsq[:, :], 128, p_n, nrow, nmax_q,
                                             [tmax], [tq], [pn_gate, py_last])
            qsq_tok = tn
            return tq

        for qp in range(NQP):
            qtoks = cfg["build_q"](p, qp, q_pan, SB, q_consumed)
            q_consumed = []
            norm_jobs = [(mc, qtoks[mc]) for mc in range(KC)]
            if qp == 0:
                for mc, qtok in norm_jobs:
                    q_consumed.append(emit_norm_chunk(mc, qtok))
                norm_jobs = []
            tb1 = p.op("dve", lambda e: e.tensor_mul(scs[0:1, 0:1], nmax_q[0:1, 0:1],
                                                     cfg["nmax_k"][0:1, 0:1]),
                       reads=[tmax, kv_tok], drain=True)
            tb2a = p.op("act", lambda e: e.activation(scs[0:1, 1:2], scs[0:1, 0:1],
                                                      AF.Ln, bias=C.fconst[0:1, 1:2]),
                        reads=[tb1])
            tb2 = p.op("act", lambda e: e.activation(scs[0:1, 4:5], scs[0:1, 1:2],
                                                     AF.Exp, scale=0.5),
                       reads=[tb2a], drain=True)
            tb3 = p.op("dve", lambda e: e.tensor_scalar_mul(scs[0:1, 2:3], scs[0:1, 4:5],
                                                            -0.125), reads=[tb2], drain=True)
            tb4 = p.op("pe", lambda e: e.matmul(p_misc[:, 0:1], C.ones_fr[0:1, :],
                                                scs[0:1, 2:3], start=True, stop=True),
                       reads=[tb3, pn_gate])
            tbound = p.op("dve", lambda e: e.tensor_copy(mkneg[:, 0:1], p_misc[:, 0:1]),
                          reads=[tb4], drain=True)
            pr_gate = tbound
            nkt = 4 * qp + 4
            o_gate = list(o_consumed)
            o_consumed = []
            odd_dma_toks = []
            for c in range(KC):
                hA, hB = 2 * c, 2 * c + 1
                rhs_qA = q_pan[0:HD, c * SUB:(c + 1) * SUB]
                rhs_qB = q_pan[HD:128, c * SUB:(c + 1) * SUB]
                tS_last = None

                def emit_S(kt):
                    i2 = kt % 2
                    ps = p_s[i2]
                    gate = [kv_tok, qtoks[c], SB.ps_last[i2]]
                    tSA = p.op("pe", lambda e, hA=hA, kt=kt, ps=ps, rhs_qA=rhs_qA: e.matmul(
                        ps[:, 0:SUB], cfg["k_lhsT"](hA, kt), rhs_qA,
                        start=True, stop=True, tile_position=(0, 0)), reads=gate)
                    tSB = p.op("pe", lambda e, hB=hB, kt=kt, ps=ps, rhs_qB=rhs_qB: e.matmul(
                        ps[:, SUB:2 * SUB], cfg["k_lhsT"](hB, kt), rhs_qB,
                        start=True, stop=True, tile_position=(64, 0)), reads=gate)
                    return tSA, tSB

                def emit_exp(kt, tSA, tSB):
                    i2 = kt % 2
                    ps = p_s[i2]
                    texpA = p.op("act", lambda e, ps=ps, i2=i2: e.activation(
                        pt_b[i2][:, 0:SUB], ps[:, 0:SUB], AF.Exp, bias=mkneg[:, 0:1],
                        scale=0.125), reads=[tSA, tbound, pt_last[i2]])
                    texp = p.op("act", lambda e, ps=ps, i2=i2: e.activation(
                        pt_b[i2][:, SUB:2 * SUB], ps[:, SUB:2 * SUB], AF.Exp,
                        bias=mkneg[:, 0:1], scale=0.125), reads=[tSB])
                    SB.ps_last[i2] = texp
                    tm0, tm1 = texpA, texp
                    if kt >= nkt - 4:
                        j = kt - (nkt - 4)
                        tm0 = p.op("pool", lambda e, i2=i2, j=j: e.tensor_mul(
                            pt_b[i2][:, 0:SUB], pt_b[i2][:, 0:SUB],
                            C.cmask[:, j * SUB:(j + 1) * SUB]), reads=[texpA])
                        tm1 = p.op("pool", lambda e, i2=i2, j=j: e.tensor_mul(
                            pt_b[i2][:, SUB:2 * SUB], pt_b[i2][:, SUB:2 * SUB],
                            C.cmask[:, j * SUB:(j + 1) * SUB]), reads=[texp])
                    return tm0, tm1

                def emit_PV(kt, tm0, tm1):
                    i2 = kt % 2
                    tPVA = p.op("pe", lambda e, hA=hA, kt=kt, i2=i2: e.matmul(
                        p_oa[0][0:HD + 1, :], cfg["vtm_slice"](hA, kt), pt_b[i2][:, 0:SUB],
                        start=(kt == 0), stop=(kt == nkt - 1)),
                        reads=[tm0, kv_tok] + ([oa_last[0]] if kt == 0 else []))
                    tPVB = p.op("pe", lambda e, hB=hB, kt=kt, i2=i2: e.matmul(
                        p_oa[1][0:HD + 1, :], cfg["vtm_slice"](hB, kt), pt_b[i2][:, SUB:2 * SUB],
                        start=(kt == 0), stop=(kt == nkt - 1)),
                        reads=[tm1] + ([oa_last[1]] if kt == 0 else []))
                    pt_last[i2] = tPVB
                    return tPVB

                sx = emit_S(0)
                while pending_epi:
                    pending_epi.pop(0)()
                ex_prev = emit_exp(0, *sx)
                tS_last = sx[1]
                for kt in range(1, nkt):
                    sx = emit_S(kt)
                    tS_last = sx[1]
                    emit_PV(kt - 1, *ex_prev)
                    ex_prev = emit_exp(kt, *sx)
                emit_PV(nkt - 1, *ex_prev)
                q_consumed.append(tS_last)
                def make_epilogue(h, hc, hoa, tPVl=pt_last[(nkt - 1) % 2]):
                    poa = p_oa[hoa]
                    def run():
                        nonlocal scr_gate, rc2_gate, pr_gate, otmp_last
                        tln = p.op("act", lambda e: e.activation(
                            recb[HD:HD + 1, :], poa[HD:HD + 1, :], AF.Ln),
                            reads=[tPVl, scr_gate], drain=True)
                        scr_gate = tln
                        trc2 = p.op("act", lambda e: e.activation(
                            recb2[HD:HD + 1, :], recb[HD:HD + 1, :], AF.Exp, scale=-1.0),
                            reads=[tln, rc2_gate], drain=True)
                        tbc = p.op("pe", lambda e: e.matmul(
                            p_misc[:, :], C.ones_br[HD:HD + 1, 0:128],
                            recb2[HD:HD + 1, :], start=True, stop=True),
                            reads=[trc2, pr_gate])
                        rc2_gate = tbc
                        trbc = p.op("dve", lambda e: e.tensor_copy(rbc[:, :], p_misc[:, :]),
                                    reads=[tbc])
                        pr_gate = trbc
                        if h % 2 == 0:
                            dest = o_pan[0:HD, hc * SUB:(hc + 1) * SUB]
                            tnrm = p.op("dve", lambda e: e.tensor_mul(
                                dest, poa[0:HD, :], rbc[0:HD, :]),
                                reads=[trbc] + o_gate, drain=True)
                        else:
                            tnrm = p.op("dve", lambda e: e.tensor_mul(
                                o_tmp[0:HD, :], poa[0:HD, :], rbc[0:HD, :]),
                                reads=[trbc, otmp_last], drain=True)
                            todd = p.dma("pool", lambda e: e.dma_start(
                                out=o_pan[HD:128, hc * SUB:(hc + 1) * SUB],
                                in_=o_tmp[0:HD, :]), reads=[tnrm] + o_gate)
                            otmp_last = todd
                            odd_dma_toks.append(todd)
                        oa_last[hoa] = tnrm
                    return run
                pending_epi.append(make_epilogue(hA, c, 0))
                pending_epi.append(make_epilogue(hB, c, 1))
                if norm_jobs and c < KC:
                    mc_, qtok_ = norm_jobs[c]
                    q_consumed.append(emit_norm_chunk(mc_, qtok_))
                    if c == KC - 1:
                        norm_jobs = []

            while pending_epi:
                pending_epi.pop(0)()
            opan_toks = [oa_last[0], oa_last[1]] + odd_dma_toks[-1:]
            def wo_load(mc):
                i = mc % 2
                return p.dma("pool", lambda e, mc=mc, i=i: e.dma_start(
                    out=wo_t[i][:, :].rearrange("p (kc m) -> p kc m", kc=KC),
                    in_=cfg["wo"](mc)), reads=[wo_last[i]])
            if wo_pend[0] is None:
                wo_pend[0] = wo_load(0)
            for mc in range(KC):
                i = mc % 2
                tw = wo_pend[0]
                wo_pend[0] = wo_load((mc + 1) % KC)
                tx = p.dma("sp", lambda e, mc=mc, qp=qp, i=i: e.dma_start(
                    out=xf_t[i][:, :], in_=xin3[:, mc, qp * SUB:(qp + 1) * SUB]),
                    reads=[xf_last[i]])
                tf = p.dma("sp", lambda e, mc=mc, qp=qp, i=i: e.dma_start(
                    out=ff_t[i][:, :], in_=fsc3[:, mc, qp * SUB:(qp + 1) * SUB]),
                    reads=[ff_last[i], cfg["fsc_tok"]])
                ts1 = p.op("dve", lambda e, i=i: e.tensor_add(
                    s1_t[i][:, :], xf_t[i][:, :], ff_t[i][:, :]),
                    reads=[tx, tf, s1_last[i]])
                xf_last[i] = ts1
                ff_last[i] = ts1
                tmy = None
                for kc in range(KC):
                    tmy = p.op("pe", lambda e, kc=kc, mc=mc, i=i: e.matmul(
                        p_y[:, :], wo_t[i][:, kc * 128:(kc + 1) * 128],
                        o_pan[:, kc * SUB:(kc + 1) * SUB],
                        start=(kc == 0), stop=(kc == KC - 1)),
                        reads=[tw, py_last, pn_gate] + opan_toks)
                o_consumed.append(tmy)
                wo_last[i] = tmy
                if cfg["outb"] is not None:
                    bbuf, btok = cfg["outb"]
                    ty1 = p.op("dve", lambda e, mc=mc, i=i, bbuf=bbuf: e.tensor_scalar(
                        y1_t[i][:, :], p_y[:, :], bbuf[0:128, mc:mc + 1], None, ALU.add),
                        reads=[tmy, btok, y1_last[i]])
                    py_last = ty1
                    txo = p.op("dve", lambda e, i=i: e.tensor_add(
                        s1_t[i][:, :], y1_t[i][:, :], s1_t[i][:, :]),
                        reads=[ty1, ts1], drain=True)
                    y1_last[i] = txo
                else:
                    txo = p.op("dve", lambda e, i=i: e.tensor_add(
                        s1_t[i][:, :], p_y[:, :], s1_t[i][:, :]),
                        reads=[tmy, ts1], drain=True)
                    py_last = txo
                tst = p.dma("sp", lambda e, mc=mc, qp=qp, i=i: e.dma_start(
                    out=xout3[:, mc, qp * SUB:(qp + 1) * SUB], in_=s1_t[i][:, :]),
                    reads=[txo])
                s1_last[i] = tst
                store_toks.append(tst)
        p.barrier()
        return store_toks


class _Stop(Exception):
    pass


def build_program(taps=(), upto="full"):
    nc = bass.Bass("TRN2", target_bir_lowering=False)
    dr = {}
    dr["xT"] = nc.dram_tensor("xT", [D, T], F32, kind="ExternalInput").ap()
    for n, sh in [("c_cos", [128, T]), ("c_sin", [128, T]), ("c_p128", [128, 128]),
                  ("c_cmask", [128, 4 * SUB]), ("c_ident", [128, 128]),
                  ("c_ohsel", [NE, NE * 128])]:
        dr[n] = nc.dram_tensor(n, sh, F32, kind="ExternalInput").ap()
    wshapes = dict(
        l0_qkvT=[D, 1152], l0_qkvb=[128, 10], l0_outT=[D, D], l0_outb=[128, 8],
        l0_WgT=[D, NE], l0_bg=[NE, 1], l0_w1T=[NE, D, 2 * D], l0_w2T=[NE, D, D],
        l1_Wdq=[D, 256], l1_Wuq=[256, D], l1_Wdkv=[D, 256], l1_Wuk=[256, D],
        l1_Wuv=[256, D], l1_Wo=[D, D], l1_router=[D, NE], l1_fc1=[NE, D, 2 * D],
        l1_fc2=[NE, D, D], l1_gate=[D, 2 * D], l1_up=[D, 2 * D], l1_down=[2 * D, D])
    for n, sh in wshapes.items():
        dr[n] = nc.dram_tensor(n, sh, F32, kind="ExternalInput").ap()
    resid = nc.dram_tensor("resid", [D, T], F32,
                           kind="ExternalOutput" if "resid" in taps else "Internal").ap()
    fsc = nc.dram_tensor("fsc", [D, T], F32,
                         kind="ExternalOutput" if "fsc" in taps else "Internal").ap()
    outT = nc.dram_tensor("outT", [D, T], F32, kind="ExternalOutput").ap()
    tap_dr = {}
    for tn in taps:
        if tn in ("resid", "fsc"):
            continue
        shape = {"xn": [128, KC * T], "call": [NE, T], "kdup": [128, T],
                 "kfm": [128, KC * T], "vtm0": [128, NKT * (HD + 1)],
                 "vtm1": [128, NKT * NH * (HD + 1)], "cq": [128, 2 * T],
                 "ckv": [128, 2 * T]}[tn.rstrip("01") if tn[-1] in "01" and not tn.startswith("vtm") else tn]
        tap_dr[tn] = nc.dram_tensor(f"tap_{tn}", shape, F32, kind="ExternalOutput").ap()

    try:
        nc = _assemble(nc, dr, resid, fsc, outT, tap_dr, taps, upto)
    except _Stop:
        pass
    return nc


def _assemble(nc, dr, resid, fsc, outT, tap_dr, taps, upto):
    with ExitStack() as st:
        p = Prog(nc, st)
        C = emit_consts(p, st, dr)

        def ck(name):
            if upto == name:
                p.barrier()
                p.emit()
                raise _Stop

        def tap(name, buf, cols):
            if name in tap_dr:
                with ExitStack() as th:
                    tb = _sb(th, nc, f"tapb_{name}", [128, 2048], F32)
                    np_ = buf.shape[0]
                    for c0 in range(0, cols, 2048):
                        w = min(2048, cols - c0)
                        tt = p.op("dve", lambda e, c0=c0, w=w, np_=np_: e.tensor_copy(
                            tb[0:np_, 0:w], buf[0:np_, c0:c0 + w]))
                        p.dma("sp", lambda e, c0=c0, w=w, np_=np_: e.dma_start(
                            out=tap_dr[name][0:np_, c0:c0 + w],
                            in_=tb[0:np_, 0:w]), reads=[tt])
                        p.barrier()

        if True:
            with ExitStack() as l0:
                qkvb_s = _sb(l0, nc, "qkvb0", [128, 10], F32)
                outb_s = _sb(l0, nc, "outb0", [128, 8], F32)
                xn = _sb(l0, nc, "xn0", [128, KC * T], BF16)
                tqb = p.dma("sp", lambda e: e.dma_start(out=qkvb_s[:, :], in_=dr["l0_qkvb"][:, :]))
                tob = p.dma("sp", lambda e: e.dma_start(out=outb_s[:, :], in_=dr["l0_outb"][:, :]))
                p.barrier()

                emit_ln(p, st, C, dr["xT"], xn, 0)
                tap("xn0", xn, KC * T)
                ck("ln0")

                with ExitStack() as lf:
                    c_all = _sb(lf, nc, "call0", [NE, T], BF16)
                    emit_routing(p, st, C, xn, c_all, dr["l0_WgT"], dr["l0_bg"], 0)
                    tap("call0", c_all, T)
                    ck("rt0")
                    w1T = dr["l0_w1T"]
                    w2T = dr["l0_w2T"]
                    experts = []
                    for e_ in range(NE):
                        experts.append(dict(
                            w1a=lambda ocp, e_=e_: _chunk3(w1T[e_])[:, :, ocp * 128:(ocp + 1) * 128],
                            w1b=lambda ocp, e_=e_: _chunk3(w1T[e_])[:, :, D + ocp * 128: D + (ocp + 1) * 128],
                            w2=lambda mc, e_=e_: _chunk3(w2T[e_])[:, :, mc * 128:(mc + 1) * 128],
                            use_c=True, e_idx=e_, c_tok=None))
                    emit_moe(p, st, C, xn, c_all, fsc, 0, experts)
                tap("fsc", None, 0)
                ck("ffn0")

                cosb = _sb(l0, nc, "cos0", [128, T], F32)
                sinb = _sb(l0, nc, "sin0", [128, T], F32)
                p128f = _sb(l0, nc, "p128f", [128, 128], F32)
                p.dma("sp", lambda e: e.dma_start(out=cosb[:, :], in_=dr["c_cos"][:, :]))
                p.dma("sp", lambda e: e.dma_start(out=sinb[:, :], in_=dr["c_sin"][:, :]))
                p.dma("sp", lambda e: e.dma_start(out=p128f[:, :], in_=dr["c_p128"][:, :]))
                p.barrier()
                kdup = _sb(l0, nc, "kdup0", [128, T], BF16, side="right")
                vtm0 = _sb(l0, nc, "vtm0", [128, NKT * (HD + 1)], BF16, side="right")
                nmax_k = _sb(l0, nc, "nmk0", [128, 4], F32, side="right")
                lw0 = dict(qkvT=dr["l0_qkvT"], qkvb_s=qkvb_s, qkvb_tok=tqb,
                           p128f=p128f, cos=cosb, sin=sinb)
                emit_kv_l0(p, st, C, lw0, xn, kdup, vtm0, nmax_k, 0)
                tap("kdup", kdup, T)
                ck("kv0")

                qkvT3 = _chunk3(dr["l0_qkvT"][:, :])

                def wq_load(SB, mc, wq_last):
                    i = mc % 2
                    return p.dma("pool", lambda e, mc=mc, i=i: e.dma_start(
                        out=SB.wq[i][:, :].rearrange("p (kc m) -> p kc m", kc=KC),
                        in_=qkvT3[:, :, mc * 128:(mc + 1) * 128]), reads=[wq_last[i]])

                def build_q0(p, qp, q_pan, SB, q_consumed, wq_last=[None, None],
                             qraw_last=[], t1_last=[], t2_last=[], pend=[None]):
                    toks = []
                    if pend[0] is None:
                        pend[0] = wq_load(SB, 0, wq_last)
                    for mc in range(KC):
                        i = mc % 2
                        tw = pend[0]
                        pend[0] = wq_load(SB, (mc + 1) % KC, wq_last)
                        tm = None
                        for kc in range(KC):
                            tm = p.op("pe", lambda e, kc=kc, qp=qp, i=i: e.matmul(
                                SB.p_s[0][:, 0:SUB], SB.wq[i][:, kc * 128:(kc + 1) * 128],
                                xn[:, kc * T + qp * SUB: kc * T + (qp + 1) * SUB],
                                start=(kc == 0), stop=(kc == KC - 1)),
                                reads=[tw, SB.ps_last[0]])
                        wq_last[i] = tm
                        tev = p.op("dve", lambda e, mc=mc: e.tensor_scalar(
                            SB.qraw[:, :], SB.p_s[0][:, 0:SUB],
                            qkvb_s[0:128, mc:mc + 1], None, ALU.add),
                            reads=[tm, tqb] + qraw_last)
                        SB.ps_last[0] = tev
                        tsw = p.op("pe", lambda e: e.matmul(
                            SB.p_s[1][:, 0:SUB], p128f[:, :], SB.qraw[:, :],
                            start=True, stop=True), reads=[tev, SB.ps_last[1]])
                        tt1 = p.op("dve", lambda e, qp=qp: e.tensor_mul(
                            SB.t1b[:, :], SB.qraw[:, :],
                            cosb[:, qp * SUB:(qp + 1) * SUB]),
                            reads=[tev] + t1_last, drain=True)
                        tt2 = p.op("dve", lambda e, qp=qp: e.tensor_mul(
                            SB.t2b[:, :], SB.p_s[1][:, 0:SUB],
                            sinb[:, qp * SUB:(qp + 1) * SUB]),
                            reads=[tsw] + t2_last, drain=True)
                        SB.ps_last[1] = tt2
                        tq = p.op("dve", lambda e, mc=mc: e.tensor_add(
                            q_pan[:, mc * SUB:(mc + 1) * SUB], SB.t1b[:, :], SB.t2b[:, :]),
                            reads=[tt1, tt2] + q_consumed, drain=True)
                        qraw_last[:] = [tsw, tt1]
                        t1_last[:] = [tq]
                        t2_last[:] = [tq]
                        toks.append(tq)
                    return toks

                cfg0 = dict(
                    build_q=build_q0,
                    k_lhsT=lambda h, kt: kdup[64 * (h & 1):64 * (h & 1) + 64,
                                              kt * 128:(kt + 1) * 128],
                    vtm_slice=lambda h, kt: vtm0[:, kt * (HD + 1): (kt + 1) * (HD + 1)],
                    nmax_k=nmax_k, kv_tok=("dgp", p.counts["dgp"]),
                    wo=lambda mc: _chunk3(dr["l0_outT"][:, :])[:, :, mc * 128:(mc + 1) * 128],
                    outb=(outb_s, tob), resid_in=dr["xT"], resid_out=resid, fsc=fsc,
                    fsc_tok=("dgp", p.counts["dgp"]), rope_scratch=True)
                emit_scores(p, st, C, 0, cfg0)
                tap("resid", None, 0)
                ck("l0")

            with ExitStack() as l1:
                cq = _sb(l1, nc, "cq1", [128, 2 * T], BF16)
                with ExitStack() as ly:
                    ckv = _sb(ly, nc, "ckv1", [128, 2 * T], BF16)
                    with ExitStack() as lx:
                        xn1 = _sb(lx, nc, "xn1", [128, KC * T], BF16)
                        emit_ln(p, st, C, resid, xn1, 1)
                        tap("xn1", xn1, KC * T)
                        ck("ln1")
                        emit_lowrank_proj(p, st, C, xn1, dr["l1_Wdq"], cq, 256, "cq")
                        emit_lowrank_proj(p, st, C, xn1, dr["l1_Wdkv"], ckv, 256, "ckv")
                        tap("cq", cq, 2 * T)
                        tap("ckv", ckv, 2 * T)
                        ck("proj1")

                        with ExitStack() as lf:
                            c_all = _sb(lf, nc, "call1", [NE, T], BF16)
                            emit_routing(p, st, C, xn1, c_all, dr["l1_router"], None, 1)
                            tap("call1", c_all, T)
                            ck("rt1")
                            fc1 = dr["l1_fc1"]
                            fc2 = dr["l1_fc2"]
                            experts = []
                            for e_ in range(NE):
                                experts.append(dict(
                                    w1a=lambda ocp, e_=e_: _chunk3(fc1[e_])[:, :, ocp * 128:(ocp + 1) * 128],
                                    w1b=lambda ocp, e_=e_: _chunk3(fc1[e_])[:, :, D + ocp * 128: D + (ocp + 1) * 128],
                                    w2=lambda mc, e_=e_: _chunk3(fc2[e_])[:, :, mc * 128:(mc + 1) * 128],
                                    use_c=True, e_idx=e_, c_tok=None))
                            for sh in range(2):
                                experts.append(dict(
                                    w1a=lambda ocp, sh=sh: _chunk3(dr["l1_gate"][:, :])[:, :, sh * D + ocp * 128: sh * D + (ocp + 1) * 128],
                                    w1b=lambda ocp, sh=sh: _chunk3(dr["l1_up"][:, :])[:, :, sh * D + ocp * 128: sh * D + (ocp + 1) * 128],
                                    w2=lambda mc, sh=sh: dr["l1_down"][:, :].rearrange(
                                        "(kc p) m -> p kc m", p=128)[:, sh * KC:(sh + 1) * KC,
                                                                     mc * 128:(mc + 1) * 128],
                                    use_c=False, e_idx=0, c_tok=None))
                            emit_moe(p, st, C, xn1, c_all, fsc, 1, experts)
                    ck("ffn1")
                    kfm = _sb(l1, nc, "kfm1", [128, KC * T], BF16, side="right")
                    vtm1 = _sb(l1, nc, "vtm1", [128, NKT * NH * (HD + 1)], BF16, side="right")
                    nmax_k1 = _sb(l1, nc, "nmk1", [128, 4], F32, side="right")
                    lw1 = dict(Wuk=dr["l1_Wuk"], Wuv=dr["l1_Wuv"])
                    emit_kv_l1(p, st, C, lw1, ckv, kfm, vtm1, nmax_k1, 1)
                    tap("kfm", kfm, KC * T)
                    ck("kv1")

                wuq = _sb(l1, nc, "wuq1", [128, 2 * D], BF16, side="right")
                twq = p.dma("pool", lambda e: e.dma_start(
                    out=wuq[:, :].rearrange("p (kc m) -> p kc m", kc=2),
                    in_=dr["l1_Wuq"][:, :].rearrange("(kc p) m -> p kc m", p=128)))

                def build_q1(p, qp, q_pan, SB, q_consumed):
                    toks = []
                    for mc in range(KC):
                        tm = None
                        for kc in range(2):
                            tm = p.op("pe", lambda e, kc=kc, mc=mc, qp=qp: e.matmul(
                                SB.p_s[0][:, 0:SUB],
                                wuq[:, (kc * KC + mc) * 128:(kc * KC + mc + 1) * 128],
                                cq[:, kc * T + qp * SUB: kc * T + (qp + 1) * SUB],
                                start=(kc == 0), stop=(kc == 1)),
                                reads=[twq, SB.ps_last[0]])
                        tq = p.op("dve", lambda e, mc=mc: e.tensor_copy(
                            q_pan[:, mc * SUB:(mc + 1) * SUB], SB.p_s[0][:, 0:SUB]),
                            reads=[tm] + q_consumed)
                        SB.ps_last[0] = tq
                        toks.append(tq)
                    return toks

                cfg1 = dict(
                    build_q=build_q1,
                    k_lhsT=lambda h, kt: kfm[64 * (h & 1):64 * (h & 1) + 64,
                                             (h >> 1) * T + kt * 128:(h >> 1) * T + (kt + 1) * 128],
                    vtm_slice=lambda h, kt: vtm1[:, (kt * NH + h) * (HD + 1):
                                                 (kt * NH + h) * (HD + 1) + HD + 1],
                    nmax_k=nmax_k1, kv_tok=("dve", p.counts["dve"]),
                    wo=lambda mc: _chunk3(dr["l1_Wo"][:, :])[:, :, mc * 128:(mc + 1) * 128],
                    outb=None, resid_in=resid, resid_out=outT, fsc=fsc,
                    fsc_tok=("dgp", p.counts["dgp"]))
                emit_scores(p, st, C, 1, cfg1)

        p.barrier()
        p.emit()
    return nc


_CACHE = {}


def kernel(**inputs):
    consts = _host_consts()
    w = _prep_weights(inputs)
    x = np.asarray(inputs["L0_attn_x"], dtype=np.float32)
    key = "full"
    if key not in _CACHE:
        _CACHE[key] = build_program()
    nc = _CACHE[key]
    in_maps = []
    for b in range(8):
        m = dict(consts)
        m.update(w)
        m["xT"] = np.ascontiguousarray(x[b].T)
        in_maps.append(m)
    res = run_bass_kernel_spmd(nc, in_maps, core_ids=list(range(8)))
    out = np.empty((8, T, D), np.float32)
    for b in range(8):
        out[b] = res.results[b]["outT"].T
    return out


# version 11
# speedup vs baseline: 1.2987x; 1.0449x over previous
import numpy as np
import ml_dtypes
from contextlib import ExitStack

import concourse.bass as bass
import concourse.mybir as mybir
from concourse.bass_utils import run_bass_kernel_spmd

F32 = mybir.dt.float32
BF16 = mybir.dt.bfloat16
AF = mybir.ActivationFunctionType
ALU = mybir.AluOpType
AX = mybir.AxisListType
BF = ml_dtypes.bfloat16

T = 4096
D = 1024
KC = 8
SUB = 512
NSUB = T // SUB
NH = 16
HD = 64
NE = 8
NQP = T // SUB
NKT = T // 128
ROPE_THETA = 10000.0

DMA_CH = {"sp": "dsp", "pool": "dgp"}
ENG_SECT = {"pe": "tensor", "act": "scalar", "dve": "vector", "pool": "gpsimd", "sp": "sync"}
CHANNELS = ("pe", "act", "dve", "pool", "sp", "dsp", "dgp")


class Prog:

    def __init__(self, nc, stack):
        self.nc = nc
        self.sems = {ch: stack.enter_context(nc.semaphore(f"s_{ch}")) for ch in CHANNELS}
        self.counts = {ch: 0 for ch in CHANNELS}
        self.known = {e: {ch: 0 for ch in CHANNELS} for e in ENG_SECT}
        self.thunks = {e: [] for e in ENG_SECT}

    def _waits_for(self, eng, reads):
        merged = {}
        for tok in reads:
            if tok is None:
                continue
            ch, v = tok
            if v > merged.get(ch, 0):
                merged[ch] = v
        waits = []
        for ch, v in merged.items():
            if self.known[eng][ch] < v:
                waits.append((self.sems[ch], v))
                self.known[eng][ch] = v
        return waits

    def op(self, eng, fn, reads=(), drain=False):
        waits = self._waits_for(eng, reads)
        self.counts[eng] += 1
        tok = (eng, self.counts[eng])
        sem = self.sems[eng]

        def thunk(e, waits=waits, fn=fn, drain=drain, sem=sem):
            for s, v in waits:
                e.wait_ge(s, v)
            if drain:
                e.drain()
            fn(e).then_inc(sem, 1)

        self.thunks[eng].append(thunk)
        return tok

    def dma(self, eng, fn, reads=()):
        waits = self._waits_for(eng, reads)
        ch = DMA_CH[eng]
        self.counts[ch] += 16
        tok = (ch, self.counts[ch])
        sem = self.sems[ch]

        def thunk(e, waits=waits, fn=fn, sem=sem):
            for s, v in waits:
                e.wait_ge(s, v)
            fn(e).then_inc(sem, 16)

        self.thunks[eng].append(thunk)
        return tok

    def barrier(self):
        toks = [(ch, self.counts[ch]) for ch in CHANNELS]
        for eng in ENG_SECT:
            waits = self._waits_for(eng, toks)
            if waits:
                def thunk(e, waits=waits):
                    for s, v in waits:
                        e.wait_ge(s, v)
                self.thunks[eng].append(thunk)

    def emit(self):
        with self.nc.Block() as blk:
            for eng, sect in ENG_SECT.items():
                getattr(blk, sect)(self._make_runner(eng))

    def _make_runner(self, eng):
        thunks = self.thunks[eng]

        def run(e):
            for t in thunks:
                t(e)
        return run



def _host_consts():
    i = np.arange(HD // 2, dtype=np.float64)
    inv_freq = 1.0 / ROPE_THETA ** (2 * i / HD)
    pos = np.arange(T, dtype=np.float64)
    ang = pos[None, :] * inv_freq[:, None]
    cos32 = np.cos(ang)
    sin32 = np.sin(ang)
    cos64 = np.repeat(cos32, 2, axis=0)
    sin64s = np.empty((HD, T))
    sin64s[0::2] = -sin32
    sin64s[1::2] = sin32
    cos128 = np.concatenate([cos64, cos64], 0).astype(np.float32)
    sin128 = np.concatenate([sin64s, sin64s], 0).astype(np.float32)

    p64 = np.zeros((HD, HD), np.float32)
    for k in range(0, HD, 2):
        p64[k, k + 1] = 1.0
        p64[k + 1, k] = 1.0
    p128 = np.zeros((128, 128), np.float32)
    p128[:64, :64] = p64
    p128[64:, 64:] = p64

    cmask = np.zeros((128, 4 * SUB), np.float32)
    dk = np.arange(128)[:, None]
    qr = np.arange(SUB)[None, :]
    for j in range(4):
        cmask[:, j * SUB:(j + 1) * SUB] = (qr >= 128 * j + dk).astype(np.float32)

    ident = np.eye(128, dtype=np.float32)
    ohsel = np.zeros((NE, NE * 128), np.float32)
    for e in range(NE):
        ohsel[e, e * 128:(e + 1) * 128] = 1.0
    return dict(c_cos=cos128, c_sin=sin128, c_p128=p128, c_cmask=cmask,
                c_ident=ident, c_ohsel=ohsel)


def _prep_weights(inp):
    c = lambda a: np.ascontiguousarray(np.asarray(a, dtype=np.float32))
    w = {}
    w["l0_qkvT"] = c(np.asarray(inp["L0_attn_qkv_w"]).T)
    qb = np.asarray(inp["L0_attn_qkv_b"])
    qkvb = np.zeros((128, 10), np.float32)
    qkvb[:, 0:9] = np.asarray(qb).reshape(9, 128).T
    qkvb[0:64, 9] = qb[1088:1152]
    qkvb[64:128, 8] = 0.0
    w["l0_qkvb"] = c(qkvb)
    w["l0_outT"] = c(np.asarray(inp["L0_attn_out_w"]).T)
    w["l0_outb"] = c(np.asarray(inp["L0_attn_out_b"]).reshape(8, 128).T)
    w["l0_WgT"] = c(np.asarray(inp["L0_ffn_Wg"]).T)
    w["l0_bg"] = c(np.asarray(inp["L0_ffn_bg"]).reshape(8, 1))
    w["l0_w1T"] = c(np.asarray(inp["L0_ffn_w1"]).transpose(0, 2, 1))
    w["l0_w2T"] = c(np.asarray(inp["L0_ffn_w2"]).transpose(0, 2, 1))
    w["l1_Wdq"] = c(inp["L1_attn_W_dq"])
    w["l1_Wuq"] = c(inp["L1_attn_W_uq"])
    w["l1_Wdkv"] = c(inp["L1_attn_W_dkv"])
    w["l1_Wuk"] = c(inp["L1_attn_W_uk"])
    w["l1_Wuv"] = c(inp["L1_attn_W_uv"])
    w["l1_Wo"] = c(inp["L1_attn_W_o"])
    w["l1_router"] = c(inp["L1_ffn_w_router"])
    w["l1_fc1"] = c(inp["L1_ffn_fc1_w"])
    w["l1_fc2"] = c(inp["L1_ffn_fc2_w"])
    w["l1_gate"] = c(inp["L1_ffn_gate_w"])
    w["l1_up"] = c(inp["L1_ffn_up_w"])
    w["l1_down"] = c(inp["L1_ffn_down_w"])
    return w


class Bufs:
    pass


def _sb(stack, nc, name, shape, dtype, side=None):
    kw = {} if side is None else {"side": side}
    return stack.enter_context(nc.sbuf_tensor(name, shape, dtype, **kw))


def _ps(stack, nc, name, shape, dtype=F32):
    return stack.enter_context(nc.psum_tensor(name, shape, dtype))


def _chunk3(dram_ap):
    return dram_ap.rearrange("(kc p) m -> p kc m", p=128)



def emit_consts(p, st, dr):
    nc = p.nc
    C = Bufs()
    C.ident_f = _sb(st, nc, "ident_f", [128, 128], F32, side="right")
    C.ident_b = _sb(st, nc, "ident_b", [128, 128], BF16, side="right")
    C.cmask = _sb(st, nc, "cmask", [128, 4 * SUB], BF16, side="right")
    C.ohsel = _sb(st, nc, "ohsel", [NE, NE * 128], BF16, side="right")
    C.ones_b = _sb(st, nc, "ones_b", [128, 1], BF16, side="right")
    C.ones_fr = _sb(st, nc, "ones_fr", [128, 128], F32, side="right")
    C.ones_br = _sb(st, nc, "ones_br", [128, 128], BF16, side="right")
    C.fconst = _sb(st, nc, "fconst", [1, 2], F32, side="right")
    p.op("dve", lambda e: e.memset(C.fconst[0:1, 0:1], 1e-5))
    p.op("dve", lambda e: e.memset(C.fconst[0:1, 1:2], 1e-30))
    p.dma("sp", lambda e: e.dma_start(out=C.ident_f[:, :], in_=dr["c_ident"][:, :]))
    p.dma("pool", lambda e: e.dma_start(out=C.cmask[:, :], in_=dr["c_cmask"][:, :]))
    p.dma("pool", lambda e: e.dma_start(out=C.ident_b[:, :], in_=dr["c_ident"][:, :]))
    p.dma("pool", lambda e: e.dma_start(out=C.ohsel[:, :], in_=dr["c_ohsel"][:, :]))
    p.op("dve", lambda e: e.memset(C.ones_b[:, :], 1.0))
    p.op("dve", lambda e: e.memset(C.ones_fr[:, :], 1.0))
    p.op("dve", lambda e: e.memset(C.ones_br[:, :], 1.0))
    p.barrier()
    return C

def emit_ln(p, st, C, src_dram, xn, layer):
    nc = p.nc
    with ExitStack() as ph:
        x_f = [_sb(ph, nc, f"ln_xf{layer}_{i}", [128, KC * SUB], F32) for i in range(2)]
        x_b = [_sb(ph, nc, f"ln_xb{layer}_{i}", [128, KC * SUB], BF16) for i in range(2)]
        xsq = _sb(ph, nc, f"ln_xsq{layer}", [128, KC * SUB], BF16)
        tmp = _sb(ph, nc, f"ln_tmp{layer}", [128, KC * SUB], F32)
        rows = [_sb(ph, nc, f"ln_rows{layer}_{i}", [1, 6 * SUB], F32) for i in range(2)]
        p_sum = [_ps(ph, nc, f"lnps{layer}_{i}", [1, SUB]) for i in range(2)]
        p_sq = [_ps(ph, nc, f"lnpq{layer}_{i}", [1, SUB]) for i in range(2)]
        p_M = [_ps(ph, nc, f"lnpM{layer}_{i}", [128, SUB]) for i in range(2)]
        p_R = [_ps(ph, nc, f"lnpR{layer}_{i}", [128, SUB]) for i in range(2)]

        src3 = src_dram[:, :].rearrange("(kc p) t -> p kc t", p=128)
        xf_gate = [None, None]
        xb_gate = [[], []]
        xsq_gate = None
        tmp_gate = None
        rows_gate = [None, None]
        psum_gate = [None, None]
        psq_gate = [None, None]
        pM_gate = [None, None]
        pR_gate = [None, None]
        for s in range(NSUB):
            i = s % 2
            tl = p.dma("sp", lambda e, s=s, i=i: e.dma_start(
                out=x_f[i][:, :].rearrange("p (kc t) -> p kc t", kc=KC),
                in_=src3[:, :, s * SUB:(s + 1) * SUB]), reads=[xf_gate[i]])
            tlb = p.dma("pool", lambda e, s=s, i=i: e.dma_start(
                out=x_b[i][:, :].rearrange("p (kc t) -> p kc t", kc=KC),
                in_=src3[:, :, s * SUB:(s + 1) * SUB]), reads=xb_gate[i])
            tsq = []
            for kc in range(KC):
                tsq.append(p.op("act", lambda e, kc=kc, i=i: e.activation(
                    xsq[:, kc * SUB:(kc + 1) * SUB], x_b[i][:, kc * SUB:(kc + 1) * SUB],
                    AF.Square), reads=[tlb, xsq_gate]))
            tm = tv = None
            for kc in range(KC):
                tm = p.op("pe", lambda e, kc=kc, i=i: e.matmul(
                    p_sum[i][:, :], C.ones_b[:, :], x_b[i][:, kc * SUB:(kc + 1) * SUB],
                    start=(kc == 0), stop=(kc == KC - 1)), reads=[tlb, psum_gate[i]])
            for kc in range(KC):
                tv = p.op("pe", lambda e, kc=kc, i=i: e.matmul(
                    p_sq[i][:, :], C.ones_b[:, :], xsq[:, kc * SUB:(kc + 1) * SUB],
                    start=(kc == 0), stop=(kc == KC - 1)), reads=[tsq[kc], psq_gate[i]])
            xsq_gate = tv
            R = lambda j, i=i: rows[i][0:1, j * SUB:(j + 1) * SUB]
            t1 = p.op("dve", lambda e, R=R, i=i: e.tensor_scalar_mul(R(0), p_sum[i][:, :], 1.0 / D),
                      reads=[tm, rows_gate[i]])
            t2 = p.op("dve", lambda e, R=R, i=i: e.tensor_scalar_mul(R(1), p_sq[i][:, :], 1.0 / D),
                      reads=[tv], drain=True)
            psum_gate[i] = t1
            psq_gate[i] = t2
            t3 = p.op("dve", lambda e, R=R: e.tensor_mul(R(2), R(0), R(0)),
                      reads=[t1], drain=True)
            t4 = p.op("dve", lambda e, R=R: e.tensor_sub(R(3), R(1), R(2)),
                      reads=[t2, t3], drain=True)
            t5 = p.op("act", lambda e, R=R: e.activation(R(4), R(3), AF.Ln,
                                                         bias=C.fconst[0:1, 0:1]),
                      reads=[t4])
            t6 = p.op("act", lambda e, R=R: e.activation(R(5), R(4), AF.Exp, scale=-0.5),
                      reads=[t5], drain=True)
            tM = p.op("pe", lambda e, R=R, i=i: e.matmul(p_M[i][:, :], C.ones_fr[0:1, :],
                                                         R(0), start=True, stop=True),
                      reads=[t1, t3, pM_gate[i]])
            tR = p.op("pe", lambda e, R=R, i=i: e.matmul(p_R[i][:, :], C.ones_fr[0:1, :],
                                                         R(5), start=True, stop=True),
                      reads=[t6, pR_gate[i]])
            rows_gate[i] = tR
            tsub = []
            for kc in range(KC):
                tsub.append(p.op("dve", lambda e, kc=kc, i=i: e.tensor_sub(
                    tmp[:, kc * SUB:(kc + 1) * SUB], x_f[i][:, kc * SUB:(kc + 1) * SUB],
                    p_M[i][:, :]), reads=[tl, tM, tmp_gate]))
            tout = []
            for kc in range(KC):
                tout.append(p.op("dve", lambda e, kc=kc, s=s, i=i: e.tensor_mul(
                    xn[:, kc * T + s * SUB: kc * T + (s + 1) * SUB],
                    tmp[:, kc * SUB:(kc + 1) * SUB], p_R[i][:, :]), reads=[tsub[kc], tR]))
            tmp_gate = tout[-1]
            pM_gate[i] = tout[-1]
            pR_gate[i] = tout[-1]
            xf_gate[i] = tout[-1]
            xb_gate[i] = [tm, tsq[-1]]
        p.barrier()


def emit_routing(p, st, C, xn, c_all, router_dram, bias_dram, layer):
    nc = p.nc
    NQ = SUB // 128
    with ExitStack() as ph:
        wrt = _sb(ph, nc, f"rt_w{layer}", [128, KC * NE], BF16)
        lg = _sb(ph, nc, f"rt_lg{layer}", [NE, SUB], F32)
        bias_s = _sb(ph, nc, f"rt_b{layer}", [NE, 1], F32)
        ltok = [_sb(ph, nc, f"rt_lt{layer}_{q}", [128, NE], F32) for q in range(NQ)]
        sc = [_sb(ph, nc, f"rt_sc{layer}_{q}", [128, 8], F32) for q in range(NQ)]
        msk = [_sb(ph, nc, f"rt_msk{layer}_{q}", [128, NE], F32) for q in range(NQ)]
        pex = [_sb(ph, nc, f"rt_pex{layer}_{q}", [128, NE], F32) for q in range(NQ)]
        ctok = [_sb(ph, nc, f"rt_ct{layer}_{q}", [128, NE], F32) for q in range(NQ)]
        p_lg = [_ps(ph, nc, f"rt_plg{layer}_{i}", [NE, SUB]) for i in range(2)]
        p_t = [_ps(ph, nc, f"rt_pt{layer}_{i}", [128, NE]) for i in range(2)]
        p_ct = [_ps(ph, nc, f"rt_pct{layer}_{i}", [NE, 128]) for i in range(2)]

        tw = p.dma("pool", lambda e: e.dma_start(
            out=wrt[:, :].rearrange("p (kc e) -> p kc e", kc=KC),
            in_=_chunk3(router_dram[:, :])))
        if bias_dram is not None:
            tb = p.dma("sp", lambda e: e.dma_start(out=bias_s[:, :], in_=bias_dram[:, :]))
        else:
            tb = p.op("dve", lambda e: e.memset(bias_s[:, :], 0.0))

        plg_gate = [None, None]
        pt_gate = [None, None]
        pct_gate = [None, None]
        lg_gate = None
        for s in range(NSUB):
            il = s % 2
            tmm = None
            for kc in range(KC):
                tmm = p.op("pe", lambda e, kc=kc, s=s, il=il: e.matmul(
                    p_lg[il][:, :], wrt[:, kc * NE:(kc + 1) * NE],
                    xn[:, kc * T + s * SUB: kc * T + (s + 1) * SUB],
                    start=(kc == 0), stop=(kc == KC - 1)), reads=[tw, plg_gate[il]])
            te = p.op("dve", lambda e, il=il: e.tensor_scalar(lg[:, :], p_lg[il][:, :],
                                                              bias_s[0:NE, 0:1], None,
                                                              ALU.add),
                      reads=[tmm, tb, lg_gate])
            plg_gate[il] = te
            t0s = []
            for q in range(NQ):
                it = q % 2
                tt = p.op("pe", lambda e, q=q, it=it: e.transpose(
                    p_t[it][:, :], lg[:, q * 128:(q + 1) * 128], C.ident_f[0:NE, 0:NE]),
                    reads=[te, pt_gate[it]])
                t0 = p.op("dve", lambda e, q=q, it=it: e.tensor_copy(ltok[q][:, :],
                                                                     p_t[it][:, :]),
                          reads=[tt])
                pt_gate[it] = t0
                t0s.append(t0)
            lg_gate = t0s[-1]
            tA = [p.op("dve", lambda e, q=q: e.reduce_max(sc[q][:, 0:1], ltok[q][:, :],
                                                          axis=AX.X, negate=True),
                       reads=[t0s[q]]) for q in range(NQ)]
            tB = [p.op("dve", lambda e, q=q: e.tensor_scalar(msk[q][:, :], ltok[q][:, :],
                                                             sc[q][:, 0:1], None, ALU.add),
                       reads=[tA[q]]) for q in range(NQ)]
            tC = [p.op("dve", lambda e, q=q: e.tensor_scalar(msk[q][:, :], msk[q][:, :],
                                                             0.0, None, ALU.is_ge),
                       reads=[tB[q]]) for q in range(NQ)]
            tD = [p.op("dve", lambda e, q=q: e.scalar_tensor_tensor(
                msk[q][:, :], msk[q][:, :], -1e9, ltok[q][:, :], ALU.mult, ALU.add),
                reads=[tC[q]]) for q in range(NQ)]
            tE = [p.op("dve", lambda e, q=q: e.reduce_max(sc[q][:, 1:2], msk[q][:, :],
                                                          axis=AX.X), reads=[tD[q]])
                  for q in range(NQ)]
            tF = [p.op("act", lambda e, q=q: e.activation(pex[q][:, :], ltok[q][:, :],
                                                          AF.Exp, bias=sc[q][:, 0:1]),
                       reads=[tA[q]]) for q in range(NQ)]
            tG = [p.op("dve", lambda e, q=q: e.tensor_scalar(ctok[q][:, :], ltok[q][:, :],
                                                             sc[q][:, 1:2], None, ALU.is_ge),
                       reads=[tE[q]]) for q in range(NQ)]
            tH = [p.op("dve", lambda e, q=q: e.tensor_mul(ctok[q][:, :], ctok[q][:, :],
                                                          pex[q][:, :]),
                       reads=[tF[q], tG[q]]) for q in range(NQ)]
            tI = [p.op("dve", lambda e, q=q: e.reduce_sum(sc[q][:, 2:3], ctok[q][:, :],
                                                          axis=AX.X), reads=[tH[q]])
                  for q in range(NQ)]
            tJ = [p.op("dve", lambda e, q=q: e.reciprocal(sc[q][:, 3:4], sc[q][:, 2:3]),
                       reads=[tI[q]]) for q in range(NQ)]
            tK = [p.op("dve", lambda e, q=q: e.tensor_scalar(ctok[q][:, :], ctok[q][:, :],
                                                             sc[q][:, 3:4], None, ALU.mult),
                       reads=[tJ[q]]) for q in range(NQ)]
            for q in range(NQ):
                it = q % 2
                tc = p.op("pe", lambda e, q=q, it=it: e.transpose(
                    p_ct[it][:, :], ctok[q][:, :], C.ident_f[:, :]),
                    reads=[tK[q], pct_gate[it]])
                tcp = p.op("dve", lambda e, s=s, q=q, it=it: e.tensor_copy(
                    c_all[:, s * SUB + q * 128: s * SUB + (q + 1) * 128], p_ct[it][:, :]),
                    reads=[tc])
                pct_gate[it] = tcp
        p.barrier()


def emit_moe(p, st, C, xn, c_all, fsc, layer, experts):
    nc = p.nc
    with ExitStack() as ph:
        a_buf = _sb(ph, nc, f"moe_a{layer}", [128, KC * T], BF16)
        w1s = [_sb(ph, nc, f"moe_w1{layer}_{i}", [128, 2 * KC * 128], BF16) for i in range(2)]
        w2s = [_sb(ph, nc, f"moe_w2{layer}_{i}", [128, KC * 128], BF16) for i in range(2)]
        silu_t = [_sb(ph, nc, f"moe_st{layer}_{i}", [128, SUB], BF16) for i in range(2)]
        c_sb = _sb(ph, nc, f"moe_cb{layer}", [128, T], BF16)
        o_t = [_sb(ph, nc, f"moe_ot{layer}_{i}", [128, SUB], F32) for i in range(3)]
        p_h = [_ps(ph, nc, f"moe_ph{layer}_{i}", [128, SUB]) for i in range(4)]
        p_o = [_ps(ph, nc, f"moe_po{layer}_{i}", [128, SUB]) for i in range(2)]
        p_c = _ps(ph, nc, f"moe_pc{layer}", [128, SUB])

        fsc3 = fsc[:, :].rearrange("(mc p) t -> p mc t", p=128)
        region_tok = {}
        w1_last = [None, None]
        w2_last = [None, None]
        ph_last = [None] * 4
        po_last = [None] * 2
        ot_last = [None] * 3
        cs_last = None
        si_last = [None, None]
        pc_last = None

        def w1_load(ex, ocp, slot):
            twa = p.dma("pool", lambda e, ex=ex, ocp=ocp, slot=slot: e.dma_start(
                out=w1s[slot][:, 0:KC * 128].rearrange("p (kc m) -> p kc m", kc=KC),
                in_=ex["w1a"](ocp)), reads=[w1_last[slot]])
            twb = p.dma("pool", lambda e, ex=ex, ocp=ocp, slot=slot: e.dma_start(
                out=w1s[slot][:, KC * 128:2 * KC * 128].rearrange("p (kc m) -> p kc m", kc=KC),
                in_=ex["w1b"](ocp)), reads=[w1_last[slot]])
            return twa, twb

        w1_pending = {}
        for ei, ex in enumerate(experts):
            if (ei, 0) not in w1_pending:
                w1_pending[(ei, 0)] = w1_load(ex, 0, 0)
            for ocp in range(KC):
                wb = w1s[ocp % 2]
                twa, twb = w1_pending.pop((ei, ocp))
                if ocp + 1 < KC:
                    w1_pending[(ei, ocp + 1)] = w1_load(ex, ocp + 1, (ocp + 1) % 2)
                elif ei + 1 < len(experts):
                    w1_pending[(ei + 1, 0)] = w1_load(experts[ei + 1], 0, 0)
                last_a = None
                for s in range(NSUB):
                    ia, ib = (2 * s) % 4, (2 * s + 1) % 4
                    pa, pb = p_h[ia], p_h[ib]
                    tma = tmb = None
                    for kc in range(KC):
                        tma = p.op("pe", lambda e, kc=kc, s=s, wb=wb, pa=pa: e.matmul(
                            pa[:, :], wb[:, kc * 128:(kc + 1) * 128],
                            xn[:, kc * T + s * SUB: kc * T + (s + 1) * SUB],
                            start=(kc == 0), stop=(kc == KC - 1)),
                            reads=[twa, ph_last[ia]])
                    for kc in range(KC):
                        tmb = p.op("pe", lambda e, kc=kc, s=s, wb=wb, pb=pb: e.matmul(
                            pb[:, :], wb[:, (KC + kc) * 128:(KC + kc + 1) * 128],
                            xn[:, kc * T + s * SUB: kc * T + (s + 1) * SUB],
                            start=(kc == 0), stop=(kc == KC - 1)),
                            reads=[twb, ph_last[ib]])
                    stl = silu_t[s % 2]
                    ts = p.op("act", lambda e, stl=stl, pa=pa: e.activation(
                        stl[:, :], pa[:, :], AF.Silu), reads=[tma, si_last[s % 2]])
                    ta = p.op("dve", lambda e, ocp=ocp, s=s, stl=stl, pb=pb: e.tensor_mul(
                        a_buf[:, ocp * T + s * SUB: ocp * T + (s + 1) * SUB],
                        stl[:, :], pb[:, :]), reads=[ts, tmb])
                    si_last[s % 2] = ta
                    ph_last[ia] = ts
                    ph_last[ib] = ta
                    last_a = ta
                w1_last[ocp % 2] = last_a
            if ex["use_c"]:
                for s in range(NSUB):
                    tcm = p.op("pe", lambda e, ex=ex, s=s: e.matmul(
                        p_c[:, :], C.ohsel[:, ex["e_idx"] * 128:(ex["e_idx"] + 1) * 128],
                        c_all[:, s * SUB:(s + 1) * SUB], start=True, stop=True),
                        reads=[pc_last, ex["c_tok"]])
                    tce = p.op("dve", lambda e, s=s: e.tensor_copy(
                        c_sb[:, s * SUB:(s + 1) * SUB], p_c[:, :]),
                        reads=[tcm, cs_last])
                    pc_last = tce
                cs_tok = pc_last
            else:
                cs_tok = None
            def w2_load(mc):
                return p.dma("pool", lambda e, ex=ex, mc=mc: e.dma_start(
                    out=w2s[mc % 2][:, :].rearrange("p (kc m) -> p kc m", kc=KC),
                    in_=ex["w2"](mc)), reads=[w2_last[mc % 2]])
            w2_next = w2_load(0)
            for mc in range(KC):
                tw2 = w2_next
                if mc + 1 < KC:
                    w2_next = w2_load(mc + 1)
                last_ev = None
                for s in range(NSUB):
                    po = p_o[s % 2]
                    tmo = None
                    for kc in range(KC):
                        tmo = p.op("pe", lambda e, kc=kc, s=s, mc=mc, po=po: e.matmul(
                            po[:, :], w2s[mc % 2][:, kc * 128:(kc + 1) * 128],
                            a_buf[:, kc * T + s * SUB: kc * T + (s + 1) * SUB],
                            start=(kc == 0), stop=(kc == KC - 1)),
                            reads=[tw2, po_last[s % 2]])
                    i3 = (mc * NSUB + s) % 3
                    ot = o_t[i3]
                    if ex["use_c"]:
                        tev = p.op("dve", lambda e, ot=ot, po=po, s=s: e.tensor_mul(
                            ot[:, :], po[:, :], c_sb[:, s * SUB:(s + 1) * SUB]),
                            reads=[tmo, cs_tok, ot_last[i3]])
                    else:
                        tev = p.op("dve", lambda e, ot=ot, po=po: e.tensor_copy(
                            ot[:, :], po[:, :]), reads=[tmo, ot_last[i3]])
                    po_last[s % 2] = tev
                    last_ev = tev
                    accum = ALU.add if ei > 0 else ALU.bypass
                    td = p.dma("pool", lambda e, mc=mc, s=s, ot=ot, accum=accum: e.dma_start(
                        out=fsc3[:, mc, s * SUB:(s + 1) * SUB], in_=ot[:, :],
                        accum_op=accum), reads=[tev, region_tok.get((mc, s))])
                    region_tok[(mc, s)] = td
                    ot_last[i3] = td
                w2_last[mc % 2] = last_ev
            if ex["use_c"]:
                cs_last = w2_last[0]
        p.barrier()


def emit_lowrank_proj(p, st, C, xn, projs, ncol, tag):
    nc = p.nc
    nmc = ncol // 128
    with ExitStack() as ph:
        wts = [_sb(ph, nc, f"lr_w_{tag}{j}", [128, KC * ncol], BF16)
               for j in range(len(projs))]
        p_x = [_ps(ph, nc, f"lr_p_{tag}{i}", [128, SUB]) for i in range(4)]
        tws = []
        for j, (w_dram, _) in enumerate(projs):
            tws.append(p.dma("pool", lambda e, j=j, w_dram=w_dram: e.dma_start(
                out=wts[j][:, :].rearrange("p (kc m) -> p kc m", kc=KC),
                in_=_chunk3(w_dram[:, :]))))
        last = [None] * 4
        i = 0
        for s in range(NSUB):
            for j, (_, out_fm) in enumerate(projs):
                for mc in range(nmc):
                    px = p_x[i % 4]
                    tm = None
                    for kc in range(KC):
                        tm = p.op("pe", lambda e, kc=kc, mc=mc, s=s, px=px, j=j: e.matmul(
                            px[:, :], wts[j][:, (kc * nmc + mc) * 128:(kc * nmc + mc + 1) * 128],
                            xn[:, kc * T + s * SUB: kc * T + (s + 1) * SUB],
                            start=(kc == 0), stop=(kc == KC - 1)),
                            reads=[tws[j], last[i % 4]])
                    te = p.op("dve", lambda e, mc=mc, s=s, px=px, out_fm=out_fm: e.tensor_copy(
                        out_fm[:, mc * T + s * SUB: mc * T + (s + 1) * SUB], px[:, :]),
                        reads=[tm])
                    last[i % 4] = te
                    i += 1
        p.barrier()


def _norm_update(p, C, src_ap, nparts, p_n, nrow, nmax, prev_toks, reads, pn_gate):
    gates = pn_gate if isinstance(pn_gate, list) else [pn_gate]
    tn = p.op("pe", lambda e: e.matmul(p_n[:, :], C.ones_b[0:nparts, :], src_ap,
                                       start=True, stop=True),
              reads=list(reads) + gates)
    t2 = p.op("dve", lambda e: e.tensor_copy(nrow[:, :], p_n[:, :]), reads=[tn])
    t3 = p.op("dve", lambda e: e.reduce_max(nmax[0:1, 1:2], nrow[:, :], axis=AX.X),
              reads=[t2], drain=True)
    t4 = p.op("dve", lambda e: e.tensor_max(nmax[0:1, 0:1], nmax[0:1, 0:1],
                                            nmax[0:1, 1:2]),
              reads=[t3] + list(prev_toks), drain=True)
    return tn, t2, t4


def emit_kv_l0(p, st, C, lw, xn, kdup, vtm, nmax_k, layer):
    nc = p.nc
    with ExitStack() as ph:
        wkt = _sb(ph, nc, "kv0_wk", [128, KC * HD], BF16)
        wvt = _sb(ph, nc, "kv0_wv", [128, KC * HD], BF16)
        kraw = _sb(ph, nc, "kv0_kraw", [128, SUB], F32)
        t1b = _sb(ph, nc, "kv0_t1", [128, SUB], F32)
        t2b = _sb(ph, nc, "kv0_t2", [128, SUB], F32)
        v_b = _sb(ph, nc, "kv0_vb", [128, SUB], BF16)
        ksq = _sb(ph, nc, "kv0_ksq", [128, SUB], BF16)
        nrow = _sb(ph, nc, "kv0_nrow", [1, SUB], F32)
        p_k = _ps(ph, nc, "kv0_pk", [128, SUB])
        p_v = _ps(ph, nc, "kv0_pv", [128, SUB])
        p_sw = _ps(ph, nc, "kv0_psw", [128, SUB])
        p_t = [_ps(ph, nc, f"kv0_pt{i}", [128, 128], BF16) for i in range(2)]
        p_n = _ps(ph, nc, "kv0_pn", [1, SUB])

        qkvT3 = _chunk3(lw["qkvT"][:, :])
        tk = p.dma("pool", lambda e: e.dma_start(
            out=wkt[:, :].rearrange("p (kc m) -> p kc m", kc=KC),
            in_=qkvT3[:, :, 1024:1088]))
        tv = p.dma("pool", lambda e: e.dma_start(
            out=wvt[:, :].rearrange("p (kc m) -> p kc m", kc=KC),
            in_=qkvT3[:, :, 1088:1152]))
        tmx = p.op("dve", lambda e: e.memset(nmax_k[:, :], 0.0))
        vtm_tok = p.op("dve", lambda e: e.memset(vtm[:, :], 1.0))
        pt_gate = [None, None]
        kraw_tok = []
        t1_tok = t2_tok = krot_tok = None
        v_tok = []
        ksq_tok = None
        pn_gate = None
        for s in range(NSUB):
            tmk = tmv = None
            for kc in range(KC):
                tmk = p.op("pe", lambda e, kc=kc, s=s: e.matmul(
                    p_k[0:HD, :], wkt[:, kc * HD:(kc + 1) * HD],
                    xn[:, kc * T + s * SUB: kc * T + (s + 1) * SUB],
                    start=(kc == 0), stop=(kc == KC - 1)), reads=[tk] + kraw_tok)
            for kc in range(KC):
                tmv = p.op("pe", lambda e, kc=kc, s=s: e.matmul(
                    p_v[0:HD, :], wvt[:, kc * HD:(kc + 1) * HD],
                    xn[:, kc * T + s * SUB: kc * T + (s + 1) * SUB],
                    start=(kc == 0), stop=(kc == KC - 1)), reads=[tv] + v_tok)
            tke = p.op("dve", lambda e: e.tensor_scalar(
                kraw[0:HD, :], p_k[0:HD, :], lw["qkvb_s"][0:HD, 8:9], None, ALU.add),
                reads=[tmk, lw["qkvb_tok"]])
            tsw = p.op("pe", lambda e: e.matmul(
                p_sw[0:HD, :], lw["p128f"][0:HD, 0:HD], kraw[0:HD, :],
                start=True, stop=True), reads=[tke, t2_tok])
            tt1 = p.op("dve", lambda e, s=s: e.tensor_mul(
                t1b[0:HD, :], kraw[0:HD, :], lw["cos"][0:HD, s * SUB:(s + 1) * SUB]),
                reads=[tke, krot_tok], drain=True)
            tt2 = p.op("dve", lambda e, s=s: e.tensor_mul(
                t2b[0:HD, :], p_sw[0:HD, :], lw["sin"][0:HD, s * SUB:(s + 1) * SUB]),
                reads=[tsw, krot_tok], drain=True)
            tkr = p.op("dve", lambda e, s=s: e.tensor_add(
                kdup[0:HD, s * SUB:(s + 1) * SUB], t1b[0:HD, :], t2b[0:HD, :]),
                reads=[tt1, tt2], drain=True)
            kraw_tok = [tsw, tt1]
            t2_tok = tt2
            krot_tok = tkr
            tq = p.op("pool", lambda e, s=s: e.tensor_mul(
                ksq[0:HD, :], kdup[0:HD, s * SUB:(s + 1) * SUB],
                kdup[0:HD, s * SUB:(s + 1) * SUB]), reads=[tkr, ksq_tok])
            tn, pn_gate, tmx = _norm_update(p, C, ksq[0:HD, :], HD, p_n, nrow, nmax_k,
                                            [tmx], [tq], pn_gate)
            ksq_tok = tn
            tve = p.op("dve", lambda e: e.tensor_scalar(
                v_b[0:HD, :], p_v[0:HD, :], lw["qkvb_s"][0:HD, 9:10], None, ALU.add),
                reads=[tmv, lw["qkvb_tok"]])
            v_tok = [tve]
            for q in range(SUB // 128):
                kt = s * (SUB // 128) + q
                it = q % 2
                pt = p_t[it]
                ttr = p.op("pe", lambda e, q=q, pt=pt: e.transpose(
                    pt[:, 0:HD], v_b[0:HD, q * 128:(q + 1) * 128], C.ident_b[0:HD, 0:HD]),
                    reads=[tve, pt_gate[it]])
                tcp = p.op("dve", lambda e, kt=kt, pt=pt: e.tensor_copy(
                    vtm[:, kt * (HD + 1): kt * (HD + 1) + HD], pt[:, 0:HD]),
                    reads=[ttr, vtm_tok])
                pt_gate[it] = tcp
                v_tok.append(ttr)
        p.dma("pool", lambda e: e.dma_start(out=kdup[HD:128, :], in_=kdup[0:HD, :]),
              reads=[krot_tok])
        p.barrier()


def emit_kv_l1(p, st, C, lw, ckv, kfm, vtm, nmax_k, layer):
    nc = p.nc
    with ExitStack() as ph:
        wuk = _sb(ph, nc, "kv1_wuk", [128, 2 * D], BF16)
        wuv = _sb(ph, nc, "kv1_wuv", [128, 2 * D], BF16)
        v_bs = [_sb(ph, nc, f"kv1_vb{i}", [128, SUB], BF16) for i in range(2)]
        ksq = _sb(ph, nc, "kv1_ksq", [128, SUB], BF16)
        nrow = _sb(ph, nc, "kv1_nrow", [1, SUB], F32)
        p_k = [_ps(ph, nc, f"kv1_pk{i}", [128, SUB]) for i in range(2)]
        p_v = _ps(ph, nc, "kv1_pv", [128, SUB])
        p_t = [_ps(ph, nc, f"kv1_pt{i}", [128, 128], BF16) for i in range(2)]
        p_n = _ps(ph, nc, "kv1_pn", [1, SUB])

        tk = p.dma("pool", lambda e: e.dma_start(
            out=wuk[:, :].rearrange("p (kc m) -> p kc m", kc=2),
            in_=lw["Wuk"][:, :].rearrange("(kc p) m -> p kc m", p=128)))
        tv = p.dma("pool", lambda e: e.dma_start(
            out=wuv[:, :].rearrange("p (kc m) -> p kc m", kc=2),
            in_=lw["Wuv"][:, :].rearrange("(kc p) m -> p kc m", p=128)))
        tmx = p.op("dve", lambda e: e.memset(nmax_k[:, :], 0.0))
        vtm_tok = p.op("dve", lambda e: e.memset(vtm[:, :], 1.0))
        pt_gate = [None, None]
        vb_gate = [[], []]
        klast = [None, None]
        ksq_tok = None
        pn_gate = None
        v_tok = []
        for s in range(NSUB):
            for mc in range(KC):
                i = mc % 2
                pk = p_k[i]
                tm = None
                for kc in range(2):
                    tm = p.op("pe", lambda e, kc=kc, mc=mc, s=s, pk=pk: e.matmul(
                        pk[:, :], wuk[:, (kc * KC + mc) * 128:(kc * KC + mc + 1) * 128],
                        ckv[:, kc * T + s * SUB: kc * T + (s + 1) * SUB],
                        start=(kc == 0), stop=(kc == 1)), reads=[tk, klast[i]])
                te = p.op("dve", lambda e, mc=mc, s=s, pk=pk: e.tensor_copy(
                    kfm[:, mc * T + s * SUB: mc * T + (s + 1) * SUB], pk[:, :]), reads=[tm])
                klast[i] = te
                tq = p.op("pool", lambda e, mc=mc, s=s: e.tensor_mul(
                    ksq[:, :], kfm[:, mc * T + s * SUB: mc * T + (s + 1) * SUB],
                    kfm[:, mc * T + s * SUB: mc * T + (s + 1) * SUB]),
                    reads=[te, ksq_tok])
                tn, pn_gate, tmx = _norm_update(p, C, ksq[:, :], 128, p_n, nrow, nmax_k,
                                                [tmx], [tq], pn_gate)
                ksq_tok = tn
            for h in range(NH):
                v_b = v_bs[h % 2]
                tm = None
                for kc in range(2):
                    tm = p.op("pe", lambda e, kc=kc, h=h, s=s: e.matmul(
                        p_v[0:HD, :],
                        wuv[:, kc * KC * 128 + h * HD: kc * KC * 128 + (h + 1) * HD],
                        ckv[:, kc * T + s * SUB: kc * T + (s + 1) * SUB],
                        start=(kc == 0), stop=(kc == 1)), reads=[tv] + v_tok)
                tve = p.op("dve", lambda e, v_b=v_b: e.tensor_copy(v_b[0:HD, :],
                                                                   p_v[0:HD, :]),
                           reads=[tm] + vb_gate[h % 2])
                v_tok = [tve]
                vb_gate[h % 2] = []
                for q in range(SUB // 128):
                    kt = s * (SUB // 128) + q
                    slot = (kt * NH + h) * (HD + 1)
                    it = q % 2
                    pt = p_t[it]
                    ttr = p.op("pe", lambda e, q=q, pt=pt, v_b=v_b: e.transpose(
                        pt[:, 0:HD], v_b[0:HD, q * 128:(q + 1) * 128],
                        C.ident_b[0:HD, 0:HD]), reads=[tve, pt_gate[it]])
                    vb_gate[h % 2].append(ttr)
                    tcp = p.op("dve", lambda e, slot=slot, pt=pt: e.tensor_copy(
                        vtm[:, slot: slot + HD], pt[:, 0:HD]), reads=[ttr, vtm_tok])
                    pt_gate[it] = tcp
                    v_tok.append(ttr)
        p.barrier()

def emit_scores(p, st, C, layer, cfg):
    nc = p.nc
    with ExitStack() as ph:
        SB = Bufs()
        q_pan = _sb(ph, nc, f"sc_q{layer}", [128, KC * SUB], BF16)
        o_pan = _sb(ph, nc, f"sc_o{layer}", [128, KC * SUB], BF16)
        o_tmp = _sb(ph, nc, f"sc_ot{layer}", [128, SUB], BF16)
        pt_b = [_sb(ph, nc, f"sc_pt{layer}_{i}", [128, 2 * SUB], BF16) for i in range(2)]
        rbc = _sb(ph, nc, f"sc_rbc{layer}", [128, SUB], BF16)
        recb = _sb(ph, nc, f"sc_rec{layer}", [128, SUB], F32)
        recb2 = _sb(ph, nc, f"sc_rc2{layer}", [128, SUB], BF16)
        qsq = _sb(ph, nc, f"sc_qsq{layer}", [128, SUB], BF16)
        nrow = _sb(ph, nc, f"sc_nrow{layer}", [1, SUB], F32)
        nmax_q = _sb(ph, nc, f"sc_nq{layer}", [128, 4], F32)
        scs = _sb(ph, nc, f"sc_scs{layer}", [1, 8], F32)
        mkneg = _sb(ph, nc, f"sc_mk{layer}", [128, 1], F32)
        wo_t = [_sb(ph, nc, f"sc_wo{layer}_{i}", [128, KC * 128], BF16) for i in range(2)]
        if cfg["outb"] is not None:
            y1_t = [_sb(ph, nc, f"sc_y1{layer}_{i}", [128, SUB], F32) for i in range(2)]
        xf_t = [_sb(ph, nc, f"sc_xf{layer}_{i}", [128, SUB], F32) for i in range(2)]
        ff_t = [_sb(ph, nc, f"sc_ff{layer}_{i}", [128, SUB], F32) for i in range(2)]
        s1_t = [_sb(ph, nc, f"sc_s1{layer}_{i}", [128, SUB], F32) for i in range(2)]
        if cfg.get("rope_scratch"):
            SB.qraw = _sb(ph, nc, f"sc_qraw{layer}", [128, SUB], F32)
            SB.t1b = _sb(ph, nc, f"sc_t1{layer}", [128, SUB], F32)
            SB.t2b = _sb(ph, nc, f"sc_t2{layer}", [128, SUB], F32)
            SB.wq = [_sb(ph, nc, f"sc_wq{layer}_{i}", [128, KC * 128], BF16)
                     for i in range(2)]
        p_s = [_ps(ph, nc, f"sc_ps{layer}_{i}", [128, 2 * SUB]) for i in range(2)]
        p_oa = [_ps(ph, nc, f"sc_poa{layer}_{i}", [128, SUB]) for i in range(2)]
        p_misc = _ps(ph, nc, f"sc_pm{layer}", [128, SUB])
        p_y = _ps(ph, nc, f"sc_py{layer}", [128, SUB])
        p_n = p_y[0:1, 0:SUB]
        SB.p_s = p_s
        SB.ps_last = [None, None]

        fsc3 = cfg["fsc"][:, :].rearrange("(mc p) t -> p mc t", p=128)
        xin3 = cfg["resid_in"][:, :].rearrange("(mc p) t -> p mc t", p=128)
        xout3 = cfg["resid_out"][:, :].rearrange("(mc p) t -> p mc t", p=128)

        pt_last = [None, None]
        py_last = None
        wo_last = [None, None]
        xf_last = [None, None]
        ff_last = [None, None]
        s1_last = [None, None]
        y1_last = [None, None]
        oa_last = [None, None]
        pr_gate = None
        rc2_gate = None
        scr_gate = None
        otmp_last = None
        q_consumed = []
        o_consumed = []
        store_toks = []
        qsq_tok = None
        pn_gate = None
        kv_tok = cfg["kv_tok"]
        wo_pend = [None]
        pending_epi = []
        tmax = p.op("dve", lambda e: e.memset(nmax_q[:, :], 0.0))
        norm_jobs = []

        def emit_norm_chunk(mc, qtok):
            nonlocal pn_gate, tmax, qsq_tok
            tq = p.op("dve", lambda e, mc=mc: e.tensor_mul(
                qsq[:, :], q_pan[:, mc * SUB:(mc + 1) * SUB],
                q_pan[:, mc * SUB:(mc + 1) * SUB]), reads=[qtok, qsq_tok])
            tn, pn_gate, tmax = _norm_update(p, C, qsq[:, :], 128, p_n, nrow, nmax_q,
                                             [tmax], [tq], [pn_gate, py_last])
            qsq_tok = tn
            return tq

        qtoks_next = None
        for qp in range(NQP):
            if qtoks_next is None:
                qtoks = [cfg["build_q_chunk"](p, qp, mc, q_pan, SB, q_consumed)
                         for mc in range(KC)]
            else:
                qtoks = qtoks_next
            q_consumed = []
            norm_jobs = [(mc, qtoks[mc]) for mc in range(KC)]
            if qp == 0:
                for mc, qtok in norm_jobs:
                    q_consumed.append(emit_norm_chunk(mc, qtok))
                norm_jobs = []
            tb1 = p.op("dve", lambda e: e.tensor_mul(scs[0:1, 0:1], nmax_q[0:1, 0:1],
                                                     cfg["nmax_k"][0:1, 0:1]),
                       reads=[tmax, kv_tok], drain=True)
            tb2a = p.op("act", lambda e: e.activation(scs[0:1, 1:2], scs[0:1, 0:1],
                                                      AF.Ln, bias=C.fconst[0:1, 1:2]),
                        reads=[tb1])
            tb2 = p.op("act", lambda e: e.activation(scs[0:1, 4:5], scs[0:1, 1:2],
                                                     AF.Exp, scale=0.5),
                       reads=[tb2a], drain=True)
            tb3 = p.op("dve", lambda e: e.tensor_scalar_mul(scs[0:1, 2:3], scs[0:1, 4:5],
                                                            -0.125), reads=[tb2], drain=True)
            tb4 = p.op("pe", lambda e: e.matmul(p_misc[:, 0:1], C.ones_fr[0:1, :],
                                                scs[0:1, 2:3], start=True, stop=True),
                       reads=[tb3, pn_gate])
            tbound = p.op("dve", lambda e: e.tensor_copy(mkneg[:, 0:1], p_misc[:, 0:1]),
                          reads=[tb4], drain=True)
            pr_gate = tbound
            nkt = 4 * qp + 4
            o_gate = list(o_consumed)
            o_consumed = []
            odd_dma_toks = []
            for c in range(KC):
                hA, hB = 2 * c, 2 * c + 1
                rhs_qA = q_pan[0:HD, c * SUB:(c + 1) * SUB]
                rhs_qB = q_pan[HD:128, c * SUB:(c + 1) * SUB]
                tS_last = None

                def emit_S(kt):
                    i2 = kt % 2
                    ps = p_s[i2]
                    gate = [kv_tok, qtoks[c], SB.ps_last[i2]]
                    tSA = p.op("pe", lambda e, hA=hA, kt=kt, ps=ps, rhs_qA=rhs_qA: e.matmul(
                        ps[:, 0:SUB], cfg["k_lhsT"](hA, kt), rhs_qA,
                        start=True, stop=True, tile_position=(0, 0)), reads=gate)
                    tSB = p.op("pe", lambda e, hB=hB, kt=kt, ps=ps, rhs_qB=rhs_qB: e.matmul(
                        ps[:, SUB:2 * SUB], cfg["k_lhsT"](hB, kt), rhs_qB,
                        start=True, stop=True, tile_position=(64, 0)), reads=gate)
                    return tSA, tSB

                def emit_exp(kt, tSA, tSB):
                    i2 = kt % 2
                    ps = p_s[i2]
                    texpA = p.op("act", lambda e, ps=ps, i2=i2: e.activation(
                        pt_b[i2][:, 0:SUB], ps[:, 0:SUB], AF.Exp, bias=mkneg[:, 0:1],
                        scale=0.125), reads=[tSA, tbound, pt_last[i2]])
                    texp = p.op("act", lambda e, ps=ps, i2=i2: e.activation(
                        pt_b[i2][:, SUB:2 * SUB], ps[:, SUB:2 * SUB], AF.Exp,
                        bias=mkneg[:, 0:1], scale=0.125), reads=[tSB])
                    SB.ps_last[i2] = texp
                    tm0, tm1 = texpA, texp
                    if kt >= nkt - 4:
                        j = kt - (nkt - 4)
                        tm0 = p.op("pool", lambda e, i2=i2, j=j: e.tensor_mul(
                            pt_b[i2][:, 0:SUB], pt_b[i2][:, 0:SUB],
                            C.cmask[:, j * SUB:(j + 1) * SUB]), reads=[texpA])
                        tm1 = p.op("pool", lambda e, i2=i2, j=j: e.tensor_mul(
                            pt_b[i2][:, SUB:2 * SUB], pt_b[i2][:, SUB:2 * SUB],
                            C.cmask[:, j * SUB:(j + 1) * SUB]), reads=[texp])
                    return tm0, tm1

                def emit_PV(kt, tm0, tm1):
                    i2 = kt % 2
                    tPVA = p.op("pe", lambda e, hA=hA, kt=kt, i2=i2: e.matmul(
                        p_oa[0][0:HD + 1, :], cfg["vtm_slice"](hA, kt), pt_b[i2][:, 0:SUB],
                        start=(kt == 0), stop=(kt == nkt - 1)),
                        reads=[tm0, kv_tok] + ([oa_last[0]] if kt == 0 else []))
                    tPVB = p.op("pe", lambda e, hB=hB, kt=kt, i2=i2: e.matmul(
                        p_oa[1][0:HD + 1, :], cfg["vtm_slice"](hB, kt), pt_b[i2][:, SUB:2 * SUB],
                        start=(kt == 0), stop=(kt == nkt - 1)),
                        reads=[tm1] + ([oa_last[1]] if kt == 0 else []))
                    pt_last[i2] = tPVB
                    return tPVB

                sx = emit_S(0)
                while pending_epi:
                    pending_epi.pop(0)()
                ex_prev = emit_exp(0, *sx)
                tS_last = sx[1]
                for kt in range(1, nkt):
                    sx = emit_S(kt)
                    tS_last = sx[1]
                    emit_PV(kt - 1, *ex_prev)
                    ex_prev = emit_exp(kt, *sx)
                emit_PV(nkt - 1, *ex_prev)
                q_consumed.append(tS_last)
                def make_epilogue(h, hc, hoa, tPVl=pt_last[(nkt - 1) % 2]):
                    poa = p_oa[hoa]
                    def run():
                        nonlocal scr_gate, rc2_gate, pr_gate, otmp_last
                        tln = p.op("act", lambda e: e.activation(
                            recb[HD:HD + 1, :], poa[HD:HD + 1, :], AF.Ln),
                            reads=[tPVl, scr_gate], drain=True)
                        scr_gate = tln
                        trc2 = p.op("act", lambda e: e.activation(
                            recb2[HD:HD + 1, :], recb[HD:HD + 1, :], AF.Exp, scale=-1.0),
                            reads=[tln, rc2_gate], drain=True)
                        tbc = p.op("pe", lambda e: e.matmul(
                            p_misc[:, :], C.ones_br[HD:HD + 1, 0:128],
                            recb2[HD:HD + 1, :], start=True, stop=True),
                            reads=[trc2, pr_gate])
                        rc2_gate = tbc
                        trbc = p.op("dve", lambda e: e.tensor_copy(rbc[:, :], p_misc[:, :]),
                                    reads=[tbc])
                        pr_gate = trbc
                        if h % 2 == 0:
                            dest = o_pan[0:HD, hc * SUB:(hc + 1) * SUB]
                            tnrm = p.op("dve", lambda e: e.tensor_mul(
                                dest, poa[0:HD, :], rbc[0:HD, :]),
                                reads=[trbc] + o_gate, drain=True)
                        else:
                            tnrm = p.op("dve", lambda e: e.tensor_mul(
                                o_tmp[0:HD, :], poa[0:HD, :], rbc[0:HD, :]),
                                reads=[trbc, otmp_last], drain=True)
                            todd = p.dma("pool", lambda e: e.dma_start(
                                out=o_pan[HD:128, hc * SUB:(hc + 1) * SUB],
                                in_=o_tmp[0:HD, :]), reads=[tnrm] + o_gate)
                            otmp_last = todd
                            odd_dma_toks.append(todd)
                        oa_last[hoa] = tnrm
                    return run
                pending_epi.append(make_epilogue(hA, c, 0))
                pending_epi.append(make_epilogue(hB, c, 1))
                if norm_jobs and c < KC:
                    mc_, qtok_ = norm_jobs[c]
                    q_consumed.append(emit_norm_chunk(mc_, qtok_))
                    if c == KC - 1:
                        norm_jobs = []

            while pending_epi:
                pending_epi.pop(0)()
            opan_toks = [oa_last[0], oa_last[1]] + odd_dma_toks[-1:]
            def wo_load(mc):
                i = mc % 2
                return p.dma("pool", lambda e, mc=mc, i=i: e.dma_start(
                    out=wo_t[i][:, :].rearrange("p (kc m) -> p kc m", kc=KC),
                    in_=cfg["wo"](mc)), reads=[wo_last[i]])
            if wo_pend[0] is None:
                wo_pend[0] = wo_load(0)
            qtoks_next = [] if qp + 1 < NQP else None
            qc_gate = q_consumed
            q_consumed = []
            for mc in range(KC):
                i = mc % 2
                tw = wo_pend[0]
                wo_pend[0] = wo_load((mc + 1) % KC)
                tx = p.dma("sp", lambda e, mc=mc, qp=qp, i=i: e.dma_start(
                    out=xf_t[i][:, :], in_=xin3[:, mc, qp * SUB:(qp + 1) * SUB]),
                    reads=[xf_last[i]])
                tf = p.dma("sp", lambda e, mc=mc, qp=qp, i=i: e.dma_start(
                    out=ff_t[i][:, :], in_=fsc3[:, mc, qp * SUB:(qp + 1) * SUB]),
                    reads=[ff_last[i], cfg["fsc_tok"]])
                ts1 = p.op("dve", lambda e, i=i: e.tensor_add(
                    s1_t[i][:, :], xf_t[i][:, :], ff_t[i][:, :]),
                    reads=[tx, tf, s1_last[i]])
                xf_last[i] = ts1
                ff_last[i] = ts1
                tmy = None
                for kc in range(KC):
                    tmy = p.op("pe", lambda e, kc=kc, mc=mc, i=i: e.matmul(
                        p_y[:, :], wo_t[i][:, kc * 128:(kc + 1) * 128],
                        o_pan[:, kc * SUB:(kc + 1) * SUB],
                        start=(kc == 0), stop=(kc == KC - 1)),
                        reads=[tw, py_last, pn_gate] + opan_toks)
                o_consumed.append(tmy)
                wo_last[i] = tmy
                if cfg["outb"] is not None:
                    bbuf, btok = cfg["outb"]
                    ty1 = p.op("dve", lambda e, mc=mc, i=i, bbuf=bbuf: e.tensor_scalar(
                        y1_t[i][:, :], p_y[:, :], bbuf[0:128, mc:mc + 1], None, ALU.add),
                        reads=[tmy, btok, y1_last[i]])
                    py_last = ty1
                    txo = p.op("dve", lambda e, i=i: e.tensor_add(
                        s1_t[i][:, :], y1_t[i][:, :], s1_t[i][:, :]),
                        reads=[ty1, ts1], drain=True)
                    y1_last[i] = txo
                else:
                    txo = p.op("dve", lambda e, i=i: e.tensor_add(
                        s1_t[i][:, :], p_y[:, :], s1_t[i][:, :]),
                        reads=[tmy, ts1], drain=True)
                    py_last = txo
                tst = p.dma("sp", lambda e, mc=mc, qp=qp, i=i: e.dma_start(
                    out=xout3[:, mc, qp * SUB:(qp + 1) * SUB], in_=s1_t[i][:, :]),
                    reads=[txo])
                s1_last[i] = tst
                store_toks.append(tst)
                if qtoks_next is not None:
                    qtoks_next.append(cfg["build_q_chunk"](p, qp + 1, mc, q_pan, SB,
                                                           qc_gate))
        p.barrier()
        return store_toks


class _Stop(Exception):
    pass


def build_program(taps=(), upto="full"):
    nc = bass.Bass("TRN2", target_bir_lowering=False)
    dr = {}
    dr["xT"] = nc.dram_tensor("xT", [D, T], F32, kind="ExternalInput").ap()
    for n, sh in [("c_cos", [128, T]), ("c_sin", [128, T]), ("c_p128", [128, 128]),
                  ("c_cmask", [128, 4 * SUB]), ("c_ident", [128, 128]),
                  ("c_ohsel", [NE, NE * 128])]:
        dr[n] = nc.dram_tensor(n, sh, F32, kind="ExternalInput").ap()
    wshapes = dict(
        l0_qkvT=[D, 1152], l0_qkvb=[128, 10], l0_outT=[D, D], l0_outb=[128, 8],
        l0_WgT=[D, NE], l0_bg=[NE, 1], l0_w1T=[NE, D, 2 * D], l0_w2T=[NE, D, D],
        l1_Wdq=[D, 256], l1_Wuq=[256, D], l1_Wdkv=[D, 256], l1_Wuk=[256, D],
        l1_Wuv=[256, D], l1_Wo=[D, D], l1_router=[D, NE], l1_fc1=[NE, D, 2 * D],
        l1_fc2=[NE, D, D], l1_gate=[D, 2 * D], l1_up=[D, 2 * D], l1_down=[2 * D, D])
    for n, sh in wshapes.items():
        dr[n] = nc.dram_tensor(n, sh, F32, kind="ExternalInput").ap()
    resid = nc.dram_tensor("resid", [D, T], F32,
                           kind="ExternalOutput" if "resid" in taps else "Internal").ap()
    fsc = nc.dram_tensor("fsc", [D, T], F32,
                         kind="ExternalOutput" if "fsc" in taps else "Internal").ap()
    outT = nc.dram_tensor("outT", [D, T], F32, kind="ExternalOutput").ap()
    tap_dr = {}
    for tn in taps:
        if tn in ("resid", "fsc"):
            continue
        shape = {"xn": [128, KC * T], "call": [NE, T], "kdup": [128, T],
                 "kfm": [128, KC * T], "vtm0": [128, NKT * (HD + 1)],
                 "vtm1": [128, NKT * NH * (HD + 1)], "cq": [128, 2 * T],
                 "ckv": [128, 2 * T]}[tn.rstrip("01") if tn[-1] in "01" and not tn.startswith("vtm") else tn]
        tap_dr[tn] = nc.dram_tensor(f"tap_{tn}", shape, F32, kind="ExternalOutput").ap()

    try:
        nc = _assemble(nc, dr, resid, fsc, outT, tap_dr, taps, upto)
    except _Stop:
        pass
    return nc


def _assemble(nc, dr, resid, fsc, outT, tap_dr, taps, upto):
    with ExitStack() as st:
        p = Prog(nc, st)
        C = emit_consts(p, st, dr)

        def ck(name):
            if upto == name:
                p.barrier()
                p.emit()
                raise _Stop

        def tap(name, buf, cols):
            if name in tap_dr:
                with ExitStack() as th:
                    tb = _sb(th, nc, f"tapb_{name}", [128, 2048], F32)
                    np_ = buf.shape[0]
                    for c0 in range(0, cols, 2048):
                        w = min(2048, cols - c0)
                        tt = p.op("dve", lambda e, c0=c0, w=w, np_=np_: e.tensor_copy(
                            tb[0:np_, 0:w], buf[0:np_, c0:c0 + w]))
                        p.dma("sp", lambda e, c0=c0, w=w, np_=np_: e.dma_start(
                            out=tap_dr[name][0:np_, c0:c0 + w],
                            in_=tb[0:np_, 0:w]), reads=[tt])
                        p.barrier()

        if True:
            with ExitStack() as l0:
                qkvb_s = _sb(l0, nc, "qkvb0", [128, 10], F32)
                outb_s = _sb(l0, nc, "outb0", [128, 8], F32)
                xn = _sb(l0, nc, "xn0", [128, KC * T], BF16)
                tqb = p.dma("sp", lambda e: e.dma_start(out=qkvb_s[:, :], in_=dr["l0_qkvb"][:, :]))
                tob = p.dma("sp", lambda e: e.dma_start(out=outb_s[:, :], in_=dr["l0_outb"][:, :]))
                p.barrier()

                emit_ln(p, st, C, dr["xT"], xn, 0)
                tap("xn0", xn, KC * T)
                ck("ln0")

                with ExitStack() as lf:
                    c_all = _sb(lf, nc, "call0", [NE, T], BF16)
                    emit_routing(p, st, C, xn, c_all, dr["l0_WgT"], dr["l0_bg"], 0)
                    tap("call0", c_all, T)
                    ck("rt0")
                    w1T = dr["l0_w1T"]
                    w2T = dr["l0_w2T"]
                    experts = []
                    for e_ in range(NE):
                        experts.append(dict(
                            w1a=lambda ocp, e_=e_: _chunk3(w1T[e_])[:, :, ocp * 128:(ocp + 1) * 128],
                            w1b=lambda ocp, e_=e_: _chunk3(w1T[e_])[:, :, D + ocp * 128: D + (ocp + 1) * 128],
                            w2=lambda mc, e_=e_: _chunk3(w2T[e_])[:, :, mc * 128:(mc + 1) * 128],
                            use_c=True, e_idx=e_, c_tok=None))
                    emit_moe(p, st, C, xn, c_all, fsc, 0, experts)
                tap("fsc", None, 0)
                ck("ffn0")

                cosb = _sb(l0, nc, "cos0", [128, T], F32)
                sinb = _sb(l0, nc, "sin0", [128, T], F32)
                p128f = _sb(l0, nc, "p128f", [128, 128], F32)
                p.dma("sp", lambda e: e.dma_start(out=cosb[:, :], in_=dr["c_cos"][:, :]))
                p.dma("sp", lambda e: e.dma_start(out=sinb[:, :], in_=dr["c_sin"][:, :]))
                p.dma("sp", lambda e: e.dma_start(out=p128f[:, :], in_=dr["c_p128"][:, :]))
                p.barrier()
                kdup = _sb(l0, nc, "kdup0", [128, T], BF16, side="right")
                vtm0 = _sb(l0, nc, "vtm0", [128, NKT * (HD + 1)], BF16, side="right")
                nmax_k = _sb(l0, nc, "nmk0", [128, 4], F32, side="right")
                lw0 = dict(qkvT=dr["l0_qkvT"], qkvb_s=qkvb_s, qkvb_tok=tqb,
                           p128f=p128f, cos=cosb, sin=sinb)
                emit_kv_l0(p, st, C, lw0, xn, kdup, vtm0, nmax_k, 0)
                tap("kdup", kdup, T)
                ck("kv0")

                qkvT3 = _chunk3(dr["l0_qkvT"][:, :])

                def wq_load(SB, mc, wq_last):
                    i = mc % 2
                    return p.dma("pool", lambda e, mc=mc, i=i: e.dma_start(
                        out=SB.wq[i][:, :].rearrange("p (kc m) -> p kc m", kc=KC),
                        in_=qkvT3[:, :, mc * 128:(mc + 1) * 128]), reads=[wq_last[i]])

                def build_q0(p, qp, mc, q_pan, SB, q_consumed, wq_last=[None, None],
                             qraw_last=[], t1_last=[], t2_last=[], pend=[None]):
                    if True:
                        i = mc % 2
                        if pend[0] is None:
                            pend[0] = wq_load(SB, mc, wq_last)
                        tw = pend[0]
                        pend[0] = wq_load(SB, (mc + 1) % KC, wq_last)
                        tm = None
                        for kc in range(KC):
                            tm = p.op("pe", lambda e, kc=kc, qp=qp, i=i: e.matmul(
                                SB.p_s[0][:, 0:SUB], SB.wq[i][:, kc * 128:(kc + 1) * 128],
                                xn[:, kc * T + qp * SUB: kc * T + (qp + 1) * SUB],
                                start=(kc == 0), stop=(kc == KC - 1)),
                                reads=[tw, SB.ps_last[0]])
                        wq_last[i] = tm
                        tev = p.op("dve", lambda e, mc=mc: e.tensor_scalar(
                            SB.qraw[:, :], SB.p_s[0][:, 0:SUB],
                            qkvb_s[0:128, mc:mc + 1], None, ALU.add),
                            reads=[tm, tqb] + qraw_last)
                        SB.ps_last[0] = tev
                        tsw = p.op("pe", lambda e: e.matmul(
                            SB.p_s[1][:, 0:SUB], p128f[:, :], SB.qraw[:, :],
                            start=True, stop=True), reads=[tev, SB.ps_last[1]])
                        tt1 = p.op("dve", lambda e, qp=qp: e.tensor_mul(
                            SB.t1b[:, :], SB.qraw[:, :],
                            cosb[:, qp * SUB:(qp + 1) * SUB]),
                            reads=[tev] + t1_last, drain=True)
                        tt2 = p.op("dve", lambda e, qp=qp: e.tensor_mul(
                            SB.t2b[:, :], SB.p_s[1][:, 0:SUB],
                            sinb[:, qp * SUB:(qp + 1) * SUB]),
                            reads=[tsw] + t2_last, drain=True)
                        SB.ps_last[1] = tt2
                        tq = p.op("dve", lambda e, mc=mc: e.tensor_add(
                            q_pan[:, mc * SUB:(mc + 1) * SUB], SB.t1b[:, :], SB.t2b[:, :]),
                            reads=[tt1, tt2] + q_consumed, drain=True)
                        qraw_last[:] = [tsw, tt1]
                        t1_last[:] = [tq]
                        t2_last[:] = [tq]
                    return tq

                cfg0 = dict(
                    build_q_chunk=build_q0,
                    k_lhsT=lambda h, kt: kdup[64 * (h & 1):64 * (h & 1) + 64,
                                              kt * 128:(kt + 1) * 128],
                    vtm_slice=lambda h, kt: vtm0[:, kt * (HD + 1): (kt + 1) * (HD + 1)],
                    nmax_k=nmax_k, kv_tok=("dgp", p.counts["dgp"]),
                    wo=lambda mc: _chunk3(dr["l0_outT"][:, :])[:, :, mc * 128:(mc + 1) * 128],
                    outb=(outb_s, tob), resid_in=dr["xT"], resid_out=resid, fsc=fsc,
                    fsc_tok=("dgp", p.counts["dgp"]), rope_scratch=True)
                emit_scores(p, st, C, 0, cfg0)
                tap("resid", None, 0)
                ck("l0")

            with ExitStack() as l1:
                cq = _sb(l1, nc, "cq1", [128, 2 * T], BF16)
                with ExitStack() as ly:
                    ckv = _sb(ly, nc, "ckv1", [128, 2 * T], BF16)
                    with ExitStack() as lx:
                        xn1 = _sb(lx, nc, "xn1", [128, KC * T], BF16)
                        emit_ln(p, st, C, resid, xn1, 1)
                        tap("xn1", xn1, KC * T)
                        ck("ln1")
                        emit_lowrank_proj(p, st, C, xn1,
                                          [(dr["l1_Wdq"], cq), (dr["l1_Wdkv"], ckv)],
                                          256, "cqkv")
                        tap("cq", cq, 2 * T)
                        tap("ckv", ckv, 2 * T)
                        ck("proj1")

                        with ExitStack() as lf:
                            c_all = _sb(lf, nc, "call1", [NE, T], BF16)
                            emit_routing(p, st, C, xn1, c_all, dr["l1_router"], None, 1)
                            tap("call1", c_all, T)
                            ck("rt1")
                            fc1 = dr["l1_fc1"]
                            fc2 = dr["l1_fc2"]
                            experts = []
                            for e_ in range(NE):
                                experts.append(dict(
                                    w1a=lambda ocp, e_=e_: _chunk3(fc1[e_])[:, :, ocp * 128:(ocp + 1) * 128],
                                    w1b=lambda ocp, e_=e_: _chunk3(fc1[e_])[:, :, D + ocp * 128: D + (ocp + 1) * 128],
                                    w2=lambda mc, e_=e_: _chunk3(fc2[e_])[:, :, mc * 128:(mc + 1) * 128],
                                    use_c=True, e_idx=e_, c_tok=None))
                            for sh in range(2):
                                experts.append(dict(
                                    w1a=lambda ocp, sh=sh: _chunk3(dr["l1_gate"][:, :])[:, :, sh * D + ocp * 128: sh * D + (ocp + 1) * 128],
                                    w1b=lambda ocp, sh=sh: _chunk3(dr["l1_up"][:, :])[:, :, sh * D + ocp * 128: sh * D + (ocp + 1) * 128],
                                    w2=lambda mc, sh=sh: dr["l1_down"][:, :].rearrange(
                                        "(kc p) m -> p kc m", p=128)[:, sh * KC:(sh + 1) * KC,
                                                                     mc * 128:(mc + 1) * 128],
                                    use_c=False, e_idx=0, c_tok=None))
                            emit_moe(p, st, C, xn1, c_all, fsc, 1, experts)
                    ck("ffn1")
                    kfm = _sb(l1, nc, "kfm1", [128, KC * T], BF16, side="right")
                    vtm1 = _sb(l1, nc, "vtm1", [128, NKT * NH * (HD + 1)], BF16, side="right")
                    nmax_k1 = _sb(l1, nc, "nmk1", [128, 4], F32, side="right")
                    lw1 = dict(Wuk=dr["l1_Wuk"], Wuv=dr["l1_Wuv"])
                    emit_kv_l1(p, st, C, lw1, ckv, kfm, vtm1, nmax_k1, 1)
                    tap("kfm", kfm, KC * T)
                    ck("kv1")

                wuq = _sb(l1, nc, "wuq1", [128, 2 * D], BF16, side="right")
                twq = p.dma("pool", lambda e: e.dma_start(
                    out=wuq[:, :].rearrange("p (kc m) -> p kc m", kc=2),
                    in_=dr["l1_Wuq"][:, :].rearrange("(kc p) m -> p kc m", p=128)))

                def build_q1(p, qp, mc, q_pan, SB, q_consumed):
                    tm = None
                    for kc in range(2):
                        tm = p.op("pe", lambda e, kc=kc, mc=mc, qp=qp: e.matmul(
                            SB.p_s[0][:, 0:SUB],
                            wuq[:, (kc * KC + mc) * 128:(kc * KC + mc + 1) * 128],
                            cq[:, kc * T + qp * SUB: kc * T + (qp + 1) * SUB],
                            start=(kc == 0), stop=(kc == 1)),
                            reads=[twq, SB.ps_last[0]])
                    tq = p.op("dve", lambda e, mc=mc: e.tensor_copy(
                        q_pan[:, mc * SUB:(mc + 1) * SUB], SB.p_s[0][:, 0:SUB]),
                        reads=[tm] + q_consumed)
                    SB.ps_last[0] = tq
                    return tq

                cfg1 = dict(
                    build_q_chunk=build_q1,
                    k_lhsT=lambda h, kt: kfm[64 * (h & 1):64 * (h & 1) + 64,
                                             (h >> 1) * T + kt * 128:(h >> 1) * T + (kt + 1) * 128],
                    vtm_slice=lambda h, kt: vtm1[:, (kt * NH + h) * (HD + 1):
                                                 (kt * NH + h) * (HD + 1) + HD + 1],
                    nmax_k=nmax_k1, kv_tok=("dve", p.counts["dve"]),
                    wo=lambda mc: _chunk3(dr["l1_Wo"][:, :])[:, :, mc * 128:(mc + 1) * 128],
                    outb=None, resid_in=resid, resid_out=outT, fsc=fsc,
                    fsc_tok=("dgp", p.counts["dgp"]))
                emit_scores(p, st, C, 1, cfg1)

        p.barrier()
        p.emit()
    return nc


_CACHE = {}


def kernel(**inputs):
    consts = _host_consts()
    w = _prep_weights(inputs)
    x = np.asarray(inputs["L0_attn_x"], dtype=np.float32)
    key = "full"
    if key not in _CACHE:
        _CACHE[key] = build_program()
    nc = _CACHE[key]
    in_maps = []
    for b in range(8):
        m = dict(consts)
        m.update(w)
        m["xT"] = np.ascontiguousarray(x[b].T)
        in_maps.append(m)
    res = run_bass_kernel_spmd(nc, in_maps, core_ids=list(range(8)))
    out = np.empty((8, T, D), np.float32)
    for b in range(8):
        out[b] = res.results[b]["outT"].T
    return out
